# version 06
# baseline (speedup 1.0000x reference)
import numpy as np
import ml_dtypes
import concourse.bass as bass
import concourse.mybir as mybir
from concourse.bass_utils import run_bass_kernel_spmd
from contextlib import ExitStack

F32 = mybir.dt.float32
BF16 = mybir.dt.bfloat16
AF = mybir.ActivationFunctionType
ALU = mybir.AluOpType
AX = mybir.AxisListType
BF16NP = ml_dtypes.bfloat16
IM = np.float32

ENGINES = ("sync", "scalar", "vector", "tensor", "gpsimd")
ESEM = {"sync": "syn", "scalar": "act", "vector": "dve", "tensor": "pe", "gpsimd": "pool"}
DMA_POOL = {"sync": [f"dmas{i}" for i in range(3)],
            "gpsimd": [f"dmag{i}" for i in range(6)],
            "scalar": [f"dmac{i}" for i in range(3)]}
ST_POOL = [f"st{i}" for i in range(2)]
SEM_NAMES = ("pe", "act", "dve", "pool", "syn") + tuple(
    s for pool in DMA_POOL.values() for s in pool) + tuple(ST_POOL)

LN_EPS = 1e-5
NTILE = 32
TPS = 16
D = 1024


class Prog:

    def __init__(self, nc):
        self.nc = nc
        self.ops = {e: [] for e in ENGINES}
        self.cnt = {s: 0 for s in SEM_NAMES}
        self.handles = {}
        self.waits = []
        self.waited = {e: {s: -1 for s in SEM_NAMES} for e in ENGINES}
        self.last_tok = {e: None for e in ENGINES}
        self.dirty = {e: False for e in ENGINES}
        self.dma_rr = {}
        self.tag = ""
        self._fence_scratch = None

    def wait(self, e, tok):
        if tok is None:
            return
        s, v = tok
        if self.waited[e][s] >= v:
            return
        self.waited[e][s] = v
        self.waits.append((e, s, v))
        self.ops[e].append(lambda eng, s=s, v=v: eng.wait_ge(self.handles[s], v))

    def do(self, e, fn, deps=(), inc="auto", inc_by=1):
        for d in deps:
            self.wait(e, d)
        if inc == "auto":
            inc = ESEM[e] if e in ("vector", "scalar", "gpsimd") else None
        if inc is None:
            self.ops[e].append(fn)
            self.dirty[e] = True
            return None
        self.cnt[inc] += inc_by
        tok = (inc, self.cnt[inc])
        self.ops[e].append(
            lambda eng, fn=fn, s=inc, b=inc_by: fn(eng).then_inc(self.handles[s], b))
        self.last_tok[e] = tok
        self.dirty[e] = False
        return tok

    def dma(self, queue, fn, deps=(), pool=None):
        pool = DMA_POOL[queue] if pool is None else pool
        idx = self.dma_rr.get(queue, 0)
        self.dma_rr[queue] = idx + 1
        sem = pool[idx % len(pool)]
        prev = (sem, self.cnt[sem]) if self.cnt[sem] > 0 else None
        return self.do(queue, fn, deps=tuple(deps) + ((prev,) if prev else ()),
                       inc=sem, inc_by=16)

    def store(self, queue, fn, deps=()):
        idx = self.dma_rr.get("st", 0)
        self.dma_rr["st"] = idx + 1
        sem = ST_POOL[idx % len(ST_POOL)]
        prev = (sem, self.cnt[sem]) if self.cnt[sem] > 0 else None
        return self.do(queue, fn, deps=tuple(deps) + ((prev,) if prev else ()),
                       inc=sem, inc_by=16)

    def fence(self, e):
        if self.last_tok[e] is not None and not self.dirty[e]:
            return self.last_tok[e]
        sem = ESEM[e]
        if e in ("vector", "gpsimd"):
            return self.do(e, lambda eng: eng.engine_nop(), inc=sem)
        if e == "scalar":
            fs = self._fence_scratch
            return self.do(e, lambda eng: eng.copy(out=fs[0:1, 0:1], in_=fs[0:1, 0:1]),
                           inc=sem)
        return self.do(e, lambda eng: eng.nop(), inc=sem)

    def barrier(self, engines=("scalar", "vector", "tensor", "gpsimd", "sync")):
        toks = {e: self.fence(e) for e in engines}
        for e in engines:
            for o, t in toks.items():
                if o != e:
                    self.wait(e, t)

    def finalize_check(self):
        for e, s, v in self.waits:
            assert v <= self.cnt[s], f"wait on {s}>={v} but final count {self.cnt[s]} (engine {e})"

    def build(self, ctx):
        self.finalize_check()
        for s in SEM_NAMES:
            self.handles[s] = ctx.enter_context(self.nc.semaphore(s))
        block = ctx.enter_context(self.nc.Block())
        for e in ENGINES:
            ops = self.ops[e]
            if not ops:
                continue
            def runner(eng, ops=ops):
                for op in ops:
                    op(eng)
            getattr(block, e)(runner)



def alloc_common(P, ctx, evs=True):
    nc = P.nc
    sb = lambda name, shape, dt: ctx.enter_context(nc.sbuf_tensor(name, shape, dt))
    P._fence_scratch = sb("fence_scr", [1, 1], F32)
    P.R = sb("resid", [128, NTILE * D], BF16)
    P.XN = sb("xn", [128, 2 * D], BF16)
    P.XT = sb("xt", [128, D], BF16)
    P.STAT = sb("stat", [128, 12], F32)
    P.MV = sb("mv", [128, 2], F32)
    P.NEGMEAN = sb("negmean", [128, 1], F32)
    P.RSTD = sb("rstd", [128, 1], F32)
    P.IDN = sb("idn_sb", [128, 128], BF16)

    P.ONES1 = sb("ones1", [1, 128], F32)
    P.ONESC = sb("onesc", [128, 1], F32)
    t_o1 = P.do("vector", lambda v: v.memset(P.ONES1[0:1, :], 1.0), inc="dve")
    P.ones_tok = P.do("vector", lambda v: v.memset(P.ONESC[:, 0:1], 1.0), inc="dve")
    P.PS = [ctx.enter_context(nc.psum_tensor(f"ps{i}", [128, 512], F32)) for i in range(6)]
    P.PT = [ctx.enter_context(nc.psum_tensor(f"pt{i}", [128, 1024], BF16)) for i in range(2)]
    P.ps_free = [None] * 6
    P.pt_free = [None] * 2
    P.idn_tok = None


def load_identity(P, idn_dram):
    P.idn_tok = P.dma("sync", lambda s: s.dma_start(out=P.IDN[:], in_=idn_dram))


def load_residual(P, x_dram):
    toks = []
    for s in range(2):
        src = x_dram[s].rearrange("(t p) d -> p t d", p=128)
        dst = P.R[:, s * TPS * D:(s + 1) * TPS * D].rearrange("p (t d) -> p t d", d=D)
        toks.append(P.dma("gpsimd", lambda g, src=src, dst=dst: g.dma_start(out=dst, in_=src)))
    P.r_toks = toks
    return toks


def store_residual(P, y_dram, deps):
    toks = []
    for s in range(2):
        dst = y_dram[s].rearrange("(t p) d -> p t d", p=128)
        src = P.R[:, s * TPS * D:(s + 1) * TPS * D].rearrange("p (t d) -> p t d", d=D)
        toks.append(P.store("gpsimd",
                            lambda g, src=src, dst=dst: g.dma_start(out=dst, in_=src),
                            deps=deps))
    return toks


def r_tile(P, t):
    return P.R[:, t * D:(t + 1) * D]


def emit_ln_tile(P, t, xn_war=None, xn_buf=None, extra_dep=None):
    xn = P.XN[:, 0:D] if xn_buf is None else xn_buf
    rt = r_tile(P, t)
    prev_xn, prev_act = getattr(P, "ln_prev", (None, None))
    t1 = P.do("vector", lambda v: v.bn_stats(out=P.STAT[:, 0:6], in_=rt[:, 0:512]),
              deps=tuple(P.r_toks) + (prev_xn, extra_dep))
    t2 = P.do("vector", lambda v: v.bn_stats(out=P.STAT[:, 6:12], in_=rt[:, 512:1024]))
    t3 = P.do("vector", lambda v: v.bn_aggr(out=P.MV[:], in_=P.STAT[:]),
              deps=(t2, prev_act))
    t4 = P.do("vector", lambda v: v.tensor_scalar(out=P.NEGMEAN[:], in0=P.MV[:, 0:1],
                                                  scalar1=-1.0, scalar2=None, op0=ALU.mult),
              deps=(t3,))
    t5 = P.do("scalar", lambda s: s.activation(out=P.RSTD[:], in_=P.MV[:, 1:2],
                                               func=AF.Ln, bias=IM(LN_EPS), scale=IM(1.0)),
              deps=(t3,))
    t6 = P.do("scalar", lambda s: s.activation(out=P.RSTD[:], in_=P.RSTD[:],
                                               func=AF.Exp, bias=IM(0.0), scale=IM(-0.5)),
              deps=(t5,))
    tok = P.do("vector", lambda v: v.tensor_scalar(out=xn[:, :], in0=rt,
                                                   scalar1=P.NEGMEAN[:], scalar2=P.RSTD[:],
                                                   op0=ALU.add, op1=ALU.mult),
               deps=(t4, t6, xn_war))
    P.ln_prev = (tok, t6)
    return tok


def emit_transpose_1024(P, src, dst, deps, pt_bank=0, war_dep=None, nchunk=8):
    pt = P.PT[pt_bank]
    mm_toks = []
    for c in range(nchunk):
        dd = (list(deps) + [P.pt_free[pt_bank], P.idn_tok]) if c == 0 else ()
        mm_toks.append(P.do("tensor",
                            lambda t, c=c: t.transpose(pt[:, c * 128:(c + 1) * 128],
                                                       src[:, c * 128:(c + 1) * 128],
                                                       P.IDN[:]),
                            deps=dd, inc="pe" if c == nchunk - 1 else None))
    w = 128 * nchunk
    ev = P.do("vector", lambda v: v.tensor_copy(out=dst[:, 0:w], in_=pt[:, 0:w]),
              deps=(mm_toks[-1], war_dep), inc="dve")
    P.pt_free[pt_bank] = ev
    return ev


class XnPipe:

    def __init__(self, P, make_xn, T):
        self.P, self.make_xn, self.T = P, make_xn, T
        self.xn_tok, self.xt_tok = {}, {}
        self.xn_reader, self.xt_reader = {}, {}
        self._emit(0)

    def xn_ap(self, t):
        sl = t % 2
        return self.P.XN[:, sl * D:(sl + 1) * D]

    def xt_ap(self, t):
        return self.P.XT[:, 0:D]

    def _emit(self, t):
        tok = self.make_xn(t, self.xn_ap(t), self.xn_reader.get(t - 2))
        tr = emit_transpose_1024(self.P, self.xn_ap(t), self.xt_ap(t), (tok,),
                                 pt_bank=t % 2, war_dep=self.xt_reader.get(t - 1))
        self.xn_tok[t], self.xt_tok[t] = tok, tr
        self.xn_reader[t] = tr

    def done(self, t, reader_tok):
        self.xt_reader[t] = reader_tok
        if t + 1 < self.T:
            self._emit(t + 1)

    def set_xn_reader(self, t, tok):
        self.xn_reader[t] = tok

    def set_xt_reader(self, t, tok):
        self.xt_reader[t] = tok


def emit_proj_1024(P, xT, w_sbuf, w_tok, out_banks, fout, extra_deps=(), nk=8,
                   accumulate=False):
    toks = []
    for bi, bank in enumerate(out_banks):
        ncols = min(512, fout - 512 * bi)
        ps = P.PS[bank]
        for k in range(nk):
            deps = []
            if k == 0:
                deps = ([] if accumulate else [P.ps_free[bank]]) + [w_tok, *extra_deps]
            tok = P.do("tensor",
                       lambda t, k=k, bi=bi, ncols=ncols, ps=ps:
                       t.matmul(ps[:, 0:ncols],
                                xT[:, k * 128:(k + 1) * 128],
                                w_sbuf[:, k * fout + 512 * bi: k * fout + 512 * bi + ncols],
                                start=(k == 0 and not accumulate), stop=(k == nk - 1)),
                       deps=deps, inc="pe" if k == nk - 1 else None)
        toks.append(tok)
    return toks


class WBuf:

    def __init__(self, P, ctx, name, k, f):
        assert k % 128 == 0
        self.k, self.f = k, f
        self.nchunk = k // 128
        self.sb = ctx.enter_context(
            P.nc.sbuf_tensor(P.tag + name, [128, self.nchunk * f], BF16))
        self.tok = None

    def load(self, P, dram_ap):
        src = dram_ap.rearrange("(c p) f -> p c f", p=128)
        dst = self.sb[:, :].rearrange("p (c f) -> p c f", f=self.f)
        self.tok = P.dma("gpsimd", lambda g: g.dma_start(out=dst, in_=src))
        return self.tok

    def chunk(self, c, lo=0, hi=None):
        hi = self.f if hi is None else hi
        return self.sb[:, c * self.f + lo: c * self.f + hi]


def load_row_f32(P, ctx, name, dram_ap, n):
    sb = ctx.enter_context(P.nc.sbuf_tensor(P.tag + name, [1, n], F32))
    tok = P.dma("sync", lambda s: s.dma_start(out=sb[0:1, :], in_=dram_ap))
    return sb, tok


def emit_bcast_row(P, ctx, name, row_sb, row_tok, n, scale=None, banks=(0, 1)):
    out = ctx.enter_context(P.nc.sbuf_tensor(P.tag + name, [128, n], F32))
    toks = []
    for ci in range((n + 511) // 512):
        lo, hi = ci * 512, min(n, (ci + 1) * 512)
        bank = banks[ci % len(banks)]
        mm = P.do("tensor", lambda t, lo=lo, hi=hi, bank=bank: t.matmul(
            P.PS[bank][:, 0:hi - lo], P.ONES1[0:1, :], row_sb[0:1, lo:hi],
            start=True, stop=True),
            deps=(row_tok, P.ones_tok, P.ps_free[bank]), inc="pe")
        if scale is None:
            ev = P.do("vector", lambda v, lo=lo, hi=hi, bank=bank: v.tensor_copy(
                out=out[:, lo:hi], in_=P.PS[bank][:, 0:hi - lo]), deps=(mm,), inc="dve")
        else:
            ev = P.do("vector", lambda v, lo=lo, hi=hi, bank=bank: v.tensor_scalar(
                out=out[:, lo:hi], in0=P.PS[bank][:, 0:hi - lo],
                scalar1=float(scale), scalar2=None, op0=ALU.mult), deps=(mm,), inc="dve")
        P.ps_free[bank] = ev
        toks.append(ev)
    return out, toks[-1]



def emit_l2ffn_prep(P, ctx, aps):
    nc = P.nc
    sb = lambda name, shape, dt: ctx.enter_context(nc.sbuf_tensor(P.tag + name, shape, dt))
    wb = WBuf(P, ctx, "l2f_w", 1024, 1024)
    wtok = wb.load(P, aps["W"])
    ld_t = sb("l2f_ldT", [128, 4], BF16)
    lu_t = sb("l2f_luT", [128, 50], BF16)
    da = sb("l2f_da", [128, 1024], BF16)
    ua = WBuf(P, ctx, "l2f_ua", 1024, 50)
    t_ld = P.dma("gpsimd", lambda g: g.dma_start(
        out=ld_t[0:100, :], in_=aps["lora_down"].rearrange("r d -> d r"),
        allow_slow_non_contiguous=True))
    t_lu = P.dma("gpsimd", lambda g: g.dma_start(
        out=lu_t[0:4, :], in_=aps["lora_up"].rearrange("u r -> r u"),
        allow_slow_non_contiguous=True))
    t_da = P.dma("gpsimd", lambda g: g.dma_start(out=da[0:100, :], in_=aps["down_aux"]))
    t_ua = ua.load(P, aps["up_aux"])

    t1 = sb("l2f_t1", [128, 1024], BF16)
    t2 = sb("l2f_t2", [128, 1024], BF16)
    uaT = sb("l2f_uaT", [128, 8 * 128], BF16)
    wsum = sb("l2f_wsum", [128, 8 * 1024], BF16)
    wsumT = sb("l2f_wsumT", [128, 8 * 1024], BF16)

    ps, pt = P.PS, P.PT
    mm1 = []
    for half in range(2):
        tok = P.do("tensor", lambda t, h=half: t.matmul(
            ps[h][0:4, 0:512], ld_t[0:100, :], da[0:100, h * 512:(h + 1) * 512],
            start=True, stop=True),
            deps=(t_ld, t_da, P.ps_free[half]), inc="pe")
        mm1.append(tok)
    ev1 = P.do("vector", lambda v: v.tensor_copy(out=t1[0:4, 0:512], in_=ps[0][0:4, 0:512]),
               deps=(mm1[0],), inc="dve")
    ev1b = P.do("vector", lambda v: v.tensor_copy(out=t1[0:4, 512:1024], in_=ps[1][0:4, 0:512]),
                deps=(mm1[1],), inc="dve")
    mm2 = []
    for half in range(2):
        tok = P.do("tensor", lambda t, h=half: t.matmul(
            ps[2 + h][0:50, 0:512], lu_t[0:4, :], t1[0:4, h * 512:(h + 1) * 512],
            start=True, stop=True),
            deps=(t_lu, ev1, ev1b, P.ps_free[2 + half]), inc="pe")
        mm2.append(tok)
    ev2 = P.do("vector", lambda v: v.tensor_copy(out=t2[0:50, 0:512], in_=ps[2][0:50, 0:512]),
               deps=(mm2[0],), inc="dve")
    ev2b = P.do("vector", lambda v: v.tensor_copy(out=t2[0:50, 512:1024], in_=ps[3][0:50, 0:512]),
                deps=(mm2[1],), inc="dve")
    tr_toks = []
    for c in range(8):
        tok = P.do("tensor", lambda t, c=c: t.transpose(
            pt[0][0:50, c * 128:(c + 1) * 128], ua.chunk(c), P.IDN[:]),
            deps=(t_ua, P.idn_tok, P.pt_free[0]) if c == 0 else (), inc="pe" if c == 7 else None)
        tr_toks.append(tok)
    evT = P.do("vector", lambda v: v.tensor_copy(out=uaT[0:50, :], in_=pt[0][0:50, :]),
               deps=(tr_toks[-1],), inc="dve")
    P.pt_free[0] = evT
    for c in range(8):
        for half in range(2):
            bank = 4 + half
            mm = P.do("tensor", lambda t, c=c, h=half: t.matmul(
                P.PS[4 + h][:, 0:512], uaT[0:50, c * 128:(c + 1) * 128],
                t2[0:50, h * 512:(h + 1) * 512], start=True, stop=True),
                deps=(evT, ev2, ev2b, P.ps_free[bank]), inc="pe")
            ev = P.do("vector", lambda v, c=c, h=half: v.scalar_tensor_tensor(
                out=wsum[:, c * 1024 + h * 512: c * 1024 + (h + 1) * 512],
                in0=P.PS[4 + h][:, 0:512], scalar=1.0,
                in1=wb.chunk(c, h * 512, (h + 1) * 512),
                op0=ALU.mult, op1=ALU.add),
                deps=(mm, wtok), inc="dve")
            P.ps_free[bank] = ev
    wsum_tok = P.last_tok["vector"]
    for ci in range(8):
        pt_b = ci % 2
        for co in range(8):
            tok = P.do("tensor", lambda t, ci=ci, co=co, pt_b=pt_b: t.transpose(
                P.PT[pt_b][:, co * 128:(co + 1) * 128],
                wsum[:, co * 1024 + ci * 128: co * 1024 + (ci + 1) * 128],
                P.IDN[:]),
                deps=(wsum_tok, P.pt_free[pt_b]) if co == 0 else (),
                inc="pe" if co == 7 else None)
        ev = P.do("vector", lambda v, ci=ci, pt_b=pt_b: v.tensor_copy(
            out=wsumT[:, ci * 1024:(ci + 1) * 1024], in_=P.PT[pt_b][:, :]),
            deps=(tok,), inc="dve")
        P.pt_free[pt_b] = ev
    wsumT_tok = P.last_tok["vector"]
    return wsum, wsumT, wsumT_tok


def emit_l2ffn(P, ctx, aps, prep):
    _, wsumT, wtok = prep
    EVS = ctx.enter_context(P.nc.sbuf_tensor(P.tag + "l2f_evs", [128, 1024], F32))
    brow, btok = load_row_f32(P, ctx, "l2f_b", aps["b"], 1024)
    bb, bb_tok = emit_bcast_row(P, ctx, "l2f_bb", brow, btok, 1024)
    pipe = XnPipe(P, lambda t, ap, war: emit_ln_tile(P, t, xn_war=war, xn_buf=ap),
                  NTILE)
    for t in range(NTILE):
        banks = [2 * (t % 2), 1 + 2 * (t % 2)]
        mm = emit_proj_1024(P, pipe.xt_ap(t), wsumT, wtok, out_banks=banks, fout=1024,
                            extra_deps=(pipe.xt_tok[t],))
        pipe.done(t, mm[-1])
        rt = r_tile(P, t)
        for bi, bank in enumerate(banks):
            lo, hi = bi * 512, (bi + 1) * 512
            e1 = P.do("vector", lambda v, bank=bank, lo=lo, hi=hi: v.tensor_tensor(
                out=EVS[:, lo:hi], in0=P.PS[bank][:, 0:512], in1=bb[:, lo:hi], op=ALU.add),
                deps=(mm[bi], bb_tok, P.ps_free[bank]), inc="dve")
            e2 = P.do("vector", lambda v, bank=bank, lo=lo, hi=hi, rt=rt: v.scalar_tensor_tensor(
                out=rt[:, lo:hi], in0=EVS[:, lo:hi], scalar=0.5, in1=rt[:, lo:hi],
                op0=ALU.mult, op1=ALU.add),
                deps=(e1,), inc="dve")
            P.ps_free[bank] = e2
    P.barrier()


def emit_L2_ffn(P, ctx, aps):
    prep = emit_l2ffn_prep(P, ctx, aps)
    emit_l2ffn(P, ctx, aps, prep)


def emit_grid_transpose(P, ctx, src, dst, src_tok, name=None):
    ev = None
    for ci in range(8):
        pt_b = ci % 2
        for co in range(8):
            tok = P.do("tensor", lambda t, ci=ci, co=co, pt_b=pt_b: t.transpose(
                P.PT[pt_b][:, co * 128:(co + 1) * 128],
                src[:, co * 1024 + ci * 128: co * 1024 + (ci + 1) * 128],
                P.IDN[:]),
                deps=(src_tok, P.pt_free[pt_b], P.idn_tok) if co == 0 else (),
                inc="pe" if co == 7 else None)
        ev = P.do("vector", lambda v, ci=ci, pt_b=pt_b: v.tensor_copy(
            out=dst[:, ci * 1024:(ci + 1) * 1024], in_=P.PT[pt_b][:, :]),
            deps=(tok,), inc="dve")
        P.pt_free[pt_b] = ev
    return ev



def emit_L3_ffn(P, ctx, aps):
    nc = P.nc
    sb = lambda name, shape, dt: ctx.enter_context(nc.sbuf_tensor(P.tag + name, shape, dt))
    wf = sb("l3f_wf", [128, 8 * 1024], F32)
    WQ = sb("l3f_wq", [128, 8 * 1024], BF16)
    WQT = sb("l3f_wqT", [128, 8 * 1024], BF16)
    U = sb("l3f_u", [128, 1024], F32)
    U2 = sb("l3f_u2", [128, 1024], F32)
    SG = sb("l3f_sg", [128, 1024], BF16)
    MSK = sb("l3f_msk", [128, 1024], BF16)
    ABS1 = sb("l3f_abs1", [128, 9], F32)
    WS = sb("l3f_ws", [1, 1], F32)
    INVB = sb("l3f_invb", [128, 1], F32)
    WSB = sb("l3f_wsb", [128, 1], F32)
    ASC = sb("l3f_asc", [128, 1], F32)
    RECA = sb("l3f_reca", [128, 1], F32)
    SV = sb("l3f_sv", [128, 2], F32)
    Z = sb("l3f_z", [128, 1024], F32)
    AQ = sb("l3f_aq", [128, 1024], BF16)

    src = aps["weight"].rearrange("(c p) f -> p c f", p=128)
    wtok = P.dma("sync", lambda s: s.dma_start(
        out=wf[:, :].rearrange("p (c f) -> p c f", f=1024), in_=src))

    tprev = None
    for c in range(8):
        tprev = P.do("vector", lambda v, c=c: v.reduce_sum(
            out=ABS1[:, c:c + 1], in_=wf[:, c * 1024:(c + 1) * 1024], axis=AX.X,
            apply_absolute_value=True), deps=(wtok,))
    t_rs = P.do("vector", lambda v: v.reduce_sum(out=ABS1[:, 8:9], in_=ABS1[:, 0:8],
                                                 axis=AX.X), deps=(tprev,))
    mm_ws = P.do("tensor", lambda t: t.matmul(P.PS[0][0:1, 0:1], ABS1[:, 8:9],
                                              P.ONESC[:, 0:1], start=True, stop=True),
                 deps=(t_rs, P.ones_tok, P.ps_free[0]), inc="pe")
    t_ws = P.do("vector", lambda v: v.tensor_scalar(
        out=WS[0:1, 0:1], in0=P.PS[0][0:1, 0:1], scalar1=1.0 / (1024.0 * 1024.0),
        scalar2=1e-8, op0=ALU.mult, op1=ALU.add), deps=(mm_ws,))
    P.ps_free[0] = t_ws
    t_inv = P.do("vector", lambda v: v.reciprocal(out=WS[0:1, 0:1], in_=WS[0:1, 0:1]),
                 deps=(t_ws,))
    mm_b1 = P.do("tensor", lambda t: t.matmul(P.PS[0][:, 0:1], P.ONES1[0:1, :],
                                              WS[0:1, 0:1], start=True, stop=True),
                 deps=(t_inv, P.ps_free[0]), inc="pe")
    t_invb = P.do("vector", lambda v: v.tensor_copy(out=INVB[:, 0:1], in_=P.PS[0][:, 0:1]),
                  deps=(mm_b1,))
    P.ps_free[0] = t_invb
    t_wsb0 = P.do("vector", lambda v: v.reciprocal(out=WSB[:, 0:1], in_=INVB[:, 0:1]),
                  deps=(t_invb,))
    t_wsb = P.do("vector", lambda v: v.tensor_scalar(out=WSB[:, 0:1], in0=WSB[:, 0:1],
                                                     scalar1=0.5 / 127.0, scalar2=None,
                                                     op0=ALU.mult), deps=(t_wsb0,))
    wq_tok = None
    prev_sg = None
    prev_mk = None
    for c in range(8):
        t_u = P.do("vector", lambda v, c=c: v.tensor_scalar(
            out=U[:, :], in0=wf[:, c * 1024:(c + 1) * 1024], scalar1=INVB[:, 0:1],
            scalar2=None, op0=ALU.mult), deps=(t_invb, wq_tok, prev_sg))
        t_sg = P.do("scalar", lambda s: s.sign(out=SG[:, :], in_=U[:, :]), deps=(t_u,))
        t_sq = P.do("scalar", lambda s: s.square(out=U2[:, :], in_=U[:, :]),
                    deps=(t_u, prev_mk))
        prev_sg = t_sq
        t_mk = P.do("vector", lambda v: v.tensor_scalar(
            out=MSK[:, :], in0=U2[:, :], scalar1=0.25, scalar2=None,
            op0=ALU.is_ge), deps=(t_sq,))
        wq_tok = P.do("vector", lambda v, c=c: v.tensor_tensor(
            out=WQ[:, c * 1024:(c + 1) * 1024], in0=SG[:, :], in1=MSK[:, :],
            op=ALU.mult), deps=(t_sg, t_mk))
        prev_mk = wq_tok
    wqT_tok = emit_grid_transpose(P, ctx, WQ, WQT, wq_tok)

    sv_toks = {}

    def mk_l3f(t, ap, war):
        sl = t % 2
        xn_tok = emit_ln_tile(P, t, xn_war=war)
        t_am = P.do("vector", lambda v: v.tensor_reduce(
            out=ASC[:, 0:1], in_=P.XN[:, 0:1024], axis=AX.X, op=ALU.max,
            apply_absolute_value=True), deps=(xn_tok,))
        t_am2 = P.do("vector", lambda v: v.tensor_scalar(
            out=ASC[:, 0:1], in0=ASC[:, 0:1], scalar1=1e-8, scalar2=None,
            op0=ALU.add), deps=(t_am,))
        t_rec = P.do("vector", lambda v: v.reciprocal(out=RECA[:, 0:1], in_=ASC[:, 0:1]),
                     deps=(t_am2,))
        t_z = P.do("vector", lambda v: v.tensor_scalar(
            out=Z[:, :], in0=P.XN[:, 0:1024], scalar1=RECA[:, 0:1], scalar2=127.0,
            op0=ALU.mult, op1=ALU.mult), deps=(t_rec,))
        t_aq = P.do("vector", lambda v, ap=ap: v.tensor_scalar(
            out=ap, in0=Z[:, :], scalar1=8388608.0, scalar2=-8388608.0,
            op0=ALU.add, op1=ALU.add), deps=(t_z, war))
        sv_toks[t] = P.do("vector", lambda v, sl=sl: v.tensor_tensor(
            out=SV[:, sl:sl + 1], in0=ASC[:, 0:1], in1=WSB[:, 0:1], op=ALU.mult),
            deps=(t_wsb, t_am2, sv_toks.get((t - 2, "rd"))))
        return t_aq

    pipe = XnPipe(P, mk_l3f, NTILE)
    for t in range(NTILE):
        sl = t % 2
        banks = [2 * sl, 1 + 2 * sl]
        mm = emit_proj_1024(P, pipe.xt_ap(t), WQT, wqT_tok, out_banks=banks, fout=1024,
                            extra_deps=(pipe.xt_tok[t],))
        pipe.done(t, mm[-1])
        rt = r_tile(P, t)
        ev = None
        for bi, bank in enumerate(banks):
            lo, hi = bi * 512, (bi + 1) * 512
            ev = P.do("vector", lambda v, bank=bank, lo=lo, hi=hi, rt=rt, sl=sl:
                      v.scalar_tensor_tensor(
                          out=rt[:, lo:hi], in0=P.PS[bank][:, 0:512],
                          scalar=SV[:, sl:sl + 1],
                          in1=rt[:, lo:hi], op0=ALU.mult, op1=ALU.add),
                      deps=(mm[bi], sv_toks[t], P.ps_free[bank]))
            P.ps_free[bank] = ev
        sv_toks[(t, "rd")] = ev
    P.barrier()



def emit_L1_ffn(P, ctx, aps):
    nc = P.nc
    sb = lambda name, shape, dt: ctx.enter_context(nc.sbuf_tensor(P.tag + name, shape, dt))
    EVS = sb("evs", [128, 1024], F32)
    win = WBuf(P, ctx, "l1f_win", 1024, 64)
    wtok = win.load(P, aps["Win"])
    wout = sb("l1f_wout", [64, 1024], BF16)
    wout_tok = P.dma("gpsimd", lambda g: g.dma_start(out=wout[0:64, :], in_=aps["Wout"]))

    rows = {}
    for nm, n in [("bin_", 64), ("W1", 1024), ("b1", 1024), ("W2", 1024),
                  ("b2", 64), ("bout", 1024)]:
        rows[nm] = load_row_f32(P, ctx, f"l1f_{nm}r", aps[nm], n)
    BINB, t_binb = emit_bcast_row(P, ctx, "l1f_binb", *rows["bin_"], 64)
    W1B, t_w1b = emit_bcast_row(P, ctx, "l1f_w1b", *rows["W1"], 1024)
    B1B, t_b1b = emit_bcast_row(P, ctx, "l1f_b1b", *rows["b1"], 1024)
    W2B, t_w2b = emit_bcast_row(P, ctx, "l1f_w2b", *rows["W2"], 1024)
    B2B, t_b2b = emit_bcast_row(P, ctx, "l1f_b2b", *rows["b2"], 64)
    BOUTB, t_boutb = emit_bcast_row(P, ctx, "l1f_boutb", *rows["bout"], 1024)

    S = sb("l1f_s", [128, 64], F32)
    H = sb("l1f_h", [128, 1024], F32)
    H3 = sb("l1f_h3", [128, 1024], F32)
    G1 = sb("l1f_g1", [128, 1024], F32)
    G = sb("l1f_g", [128, 64], BF16)
    GT = sb("l1f_gt", [64, 128], BF16)

    xn_war = None
    xt_war = None
    gt_war = None
    for t in range(NTILE):
        xn_tok = emit_ln_tile(P, t, xn_war=xn_war)
        xt_tok = emit_transpose_1024(P, P.XN, P.XT, (xn_tok,), pt_bank=t % 2,
                                     war_dep=xt_war)
        xn_war = xt_tok
        bank_s = 4
        mm_s = emit_proj_1024(P, P.XT, win.sb, wtok, out_banks=[bank_s], fout=64,
                              extra_deps=(xt_tok,))
        xt_war = mm_s[-1]
        t_s = P.do("vector", lambda v: v.tensor_tensor(
            out=S[:, :], in0=P.PS[bank_s][:, 0:64], in1=BINB[:, :], op=ALU.add),
            deps=(mm_s[0], t_binb, P.ps_free[bank_s]))
        P.ps_free[bank_s] = t_s
        s_view = S[:, :].unsqueeze(2).broadcast_to([128, 64, 16])
        w1_view = W1B[:, :].rearrange("p (m j) -> p m j", j=16)
        h_view = H[:, :].rearrange("p (m j) -> p m j", j=16)
        t_h = P.do("vector", lambda v: v.tensor_tensor(
            out=h_view, in0=s_view, in1=w1_view, op=ALU.mult), deps=(t_s, t_w1b))
        t_h2 = P.do("vector", lambda v: v.tensor_tensor(
            out=H[:, :], in0=H[:, :], in1=B1B[:, :], op=ALU.add), deps=(t_h, t_b1b))
        t_h3 = P.do("scalar", lambda s: s.activation(out=H3[:, :], in_=H[:, :],
                                                     func=AF.Relu), deps=(t_h2,))
        t_g1 = P.do("vector", lambda v: v.tensor_tensor(
            out=G1[:, :], in0=H3[:, :], in1=W2B[:, :], op=ALU.mult), deps=(t_h3, t_w2b))
        g1_view = G1[:, :].rearrange("p (m j) -> p m j", j=16)
        t_gr = P.do("vector", lambda v: v.tensor_reduce(
            out=G1[:, 0:64], in_=g1_view, axis=AX.X, op=ALU.add), deps=(t_g1,))
        t_g = P.do("vector", lambda v: v.tensor_tensor(
            out=G[:, :], in0=G1[:, 0:64], in1=B2B[:, :], op=ALU.add),
            deps=(t_gr, t_b2b, gt_war))
        pt_b = t % 2
        tr = P.do("tensor", lambda t_, pt_b=pt_b: t_.transpose(
            P.PT[pt_b][0:64, 0:128], G[:, :], P.IDN[:]),
            deps=(t_g, P.pt_free[pt_b], P.idn_tok), inc="pe")
        t_gt = P.do("vector", lambda v, pt_b=pt_b: v.tensor_copy(
            out=GT[0:64, :], in_=P.PT[pt_b][0:64, 0:128]), deps=(tr, gt_war))
        P.pt_free[pt_b] = t_gt
        rt = r_tile(P, t)
        for bi in range(2):
            bank = bi
            lo, hi = bi * 512, (bi + 1) * 512
            mm_o = P.do("tensor", lambda t_, bank=bank, lo=lo, hi=hi: t_.matmul(
                P.PS[bank][:, 0:512], GT[0:64, :], wout[0:64, lo:hi],
                start=True, stop=True),
                deps=(t_gt, wout_tok, P.ps_free[bank]), inc="pe")
            gt_war = mm_o
            e1 = P.do("vector", lambda v, bank=bank, lo=lo, hi=hi: v.tensor_tensor(
                out=EVS[:, lo:hi], in0=P.PS[bank][:, 0:512], in1=BOUTB[:, lo:hi],
                op=ALU.add), deps=(mm_o, t_boutb, P.ps_free[bank]))
            e2 = P.do("vector", lambda v, lo=lo, hi=hi, rt=rt: v.scalar_tensor_tensor(
                out=rt[:, lo:hi], in0=EVS[:, lo:hi], scalar=0.5, in1=rt[:, lo:hi],
                op0=ALU.mult, op1=ALU.add), deps=(e1,))
            P.ps_free[bank] = e2
    P.barrier()



def emit_L0_ffn(P, ctx, aps):
    nc = P.nc
    sb = lambda name, shape, dt: ctx.enter_context(nc.sbuf_tensor(P.tag + name, shape, dt))
    EVS = sb("evs", [128, 1024], F32)
    COEF = sb("moe_coef", [128, 8 * NTILE], F32)
    MRS = sb("moe_mrs", [128, 2 * NTILE], F32)
    PRB = sb("moe_prb", [128, 8], F32)
    SUME = sb("moe_sume", [128, 1], F32)
    NMX = sb("moe_nmx", [128, 1], F32)
    RCP = sb("moe_rcp", [128, 1], F32)
    EQ = sb("moe_eq", [128, 8], F32)
    PU2 = sb("moe_pu2", [128, 8], F32)
    M2 = sb("moe_m2", [128, 1], F32)
    MK2 = sb("moe_mk2", [128, 8], F32)
    SIL = sb("moe_sil", [128, 1024], F32)
    INTER = sb("moe_inter", [128, 4 * 1024], BF16)
    OUT = [sb(f"moe_out{i}", [128, 1024], F32) for i in range(2)]
    PRT = [sb(f"moe_prt{i}", [128, 1024], F32) for i in range(2)]
    WGU4 = sb("moe_wgu4", [128, 4 * 8 * 1024], BF16)
    WD4 = sb("moe_wd4", [128, 4 * 4 * 1024], BF16)
    WSH = WGU4[:, 0:8 * 2048]
    WSHD = WGU4[:, 8 * 2048:8 * 2048 + 8 * 1024]
    wr = WBuf(P, ctx, "moe_wr", 1024, 8)

    part = nc.dram_tensor(P.tag + "moe_part", [NTILE, 128, 1024], F32, kind="Internal").ap()

    wr_tok = wr.load(P, aps["router"])
    xn_war, xt_war = None, None
    for t in range(NTILE):
        xn_tok = emit_ln_tile(P, t, xn_war=xn_war)
        c_m = P.do("vector", lambda v, t=t: v.tensor_copy(
            out=MRS[:, 2 * t:2 * t + 1], in_=P.NEGMEAN[:, 0:1]), deps=(xn_tok,))
        c_r = P.do("vector", lambda v, t=t: v.tensor_copy(
            out=MRS[:, 2 * t + 1:2 * t + 2], in_=P.RSTD[:, 0:1]))
        xt_tok = emit_transpose_1024(P, P.XN, P.XT, (xn_tok,), pt_bank=t % 2,
                                     war_dep=xt_war)
        xn_war = xt_tok
        bank = 4 + (t % 2)
        mm = emit_proj_1024(P, P.XT, wr.sb, wr_tok, out_banks=[bank], fout=8,
                            extra_deps=(xt_tok,))
        xt_war = mm[-1]
        t_mx = P.do("vector", lambda v, bank=bank: v.tensor_reduce(
            out=NMX[:, 0:1], in_=P.PS[bank][:, 0:8], axis=AX.X, op=ALU.max,
            negate=True), deps=(mm[0], P.ps_free[bank]))
        t_pb = P.do("scalar", lambda s, bank=bank: s.activation(
            out=PRB[:, :], in_=P.PS[bank][:, 0:8], func=AF.Exp, bias=NMX[:, 0:1],
            scale=IM(1.0), accum_out=SUME[:, 0:1]), deps=(t_mx,))
        P.ps_free[bank] = t_pb
        t_rc = P.do("vector", lambda v: v.reciprocal(out=RCP[:, 0:1], in_=SUME[:, 0:1]),
                    deps=(t_pb,))
        t_eq = P.do("vector", lambda v: v.tensor_scalar(
            out=EQ[:, :], in0=PRB[:, :], scalar1=1.0, scalar2=None, op0=ALU.is_ge),
            deps=(t_pb,))
        t_p2 = P.do("vector", lambda v: v.tensor_tensor(
            out=PU2[:, :], in0=PRB[:, :], in1=EQ[:, :], op=ALU.subtract), deps=(t_eq,))
        t_m2 = P.do("vector", lambda v: v.tensor_reduce(
            out=M2[:, 0:1], in_=PU2[:, :], axis=AX.X, op=ALU.max), deps=(t_p2,))
        t_mk = P.do("vector", lambda v: v.tensor_scalar(
            out=MK2[:, :], in0=PRB[:, :], scalar1=M2[:, 0:1], scalar2=None,
            op0=ALU.is_ge), deps=(t_m2,))
        t_cu = P.do("vector", lambda v: v.tensor_tensor(
            out=MK2[:, :], in0=PRB[:, :], in1=MK2[:, :], op=ALU.mult), deps=(t_mk,))
        P.do("vector", lambda v, t=t: v.tensor_scalar(
            out=COEF[:, 8 * t:8 * t + 8], in0=MK2[:, :], scalar1=RCP[:, 0:1],
            scalar2=None, op0=ALU.mult), deps=(t_cu, t_rc))
    P.barrier()

    def load_expert_block(e0):
        toks = []
        for el in range(4):
            e = e0 + el
            dstg = WGU4[:, el * 8192:(el + 1) * 8192].rearrange(
                "p (c f) -> p c f", f=1024)[:, :, 0:512]
            dstu = WGU4[:, el * 8192:(el + 1) * 8192].rearrange(
                "p (c f) -> p c f", f=1024)[:, :, 512:1024]
            srcg = aps["experts_gate"][e0 + el].rearrange("(c p) f -> p c f", p=128)
            srcu = aps["experts_up"][e0 + el].rearrange("(c p) f -> p c f", p=128)
            toks.append(P.dma("gpsimd", lambda g, dstg=dstg, srcg=srcg:
                              g.dma_start(out=dstg, in_=srcg), deps=(w_war,)))
            toks.append(P.dma("gpsimd", lambda g, dstu=dstu, srcu=srcu:
                              g.dma_start(out=dstu, in_=srcu)))
            dstd = WD4[:, el * 4096:(el + 1) * 4096].rearrange("p (c f) -> p c f", f=1024)
            srcd = aps["experts_down"][e0 + el].rearrange("(c p) f -> p c f", p=128)
            toks.append(P.dma("gpsimd", lambda g, dstd=dstd, srcd=srcd:
                              g.dma_start(out=dstd, in_=srcd)))
        return toks

    part_tok = [None] * NTILE
    out_war = [None, None]
    prt_rd = [None, None]
    prt_war = [None, None]
    for ipass in range(2):
        e0 = 4 * ipass
        w_war = P.fence("tensor")
        w_tok = load_expert_block(e0)
        acc_free = None

        def mk_moe(t, ap, war):
            return P.do("vector", lambda v, t=t, ap=ap: v.tensor_scalar(
                out=ap, in0=r_tile(P, t), scalar1=MRS[:, 2 * t:2 * t + 1],
                scalar2=MRS[:, 2 * t + 1:2 * t + 2], op0=ALU.add, op1=ALU.mult),
                deps=(war,) + tuple(P.r_toks))

        pipe = XnPipe(P, mk_moe, NTILE)
        for t in range(NTILE):
            rt = r_tile(P, t)
            xt_tok = pipe.xt_tok[t]
            xtv = pipe.xt_ap(t)
            if ipass == 1:
                prt_rd[t % 2] = P.dma("sync", lambda s_, t=t: s_.dma_start(
                    out=PRT[t % 2][:, :], in_=part[t]),
                    deps=(prt_war[t % 2], part_tok[t]))
            gu_banks = [(0, 1), (2, 3)]
            acc_banks = (4, 5)
            int_toks = {}
            tr_toks = {}
            sil_prev = None
            down_tok = None
            last_gu = None

            def emit_trdown(e, first, last):
                nonlocal down_tok
                slot = (e % 2) * 1024
                tr_tok = emit_transpose_1024(P, INTER[:, slot:slot + 512],
                                             INTER[:, slot + 512:slot + 1024],
                                             (int_toks[e],), pt_bank=t % 2,
                                             war_dep=down_tok, nchunk=4)
                tr_toks[e] = tr_tok
                for bi, bank in enumerate(acc_banks):
                    for k in range(4):
                        deps = []
                        if k == 0 and bi == 0 and first:
                            deps = [acc_free] + w_tok
                        if k == 0:
                            deps = deps + [tr_tok]
                        tok = P.do("tensor", lambda t_, k=k, bank=bank, bi=bi, e=e, slot=slot:
                                   t_.matmul(P.PS[bank][:, 0:512],
                                             INTER[:, slot + 512 + k * 128:slot + 512 + (k + 1) * 128],
                                             WD4[:, e * 4096 + k * 1024 + 512 * bi:
                                                  e * 4096 + k * 1024 + 512 * bi + 512],
                                             start=(k == 0 and first), stop=(k == 3 and last)),
                                   deps=deps,
                                   inc="pe" if (k == 3 and bi == 1) else None)
                down_tok = tok

            for e in range(4):
                gb = gu_banks[e % 2]
                mm_gu = []
                for bi, bank in enumerate(gb):
                    for k in range(8):
                        deps = []
                        if k == 0:
                            deps = [P.ps_free[bank], xt_tok] + w_tok
                        tok = P.do("tensor", lambda t_, k=k, bank=bank, bi=bi, e=e,
                                   xtv=xtv:
                                   t_.matmul(P.PS[bank][:, 0:512],
                                             xtv[:, k * 128:(k + 1) * 128],
                                             WGU4[:, e * 8192 + k * 1024 + 512 * bi:
                                                  e * 8192 + k * 1024 + 512 * bi + 512],
                                             start=(k == 0), stop=(k == 7)),
                                   deps=deps, inc="pe" if k == 7 else None)
                    mm_gu.append(tok)
                last_gu = mm_gu[-1]
                t_sil = P.do("scalar", lambda s_, gb=gb: s_.activation(
                    out=SIL[:, 0:512], in_=P.PS[gb[0]][:, 0:512], func=AF.Silu),
                    deps=(mm_gu[0], int_toks.get(e - 1)))
                slot = (e % 2) * 1024
                t_int = P.do("vector", lambda v, gb=gb, e=e, t=t, e0=e0, slot=slot:
                             v.scalar_tensor_tensor(
                                 out=INTER[:, slot:slot + 512], in0=P.PS[gb[1]][:, 0:512],
                                 scalar=COEF[:, 8 * t + e0 + e:8 * t + e0 + e + 1],
                                 in1=SIL[:, 0:512], op0=ALU.mult, op1=ALU.mult),
                             deps=(mm_gu[1], t_sil, tr_toks.get(e - 2)))
                int_toks[e] = t_int
                P.ps_free[gb[0]] = t_int
                P.ps_free[gb[1]] = t_int
                if e > 0:
                    emit_trdown(e - 1, first=(e - 1 == 0), last=False)
            emit_trdown(3, first=False, last=True)
            pipe.done(t, last_gu)
            ob = OUT[t % 2]
            ev_toks = []
            for bi, bank in enumerate(acc_banks):
                lo, hi = bi * 512, (bi + 1) * 512
                if ipass == 0:
                    ev = P.do("vector", lambda v, bank=bank, lo=lo, hi=hi, ob=ob:
                              v.tensor_scalar(out=ob[:, lo:hi], in0=P.PS[bank][:, 0:512],
                                              scalar1=0.5, scalar2=None, op0=ALU.mult),
                              deps=(down_tok, out_war[t % 2]))
                else:
                    ev = P.do("vector", lambda v, bank=bank, lo=lo, hi=hi, ob=ob, t=t:
                              v.scalar_tensor_tensor(out=ob[:, lo:hi],
                                                     in0=P.PS[bank][:, 0:512], scalar=0.5,
                                                     in1=PRT[t % 2][:, lo:hi],
                                                     op0=ALU.mult, op1=ALU.add),
                              deps=(down_tok, out_war[t % 2], prt_rd[t % 2]))
                ev_toks.append(ev)
            acc_free = ev_toks[-1]
            if ipass == 1:
                prt_war[t % 2] = ev_toks[-1]
            out_war[t % 2] = P.dma("sync", lambda s_, t=t, ob=ob: s_.dma_start(
                out=part[t], in_=ob[:, :]), deps=tuple(ev_toks) + (part_tok[t],))
            part_tok[t] = out_war[t % 2]
        P.barrier()

    w_war = P.fence("tensor")
    tg = P.dma("gpsimd", lambda g: g.dma_start(
        out=WSH.rearrange("p (c f) -> p c f", f=2048)[:, :, 0:1024],
        in_=aps["shared_gate"].rearrange("(c p) f -> p c f", p=128)), deps=(w_war,))
    tu = P.dma("gpsimd", lambda g: g.dma_start(
        out=WSH.rearrange("p (c f) -> p c f", f=2048)[:, :, 1024:2048],
        in_=aps["shared_up"].rearrange("(c p) f -> p c f", p=128)))
    td = P.dma("gpsimd", lambda g: g.dma_start(
        out=WSHD.rearrange("p (c f) -> p c f", f=1024),
        in_=aps["shared_down"].rearrange("(c p) f -> p c f", p=128)))
    acc_free = None

    def mk_moe4(t, ap, war):
        return P.do("vector", lambda v, t=t, ap=ap: v.tensor_scalar(
            out=ap, in0=r_tile(P, t), scalar1=MRS[:, 2 * t:2 * t + 1],
            scalar2=MRS[:, 2 * t + 1:2 * t + 2], op0=ALU.add, op1=ALU.mult),
            deps=(war,))

    P._l0f_p4_down = None
    pipe = XnPipe(P, mk_moe4, NTILE)

    def p4_trdown(t, int_t):
        nonlocal acc_free
        rt = r_tile(P, t)
        sl = (t % 2) * 2048
        tr_tok = emit_transpose_1024(P, INTER[:, sl:sl + 1024],
                                     INTER[:, sl + 1024:sl + 2048],
                                     (int_t[0], int_t[1]),
                                     pt_bank=t % 2, war_dep=P._l0f_p4_down)
        acc_banks = (4, 5)
        down_tok = None
        for bi, bank in enumerate(acc_banks):
            for k in range(8):
                deps = []
                if k == 0:
                    deps = [tr_tok, acc_free]
                tok = P.do("tensor", lambda t_, k=k, bank=bank, bi=bi, sl=sl:
                           t_.matmul(P.PS[bank][:, 0:512],
                                     INTER[:, sl + 1024 + k * 128:sl + 1024 + (k + 1) * 128],
                                     WSHD[:, k * 1024 + 512 * bi: k * 1024 + 512 * bi + 512],
                                     start=(k == 0), stop=(k == 7)),
                           deps=deps, inc="pe" if k == 7 else None)
            down_tok = tok
        P._l0f_p4_down = down_tok
        ev_toks = []
        for bi, bank in enumerate(acc_banks):
            lo, hi = bi * 512, (bi + 1) * 512
            e1 = P.do("vector", lambda v, bank=bank, lo=lo, hi=hi, t=t:
                      v.scalar_tensor_tensor(out=EVS[:, lo:hi], in0=P.PS[bank][:, 0:512],
                                             scalar=0.5, in1=PRT[t % 2][:, lo:hi],
                                             op0=ALU.mult, op1=ALU.add),
                      deps=(down_tok, prt_rd[t % 2]))
            e2 = P.do("vector", lambda v, lo=lo, hi=hi, rt=rt: v.tensor_tensor(
                out=rt[:, lo:hi], in0=EVS[:, lo:hi], in1=rt[:, lo:hi], op=ALU.add),
                deps=(e1,))
            ev_toks.append(e2)
        prt_war[t % 2] = ev_toks[-1]
        acc_free = ev_toks[-1]

    pend = None
    for t in range(NTILE):
        xt_tok = pipe.xt_tok[t]
        xtv = pipe.xt_ap(t)
        prt_rd[t % 2] = P.dma("sync", lambda s_, t=t: s_.dma_start(
            out=PRT[t % 2][:, :], in_=part[t]),
            deps=(prt_war[t % 2], part_tok[t]))
        mm_gu = []
        for bi in range(4):
            bank = bi
            for k in range(8):
                deps = []
                if k == 0:
                    deps = [P.ps_free[bank], tg, tu, td, xt_tok]
                tok = P.do("tensor", lambda t_, k=k, bank=bank, bi=bi, xtv=xtv:
                           t_.matmul(P.PS[bank][:, 0:512],
                                     xtv[:, k * 128:(k + 1) * 128],
                                     WSH[:, k * 2048 + 512 * bi: k * 2048 + 512 * bi + 512],
                                     start=(k == 0), stop=(k == 7)),
                           deps=deps, inc="pe" if k == 7 else None)
            mm_gu.append(tok)
        sil_t = []
        for bi in range(2):
            sil_t.append(P.do("scalar", lambda s_, bi=bi: s_.activation(
                out=SIL[:, bi * 512:(bi + 1) * 512], in_=P.PS[bi][:, 0:512],
                func=AF.Silu), deps=(mm_gu[bi], getattr(P, "_l0f_p4_int", None))))
        int_t = []
        for bi in range(2):
            sl = (t % 2) * 2048
            tok = P.do("vector", lambda v, bi=bi, sl=sl: v.scalar_tensor_tensor(
                out=INTER[:, sl + bi * 512:sl + (bi + 1) * 512], in0=P.PS[2 + bi][:, 0:512],
                scalar=1.0, in1=SIL[:, bi * 512:(bi + 1) * 512],
                op0=ALU.mult, op1=ALU.mult),
                deps=(mm_gu[2 + bi], sil_t[bi],
                      P._l0f_p4_down if t >= 2 else None))
            int_t.append(tok)
        P._l0f_p4_int = int_t[-1]
        for bi in range(4):
            P.ps_free[bi] = int_t[-1]
        if pend is not None:
            p4_trdown(*pend)
        pend = (t, int_t)
        pipe.done(t, mm_gu[3])
    p4_trdown(*pend)
    P.barrier()



def emit_L3_attn(P, ctx, aps):
    nc = P.nc
    sb = lambda name, shape, dt: ctx.enter_context(nc.sbuf_tensor(P.tag + name, shape, dt))
    MRS = sb("l3a_mrs", [128, 2 * NTILE], F32)
    WALL = WBuf(P, ctx, "l3a_wall", 1024, 3072)
    BROW = sb("l3a_brow", [1, 3072], BF16)
    ONES1B = sb("l3a_ones1b", [1, 128], BF16)
    o1b_tok = P.do("vector", lambda v: v.memset(ONES1B[0:1, :], 1.0))
    LIN = sb("l3a_lin", [128, 3072], BF16)
    QP = sb("l3a_qp", [128, 512], BF16)
    KP = sb("l3a_kp", [128, 512], BF16)
    QTS = sb("l3a_qts", [128, 4 * 2048], BF16)
    KTS = sb("l3a_kts", [128, 4 * 2048], BF16)
    VSAUG = sb("l3a_vsaug", [128, 16 * 16 * 65], BF16)
    EXPT = sb("l3a_expt", [128, 3 * 512], BF16)
    PRW = sb("l3a_prw", [1, 512], BF16)
    SPW = LIN[0:1, 0:1024]
    VPR = LIN[0:1, 1024:2048]
    KPAD17S = sb("l3a_kpad", [128, 4 * 128], BF16)
    VA17S = sb("l3a_va17", [128, 16 * 65], BF16)
    CTXT = sb("l3a_ctxt", [65, 512], BF16)
    CTX = sb("l3a_ctx", [128, 65], F32)
    REC = sb("l3a_rec", [128, 1], F32)
    STAB = 8.0

    xn_war = None
    for t in range(NTILE):
        xn_tok = emit_ln_tile(P, t, xn_war=xn_war)
        xn_war = xn_tok
        P.do("vector", lambda v, t=t: v.tensor_copy(
            out=MRS[:, 2 * t:2 * t + 1], in_=P.NEGMEAN[:, 0:1]), deps=(xn_tok,))
        P.do("vector", lambda v, t=t: v.tensor_copy(
            out=MRS[:, 2 * t + 1:2 * t + 2], in_=P.RSTD[:, 0:1]))
    P.barrier()

    offs = {"Wq": 0, "Wqc": 256, "Wk": 512, "Wkc": 768, "Wv": 1024, "Wvc": 2048}
    wtoks = []
    for nm, off in offs.items():
        f = 1024 if nm in ("Wv", "Wvc") else 256
        dst = WALL.sb[:, :].rearrange("p (c ff) -> p c ff", ff=3072)[:, :, off:off + f]
        src = aps[nm].rearrange("(c p) f -> p c f", p=128)
        wtoks.append(P.dma("gpsimd", lambda g, dst=dst, src=src: g.dma_start(
            out=dst, in_=src)))
    boffs = {"bq": 0, "bqc": 256, "bk": 512, "bkc": 768, "bv": 1024, "bvc": 2048}
    btoks = []
    for nm, off in boffs.items():
        f = 1024 if nm in ("bv", "bvc") else 256
        btoks.append(P.dma("gpsimd", lambda s_, nm=nm, off=off, f=f: s_.dma_start(
            out=BROW[0:1, off:off + f], in_=aps[nm])))
    t1 = P.do("scalar", lambda s_: s_.activation(out=SPW[0:1, 0:256],
                                                 in_=BROW[0:1, 768:1024],
                                                 func=AF.Sigmoid), deps=tuple(btoks))
    t2 = P.do("vector", lambda v: v.memset(PRW[0:1, :], 0.0))
    t3 = P.do("vector", lambda v: v.tensor_tensor(
        out=PRW[0:1, :].rearrange("o (h f) -> o h f", f=32)[:, :, 0:16],
        in0=BROW[0:1, 512:768].rearrange("o (h f) -> o h f", f=16),
        in1=SPW[0:1, 0:256].rearrange("o (h f) -> o h f", f=16),
        op=ALU.mult), deps=(t1, t2))
    km = P.do("vector", lambda v: v.memset(KPAD17S[:, :], 0.0))
    for c in range(4):
        trk = P.do("tensor", lambda t, c=c: t.transpose(
            P.PT[0][0:128, c * 128:c * 128 + 1],
            PRW[0:1, c * 128:(c + 1) * 128], P.IDN[0:1, 0:1]),
            deps=(t3, P.pt_free[0], P.idn_tok) if c == 0 else (),
            inc="pe" if c == 3 else None)
    kp_tok = None
    for c in range(4):
        kp_tok = P.do("vector", lambda v, c=c: v.tensor_copy(
            out=KPAD17S[:, c * 128:c * 128 + 1], in_=P.PT[0][0:128, c * 128:c * 128 + 1]),
            deps=(trk, km))
    P.pt_free[0] = kp_tok
    t4 = P.do("scalar", lambda s_: s_.activation(out=SPW[0:1, :],
                                                 in_=BROW[0:1, 2048:3072],
                                                 func=AF.Sigmoid), deps=(t1, t3))
    t5 = P.do("vector", lambda v: v.scalar_tensor_tensor(
        out=VPR, in0=BROW[0:1, 1024:2048], scalar=16.0, in1=SPW[0:1, :],
        op0=ALU.mult, op1=ALU.mult), deps=(t4,))
    vm = P.do("vector", lambda v: v.memset(VA17S[:, :], 0.0))
    v1 = P.do("vector", lambda v: v.tensor_copy(
        out=VA17S[0:1, :].rearrange("o (h f) -> o h f", f=65)[:, :, 0:64],
        in_=VPR.rearrange("o (h f) -> o h f", f=64)), deps=(t5, vm))
    v2 = P.do("vector", lambda v: v.memset(
        VA17S[0:1, :].rearrange("o (h f) -> o h f", f=65)[:, :, 64:65], 16.0),
        deps=(vm,))
    va17_tok = (v1, v2)
    qkm = P.do("vector", lambda v: v.memset(QP[:, :], 0.0))
    qkm2 = P.do("vector", lambda v: v.memset(KP[:, :], 0.0))
    vs_ones = P.do("vector", lambda v: v.memset(VSAUG[:, :], 1.0))

    wtok_all = tuple(wtoks)

    for s in range(2):
        qts_war, kts_war, vs_war = None, None, None
        lin_war = vs_ones if s == 0 else None

        def mk_l3a(tl, ap, war, s=s):
            t = s * TPS + tl
            return P.do("vector", lambda v, t=t, ap=ap: v.tensor_scalar(
                out=ap, in0=r_tile(P, t), scalar1=MRS[:, 2 * t:2 * t + 1],
                scalar2=MRS[:, 2 * t + 1:2 * t + 2], op0=ALU.add, op1=ALU.mult),
                deps=(war,) + tuple(P.r_toks))

        pipe = XnPipe(P, mk_l3a, TPS)
        for tl in range(TPS):
            t = s * TPS + tl
            xt_tok = pipe.xt_tok[tl]
            xtv = pipe.xt_ap(tl)
            banks = [0, 1, 2, 3, 4, 5]
            for bi, bank in enumerate(banks):
                lo, hi = bi * 512, (bi + 1) * 512
                P.do("tensor", lambda t_, bank=bank, lo=lo, hi=hi: t_.matmul(
                    P.PS[bank][:, 0:512], ONES1B[0:1, :], BROW[0:1, lo:hi],
                    start=True, stop=False),
                    deps=tuple(btoks) + (o1b_tok, P.ps_free[bank]))
            mm = emit_proj_1024(P, xtv, WALL.sb, wtok_all[-1], out_banks=banks,
                                fout=3072, extra_deps=(xt_tok,) + wtok_all,
                                accumulate=True)
            pipe.done(tl, mm[-1])
            lin_toks = []
            for bi, bank in enumerate(banks):
                lo, hi = bi * 512, (bi + 1) * 512
                tok = P.do("vector", lambda v, bank=bank, lo=lo, hi=hi: v.tensor_copy(
                    out=LIN[:, lo:hi], in_=P.PS[bank][:, 0:512]),
                    deps=(mm[bi], P.ps_free[bank], lin_war))
                lin_toks.append(tok)
                P.ps_free[bank] = tok
            sg1 = P.do("scalar", lambda s_: s_.activation(
                out=LIN[:, 256:512], in_=LIN[:, 256:512], func=AF.Sigmoid),
                deps=(lin_toks[0],))
            sg2 = P.do("scalar", lambda s_: s_.activation(
                out=LIN[:, 768:1024], in_=LIN[:, 768:1024], func=AF.Sigmoid),
                deps=(lin_toks[1],))
            sg3 = P.do("scalar", lambda s_: s_.activation(
                out=LIN[:, 2048:3072], in_=LIN[:, 2048:3072], func=AF.Sigmoid),
                deps=(lin_toks[5], lin_toks[4], lin_toks[3], lin_toks[2]))
            tq = P.do("vector", lambda v: v.tensor_tensor(
                out=QP[:, :].rearrange("p (h f) -> p h f", f=32)[:, :, 0:16],
                in0=LIN[:, 0:256].rearrange("p (h f) -> p h f", f=16),
                in1=LIN[:, 256:512].rearrange("p (h f) -> p h f", f=16),
                op=ALU.mult), deps=(sg1, qkm, qts_war))
            tk = P.do("vector", lambda v: v.tensor_tensor(
                out=KP[:, :].rearrange("p (h f) -> p h f", f=32)[:, :, 0:16],
                in0=LIN[:, 512:768].rearrange("p (h f) -> p h f", f=16),
                in1=LIN[:, 768:1024].rearrange("p (h f) -> p h f", f=16),
                op=ALU.mult), deps=(sg2, qkm2, kts_war))
            tv = P.do("vector", lambda v, tl=tl: v.tensor_tensor(
                out=VSAUG[:, tl * 1040:(tl + 1) * 1040].rearrange(
                    "p (h f) -> p h f", f=65)[:, :, 0:64],
                in0=LIN[:, 1024:2048].rearrange("p (h f) -> p h f", f=64),
                in1=LIN[:, 2048:3072].rearrange("p (h f) -> p h f", f=64),
                op=ALU.mult), deps=(sg3, vs_ones, vs_war))
            lin_war = tv
            for (src, dststore, wtok_dep) in ((QP, QTS, tq), (KP, KTS, tk)):
                pt_b = tl % 2
                for c in range(4):
                    trq = P.do("tensor", lambda t_, c=c, src=src, pt_b=pt_b: t_.transpose(
                        P.PT[pt_b][:, c * 128:(c + 1) * 128],
                        src[:, c * 128:(c + 1) * 128], P.IDN[:]),
                        deps=(wtok_dep, P.pt_free[pt_b]) if c == 0 else (),
                        inc="pe" if c == 3 else None)
                dstv = dststore[:, 0:8192].rearrange(
                    "p (c n) -> p c n", n=2048)[:, :, tl * 128:(tl + 1) * 128]
                ev = P.do("vector", lambda v, dstv=dstv, pt_b=pt_b: v.tensor_copy(
                    out=dstv, in_=P.PT[pt_b][:, 0:512].rearrange(
                        "p (c n) -> p c n", n=128)), deps=(trq,))
                P.pt_free[pt_b] = ev
                if dststore is QTS:
                    qts_done = ev
                else:
                    kts_done = ev
            qts_war = qts_done
            kts_war = kts_done
            vs_war = None
        P.barrier()

        for h in range(16):
            hc, j = h // 4, h % 4
            base = 32 * j
            tp = {} if base < 96 else {"tile_position": (base, 0)}
            slot_free = [None, None, None]
            for qb in range(4):
                av_tok = None
                for c in range(17):
                    bank = [0, 1, 3][c % 3]
                    slot = c % 3
                    if c < 16:
                        lhsT = KTS[base:base + 16,
                                   hc * 2048 + c * 128: hc * 2048 + (c + 1) * 128]
                    else:
                        lhsT = KPAD17S[base:base + 16, hc * 128:(hc + 1) * 128]
                    rhs = QTS[base:base + 16, hc * 2048 + qb * 512: hc * 2048 + (qb + 1) * 512]
                    sc = P.do("tensor", lambda t_, lhsT=lhsT, rhs=rhs, bank=bank, tp=tp:
                              t_.matmul(P.PS[bank][:, 0:512], lhsT, rhs,
                                        start=True, stop=True, **tp),
                              deps=(P.ps_free[bank],), inc="pe")
                    ex = P.do("scalar", lambda s_, slot=slot, bank=bank: s_.activation(
                        out=EXPT[:, slot * 512:(slot + 1) * 512], in_=P.PS[bank][:, 0:512],
                        func=AF.Exp, bias=IM(-STAB), scale=IM(0.125)),
                        deps=(sc, slot_free[slot]))
                    P.ps_free[bank] = ex
                    if c < 16:
                        lhsT_v = VSAUG[:, (c * 16 + h) * 65:(c * 16 + h) * 65 + 65]
                    else:
                        lhsT_v = VA17S[:, h * 65:(h + 1) * 65]
                    deps = [ex]
                    if c == 0:
                        deps += [P.ps_free[2], va17_tok[0], va17_tok[1]]
                    av_tok = P.do("tensor", lambda t_, lhsT_v=lhsT_v, c=c, slot=slot:
                                  t_.matmul(P.PS[2][0:65, 0:512], lhsT_v,
                                            EXPT[:, slot * 512:(slot + 1) * 512],
                                            start=(c == 0), stop=(c == 16)),
                                  deps=deps, inc="pe")
                    slot_free[slot] = av_tok
                ct = P.do("vector", lambda v: v.tensor_copy(
                    out=CTXT[0:65, :], in_=P.PS[2][0:65, 0:512]), deps=(av_tok,))
                P.ps_free[2] = ct
                prev_v = ct
                for i in range(4):
                    t_glob = s * TPS + qb * 4 + i
                    rt = r_tile(P, t_glob)
                    trc = P.do("tensor", lambda t_, i=i: t_.transpose(
                        P.PT[0][0:128, i * 128:i * 128 + 65],
                        CTXT[0:65, i * 128:(i + 1) * 128], P.IDN[0:65, 0:65]),
                        deps=(ct, P.pt_free[0]) if i == 0 else (),
                        inc="pe" if i == 3 else None)
                for i in range(4):
                    t_glob = s * TPS + qb * 4 + i
                    rt = r_tile(P, t_glob)
                    ev = P.do("vector", lambda v, i=i: v.tensor_copy(
                        out=CTX[:, :], in_=P.PT[0][0:128, i * 128:i * 128 + 65]),
                        deps=(trc, prev_v))
                    rc = P.do("vector", lambda v: v.reciprocal(
                        out=REC[:, 0:1], in_=CTX[:, 64:65]), deps=(ev,))
                    prev_v = P.do("vector", lambda v, rt=rt, h=h: v.scalar_tensor_tensor(
                        out=rt[:, 64 * h:64 * h + 64], in0=CTX[:, 0:64],
                        scalar=REC[:, 0:1], in1=rt[:, 64 * h:64 * h + 64],
                        op0=ALU.mult, op1=ALU.add), deps=(rc,))
                P.pt_free[0] = prev_v
        P.barrier()
    P.barrier()



L2_WCHUNKS = {0: (0, 6), 1: (2, 10), 2: (6, 14), 3: (10, 16)}


def emit_L2_attn(P, ctx, aps):
    nc = P.nc
    sb = lambda name, shape, dt: ctx.enter_context(nc.sbuf_tensor(P.tag + name, shape, dt))
    MRS = sb("l2a_mrs", [128, 2 * NTILE], F32)
    WALL = WBuf(P, ctx, "l2a_w", 1024, 1024)
    WG = WBuf(P, ctx, "l2a_wg", 1024, 1)
    QTS = sb("l2a_qts", [128, 8 * 2048], BF16)
    KTS = sb("l2a_kts", [128, 8 * 2048], BF16)
    VSAUG = sb("l2a_vsaug", [128, 16 * 16 * 65], BF16)
    GATES = sb("l2a_gates", [128, 16], F32)
    OMGS = sb("l2a_omgs", [128, 16], F32)
    GSC = sb("l2a_gsc", [128, 1], F32)
    ONESB = sb("l2a_onesb", [1, 128], BF16)
    REC2 = sb("l2a_rec2", [128, 4], F32)
    RECG = sb("l2a_recg", [128, 4], F32)
    GA = sb("l2a_ga", [128, 4], F32)
    GB = sb("l2a_gb", [128, 4], F32)
    T1 = sb("l2a_t1", [128, 260], F32)
    COMBW = sb("l2a_combw", [128, 4096], BF16)
    EXPT = sb("l2a_expt", [128, 1536], BF16)
    EXPTG = sb("l2a_exptg", [64, 512], BF16)
    CTL = sb("l2a_ctl", [65, 512], BF16)
    CTG = sb("l2a_ctg", [65, 512], BF16)
    LOCN = sb("l2a_locn", [128, 260], BF16)
    GLBN = sb("l2a_glbn", [128, 260], BF16)
    VSTRA = sb("l2a_vstra", [64, 16 * 65], BF16)
    SELC = sb("l2a_selc", [128, 1024], BF16)
    BOROW = sb("l2a_borow", [1, 1024], BF16)
    WOUTV = WALL.sb[:, :]

    o1b = P.do("vector", lambda v: v.memset(ONESB[0:1, :], 1.0))
    nbg_row, nbg_t = load_row_f32(P, ctx, "l2a_bg", aps["b_gate"], 1)
    NEGBG, negbg_t = emit_bcast_row(P, ctx, "l2a_negbg", nbg_row, nbg_t, 1, scale=-1.0)
    sel_tok = P.dma("gpsimd", lambda g: g.dma_start(out=SELC[:, :], in_=aps["selc"]))
    bo_tok = P.dma("gpsimd", lambda g: g.dma_start(out=BOROW[0:1, :], in_=aps["b_out"]))

    xn_war = None
    for t in range(NTILE):
        xn_tok = emit_ln_tile(P, t, xn_war=xn_war)
        xn_war = xn_tok
        P.do("vector", lambda v, t=t: v.tensor_copy(
            out=MRS[:, 2 * t:2 * t + 1], in_=P.NEGMEAN[:, 0:1]), deps=(xn_tok,))
        P.do("vector", lambda v, t=t: v.tensor_copy(
            out=MRS[:, 2 * t + 1:2 * t + 2], in_=P.RSTD[:, 0:1]))
    P.barrier()

    WGROW = VSAUG[0:1, 0:1024]
    wgr_tok = P.dma("gpsimd", lambda g: g.dma_start(
        out=WGROW[0:1, :], in_=aps["W_gate"].rearrange("f one -> one f")))
    for c in range(8):
        trw = P.do("tensor", lambda t, c=c: t.transpose(
            P.PT[0][0:128, c * 128:c * 128 + 1],
            WGROW[0:1, c * 128:(c + 1) * 128], P.IDN[0:1, 0:1]),
            deps=(wgr_tok, P.pt_free[0], P.idn_tok) if c == 0 else (),
            inc="pe" if c == 7 else None)
    wg_tok = None
    for c in range(8):
        wg_tok = P.do("vector", lambda v, c=c: v.tensor_copy(
            out=WG.sb[:, c:c + 1], in_=P.PT[0][0:128, c * 128:c * 128 + 1]),
            deps=(trw,))
    P.pt_free[0] = wg_tok

    for s in range(2):
        vs_ones = P.do("vector", lambda v: v.memset(VSAUG[:, :], 1.0),
                       deps=(getattr(P, "_l2_vs_war", None), wg_tok))
        for sweep in range(3):
            wlo = sweep * 1024
            w_war = P.fence("tensor")
            wq_tok = P.dma("gpsimd", lambda g, wlo=wlo: g.dma_start(
                out=WALL.sb[:, :].rearrange("p (c f) -> p c f", f=1024),
                in_=aps["W_qkv"][:, wlo:wlo + 1024].rearrange("(c p) f -> p c f", p=128)),
                deps=(w_war,))
            def mk_l2a(tl, ap, war, s=s):
                t = s * TPS + tl
                return P.do("vector", lambda v, t=t, ap=ap: v.tensor_scalar(
                    out=ap, in0=r_tile(P, t), scalar1=MRS[:, 2 * t:2 * t + 1],
                    scalar2=MRS[:, 2 * t + 1:2 * t + 2], op0=ALU.add, op1=ALU.mult),
                    deps=(war,) + tuple(P.r_toks))

            pipe = XnPipe(P, mk_l2a, TPS)
            store_war = None
            for tl in range(TPS):
                t = s * TPS + tl
                xt_tok = pipe.xt_tok[tl]
                xnv = pipe.xn_ap(tl)
                xtv = pipe.xt_ap(tl)
                if sweep == 0:
                    for k in range(8):
                        deps = [P.ps_free[2], wg_tok, xt_tok] if k == 0 else []
                        gmm = P.do("tensor", lambda t_, k=k, xtv=xtv: t_.matmul(
                            P.PS[2][:, 0:1], xtv[:, k * 128:(k + 1) * 128], WG.chunk(k),
                            start=(k == 0), stop=(k == 7)),
                            deps=deps, inc="pe" if k == 7 else None)
                    ge = P.do("scalar", lambda s_: s_.activation(
                        out=GSC[:, 0:1], in_=P.PS[2][:, 0:1], func=AF.Exp,
                        bias=NEGBG[:, 0:1], scale=IM(-1.0)), deps=(gmm, negbg_t))
                    P.ps_free[2] = ge
                    g1 = P.do("vector", lambda v: v.tensor_scalar(
                        out=GSC[:, 0:1], in0=GSC[:, 0:1], scalar1=1.0, scalar2=None,
                        op0=ALU.add), deps=(ge,))
                    g2 = P.do("vector", lambda v, tl=tl: v.reciprocal(
                        out=GATES[:, tl:tl + 1], in_=GSC[:, 0:1]), deps=(g1,))
                    P.do("vector", lambda v, tl=tl: v.tensor_scalar(
                        out=OMGS[:, tl:tl + 1], in0=GATES[:, tl:tl + 1], scalar1=-1.0,
                        scalar2=1.0, op0=ALU.mult, op1=ALU.add), deps=(g2,))
                banks = [0, 1]
                mm = emit_proj_1024(P, xtv, WALL.sb, wq_tok, out_banks=banks,
                                    fout=1024, extra_deps=(xt_tok,))
                pipe.done(tl, mm[-1])
                if sweep == 2:
                    for bi in range(2):
                        dstv = VSAUG[:, tl * 1040:(tl + 1) * 1040].rearrange(
                            "p (h f) -> p h f", f=65)[:, bi * 8:(bi + 1) * 8, 0:64]
                        e = P.do("vector", lambda v, bi=bi, dstv=dstv: v.tensor_copy(
                            out=dstv, in_=P.PS[bi][:, 0:512].rearrange(
                                "p (h f) -> p h f", f=64)),
                            deps=(mm[bi], P.ps_free[bi], vs_ones))
                        P.ps_free[bi] = e
                else:
                    evs = []
                    for bi in range(2):
                        if sweep == 0:
                            e = P.do("vector", lambda v, bi=bi, xnv=xnv: v.tensor_copy(
                                out=xnv[:, bi * 512:(bi + 1) * 512],
                                in_=P.PS[bi][:, 0:512]),
                                deps=(mm[bi], P.ps_free[bi], xt_tok))
                        else:
                            e = P.do("vector", lambda v, bi=bi, xnv=xnv: v.tensor_scalar(
                                out=xnv[:, bi * 512:(bi + 1) * 512],
                                in0=P.PS[bi][:, 0:512], scalar1=0.125, scalar2=None,
                                op0=ALU.mult),
                                deps=(mm[bi], P.ps_free[bi], xt_tok))
                        evs.append(e)
                        P.ps_free[bi] = e
                    dststore = QTS if sweep == 0 else KTS
                    pt_b = tl % 2
                    for c in range(8):
                        trq = P.do("tensor", lambda t_, c=c, pt_b=pt_b, xnv=xnv: t_.transpose(
                            P.PT[pt_b][:, c * 128:(c + 1) * 128],
                            xnv[:, c * 128:(c + 1) * 128], P.IDN[:]),
                            deps=(evs[-1], P.pt_free[pt_b], store_war) if c == 0 else (),
                            inc="pe" if c == 7 else None)
                    dstv = dststore[:, :].rearrange(
                        "p (c n) -> p c n", n=2048)[:, :, tl * 128:(tl + 1) * 128]
                    evt = P.do("vector", lambda v, dstv=dstv, pt_b=pt_b: v.tensor_copy(
                        out=dstv, in_=P.PT[pt_b][:, :].rearrange(
                            "p (c n) -> p c n", n=128)), deps=(trq,))
                    P.pt_free[pt_b] = evt
                    store_war = evt
                    pipe.set_xn_reader(tl, evt)
            P.barrier()

        w_war = P.fence("tensor")
        wo_tok = P.dma("gpsimd", lambda g: g.dma_start(
            out=WOUTV.rearrange("p (c f) -> p c f", f=1024),
            in_=aps["W_out"].rearrange("(c p) f -> p c f", p=128)), deps=(w_war,))
        vz = P.do("vector", lambda v: v.memset(VSTRA[:, :], 1.0), deps=(w_war,))
        vst_mm = [None, None]
        for tl in range(TPS):
            rhsv = VSAUG[:, tl * 1040:(tl + 1) * 1040].rearrange(
                "p (h f) -> p h f", f=65)[:, :, 0:64]
            for bi in range(2):
                deps = [sel_tok, P.ps_free[bi]] if tl == 0 else []
                vst_mm[bi] = P.do("tensor", lambda t_, tl=tl, bi=bi, rhsv=rhsv: t_.matmul(
                    P.PS[bi][0:64, 0:512], SELC[:, tl * 64:(tl + 1) * 64],
                    rhsv[:, bi * 8:(bi + 1) * 8, :],
                    start=(tl == 0), stop=(tl == TPS - 1)),
                    deps=deps, inc="pe" if tl == TPS - 1 else None)
        vstr_tok = None
        for bi in range(2):
            dstv = VSTRA[:, :].rearrange("p (h f) -> p h f", f=65)[:, bi * 8:(bi + 1) * 8, 0:64]
            vstr_tok = P.do("vector", lambda v, bi=bi, dstv=dstv: v.tensor_copy(
                out=dstv, in_=P.PS[bi][0:64, 0:512].rearrange("p (h f) -> p h f", f=64)),
                deps=(vst_mm[bi], vz))
            P.ps_free[bi] = vstr_tok

        STAB2 = 4.0
        comb_done = None
        for w in range(4):
            c0, c1 = L2_WCHUNKS[w]
            for h in range(16):
                hc, base = h // 2, 64 * (h % 2)
                qrhs = QTS[base:base + 64, hc * 2048 + w * 512: hc * 2048 + (w + 1) * 512]
                slot_free = [None, None, None]
                av_tok = None
                nk = c1 - c0
                for ci, kc in enumerate(range(c0, c1)):
                    bank = [2, 3, 0][ci % 3]
                    slot = ci % 3
                    sc = P.do("tensor", lambda t_, kc=kc, bank=bank, hc=hc, base=base,
                              qrhs=qrhs: t_.matmul(
                                  P.PS[bank][:, 0:512],
                                  KTS[base:base + 64, hc * 2048 + kc * 128: hc * 2048 + (kc + 1) * 128],
                                  qrhs, start=True, stop=True),
                              deps=(P.ps_free[bank],), inc="pe")
                    ex = P.do("scalar", lambda s_, slot=slot, bank=bank: s_.activation(
                        out=EXPT[:, slot * 512:(slot + 1) * 512], in_=P.PS[bank][:, 0:512],
                        func=AF.Exp, bias=IM(-STAB2), scale=IM(1.0)),
                        deps=(sc, slot_free[slot]))
                    P.ps_free[bank] = ex
                    deps = [ex]
                    if ci == 0:
                        deps += [P.ps_free[4], vs_ones]
                    av_tok = P.do("tensor", lambda t_, kc=kc, slot=slot, h=h, ci=ci,
                                  nk=nk: t_.matmul(
                                      P.PS[4][0:65, 0:512],
                                      VSAUG[:, (kc * 16 + h) * 65:(kc * 16 + h) * 65 + 65],
                                      EXPT[:, slot * 512:(slot + 1) * 512],
                                      start=(ci == 0), stop=(ci == nk - 1)),
                                  deps=deps, inc="pe")
                    slot_free[slot] = av_tok
                kstr = KTS[base:base + 64, hc * 2048: (hc + 1) * 2048].rearrange(
                    "p (a b) -> p a b", b=32)[:, :, 0:1].squeeze(2)
                scg = P.do("tensor", lambda t_, kstr=kstr, qrhs=qrhs: t_.matmul(
                    P.PS[2][0:64, 0:512], kstr, qrhs, start=True, stop=True),
                    deps=(P.ps_free[2],), inc="pe")
                exg = P.do("scalar", lambda s_: s_.activation(
                    out=EXPTG[0:64, 0:512], in_=P.PS[2][0:64, 0:512], func=AF.Exp,
                    bias=IM(-STAB2), scale=IM(1.0)),
                    deps=(scg, getattr(P, "_l2_avg", None)))
                P.ps_free[2] = exg
                avg = P.do("tensor", lambda t_, h=h: t_.matmul(
                    P.PS[5][0:65, 0:512], VSTRA[0:64, h * 65:(h + 1) * 65],
                    EXPTG[0:64, 0:512], start=True, stop=True),
                    deps=(exg, P.ps_free[5], vstr_tok), inc="pe")
                P._l2_avg = avg
                ctl = P.do("vector", lambda v: v.tensor_copy(
                    out=CTL[0:65, :], in_=P.PS[4][0:65, 0:512]), deps=(av_tok,))
                P.ps_free[4] = ctl
                ctg = P.do("vector", lambda v: v.tensor_copy(
                    out=CTG[0:65, :], in_=P.PS[5][0:65, 0:512]), deps=(avg,))
                P.ps_free[5] = ctg
                for i in range(4):
                    trl = P.do("tensor", lambda t_, i=i: t_.transpose(
                        P.PT[0][0:128, i * 128:i * 128 + 65],
                        CTL[0:65, i * 128:(i + 1) * 128], P.IDN[0:65, 0:65]),
                        deps=(ctl, P.pt_free[0]) if i == 0 else (),
                        inc="pe" if i == 3 else None)
                evl = P.do("vector", lambda v: v.tensor_copy(
                    out=LOCN[:, :].rearrange("p (i f) -> p i f", f=65),
                    in_=P.PT[0][:, 0:512].rearrange("p (i f) -> p i f", f=128)[:, :, 0:65]),
                    deps=(trl, comb_done))
                P.pt_free[0] = evl
                for i in range(4):
                    trg = P.do("tensor", lambda t_, i=i: t_.transpose(
                        P.PT[1][0:128, i * 128:i * 128 + 65],
                        CTG[0:65, i * 128:(i + 1) * 128], P.IDN[0:65, 0:65]),
                        deps=(ctg, P.pt_free[1]) if i == 0 else (),
                        inc="pe" if i == 3 else None)
                evg = P.do("vector", lambda v: v.tensor_copy(
                    out=GLBN[:, :].rearrange("p (i f) -> p i f", f=65),
                    in_=P.PT[1][:, 0:512].rearrange("p (i f) -> p i f", f=128)[:, :, 0:65]),
                    deps=(trg, comb_done))
                P.pt_free[1] = evg
                r1 = P.do("vector", lambda v: v.reciprocal(
                    out=REC2[:, :],
                    in_=LOCN[:, :].rearrange("p (i f) -> p i f", f=65)[:, :, 64:65].squeeze(2)),
                    deps=(evl,))
                r2 = P.do("vector", lambda v: v.reciprocal(
                    out=RECG[:, :],
                    in_=GLBN[:, :].rearrange("p (i f) -> p i f", f=65)[:, :, 64:65].squeeze(2)),
                    deps=(evg,))
                ga = P.do("vector", lambda v, w=w: v.tensor_tensor(
                    out=GA[:, :], in0=REC2[:, :], in1=GATES[:, 4 * w:4 * w + 4],
                    op=ALU.mult), deps=(r1,))
                gb = P.do("vector", lambda v, w=w: v.tensor_tensor(
                    out=GB[:, :], in0=RECG[:, :], in1=OMGS[:, 4 * w:4 * w + 4],
                    op=ALU.mult), deps=(r2,))
                tt1 = P.do("vector", lambda v: v.tensor_tensor(
                    out=T1[:, :].rearrange("p (i f) -> p i f", f=65)[:, :, 0:64],
                    in0=LOCN[:, :].rearrange("p (i f) -> p i f", f=65)[:, :, 0:64],
                    in1=GA[:, :].unsqueeze(2).broadcast_to([128, 4, 64]),
                    op=ALU.mult), deps=(ga, evl))
                cw = COMBW[:, :].rearrange("p (i d) -> p i d", d=1024)[:, :, 64 * h:64 * h + 64]
                c1_ = P.do("vector", lambda v, cw=cw: v.tensor_tensor(
                    out=cw, in0=GLBN[:, :].rearrange("p (i f) -> p i f", f=65)[:, :, 0:64],
                    in1=GB[:, :].unsqueeze(2).broadcast_to([128, 4, 64]),
                    op=ALU.mult), deps=(gb, evg, comb_done))
                comb_done = P.do("vector", lambda v, cw=cw: v.tensor_tensor(
                    out=cw, in0=cw,
                    in1=T1[:, :].rearrange("p (i f) -> p i f", f=65)[:, :, 0:64],
                    op=ALU.add), deps=(c1_, tt1))
            xt_war2 = None
            for i in range(4):
                tg = s * TPS + w * 4 + i
                rt = r_tile(P, tg)
                xt_tok = emit_transpose_1024(P, COMBW[:, i * 1024:(i + 1) * 1024],
                                             P.XT, (comb_done,), pt_bank=i % 2,
                                             war_dep=xt_war2)
                for bi in range(2):
                    P.do("tensor", lambda t_, bi=bi: t_.matmul(
                        P.PS[bi][:, 0:512], ONESB[0:1, :],
                        BOROW[0:1, bi * 512:(bi + 1) * 512], start=True, stop=False),
                        deps=(P.ps_free[bi], bo_tok, o1b))
                mmo = emit_proj_1024(P, P.XT, WOUTV, wo_tok, out_banks=[0, 1],
                                     fout=1024, extra_deps=(xt_tok,), accumulate=True)
                xt_war2 = mmo[-1]
                for bi in range(2):
                    lo, hi = bi * 512, (bi + 1) * 512
                    e = P.do("vector", lambda v, bi=bi, lo=lo, hi=hi, rt=rt: v.tensor_tensor(
                        out=rt[:, lo:hi], in0=P.PS[bi][:, 0:512], in1=rt[:, lo:hi],
                        op=ALU.add), deps=(mmo[bi],))
                    P.ps_free[bi] = e
        P._l2_vs_war = P.fence("tensor")
        P.barrier()
    P.barrier()



def emit_L0_attn(P, ctx, aps):
    nc = P.nc
    sb = lambda name, shape, dt: ctx.enter_context(nc.sbuf_tensor(P.tag + name, shape, dt))
    MRS = sb("l0a_mrs", [128, 2 * NTILE], F32)
    W1 = WBuf(P, ctx, "l0a_w1", 1024, 1024)
    QTS = sb("l0a_qts", [128, 8 * 2048], BF16)
    KTS = sb("l0a_kts", [128, 8 * 2048], BF16)
    VSAUG = sb("l0a_vsaug", [128, 16 * 16 * 65], BF16)
    BROW = sb("l0a_brow", [1, 3072], BF16)
    BOROW = sb("l0a_borow", [1, 1024], BF16)
    ONESB = sb("l0a_onesb", [1, 128], BF16)
    MTC = sb("l0a_mtc", [128, 128], BF16)
    ES = sb("l0a_es", [128, 512], F32)
    AS = sb("l0a_as", [128, 512], BF16)
    RS = sb("l0a_rs", [128, 512], BF16)
    ATM = sb("l0a_atm", [128, 256], BF16)
    KTOK1 = sb("l0a_ktok1", [128, 1024], BF16)
    NUMS = sb("l0a_nums", [128, 16 * 65], BF16)
    DEN = sb("l0a_den", [128, 16], F32)
    RECS = sb("l0a_recs", [128, 16], F32)
    OUT1 = sb("l0a_out1", [128, 1024], BF16)
    SSB = sb("l0a_ssb", [128, 8 * 65], F32)
    SBF = sb("l0a_sbf", [128, 8 * 65], BF16)

    o1b = P.do("vector", lambda v: v.memset(ONESB[0:1, :], 1.0))
    mt_tok = P.dma("gpsimd", lambda g: g.dma_start(out=MTC[:, :], in_=aps["mtc"]))
    bq_tok = P.dma("gpsimd", lambda g: g.dma_start(out=BROW[0:1, :], in_=aps["b_qkv"]))
    bo_tok = P.dma("gpsimd", lambda g: g.dma_start(out=BOROW[0:1, :], in_=aps["b_out"]))

    xn_war = None
    for t in range(NTILE):
        xn_tok = emit_ln_tile(P, t, xn_war=xn_war)
        xn_war = xn_tok
        P.do("vector", lambda v, t=t: v.tensor_copy(
            out=MRS[:, 2 * t:2 * t + 1], in_=P.NEGMEAN[:, 0:1]), deps=(xn_tok,))
        P.do("vector", lambda v, t=t: v.tensor_copy(
            out=MRS[:, 2 * t + 1:2 * t + 2], in_=P.RSTD[:, 0:1]))
    P.barrier()

    for s in range(2):
        vs_ones = P.do("vector", lambda v: v.memset(VSAUG[:, :], 1.0),
                       deps=(getattr(P, "_l0_vs_war", None),))
        for sweep in range(3):
            wlo = sweep * 1024
            w_war = P.fence("tensor")
            wq_tok = P.dma("gpsimd", lambda g, wlo=wlo: g.dma_start(
                out=W1.sb[:, :].rearrange("p (c f) -> p c f", f=1024),
                in_=aps["W_qkv"][:, wlo:wlo + 1024].rearrange("(c p) f -> p c f", p=128)),
                deps=(w_war,))
            def mk_l0a(t, ap, war):
                return P.do("vector", lambda v, t=t, ap=ap: v.tensor_scalar(
                    out=ap, in0=r_tile(P, t), scalar1=MRS[:, 2 * t:2 * t + 1],
                    scalar2=MRS[:, 2 * t + 1:2 * t + 2], op0=ALU.add, op1=ALU.mult),
                    deps=(war,) + tuple(P.r_toks))

            pipe = XnPipe(P, lambda tl, ap, war: mk_l0a(s * TPS + tl, ap, war), TPS)
            store_war = None
            for tl in range(TPS):
                t = s * TPS + tl
                xt_tok = pipe.xt_tok[tl]
                xnv = pipe.xn_ap(tl)
                for bi in range(2):
                    P.do("tensor", lambda t_, bi=bi, wlo=wlo: t_.matmul(
                        P.PS[bi][:, 0:512], ONESB[0:1, :],
                        BROW[0:1, wlo + bi * 512:wlo + (bi + 1) * 512],
                        start=True, stop=False),
                        deps=(P.ps_free[bi], bq_tok, o1b))
                mm = emit_proj_1024(P, pipe.xt_ap(tl), W1.sb, wq_tok, out_banks=[0, 1],
                                    fout=1024, extra_deps=(xt_tok,), accumulate=True)
                pipe.done(tl, mm[-1])
                if sweep == 2:
                    for bi in range(2):
                        dstv = VSAUG[:, tl * 1040:(tl + 1) * 1040].rearrange(
                            "p (h f) -> p h f", f=65)[:, bi * 8:(bi + 1) * 8, 0:64]
                        e = P.do("vector", lambda v, bi=bi, dstv=dstv: v.tensor_copy(
                            out=dstv, in_=P.PS[bi][:, 0:512].rearrange(
                                "p (h f) -> p h f", f=64)),
                            deps=(mm[bi], P.ps_free[bi], vs_ones))
                        P.ps_free[bi] = e
                else:
                    scale = 0.125 if sweep == 0 else 1.0
                    evs = []
                    for bi in range(2):
                        e1 = P.do("scalar", lambda s_, bi=bi, scale=scale: s_.activation(
                            out=ES[:, :], in_=P.PS[bi][:, 0:512], func=AF.Exp,
                            scale=IM(scale)),
                            deps=(mm[bi], getattr(P, "_l0_es_war", None)))
                        a1 = P.do("vector", lambda v: v.tensor_scalar(
                            out=AS[:, :], in0=ES[:, :], scalar1=1.0, scalar2=1e-4,
                            op0=ALU.min, op1=ALU.add), deps=(e1,))
                        r1 = P.do("vector", lambda v, bi=bi, scale=scale: v.tensor_scalar(
                            out=RS[:, :], in0=P.PS[bi][:, 0:512], scalar1=scale,
                            scalar2=0.0, op0=ALU.mult, op1=ALU.max),
                            deps=(mm[bi], P.ps_free[bi], a1))
                        e = P.do("vector", lambda v, bi=bi, xnv=xnv: v.tensor_tensor(
                            out=xnv[:, bi * 512:(bi + 1) * 512], in0=RS[:, :],
                            in1=AS[:, :], op=ALU.add), deps=(r1, xt_tok))
                        P._l0_es_war = e
                        evs.append(e)
                        P.ps_free[bi] = e
                    dststore = QTS if sweep == 0 else KTS
                    pt_b = tl % 2
                    for c in range(8):
                        trq = P.do("tensor", lambda t_, c=c, pt_b=pt_b, xnv=xnv: t_.transpose(
                            P.PT[pt_b][:, c * 128:(c + 1) * 128],
                            xnv[:, c * 128:(c + 1) * 128], P.IDN[:]),
                            deps=(evs[-1], P.pt_free[pt_b], store_war) if c == 0 else (),
                            inc="pe" if c == 7 else None)
                    dstv = dststore[:, :].rearrange(
                        "p (c n) -> p c n", n=2048)[:, :, tl * 128:(tl + 1) * 128]
                    evt = P.do("vector", lambda v, dstv=dstv, pt_b=pt_b: v.tensor_copy(
                        out=dstv, in_=P.PT[pt_b][:, :].rearrange("p (c n) -> p c n", n=128)),
                        deps=(trq,))
                    P.pt_free[pt_b] = evt
                    store_war = evt
                    pipe.set_xn_reader(tl, evt)
            P.barrier()

        w_war = P.fence("tensor")
        wo_tok = P.dma("gpsimd", lambda g: g.dma_start(
            out=W1.sb[:, :].rearrange("p (c f) -> p c f", f=1024),
            in_=aps["W_out"].rearrange("(c p) f -> p c f", p=128)), deps=(w_war,))
        sz = P.do("vector", lambda v: v.memset(SSB[:, :], 0.0))
        szb = P.do("vector", lambda v: v.memset(SBF[:, :], 0.0))
        num_evac_prev = [None, None]
        sstate_rd = None
        out_done = None
        for u in range(TPS):
            pt_b = u % 2
            for c in range(8):
                trk = P.do("tensor", lambda t_, c=c, u=u, pt_b=pt_b: t_.transpose(
                    P.PT[pt_b][:, c * 128:(c + 1) * 128],
                    KTS[:, c * 2048 + u * 128: c * 2048 + (u + 1) * 128], P.IDN[:]),
                    deps=(P.pt_free[pt_b], getattr(P, "_l0_kt_war", None)) if c == 0 else (),
                    inc="pe" if c == 7 else None)
            ktk = P.do("vector", lambda v, pt_b=pt_b: v.tensor_copy(
                out=KTOK1[:, :], in_=P.PT[pt_b][:, :]),
                deps=(trk, getattr(P, "_l0_ktok_war", None)))
            P.pt_free[pt_b] = ktk
            atm_war = getattr(P, "_l0_atm_war", {})
            state_rd = getattr(P, "_l0_state_rd", {})
            for h in range(16):
                hc, base = h // 2, 64 * (h % 2)
                sl = h % 2
                abank = sl
                nbank = 2 + sl
                sbank = 4 if sl == 0 else 5
                amm = P.do("tensor", lambda t_, hc=hc, base=base, u=u, abank=abank: t_.matmul(
                    P.PS[abank][:, 0:128],
                    KTS[base:base + 64, hc * 2048 + u * 128: hc * 2048 + (u + 1) * 128],
                    QTS[base:base + 64, hc * 2048 + u * 128: hc * 2048 + (u + 1) * 128],
                    start=True, stop=True),
                    deps=(P.ps_free[abank],), inc="pe")
                msk = P.do("vector", lambda v, abank=abank, sl=sl: v.tensor_tensor(
                    out=ATM[:, sl * 128:(sl + 1) * 128], in0=P.PS[abank][:, 0:128],
                    in1=MTC[:, :],
                    op=ALU.mult), deps=(amm, mt_tok, atm_war.get(sl)))
                P.ps_free[abank] = msk
                vsl = VSAUG[:, (u * 16 + h) * 65:(u * 16 + h) * 65 + 65]
                nmm = P.do("tensor", lambda t_, vsl=vsl, nbank=nbank, u=u, sl=sl: t_.matmul(
                    P.PS[nbank][:, 0:65], ATM[:, sl * 128:(sl + 1) * 128], vsl,
                    start=True, stop=(u == 0)),
                    deps=(msk, P.ps_free[nbank], vs_ones), inc="pe" if u == 0 else None)
                if u > 0:
                    nmm = P.do("tensor", lambda t_, hc=hc, base=base, u=u, nbank=nbank,
                               h=h: t_.matmul(
                                   P.PS[nbank][:, 0:65],
                                   QTS[base:base + 64, hc * 2048 + u * 128: hc * 2048 + (u + 1) * 128],
                                   SBF[base:base + 64, (h // 2) * 65:(h // 2 + 1) * 65],
                                   start=False, stop=True),
                               deps=(state_rd.get(h),), inc="pe")
                atm_war[sl] = nmm
                ne = P.do("vector", lambda v, h=h, nbank=nbank: v.tensor_copy(
                    out=NUMS[:, h * 65:(h + 1) * 65], in_=P.PS[nbank][:, 0:65]),
                    deps=(nmm, num_evac_prev[h % 2], out_done))
                num_evac_prev[h % 2] = ne
                P.ps_free[nbank] = ne
                blk = (h // 2) * 65
                smm = P.do("tensor", lambda t_, h=h, vsl=vsl, base=base, sbank=sbank:
                           t_.matmul(
                    P.PS[sbank][base:base + 64, 0:65], KTOK1[:, 64 * h:64 * h + 64], vsl,
                    start=True, stop=True, tile_position=(0, base)),
                    deps=(ktk, P.ps_free[sbank]), inc="pe")
                f1 = P.do("vector", lambda v, base=base, blk=blk, sbank=sbank: v.tensor_tensor(
                    out=SSB[base:base + 64, blk:blk + 65],
                    in0=P.PS[sbank][base:base + 64, 0:65],
                    in1=SSB[base:base + 64, blk:blk + 65], op=ALU.add),
                    deps=(smm, sz))
                P.ps_free[sbank] = f1
                state_rd[h] = P.do("vector", lambda v, base=base, blk=blk: v.tensor_copy(
                    out=SBF[base:base + 64, blk:blk + 65],
                    in_=SSB[base:base + 64, blk:blk + 65]),
                    deps=(f1, szb, nmm))
            P._l0_atm_war = atm_war
            P._l0_state_rd = state_rd
            P._l0_kt_war = amm
            P._l0_ktok_war = smm
            d1 = P.do("vector", lambda v: v.tensor_scalar(
                out=DEN[:, :],
                in0=NUMS[:, :].rearrange("p (h f) -> p h f", f=65)[:, :, 64:65].squeeze(2),
                scalar1=1e-6, scalar2=None, op0=ALU.add),
                deps=(num_evac_prev[0], num_evac_prev[1]))
            d2 = P.do("vector", lambda v: v.reciprocal(out=RECS[:, :], in_=DEN[:, :]),
                      deps=(d1,))
            o1 = P.do("vector", lambda v: v.tensor_tensor(
                out=OUT1[:, :].rearrange("p (h f) -> p h f", f=64),
                in0=NUMS[:, :].rearrange("p (h f) -> p h f", f=65)[:, :, 0:64],
                in1=RECS[:, :].unsqueeze(2).broadcast_to([128, 16, 64]),
                op=ALU.mult), deps=(d2, out_done))
            t_glob = s * TPS + u
            rt = r_tile(P, t_glob)
            xt_tok = emit_transpose_1024(P, OUT1, P.XT, (o1,), pt_bank=u % 2,
                                         war_dep=getattr(P, "_l0_xt_war", None))
            for bi in range(2):
                P.do("tensor", lambda t_, bi=bi: t_.matmul(
                    P.PS[bi][:, 0:512], ONESB[0:1, :],
                    BOROW[0:1, bi * 512:(bi + 1) * 512], start=True, stop=False),
                    deps=(P.ps_free[bi], bo_tok, o1b))
            mmo = emit_proj_1024(P, P.XT, W1.sb, wo_tok, out_banks=[0, 1],
                                 fout=1024, extra_deps=(xt_tok,), accumulate=True)
            P._l0_xt_war = mmo[-1]
            for bi in range(2):
                lo, hi = bi * 512, (bi + 1) * 512
                out_done = P.do("vector", lambda v, bi=bi, lo=lo, hi=hi, rt=rt:
                                v.tensor_tensor(out=rt[:, lo:hi], in0=P.PS[bi][:, 0:512],
                                                in1=rt[:, lo:hi], op=ALU.add),
                                deps=(mmo[bi],))
                P.ps_free[bi] = out_done
        P._l0_vs_war = P.fence("tensor")
        P.barrier()
    P.barrier()



LN16 = 2.772588722239781
L1_LOCAL_STAB = False


def emit_L1_attn(P, ctx, aps):
    nc = P.nc
    sb = lambda name, shape, dt: ctx.enter_context(nc.sbuf_tensor(P.tag + name, shape, dt))
    MRS = sb("l1a_mrs", [128, 2 * NTILE], F32)
    W1 = WBuf(P, ctx, "l1a_w1", 1024, 1024)
    QTS = sb("l1a_qts", [128, 8 * 2048], BF16)
    KTS = sb("l1a_kts", [128, 8 * 2048], BF16)
    VSAUG = sb("l1a_vsaug", [128, 16 * 16 * 65], BF16)
    DIAGQ = sb("l1a_diagq", [128, 16 * 16], F32)
    DIAGK = sb("l1a_diagk", [128, 16 * 16], F32)
    SQS = sb("l1a_sqs", [128, 1024], F32)
    PROJ0 = sb("l1a_proj0", [128, 2 * 64], BF16)
    PROJT = sb("l1a_projt", [128, 256], BF16)
    MAXT = sb("l1a_maxt", [128, 256], F32)
    MAX1 = sb("l1a_max1", [128, 1], F32)
    MROW = sb("l1a_mrow", [1, 128], F32)
    STABR = sb("l1a_stabr", [1, 8], F32)
    NEGST = sb("l1a_negst", [128, 1], F32)
    CTXS = sb("l1a_ctxs", [128, 16 * 130], BF16)
    CTXT = sb("l1a_ctxt", [65, 256], BF16)
    MX = sb("l1a_mx", [128, 4], F32)
    NDS = sb("l1a_nds", [128, 4], F32)
    QPB = sb("l1a_qpb", [128, 1024], BF16)
    QPT = sb("l1a_qpt", [128, 1024], BF16)
    NUMS = sb("l1a_nums", [128, 16 * 65], BF16)
    DEN = sb("l1a_den", [128, 16], F32)
    RECS = sb("l1a_recs", [128, 16], F32)
    OUT1 = sb("l1a_out1", [128, 1024], BF16)
    BOROW = sb("l1a_borow", [1, 1024], BF16)
    ONESB = sb("l1a_onesb", [1, 128], BF16)
    IDNF = sb("l1a_idnf", [128, 128], F32)
    stab_loc = nc.dram_tensor(P.tag + "l1a_stab_loc", [1, 8], F32, kind="Internal").ap()
    stab_out = nc.dram_tensor(P.tag + "l1a_stab_out", [1, 8], F32, kind="Internal").ap()

    o1b = P.do("vector", lambda v: v.memset(ONESB[0:1, :], 1.0))
    idnf_tok = P.dma("sync", lambda s_: s_.dma_start(out=IDNF[:, :], in_=aps["idn"]))
    bo_tok = P.dma("gpsimd", lambda g: g.dma_start(out=BOROW[0:1, :], in_=aps["bo"]))
    pj_tok = P.dma("gpsimd", lambda g: g.dma_start(
        out=PROJ0[:, :].rearrange("p (c f) -> p c f", f=64),
        in_=aps["proj"].rearrange("(c p) f -> p c f", p=128)))
    for c in range(2):
        trp = P.do("tensor", lambda t, c=c: t.transpose(
            P.PT[0][0:64, c * 128:c * 128 + 128], PROJ0[:, c * 64:(c + 1) * 64],
            P.IDN[:]),
            deps=(pj_tok, P.pt_free[0], P.idn_tok) if c == 0 else ())
        trp = P.do("tensor", lambda t, c=c: t.transpose(
            P.PT[0][64:128, c * 128 + 512:c * 128 + 640],
            PROJ0[:, c * 64:(c + 1) * 64], P.IDN[:], tile_position=(0, 64)),
            deps=(), inc="pe" if c == 1 else None)
    pjt_tok0 = P.do("vector", lambda v: v.tensor_scalar(
        out=PROJT[0:64, :].rearrange("p (c n) -> p c n", n=128),
        in0=P.PT[0][0:64, 0:256].rearrange("p (c n) -> p c n", n=128),
        scalar1=0.35355339059327373, scalar2=None, op0=ALU.mult), deps=(trp,))
    pjt_tok = P.do("vector", lambda v: v.tensor_scalar(
        out=PROJT[64:128, :].rearrange("p (c n) -> p c n", n=128),
        in0=P.PT[0][64:128, 512:768].rearrange("p (c n) -> p c n", n=128),
        scalar1=0.35355339059327373, scalar2=None, op0=ALU.mult), deps=(pjt_tok0,))
    P.pt_free[0] = pjt_tok

    xn_war = None
    for t in range(NTILE):
        xn_tok = emit_ln_tile(P, t, xn_war=xn_war)
        xn_war = xn_tok
        P.do("vector", lambda v, t=t: v.tensor_copy(
            out=MRS[:, 2 * t:2 * t + 1], in_=P.NEGMEAN[:, 0:1]), deps=(xn_tok,))
        P.do("vector", lambda v, t=t: v.tensor_copy(
            out=MRS[:, 2 * t + 1:2 * t + 2], in_=P.RSTD[:, 0:1]))
    P.barrier()

    def sweep(s, which, store, diag_store):
        wname = ("wq", "wk", "wv")[which]
        w_war = P.fence("tensor")
        w_tok = P.dma("gpsimd", lambda g, wname=wname: g.dma_start(
            out=W1.sb[:, :].rearrange("p (c f) -> p c f", f=1024),
            in_=aps[wname].rearrange("(c p) f -> p c f", p=128)), deps=(w_war,))
        store_war = None
        dq_war = None

        def mk_l1s(tl, ap, war, s=s):
            t = s * TPS + tl
            return P.do("vector", lambda v, t=t, ap=ap: v.tensor_scalar(
                out=ap, in0=r_tile(P, t), scalar1=MRS[:, 2 * t:2 * t + 1],
                scalar2=MRS[:, 2 * t + 1:2 * t + 2], op0=ALU.add, op1=ALU.mult),
                deps=(war,) + tuple(P.r_toks))

        pipe = XnPipe(P, mk_l1s, TPS)
        for tl in range(TPS):
            t = s * TPS + tl
            xt_tok = pipe.xt_tok[tl]
            xnv = pipe.xn_ap(tl)
            mm = emit_proj_1024(P, pipe.xt_ap(tl), W1.sb, w_tok, out_banks=[0, 1],
                                fout=1024, extra_deps=(xt_tok,))
            pipe.done(tl, mm[-1])
            if store is VSAUG:
                for bi in range(2):
                    dstv = VSAUG[:, tl * 1040:(tl + 1) * 1040].rearrange(
                        "p (h f) -> p h f", f=65)[:, bi * 8:(bi + 1) * 8, 0:64]
                    e = P.do("vector", lambda v, bi=bi, dstv=dstv: v.tensor_copy(
                        out=dstv, in_=P.PS[bi][:, 0:512].rearrange(
                            "p (h f) -> p h f", f=64)),
                        deps=(mm[bi], P.ps_free[bi], P._l1_vs_ones))
                    P.ps_free[bi] = e
            else:
                evs = []
                for bi in range(2):
                    e = P.do("vector", lambda v, bi=bi, xnv=xnv: v.tensor_copy(
                        out=xnv[:, bi * 512:(bi + 1) * 512], in_=P.PS[bi][:, 0:512]),
                        deps=(mm[bi], P.ps_free[bi], xt_tok))
                    evs.append(e)
                    P.ps_free[bi] = e
                if diag_store is not None:
                    sq = P.do("scalar", lambda s_, xnv=xnv: s_.activation(
                        out=SQS[:, :], in_=xnv, func=AF.Square),
                        deps=(evs[-1], dq_war))
                    rd = P.do("vector", lambda v, tl=tl, diag_store=diag_store: v.tensor_reduce(
                        out=diag_store[:, tl * 16:(tl + 1) * 16],
                        in_=SQS[:, :].rearrange("p (h f) -> p h f", f=64),
                        axis=AX.X, op=ALU.add), deps=(sq,))
                    dq = P.do("vector", lambda v, tl=tl, diag_store=diag_store: v.tensor_scalar(
                        out=diag_store[:, tl * 16:(tl + 1) * 16],
                        in0=diag_store[:, tl * 16:(tl + 1) * 16],
                        scalar1=0.0625, scalar2=LN16, op0=ALU.mult, op1=ALU.add),
                        deps=(rd,))
                    dq_war = dq
                pt_b = tl % 2
                for c in range(8):
                    trq = P.do("tensor", lambda t_, c=c, pt_b=pt_b, xnv=xnv: t_.transpose(
                        P.PT[pt_b][:, c * 128:(c + 1) * 128],
                        xnv[:, c * 128:(c + 1) * 128], P.IDN[:]),
                        deps=(evs[-1], P.pt_free[pt_b], store_war) if c == 0 else (),
                        inc="pe" if c == 7 else None)
                dstv = store[:, :].rearrange(
                    "p (c n) -> p c n", n=2048)[:, :, tl * 128:(tl + 1) * 128]
                evt = P.do("vector", lambda v, dstv=dstv, pt_b=pt_b: v.tensor_copy(
                    out=dstv, in_=P.PT[pt_b][:, :].rearrange("p (c n) -> p c n", n=128)),
                    deps=(trq,))
                P.pt_free[pt_b] = evt
                store_war = evt
                pipe.set_xn_reader(tl, evt)
        P.barrier()
        return

    mz = P.do("vector", lambda v: v.memset(MAXT[:, :], -1e30))
    P._l1_vs_ones = None
    for s in range(2):
        sweep(s, 1, KTS, DIAGK)
        mprev = mz
        for h in range(16):
            hc, base = h // 2, 64 * (h % 2)
            for tl in range(TPS):
                bank = 2 + (tl % 2)
                dmm = P.do("tensor", lambda t_, hc=hc, base=base, tl=tl, bank=bank:
                           t_.matmul(P.PS[bank][:, 0:256],
                                     KTS[base:base + 64, hc * 2048 + tl * 128: hc * 2048 + (tl + 1) * 128],
                                     PROJT[base:base + 64, :], start=True, stop=True),
                           deps=(P.ps_free[bank], pjt_tok), inc="pe")
                mt = P.do("vector", lambda v, bank=bank: v.tensor_tensor(
                    out=MAXT[:, :], in0=P.PS[bank][:, 0:256], in1=MAXT[:, :],
                    op=ALU.max), deps=(dmm, mprev))
                mprev = mt
                P.ps_free[bank] = mt
        P._l1_mprev = mprev
        P.barrier()
    m1 = P.do("vector", lambda v: v.tensor_reduce(out=MAX1[:, 0:1], in_=MAXT[:, :],
                                                  axis=AX.X, op=ALU.max),
              deps=(P._l1_mprev,))
    trm = P.do("tensor", lambda t: t.transpose(P.PS[5][0:1, 0:128], MAX1[:, 0:1],
                                               IDNF[:, :]),
               deps=(m1, P.ps_free[5], idnf_tok), inc="pe")
    m2 = P.do("vector", lambda v: v.tensor_copy(out=MROW[0:1, :], in_=P.PS[5][0:1, 0:128]),
              deps=(trm,))
    P.ps_free[5] = m2
    m3 = P.do("vector", lambda v: v.memset(STABR[0:1, :], -1e30), deps=())
    m4 = P.do("vector", lambda v: v.tensor_reduce(out=STABR[0:1, 0:1], in_=MROW[0:1, :],
                                                  axis=AX.X, op=ALU.max), deps=(m2, m3))
    st_tok = P.dma("gpsimd", lambda g: g.dma_start(out=stab_loc, in_=STABR[0:1, :]),
                   deps=(m4,))
    if L1_LOCAL_STAB:
        cc_tok = st_tok
        src_stab = stab_loc
    else:
        cc_tok = P.do("gpsimd", lambda g: g.collective_compute(
            "AllReduce", ALU.max, replica_groups=[list(range(8))],
            ins=[stab_loc], outs=[stab_out]), deps=(st_tok,), inc="pool")
        src_stab = stab_out
    ld_tok = P.dma("gpsimd", lambda g: g.dma_start(out=STABR[0:1, :], in_=src_stab),
                   deps=(cc_tok,))
    bmm = P.do("tensor", lambda t: t.matmul(P.PS[0][:, 0:1], P.ONES1[0:1, :],
                                            STABR[0:1, 0:1], start=True, stop=True),
               deps=(ld_tok, P.ones_tok, P.ps_free[0]), inc="pe")
    nst = P.do("vector", lambda v: v.tensor_scalar(
        out=NEGST[:, 0:1], in0=P.PS[0][:, 0:1], scalar1=-1.0, scalar2=None,
        op0=ALU.mult), deps=(bmm,))
    P.ps_free[0] = nst

    for s in range(2):
        P._l1_vs_ones = P.do("vector", lambda v: v.memset(VSAUG[:, :], 1.0),
                             deps=(getattr(P, "_l1_vs_war", None),))
        sweep(s, 0, QTS, DIAGQ)
        sweep(s, 1, KTS, DIAGK)
        sweep(s, 2, VSAUG, None)
        for h in range(16):
            hc, base = h // 2, 64 * (h % 2)
            ctx_free = getattr(P, "_l1_ctx_free", None)
            nds_rd = getattr(P, "_l1_nds_rd", {})
            qpb_rd = getattr(P, "_l1_qpb_rd", {})
            for tl in range(TPS):
                sl = tl % 4
                bank = sl
                dmm = P.do("tensor", lambda t_, hc=hc, base=base, tl=tl, bank=bank:
                           t_.matmul(P.PS[bank][:, 0:256],
                                     KTS[base:base + 64, hc * 2048 + tl * 128: hc * 2048 + (tl + 1) * 128],
                                     PROJT[base:base + 64, :], start=True, stop=True),
                           deps=(P.ps_free[bank], pjt_tok), inc="pe")
                nd = P.do("vector", lambda v, tl=tl, h=h, sl=sl: v.scalar_tensor_tensor(
                    out=NDS[:, sl:sl + 1], in0=DIAGK[:, tl * 16 + h:tl * 16 + h + 1],
                    scalar=-1.0, in1=NEGST[:, 0:1], op0=ALU.mult, op1=ALU.add),
                    deps=(nst, nds_rd.get(sl)))
                ex = P.do("scalar", lambda s_, bank=bank, sl=sl: s_.activation(
                    out=SQS[:, sl * 256:sl * 256 + 256], in_=P.PS[bank][:, 0:256],
                    func=AF.Exp, bias=NDS[:, sl:sl + 1], scale=IM(1.0)),
                    deps=(dmm, nd, qpb_rd.get(sl)))
                nds_rd[sl] = ex
                kp = P.do("scalar", lambda s_, sl=sl: s_.add(
                    out=QPB[:, sl * 256:sl * 256 + 256],
                    in_=SQS[:, sl * 256:sl * 256 + 256], add=IM(6.25e-6)), deps=(ex,))
                qpb_rd[sl] = kp
                deps = [kp]
                if tl == 0:
                    deps += [ctx_free, P._l1_vs_ones]
                cmm = P.do("tensor", lambda t_, tl=tl, h=h, sl=sl: t_.matmul(
                    P.PS[4][0:65, 0:256],
                    VSAUG[:, (tl * 16 + h) * 65:(tl * 16 + h) * 65 + 65],
                    QPB[:, sl * 256:sl * 256 + 256],
                    start=(tl == 0), stop=(tl == TPS - 1)),
                    deps=deps, inc="pe")
                qpb_rd[sl] = cmm
            P._l1_nds_rd = nds_rd
            P._l1_qpb_rd = qpb_rd
            ce = P.do("vector", lambda v: v.tensor_copy(out=CTXT[0:65, :],
                                                        in_=P.PS[4][0:65, 0:256]),
                      deps=(cmm,))
            P.ps_free[4] = ce
            for c in range(2):
                trc = P.do("tensor", lambda t_, c=c: t_.transpose(
                    P.PT[0][0:128, c * 128:c * 128 + 65],
                    CTXT[0:65, c * 128:(c + 1) * 128], P.IDN[0:65, 0:65]),
                    deps=(ce, P.pt_free[0]) if c == 0 else (),
                    inc="pe" if c == 1 else None)
            cse = P.do("vector", lambda v, h=h: v.tensor_copy(
                out=CTXS[:, h * 130:(h + 1) * 130].rearrange("p (c f) -> p c f", f=65),
                in_=P.PT[0][:, 0:256].rearrange("p (c f) -> p c f", f=128)[:, :, 0:65]),
                deps=(trc,))
            P.pt_free[0] = cse
            P._l1_ctx_free = cse
        w_war = P.fence("tensor")
        wo_tok = P.dma("gpsimd", lambda g: g.dma_start(
            out=W1.sb[:, :].rearrange("p (c f) -> p c f", f=1024),
            in_=aps["wo"].rearrange("(c p) f -> p c f", p=128)), deps=(w_war,))
        num_prev = [None, None]
        out_done = None
        mx_rd, nds_rd2, qpb_rd2, qpt_rd = {}, {}, {}, {}
        for tl in range(TPS):
            for h in range(16):
                hc, base = h // 2, 64 * (h % 2)
                sl = h % 3
                bank = 2 + sl
                nbank = h % 2
                dmm = P.do("tensor", lambda t_, hc=hc, base=base, tl=tl, bank=bank:
                           t_.matmul(P.PS[bank][:, 0:256],
                                     QTS[base:base + 64, hc * 2048 + tl * 128: hc * 2048 + (tl + 1) * 128],
                                     PROJT[base:base + 64, :], start=True, stop=True),
                           deps=(P.ps_free[bank],), inc="pe")
                mx = P.do("vector", lambda v, bank=bank, sl=sl: v.tensor_reduce(
                    out=MX[:, sl:sl + 1], in_=P.PS[bank][:, 0:256], axis=AX.X, op=ALU.max),
                    deps=(dmm, mx_rd.get(sl)))
                nd = P.do("vector", lambda v, tl=tl, h=h, sl=sl: v.scalar_tensor_tensor(
                    out=NDS[:, sl:sl + 1], in0=MX[:, sl:sl + 1], scalar=-1.0,
                    in1=DIAGQ[:, tl * 16 + h:tl * 16 + h + 1],
                    op0=ALU.mult, op1=ALU.subtract),
                    deps=(mx, nds_rd2.get(sl)))
                ex = P.do("scalar", lambda s_, bank=bank, sl=sl: s_.activation(
                    out=SQS[:, sl * 256:sl * 256 + 256], in_=P.PS[bank][:, 0:256],
                    func=AF.Exp, bias=NDS[:, sl:sl + 1], scale=IM(1.0)),
                    deps=(nd, qpb_rd2.get(sl)))
                nds_rd2[sl] = ex
                mx_rd[sl] = ex
                qp = P.do("scalar", lambda s_, sl=sl: s_.add(
                    out=QPB[:, sl * 256:sl * 256 + 256],
                    in_=SQS[:, sl * 256:sl * 256 + 256], add=IM(6.25e-6)), deps=(ex,))
                P.ps_free[bank] = qp
                pt_b = h % 2
                for c in range(2):
                    trq = P.do("tensor", lambda t_, c=c, pt_b=pt_b, sl=sl: t_.transpose(
                        P.PT[pt_b][:, c * 128:(c + 1) * 128],
                        QPB[:, sl * 256 + c * 128:sl * 256 + (c + 1) * 128], P.IDN[:]),
                        deps=(qp, P.pt_free[pt_b]) if c == 0 else (),
                        inc="pe" if c == 1 else None)
                qpb_rd2[sl] = trq
                qpt = P.do("vector", lambda v, pt_b=pt_b, sl=sl: v.tensor_copy(
                    out=QPT[:, sl * 256:sl * 256 + 256], in_=P.PT[pt_b][:, 0:256]),
                    deps=(trq, qpt_rd.get(sl)))
                P.pt_free[pt_b] = qpt
                nmm = None
                for c in range(2):
                    deps = [qpt]
                    if c == 0:
                        deps += [num_prev[nbank], cse]
                    nmm = P.do("tensor", lambda t_, c=c, h=h, nbank=nbank, sl=sl: t_.matmul(
                        P.PS[nbank][:, 0:65], QPT[:, sl * 256 + c * 128:sl * 256 + (c + 1) * 128],
                        CTXS[:, h * 130 + c * 65: h * 130 + (c + 1) * 65],
                        start=(c == 0), stop=(c == 1)),
                        deps=deps, inc="pe" if c == 1 else None)
                qpt_rd[sl] = nmm
                ne = P.do("vector", lambda v, h=h, nbank=nbank: v.tensor_copy(
                    out=NUMS[:, h * 65:(h + 1) * 65], in_=P.PS[nbank][0:128, 0:65]),
                    deps=(nmm, out_done))
                num_prev[nbank] = ne
                P.ps_free[nbank] = ne
            d1 = P.do("vector", lambda v: v.tensor_copy(
                out=DEN[:, :],
                in_=NUMS[:, :].rearrange("p (h f) -> p h f", f=65)[:, :, 64:65].squeeze(2)),
                deps=(num_prev[0], num_prev[1]))
            d2 = P.do("vector", lambda v: v.reciprocal(out=RECS[:, :], in_=DEN[:, :]),
                      deps=(d1,))
            o1 = P.do("vector", lambda v: v.tensor_tensor(
                out=OUT1[:, :].rearrange("p (h f) -> p h f", f=64),
                in0=NUMS[:, :].rearrange("p (h f) -> p h f", f=65)[:, :, 0:64],
                in1=RECS[:, :].unsqueeze(2).broadcast_to([128, 16, 64]),
                op=ALU.mult), deps=(d2, out_done))
            t_glob = s * TPS + tl
            rt = r_tile(P, t_glob)
            xt_tok = emit_transpose_1024(P, OUT1, P.XT, (o1,), pt_bank=tl % 2,
                                         war_dep=getattr(P, "_l1_xt_war", None))
            for bi in range(2):
                P.do("tensor", lambda t_, bi=bi: t_.matmul(
                    P.PS[bi][:, 0:512], ONESB[0:1, :],
                    BOROW[0:1, bi * 512:(bi + 1) * 512], start=True, stop=False),
                    deps=(P.ps_free[bi], bo_tok, o1b))
            mmo = emit_proj_1024(P, P.XT, W1.sb, wo_tok, out_banks=[0, 1],
                                 fout=1024, extra_deps=(xt_tok,), accumulate=True)
            P._l1_xt_war = mmo[-1]
            for bi in range(2):
                lo, hi = bi * 512, (bi + 1) * 512
                out_done = P.do("vector", lambda v, bi=bi, lo=lo, hi=hi, rt=rt:
                                v.tensor_tensor(out=rt[:, lo:hi], in0=P.PS[bi][:, 0:512],
                                                in1=rt[:, lo:hi], op=ALU.add),
                                deps=(mmo[bi],))
                P.ps_free[bi] = out_done
            num_prev = [out_done, out_done]
        P._l1_vs_war = P.fence("tensor")
        P.barrier()
    P.barrier()



def _declare_io(nc, names_shapes, out_shape):
    aps = {}
    for name, shape in names_shapes.items():
        aps[name] = nc.dram_tensor(name, list(shape), F32, kind="ExternalInput").ap()
    y = nc.dram_tensor("y", list(out_shape), F32, kind="ExternalOutput").ap()
    return aps, y


def _idn():
    return np.eye(128, dtype=np.float32)


def _mtc():
    return np.triu(np.ones((128, 128), np.float32))


def _selc():
    sel = np.zeros((128, 16 * 64), np.float32)
    for tl in range(16):
        for j in range(4):
            sel[32 * j, tl * 64 + 4 * tl + j] = 1.0
    return sel


SPEC = {
    "L2_ffn": dict(
        shapes={"W": (1024, 1024), "b": (1, 1024), "lora_down": (4, 100),
                "lora_up": (50, 4), "down_aux": (100, 1024), "up_aux": (1024, 50)},
        emit=emit_L2_ffn,
        in_map=lambda inp: dict(
            W=inp["L2_ffn_W"], b=np.asarray(inp["L2_ffn_b"]).reshape(1, 1024),
            lora_down=inp["L2_ffn_lora_down"], lora_up=inp["L2_ffn_lora_up"],
            down_aux=inp["L2_ffn_down_aux"], up_aux=inp["L2_ffn_up_aux"]),
    ),
    "L3_ffn": dict(
        shapes={"weight": (1024, 1024)},
        emit=emit_L3_ffn,
        in_map=lambda inp: dict(weight=inp["L3_ffn_weight"]),
    ),
    "L0_ffn": dict(
        shapes={"router": (1024, 8), "experts_gate": (8, 1024, 512),
                "experts_up": (8, 1024, 512), "experts_down": (8, 512, 1024),
                "shared_gate": (1024, 1024), "shared_up": (1024, 1024),
                "shared_down": (1024, 1024)},
        emit=emit_L0_ffn,
        in_map=lambda inp: dict(
            router=inp["L0_ffn_router"], experts_gate=inp["L0_ffn_experts_gate"],
            experts_up=inp["L0_ffn_experts_up"], experts_down=inp["L0_ffn_experts_down"],
            shared_gate=inp["L0_ffn_shared_gate"], shared_up=inp["L0_ffn_shared_up"],
            shared_down=inp["L0_ffn_shared_down"]),
    ),
    "L3_attn": dict(
        shapes={"Wq": (1024, 256), "bq": (1, 256), "Wqc": (1024, 256), "bqc": (1, 256),
                "Wk": (1024, 256), "bk": (1, 256), "Wkc": (1024, 256), "bkc": (1, 256),
                "Wv": (1024, 1024), "bv": (1, 1024), "Wvc": (1024, 1024), "bvc": (1, 1024)},
        emit=emit_L3_attn,
        in_map=lambda inp: dict(
            Wq=inp["L3_attn_Wq"], bq=np.asarray(inp["L3_attn_bq"]).reshape(1, 256),
            Wqc=inp["L3_attn_Wqc"], bqc=np.asarray(inp["L3_attn_bqc"]).reshape(1, 256),
            Wk=inp["L3_attn_Wk"], bk=np.asarray(inp["L3_attn_bk"]).reshape(1, 256),
            Wkc=inp["L3_attn_Wkc"], bkc=np.asarray(inp["L3_attn_bkc"]).reshape(1, 256),
            Wv=inp["L3_attn_Wv"], bv=np.asarray(inp["L3_attn_bv"]).reshape(1, 1024),
            Wvc=inp["L3_attn_Wvc"], bvc=np.asarray(inp["L3_attn_bvc"]).reshape(1, 1024)),
    ),
    "L2_attn": dict(
        shapes={"W_qkv": (1024, 3072), "W_gate": (1024, 1), "b_gate": (1, 1),
                "W_out": (1024, 1024), "b_out": (1, 1024), "selc": (128, 1024)},
        emit=emit_L2_attn,
        in_map=lambda inp: dict(
            W_qkv=inp["L2_attn_W_qkv"], W_gate=inp["L2_attn_W_gate"],
            b_gate=np.asarray(inp["L2_attn_b_gate"]).reshape(1, 1),
            W_out=inp["L2_attn_W_out"],
            b_out=np.asarray(inp["L2_attn_b_out"]).reshape(1, 1024),
            selc=_selc()),
    ),
    "L0_attn": dict(
        shapes={"W_qkv": (1024, 3072), "b_qkv": (1, 3072), "W_out": (1024, 1024),
                "b_out": (1, 1024), "mtc": (128, 128)},
        emit=emit_L0_attn,
        in_map=lambda inp: dict(
            W_qkv=inp["L0_attn_W_qkv"],
            b_qkv=np.asarray(inp["L0_attn_b_qkv"]).reshape(1, 3072),
            W_out=inp["L0_attn_W_out"],
            b_out=np.asarray(inp["L0_attn_b_out"]).reshape(1, 1024),
            mtc=_mtc()),
    ),
    "L1_attn": dict(
        shapes={"wq": (1024, 1024), "wk": (1024, 1024), "wv": (1024, 1024),
                "wo": (1024, 1024), "bo": (1, 1024), "proj": (256, 64)},
        emit=emit_L1_attn,
        in_map=lambda inp: dict(
            wq=inp["L1_attn_wq"], wk=inp["L1_attn_wk"], wv=inp["L1_attn_wv"],
            wo=inp["L1_attn_wo"], bo=np.asarray(inp["L1_attn_bo"]).reshape(1, 1024),
            proj=inp["L1_attn_proj"]),
    ),
    "L1_ffn": dict(
        shapes={"Win": (1024, 64), "bin_": (1, 64), "W1": (1, 1024), "b1": (1, 1024),
                "W2": (1, 1024), "b2": (1, 64), "Wout": (64, 1024), "bout": (1, 1024)},
        emit=emit_L1_ffn,
        in_map=lambda inp: dict(
            Win=inp["L1_ffn_Win"], bin_=np.asarray(inp["L1_ffn_bin_"]).reshape(1, 64),
            W1=np.asarray(inp["L1_ffn_W1"]).reshape(1, 1024),
            b1=np.asarray(inp["L1_ffn_b1"]).reshape(1, 1024),
            W2=np.asarray(inp["L1_ffn_W2"]).reshape(1, 1024),
            b2=np.asarray(inp["L1_ffn_b2"]).reshape(1, 64),
            Wout=inp["L1_ffn_Wout"],
            bout=np.asarray(inp["L1_ffn_bout"]).reshape(1, 1024)),
    ),
}

CHAIN = ["L0_ffn", "L0_attn", "L0_ffn",
         "L1_ffn", "L1_attn", "L1_ffn",
         "L2_ffn", "L2_attn", "L2_ffn",
         "L3_ffn", "L3_attn", "L3_ffn"]

_launch_cache = {}


def build_launch(sub):
    if sub in _launch_cache:
        return _launch_cache[sub]
    spec = SPEC[sub]
    nc = bass.Bass("TRN2", target_bir_lowering=False)
    shapes = {"x": (2, 2048, 1024), "idn": (128, 128)}
    shapes.update(spec["shapes"])
    aps, y = _declare_io(nc, shapes, (2, 2048, 1024))
    ctx = ExitStack()
    with ctx:
        P = Prog(nc)
        alloc_common(P, ctx, evs=(sub not in ("L3_attn", "L2_attn", "L0_attn")))
        P.idn_tok = P.dma("gpsimd", lambda g: g.dma_start(out=P.IDN[:], in_=aps["idn"]))
        load_residual(P, aps["x"])
        spec["emit"](P, ctx, aps)
        done = P.fence("vector")
        store_residual(P, y, deps=(done,))
        for s_ in ST_POOL:
            if P.cnt[s_] > 0:
                P.wait("gpsimd", (s_, P.cnt[s_]))
        P.do("gpsimd", lambda g: g.nop())
        P.build(ctx)
    _launch_cache[sub] = nc
    return nc


def build_fused():
    nc = bass.Bass("TRN2", target_bir_lowering=False)
    shapes = {"x": (2, 2048, 1024), "idn": (128, 128)}
    for sub, spec in SPEC.items():
        for nm, shp in spec["shapes"].items():
            shapes[f"{sub}_{nm}"] = shp
    aps, y = _declare_io(nc, shapes, (2, 2048, 1024))
    ctx = ExitStack()
    with ctx:
        P = Prog(nc)
        alloc_common(P, ctx, evs=True)
        P.idn_tok = P.dma("gpsimd", lambda g: g.dma_start(out=P.IDN[:], in_=aps["idn"]))
        load_residual(P, aps["x"])
        for i, sub in enumerate(CHAIN):
            P.tag = f"s{i}_"
            sub_aps = {nm: aps[f"{sub}_{nm}"] for nm in SPEC[sub]["shapes"]}
            sub_aps["idn"] = aps["idn"]
            with ExitStack() as subctx:
                SPEC[sub]["emit"](P, subctx, sub_aps)
        P.tag = ""
        done = P.fence("vector")
        store_residual(P, y, deps=(done,))
        for s_ in ST_POOL:
            if P.cnt[s_] > 0:
                P.wait("gpsimd", (s_, P.cnt[s_]))
        P.do("gpsimd", lambda g: g.nop())
        P.build(ctx)
    return nc


def run_fused(inputs):
    x = np.ascontiguousarray(inputs["L0_attn_x"])
    nc = build_fused()
    base = dict(idn=_idn())
    for sub, spec in SPEC.items():
        for nm, v in spec["in_map"](inputs).items():
            base[f"{sub}_{nm}"] = np.ascontiguousarray(v)
    in_maps = [dict(x=np.ascontiguousarray(x[2 * c:2 * c + 2]), **base)
               for c in range(8)]
    res = run_bass_kernel_spmd(nc, in_maps, core_ids=list(range(8)))
    return np.concatenate([r["y"] for r in res.results], axis=0)


def run_chain(inputs, chain):
    x = np.ascontiguousarray(inputs["L0_attn_x"])
    idn = _idn()
    cur = [np.ascontiguousarray(x[2 * c:2 * c + 2]) for c in range(8)]
    for sub in chain:
        spec = SPEC[sub]
        nc = build_launch(sub)
        wmap = spec["in_map"](inputs)
        in_maps = [dict(x=cur[c], idn=idn, **wmap) for c in range(8)]
        res = run_bass_kernel_spmd(nc, in_maps, core_ids=list(range(8)))
        cur = [np.ascontiguousarray(res.results[c]["y"]) for c in range(8)]
    return np.concatenate(cur, axis=0)


FUSED = True


def kernel(**inputs):
    if FUSED:
        return run_fused(inputs)
    chain = [s for s in CHAIN if s in SPEC]
    return run_chain(inputs, chain)


# version 07
# speedup vs baseline: 1.0761x; 1.0761x over previous
import numpy as np
import ml_dtypes
import concourse.bass as bass
import concourse.mybir as mybir
from concourse.bass_utils import run_bass_kernel_spmd
from contextlib import ExitStack

F32 = mybir.dt.float32
BF16 = mybir.dt.bfloat16
AF = mybir.ActivationFunctionType
ALU = mybir.AluOpType
AX = mybir.AxisListType
BF16NP = ml_dtypes.bfloat16
IM = np.float32

ENGINES = ("sync", "scalar", "vector", "tensor", "gpsimd")
ESEM = {"sync": "syn", "scalar": "act", "vector": "dve", "tensor": "pe", "gpsimd": "pool"}
DMA_POOL = {"sync": [f"dmas{i}" for i in range(3)],
            "gpsimd": [f"dmag{i}" for i in range(6)],
            "scalar": [f"dmac{i}" for i in range(3)]}
ST_POOL = [f"st{i}" for i in range(2)]
SEM_NAMES = ("pe", "act", "dve", "pool", "syn") + tuple(
    s for pool in DMA_POOL.values() for s in pool) + tuple(ST_POOL)

LN_EPS = 1e-5
NTILE = 32
TPS = 16
D = 1024


class Prog:

    def __init__(self, nc):
        self.nc = nc
        self.ops = {e: [] for e in ENGINES}
        self.cnt = {s: 0 for s in SEM_NAMES}
        self.handles = {}
        self.waits = []
        self.waited = {e: {s: -1 for s in SEM_NAMES} for e in ENGINES}
        self.last_tok = {e: None for e in ENGINES}
        self.dirty = {e: False for e in ENGINES}
        self.dma_rr = {}
        self.tag = ""
        self._fence_scratch = None

    def wait(self, e, tok):
        if tok is None:
            return
        s, v = tok
        if self.waited[e][s] >= v:
            return
        self.waited[e][s] = v
        self.waits.append((e, s, v))
        self.ops[e].append(lambda eng, s=s, v=v: eng.wait_ge(self.handles[s], v))

    def do(self, e, fn, deps=(), inc="auto", inc_by=1):
        for d in deps:
            self.wait(e, d)
        if inc == "auto":
            inc = ESEM[e] if e in ("vector", "scalar", "gpsimd") else None
        if inc is None:
            self.ops[e].append(fn)
            self.dirty[e] = True
            return None
        self.cnt[inc] += inc_by
        tok = (inc, self.cnt[inc])
        self.ops[e].append(
            lambda eng, fn=fn, s=inc, b=inc_by: fn(eng).then_inc(self.handles[s], b))
        self.last_tok[e] = tok
        self.dirty[e] = False
        return tok

    def dma(self, queue, fn, deps=(), pool=None):
        pool = DMA_POOL[queue] if pool is None else pool
        idx = self.dma_rr.get(queue, 0)
        self.dma_rr[queue] = idx + 1
        sem = pool[idx % len(pool)]
        prev = (sem, self.cnt[sem]) if self.cnt[sem] > 0 else None
        return self.do(queue, fn, deps=tuple(deps) + ((prev,) if prev else ()),
                       inc=sem, inc_by=16)

    def store(self, queue, fn, deps=()):
        idx = self.dma_rr.get("st", 0)
        self.dma_rr["st"] = idx + 1
        sem = ST_POOL[idx % len(ST_POOL)]
        prev = (sem, self.cnt[sem]) if self.cnt[sem] > 0 else None
        return self.do(queue, fn, deps=tuple(deps) + ((prev,) if prev else ()),
                       inc=sem, inc_by=16)

    def fence(self, e):
        if self.last_tok[e] is not None and not self.dirty[e]:
            return self.last_tok[e]
        sem = ESEM[e]
        if e in ("vector", "gpsimd"):
            return self.do(e, lambda eng: eng.engine_nop(), inc=sem)
        if e == "scalar":
            fs = self._fence_scratch
            return self.do(e, lambda eng: eng.copy(out=fs[0:1, 0:1], in_=fs[0:1, 0:1]),
                           inc=sem)
        return self.do(e, lambda eng: eng.nop(), inc=sem)

    def barrier(self, engines=("scalar", "vector", "tensor", "gpsimd", "sync")):
        toks = {e: self.fence(e) for e in engines}
        for e in engines:
            for o, t in toks.items():
                if o != e:
                    self.wait(e, t)

    def finalize_check(self):
        for e, s, v in self.waits:
            assert v <= self.cnt[s], f"wait on {s}>={v} but final count {self.cnt[s]} (engine {e})"

    def build(self, ctx):
        self.finalize_check()
        for s in SEM_NAMES:
            self.handles[s] = ctx.enter_context(self.nc.semaphore(s))
        block = ctx.enter_context(self.nc.Block())
        for e in ENGINES:
            ops = self.ops[e]
            if not ops:
                continue
            def runner(eng, ops=ops):
                for op in ops:
                    op(eng)
            getattr(block, e)(runner)



def alloc_common(P, ctx, evs=True):
    nc = P.nc
    sb = lambda name, shape, dt: ctx.enter_context(nc.sbuf_tensor(name, shape, dt))
    P._fence_scratch = sb("fence_scr", [1, 1], F32)
    P.R = sb("resid", [128, NTILE * D], BF16)
    P.XN = sb("xn", [128, 2 * D], BF16)
    P.XT = sb("xt", [128, D], BF16)
    P.STAT = sb("stat", [128, 12], F32)
    P.MV = sb("mv", [128, 2], F32)
    P.NEGMEAN = sb("negmean", [128, 1], F32)
    P.RSTD = sb("rstd", [128, 1], F32)
    P.IDN = sb("idn_sb", [128, 128], BF16)

    P.ONES1 = sb("ones1", [1, 128], F32)
    P.ONESC = sb("onesc", [128, 1], F32)
    t_o1 = P.do("vector", lambda v: v.memset(P.ONES1[0:1, :], 1.0), inc="dve")
    P.ones_tok = P.do("vector", lambda v: v.memset(P.ONESC[:, 0:1], 1.0), inc="dve")
    P.PS = [ctx.enter_context(nc.psum_tensor(f"ps{i}", [128, 512], F32)) for i in range(6)]
    P.PT = [ctx.enter_context(nc.psum_tensor(f"pt{i}", [128, 1024], BF16)) for i in range(2)]
    P.ps_free = [None] * 6
    P.pt_free = [None] * 2
    P.idn_tok = None


def load_identity(P, idn_dram):
    P.idn_tok = P.dma("sync", lambda s: s.dma_start(out=P.IDN[:], in_=idn_dram))


def load_residual(P, x_dram):
    toks = []
    for s in range(2):
        src = x_dram[s].rearrange("(t p) d -> p t d", p=128)
        dst = P.R[:, s * TPS * D:(s + 1) * TPS * D].rearrange("p (t d) -> p t d", d=D)
        toks.append(P.dma("gpsimd", lambda g, src=src, dst=dst: g.dma_start(out=dst, in_=src)))
    P.r_toks = toks
    return toks


def store_residual(P, y_dram, deps):
    toks = []
    for s in range(2):
        dst = y_dram[s].rearrange("(t p) d -> p t d", p=128)
        src = P.R[:, s * TPS * D:(s + 1) * TPS * D].rearrange("p (t d) -> p t d", d=D)
        toks.append(P.store("gpsimd",
                            lambda g, src=src, dst=dst: g.dma_start(out=dst, in_=src),
                            deps=deps))
    return toks


def r_tile(P, t):
    return P.R[:, t * D:(t + 1) * D]


def emit_ln_tile(P, t, xn_war=None, xn_buf=None, extra_dep=None):
    xn = P.XN[:, 0:D] if xn_buf is None else xn_buf
    rt = r_tile(P, t)
    prev_xn, prev_act = getattr(P, "ln_prev", (None, None))
    t1 = P.do("vector", lambda v: v.bn_stats(out=P.STAT[:, 0:6], in_=rt[:, 0:512]),
              deps=tuple(P.r_toks) + (prev_xn, extra_dep))
    t2 = P.do("vector", lambda v: v.bn_stats(out=P.STAT[:, 6:12], in_=rt[:, 512:1024]))
    t3 = P.do("vector", lambda v: v.bn_aggr(out=P.MV[:], in_=P.STAT[:]),
              deps=(t2, prev_act))
    t4 = P.do("vector", lambda v: v.tensor_scalar(out=P.NEGMEAN[:], in0=P.MV[:, 0:1],
                                                  scalar1=-1.0, scalar2=None, op0=ALU.mult),
              deps=(t3,))
    t5 = P.do("scalar", lambda s: s.activation(out=P.RSTD[:], in_=P.MV[:, 1:2],
                                               func=AF.Ln, bias=IM(LN_EPS), scale=IM(1.0)),
              deps=(t3,))
    t6 = P.do("scalar", lambda s: s.activation(out=P.RSTD[:], in_=P.RSTD[:],
                                               func=AF.Exp, bias=IM(0.0), scale=IM(-0.5)),
              deps=(t5,))
    tok = P.do("vector", lambda v: v.tensor_scalar(out=xn[:, :], in0=rt,
                                                   scalar1=P.NEGMEAN[:], scalar2=P.RSTD[:],
                                                   op0=ALU.add, op1=ALU.mult),
               deps=(t4, t6, xn_war))
    P.ln_prev = (tok, t6)
    return tok


def emit_transpose_1024(P, src, dst, deps, pt_bank=0, war_dep=None, nchunk=8):
    pt = P.PT[pt_bank]
    mm_toks = []
    for c in range(nchunk):
        dd = (list(deps) + [P.pt_free[pt_bank], P.idn_tok]) if c == 0 else ()
        mm_toks.append(P.do("tensor",
                            lambda t, c=c: t.transpose(pt[:, c * 128:(c + 1) * 128],
                                                       src[:, c * 128:(c + 1) * 128],
                                                       P.IDN[:]),
                            deps=dd, inc="pe" if c == nchunk - 1 else None))
    w = 128 * nchunk
    ev = P.do("vector", lambda v: v.tensor_copy(out=dst[:, 0:w], in_=pt[:, 0:w]),
              deps=(mm_toks[-1], war_dep), inc="dve")
    P.pt_free[pt_bank] = ev
    return ev


class XnPipe:

    def __init__(self, P, make_xn, T):
        self.P, self.make_xn, self.T = P, make_xn, T
        self.xn_tok, self.xt_tok = {}, {}
        self.xn_reader, self.xt_reader = {}, {}
        self._emit(0)

    def xn_ap(self, t):
        sl = t % 2
        return self.P.XN[:, sl * D:(sl + 1) * D]

    def xt_ap(self, t):
        return self.P.XT[:, 0:D]

    def _emit(self, t):
        tok = self.make_xn(t, self.xn_ap(t), self.xn_reader.get(t - 2))
        tr = emit_transpose_1024(self.P, self.xn_ap(t), self.xt_ap(t), (tok,),
                                 pt_bank=t % 2, war_dep=self.xt_reader.get(t - 1))
        self.xn_tok[t], self.xt_tok[t] = tok, tr
        self.xn_reader[t] = tr

    def done(self, t, reader_tok):
        self.xt_reader[t] = reader_tok
        if t + 1 < self.T:
            self._emit(t + 1)

    def set_xn_reader(self, t, tok):
        self.xn_reader[t] = tok

    def set_xt_reader(self, t, tok):
        self.xt_reader[t] = tok


def emit_proj_1024(P, xT, w_sbuf, w_tok, out_banks, fout, extra_deps=(), nk=8,
                   accumulate=False):
    toks = []
    for bi, bank in enumerate(out_banks):
        ncols = min(512, fout - 512 * bi)
        ps = P.PS[bank]
        for k in range(nk):
            deps = []
            if k == 0:
                deps = ([] if accumulate else [P.ps_free[bank]]) + [w_tok, *extra_deps]
            tok = P.do("tensor",
                       lambda t, k=k, bi=bi, ncols=ncols, ps=ps:
                       t.matmul(ps[:, 0:ncols],
                                xT[:, k * 128:(k + 1) * 128],
                                w_sbuf[:, k * fout + 512 * bi: k * fout + 512 * bi + ncols],
                                start=(k == 0 and not accumulate), stop=(k == nk - 1)),
                       deps=deps, inc="pe" if k == nk - 1 else None)
        toks.append(tok)
    return toks


class WBuf:

    def __init__(self, P, ctx, name, k, f):
        assert k % 128 == 0
        self.k, self.f = k, f
        self.nchunk = k // 128
        self.sb = ctx.enter_context(
            P.nc.sbuf_tensor(P.tag + name, [128, self.nchunk * f], BF16))
        self.tok = None

    def load(self, P, dram_ap):
        src = dram_ap.rearrange("(c p) f -> p c f", p=128)
        dst = self.sb[:, :].rearrange("p (c f) -> p c f", f=self.f)
        self.tok = P.dma("gpsimd", lambda g: g.dma_start(out=dst, in_=src))
        return self.tok

    def chunk(self, c, lo=0, hi=None):
        hi = self.f if hi is None else hi
        return self.sb[:, c * self.f + lo: c * self.f + hi]


def load_row_f32(P, ctx, name, dram_ap, n):
    sb = ctx.enter_context(P.nc.sbuf_tensor(P.tag + name, [1, n], F32))
    tok = P.dma("sync", lambda s: s.dma_start(out=sb[0:1, :], in_=dram_ap))
    return sb, tok


def emit_bcast_row(P, ctx, name, row_sb, row_tok, n, scale=None, banks=(0, 1)):
    out = ctx.enter_context(P.nc.sbuf_tensor(P.tag + name, [128, n], F32))
    toks = []
    for ci in range((n + 511) // 512):
        lo, hi = ci * 512, min(n, (ci + 1) * 512)
        bank = banks[ci % len(banks)]
        mm = P.do("tensor", lambda t, lo=lo, hi=hi, bank=bank: t.matmul(
            P.PS[bank][:, 0:hi - lo], P.ONES1[0:1, :], row_sb[0:1, lo:hi],
            start=True, stop=True),
            deps=(row_tok, P.ones_tok, P.ps_free[bank]), inc="pe")
        if scale is None:
            ev = P.do("vector", lambda v, lo=lo, hi=hi, bank=bank: v.tensor_copy(
                out=out[:, lo:hi], in_=P.PS[bank][:, 0:hi - lo]), deps=(mm,), inc="dve")
        else:
            ev = P.do("vector", lambda v, lo=lo, hi=hi, bank=bank: v.tensor_scalar(
                out=out[:, lo:hi], in0=P.PS[bank][:, 0:hi - lo],
                scalar1=float(scale), scalar2=None, op0=ALU.mult), deps=(mm,), inc="dve")
        P.ps_free[bank] = ev
        toks.append(ev)
    return out, toks[-1]



def emit_l2ffn_prep(P, ctx, aps):
    nc = P.nc
    sb = lambda name, shape, dt: ctx.enter_context(nc.sbuf_tensor(P.tag + name, shape, dt))
    wb = WBuf(P, ctx, "l2f_w", 1024, 1024)
    wtok = wb.load(P, aps["W"])
    ld_t = sb("l2f_ldT", [128, 4], BF16)
    lu_t = sb("l2f_luT", [128, 50], BF16)
    da = sb("l2f_da", [128, 1024], BF16)
    ua = WBuf(P, ctx, "l2f_ua", 1024, 50)
    t_ld = P.dma("gpsimd", lambda g: g.dma_start(
        out=ld_t[0:100, :], in_=aps["lora_down"].rearrange("r d -> d r"),
        allow_slow_non_contiguous=True))
    t_lu = P.dma("gpsimd", lambda g: g.dma_start(
        out=lu_t[0:4, :], in_=aps["lora_up"].rearrange("u r -> r u"),
        allow_slow_non_contiguous=True))
    t_da = P.dma("gpsimd", lambda g: g.dma_start(out=da[0:100, :], in_=aps["down_aux"]))
    t_ua = ua.load(P, aps["up_aux"])

    t1 = sb("l2f_t1", [128, 1024], BF16)
    t2 = sb("l2f_t2", [128, 1024], BF16)
    uaT = sb("l2f_uaT", [128, 8 * 128], BF16)
    wsum = sb("l2f_wsum", [128, 8 * 1024], BF16)
    wsumT = sb("l2f_wsumT", [128, 8 * 1024], BF16)

    ps, pt = P.PS, P.PT
    mm1 = []
    for half in range(2):
        tok = P.do("tensor", lambda t, h=half: t.matmul(
            ps[h][0:4, 0:512], ld_t[0:100, :], da[0:100, h * 512:(h + 1) * 512],
            start=True, stop=True),
            deps=(t_ld, t_da, P.ps_free[half]), inc="pe")
        mm1.append(tok)
    ev1 = P.do("vector", lambda v: v.tensor_copy(out=t1[0:4, 0:512], in_=ps[0][0:4, 0:512]),
               deps=(mm1[0],), inc="dve")
    ev1b = P.do("vector", lambda v: v.tensor_copy(out=t1[0:4, 512:1024], in_=ps[1][0:4, 0:512]),
                deps=(mm1[1],), inc="dve")
    mm2 = []
    for half in range(2):
        tok = P.do("tensor", lambda t, h=half: t.matmul(
            ps[2 + h][0:50, 0:512], lu_t[0:4, :], t1[0:4, h * 512:(h + 1) * 512],
            start=True, stop=True),
            deps=(t_lu, ev1, ev1b, P.ps_free[2 + half]), inc="pe")
        mm2.append(tok)
    ev2 = P.do("vector", lambda v: v.tensor_copy(out=t2[0:50, 0:512], in_=ps[2][0:50, 0:512]),
               deps=(mm2[0],), inc="dve")
    ev2b = P.do("vector", lambda v: v.tensor_copy(out=t2[0:50, 512:1024], in_=ps[3][0:50, 0:512]),
                deps=(mm2[1],), inc="dve")
    tr_toks = []
    for c in range(8):
        tok = P.do("tensor", lambda t, c=c: t.transpose(
            pt[0][0:50, c * 128:(c + 1) * 128], ua.chunk(c), P.IDN[:]),
            deps=(t_ua, P.idn_tok, P.pt_free[0]) if c == 0 else (), inc="pe" if c == 7 else None)
        tr_toks.append(tok)
    evT = P.do("vector", lambda v: v.tensor_copy(out=uaT[0:50, :], in_=pt[0][0:50, :]),
               deps=(tr_toks[-1],), inc="dve")
    P.pt_free[0] = evT
    for c in range(8):
        for half in range(2):
            bank = 4 + half
            mm = P.do("tensor", lambda t, c=c, h=half: t.matmul(
                P.PS[4 + h][:, 0:512], uaT[0:50, c * 128:(c + 1) * 128],
                t2[0:50, h * 512:(h + 1) * 512], start=True, stop=True),
                deps=(evT, ev2, ev2b, P.ps_free[bank]), inc="pe")
            ev = P.do("vector", lambda v, c=c, h=half: v.scalar_tensor_tensor(
                out=wsum[:, c * 1024 + h * 512: c * 1024 + (h + 1) * 512],
                in0=P.PS[4 + h][:, 0:512], scalar=1.0,
                in1=wb.chunk(c, h * 512, (h + 1) * 512),
                op0=ALU.mult, op1=ALU.add),
                deps=(mm, wtok), inc="dve")
            P.ps_free[bank] = ev
    wsum_tok = P.last_tok["vector"]
    for ci in range(8):
        pt_b = ci % 2
        for co in range(8):
            tok = P.do("tensor", lambda t, ci=ci, co=co, pt_b=pt_b: t.transpose(
                P.PT[pt_b][:, co * 128:(co + 1) * 128],
                wsum[:, co * 1024 + ci * 128: co * 1024 + (ci + 1) * 128],
                P.IDN[:]),
                deps=(wsum_tok, P.pt_free[pt_b]) if co == 0 else (),
                inc="pe" if co == 7 else None)
        ev = P.do("vector", lambda v, ci=ci, pt_b=pt_b: v.tensor_copy(
            out=wsumT[:, ci * 1024:(ci + 1) * 1024], in_=P.PT[pt_b][:, :]),
            deps=(tok,), inc="dve")
        P.pt_free[pt_b] = ev
    wsumT_tok = P.last_tok["vector"]
    return wsum, wsumT, wsumT_tok


def emit_l2ffn(P, ctx, aps, prep):
    _, wsumT, wtok = prep
    EVS = ctx.enter_context(P.nc.sbuf_tensor(P.tag + "l2f_evs", [128, 1024], F32))
    brow, btok = load_row_f32(P, ctx, "l2f_b", aps["b"], 1024)
    bb, bb_tok = emit_bcast_row(P, ctx, "l2f_bb", brow, btok, 1024)
    pipe = XnPipe(P, lambda t, ap, war: emit_ln_tile(P, t, xn_war=war, xn_buf=ap),
                  NTILE)
    for t in range(NTILE):
        banks = [2 * (t % 2), 1 + 2 * (t % 2)]
        mm = emit_proj_1024(P, pipe.xt_ap(t), wsumT, wtok, out_banks=banks, fout=1024,
                            extra_deps=(pipe.xt_tok[t],))
        pipe.done(t, mm[-1])
        rt = r_tile(P, t)
        for bi, bank in enumerate(banks):
            lo, hi = bi * 512, (bi + 1) * 512
            e1 = P.do("vector", lambda v, bank=bank, lo=lo, hi=hi: v.tensor_tensor(
                out=EVS[:, lo:hi], in0=P.PS[bank][:, 0:512], in1=bb[:, lo:hi], op=ALU.add),
                deps=(mm[bi], bb_tok, P.ps_free[bank]), inc="dve")
            e2 = P.do("vector", lambda v, bank=bank, lo=lo, hi=hi, rt=rt: v.scalar_tensor_tensor(
                out=rt[:, lo:hi], in0=EVS[:, lo:hi], scalar=0.5, in1=rt[:, lo:hi],
                op0=ALU.mult, op1=ALU.add),
                deps=(e1,), inc="dve")
            P.ps_free[bank] = e2
    P.barrier()


def emit_L2_ffn(P, ctx, aps):
    prep = emit_l2ffn_prep(P, ctx, aps)
    emit_l2ffn(P, ctx, aps, prep)


def emit_grid_transpose(P, ctx, src, dst, src_tok, name=None):
    ev = None
    for ci in range(8):
        pt_b = ci % 2
        for co in range(8):
            tok = P.do("tensor", lambda t, ci=ci, co=co, pt_b=pt_b: t.transpose(
                P.PT[pt_b][:, co * 128:(co + 1) * 128],
                src[:, co * 1024 + ci * 128: co * 1024 + (ci + 1) * 128],
                P.IDN[:]),
                deps=(src_tok, P.pt_free[pt_b], P.idn_tok) if co == 0 else (),
                inc="pe" if co == 7 else None)
        ev = P.do("vector", lambda v, ci=ci, pt_b=pt_b: v.tensor_copy(
            out=dst[:, ci * 1024:(ci + 1) * 1024], in_=P.PT[pt_b][:, :]),
            deps=(tok,), inc="dve")
        P.pt_free[pt_b] = ev
    return ev



def emit_L3_ffn(P, ctx, aps):
    nc = P.nc
    sb = lambda name, shape, dt: ctx.enter_context(nc.sbuf_tensor(P.tag + name, shape, dt))
    wf = sb("l3f_wf", [128, 8 * 1024], F32)
    WQ = sb("l3f_wq", [128, 8 * 1024], BF16)
    WQT = sb("l3f_wqT", [128, 8 * 1024], BF16)
    U = sb("l3f_u", [128, 1024], F32)
    U2 = sb("l3f_u2", [128, 1024], F32)
    SG = sb("l3f_sg", [128, 1024], BF16)
    MSK = sb("l3f_msk", [128, 1024], BF16)
    ABS1 = sb("l3f_abs1", [128, 9], F32)
    WS = sb("l3f_ws", [1, 1], F32)
    INVB = sb("l3f_invb", [128, 1], F32)
    WSB = sb("l3f_wsb", [128, 1], F32)
    ASC = sb("l3f_asc", [128, 1], F32)
    RECA = sb("l3f_reca", [128, 1], F32)
    SV = sb("l3f_sv", [128, 2], F32)
    Z = sb("l3f_z", [128, 1024], F32)
    AQ = sb("l3f_aq", [128, 1024], BF16)

    src = aps["weight"].rearrange("(c p) f -> p c f", p=128)
    wtok = P.dma("sync", lambda s: s.dma_start(
        out=wf[:, :].rearrange("p (c f) -> p c f", f=1024), in_=src))

    tprev = None
    for c in range(8):
        tprev = P.do("vector", lambda v, c=c: v.reduce_sum(
            out=ABS1[:, c:c + 1], in_=wf[:, c * 1024:(c + 1) * 1024], axis=AX.X,
            apply_absolute_value=True), deps=(wtok,))
    t_rs = P.do("vector", lambda v: v.reduce_sum(out=ABS1[:, 8:9], in_=ABS1[:, 0:8],
                                                 axis=AX.X), deps=(tprev,))
    mm_ws = P.do("tensor", lambda t: t.matmul(P.PS[0][0:1, 0:1], ABS1[:, 8:9],
                                              P.ONESC[:, 0:1], start=True, stop=True),
                 deps=(t_rs, P.ones_tok, P.ps_free[0]), inc="pe")
    t_ws = P.do("vector", lambda v: v.tensor_scalar(
        out=WS[0:1, 0:1], in0=P.PS[0][0:1, 0:1], scalar1=1.0 / (1024.0 * 1024.0),
        scalar2=1e-8, op0=ALU.mult, op1=ALU.add), deps=(mm_ws,))
    P.ps_free[0] = t_ws
    t_inv = P.do("vector", lambda v: v.reciprocal(out=WS[0:1, 0:1], in_=WS[0:1, 0:1]),
                 deps=(t_ws,))
    mm_b1 = P.do("tensor", lambda t: t.matmul(P.PS[0][:, 0:1], P.ONES1[0:1, :],
                                              WS[0:1, 0:1], start=True, stop=True),
                 deps=(t_inv, P.ps_free[0]), inc="pe")
    t_invb = P.do("vector", lambda v: v.tensor_copy(out=INVB[:, 0:1], in_=P.PS[0][:, 0:1]),
                  deps=(mm_b1,))
    P.ps_free[0] = t_invb
    t_wsb0 = P.do("vector", lambda v: v.reciprocal(out=WSB[:, 0:1], in_=INVB[:, 0:1]),
                  deps=(t_invb,))
    t_wsb = P.do("vector", lambda v: v.tensor_scalar(out=WSB[:, 0:1], in0=WSB[:, 0:1],
                                                     scalar1=0.5 / 127.0, scalar2=None,
                                                     op0=ALU.mult), deps=(t_wsb0,))
    wq_tok = None
    prev_sg = None
    prev_mk = None
    for c in range(8):
        t_u = P.do("vector", lambda v, c=c: v.tensor_scalar(
            out=U[:, :], in0=wf[:, c * 1024:(c + 1) * 1024], scalar1=INVB[:, 0:1],
            scalar2=None, op0=ALU.mult), deps=(t_invb, wq_tok, prev_sg))
        t_sg = P.do("scalar", lambda s: s.sign(out=SG[:, :], in_=U[:, :]), deps=(t_u,))
        t_sq = P.do("scalar", lambda s: s.square(out=U2[:, :], in_=U[:, :]),
                    deps=(t_u, prev_mk))
        prev_sg = t_sq
        t_mk = P.do("vector", lambda v: v.tensor_scalar(
            out=MSK[:, :], in0=U2[:, :], scalar1=0.25, scalar2=None,
            op0=ALU.is_ge), deps=(t_sq,))
        wq_tok = P.do("vector", lambda v, c=c: v.tensor_tensor(
            out=WQ[:, c * 1024:(c + 1) * 1024], in0=SG[:, :], in1=MSK[:, :],
            op=ALU.mult), deps=(t_sg, t_mk))
        prev_mk = wq_tok
    wqT_tok = emit_grid_transpose(P, ctx, WQ, WQT, wq_tok)

    sv_toks = {}

    def mk_l3f(t, ap, war):
        sl = t % 2
        xn_tok = emit_ln_tile(P, t, xn_war=war)
        t_am = P.do("vector", lambda v: v.tensor_reduce(
            out=ASC[:, 0:1], in_=P.XN[:, 0:1024], axis=AX.X, op=ALU.max,
            apply_absolute_value=True), deps=(xn_tok,))
        t_am2 = P.do("vector", lambda v: v.tensor_scalar(
            out=ASC[:, 0:1], in0=ASC[:, 0:1], scalar1=1e-8, scalar2=None,
            op0=ALU.add), deps=(t_am,))
        t_rec = P.do("vector", lambda v: v.reciprocal(out=RECA[:, 0:1], in_=ASC[:, 0:1]),
                     deps=(t_am2,))
        t_z = P.do("vector", lambda v: v.tensor_scalar(
            out=Z[:, :], in0=P.XN[:, 0:1024], scalar1=RECA[:, 0:1], scalar2=127.0,
            op0=ALU.mult, op1=ALU.mult), deps=(t_rec,))
        t_aq = P.do("vector", lambda v, ap=ap: v.tensor_scalar(
            out=ap, in0=Z[:, :], scalar1=8388608.0, scalar2=-8388608.0,
            op0=ALU.add, op1=ALU.add), deps=(t_z, war))
        sv_toks[t] = P.do("vector", lambda v, sl=sl: v.tensor_tensor(
            out=SV[:, sl:sl + 1], in0=ASC[:, 0:1], in1=WSB[:, 0:1], op=ALU.mult),
            deps=(t_wsb, t_am2, sv_toks.get((t - 2, "rd"))))
        return t_aq

    pipe = XnPipe(P, mk_l3f, NTILE)
    for t in range(NTILE):
        sl = t % 2
        banks = [2 * sl, 1 + 2 * sl]
        mm = emit_proj_1024(P, pipe.xt_ap(t), WQT, wqT_tok, out_banks=banks, fout=1024,
                            extra_deps=(pipe.xt_tok[t],))
        pipe.done(t, mm[-1])
        rt = r_tile(P, t)
        ev = None
        for bi, bank in enumerate(banks):
            lo, hi = bi * 512, (bi + 1) * 512
            ev = P.do("vector", lambda v, bank=bank, lo=lo, hi=hi, rt=rt, sl=sl:
                      v.scalar_tensor_tensor(
                          out=rt[:, lo:hi], in0=P.PS[bank][:, 0:512],
                          scalar=SV[:, sl:sl + 1],
                          in1=rt[:, lo:hi], op0=ALU.mult, op1=ALU.add),
                      deps=(mm[bi], sv_toks[t], P.ps_free[bank]))
            P.ps_free[bank] = ev
        sv_toks[(t, "rd")] = ev
    P.barrier()



def emit_L1_ffn(P, ctx, aps):
    nc = P.nc
    sb = lambda name, shape, dt: ctx.enter_context(nc.sbuf_tensor(P.tag + name, shape, dt))
    EVS = sb("evs", [128, 1024], F32)
    win = WBuf(P, ctx, "l1f_win", 1024, 64)
    wtok = win.load(P, aps["Win"])
    wout = sb("l1f_wout", [64, 1024], BF16)
    wout_tok = P.dma("gpsimd", lambda g: g.dma_start(out=wout[0:64, :], in_=aps["Wout"]))

    rows = {}
    for nm, n in [("bin_", 64), ("W1", 1024), ("b1", 1024), ("W2", 1024),
                  ("b2", 64), ("bout", 1024)]:
        rows[nm] = load_row_f32(P, ctx, f"l1f_{nm}r", aps[nm], n)
    BINB, t_binb = emit_bcast_row(P, ctx, "l1f_binb", *rows["bin_"], 64)
    W1B, t_w1b = emit_bcast_row(P, ctx, "l1f_w1b", *rows["W1"], 1024)
    B1B, t_b1b = emit_bcast_row(P, ctx, "l1f_b1b", *rows["b1"], 1024)
    W2B, t_w2b = emit_bcast_row(P, ctx, "l1f_w2b", *rows["W2"], 1024)
    B2B, t_b2b = emit_bcast_row(P, ctx, "l1f_b2b", *rows["b2"], 64)
    BOUTB, t_boutb = emit_bcast_row(P, ctx, "l1f_boutb", *rows["bout"], 1024)

    S = sb("l1f_s", [128, 64], F32)
    H = sb("l1f_h", [128, 1024], F32)
    H3 = sb("l1f_h3", [128, 1024], F32)
    G1 = sb("l1f_g1", [128, 1024], F32)
    G = sb("l1f_g", [128, 64], BF16)
    GT = sb("l1f_gt", [64, 128], BF16)

    xn_war = None
    xt_war = None
    gt_war = None
    for t in range(NTILE):
        xn_tok = emit_ln_tile(P, t, xn_war=xn_war)
        xt_tok = emit_transpose_1024(P, P.XN, P.XT, (xn_tok,), pt_bank=t % 2,
                                     war_dep=xt_war)
        xn_war = xt_tok
        bank_s = 4
        mm_s = emit_proj_1024(P, P.XT, win.sb, wtok, out_banks=[bank_s], fout=64,
                              extra_deps=(xt_tok,))
        xt_war = mm_s[-1]
        t_s = P.do("vector", lambda v: v.tensor_tensor(
            out=S[:, :], in0=P.PS[bank_s][:, 0:64], in1=BINB[:, :], op=ALU.add),
            deps=(mm_s[0], t_binb, P.ps_free[bank_s]))
        P.ps_free[bank_s] = t_s
        s_view = S[:, :].unsqueeze(2).broadcast_to([128, 64, 16])
        w1_view = W1B[:, :].rearrange("p (m j) -> p m j", j=16)
        h_view = H[:, :].rearrange("p (m j) -> p m j", j=16)
        t_h = P.do("vector", lambda v: v.tensor_tensor(
            out=h_view, in0=s_view, in1=w1_view, op=ALU.mult), deps=(t_s, t_w1b))
        t_h2 = P.do("vector", lambda v: v.tensor_tensor(
            out=H[:, :], in0=H[:, :], in1=B1B[:, :], op=ALU.add), deps=(t_h, t_b1b))
        t_h3 = P.do("scalar", lambda s: s.activation(out=H3[:, :], in_=H[:, :],
                                                     func=AF.Relu), deps=(t_h2,))
        t_g1 = P.do("vector", lambda v: v.tensor_tensor(
            out=G1[:, :], in0=H3[:, :], in1=W2B[:, :], op=ALU.mult), deps=(t_h3, t_w2b))
        g1_view = G1[:, :].rearrange("p (m j) -> p m j", j=16)
        t_gr = P.do("vector", lambda v: v.tensor_reduce(
            out=G1[:, 0:64], in_=g1_view, axis=AX.X, op=ALU.add), deps=(t_g1,))
        t_g = P.do("vector", lambda v: v.tensor_tensor(
            out=G[:, :], in0=G1[:, 0:64], in1=B2B[:, :], op=ALU.add),
            deps=(t_gr, t_b2b, gt_war))
        pt_b = t % 2
        tr = P.do("tensor", lambda t_, pt_b=pt_b: t_.transpose(
            P.PT[pt_b][0:64, 0:128], G[:, :], P.IDN[:]),
            deps=(t_g, P.pt_free[pt_b], P.idn_tok), inc="pe")
        t_gt = P.do("vector", lambda v, pt_b=pt_b: v.tensor_copy(
            out=GT[0:64, :], in_=P.PT[pt_b][0:64, 0:128]), deps=(tr, gt_war))
        P.pt_free[pt_b] = t_gt
        rt = r_tile(P, t)
        for bi in range(2):
            bank = bi
            lo, hi = bi * 512, (bi + 1) * 512
            mm_o = P.do("tensor", lambda t_, bank=bank, lo=lo, hi=hi: t_.matmul(
                P.PS[bank][:, 0:512], GT[0:64, :], wout[0:64, lo:hi],
                start=True, stop=True),
                deps=(t_gt, wout_tok, P.ps_free[bank]), inc="pe")
            gt_war = mm_o
            e1 = P.do("vector", lambda v, bank=bank, lo=lo, hi=hi: v.tensor_tensor(
                out=EVS[:, lo:hi], in0=P.PS[bank][:, 0:512], in1=BOUTB[:, lo:hi],
                op=ALU.add), deps=(mm_o, t_boutb, P.ps_free[bank]))
            e2 = P.do("vector", lambda v, lo=lo, hi=hi, rt=rt: v.scalar_tensor_tensor(
                out=rt[:, lo:hi], in0=EVS[:, lo:hi], scalar=0.5, in1=rt[:, lo:hi],
                op0=ALU.mult, op1=ALU.add), deps=(e1,))
            P.ps_free[bank] = e2
    P.barrier()



def emit_L0_ffn(P, ctx, aps):
    nc = P.nc
    sb = lambda name, shape, dt: ctx.enter_context(nc.sbuf_tensor(P.tag + name, shape, dt))
    EVS = sb("evs", [128, 1024], F32)
    COEF = sb("moe_coef", [128, 8 * NTILE], F32)
    MRS = sb("moe_mrs", [128, 2 * NTILE], F32)
    PRB = sb("moe_prb", [128, 8], F32)
    SUME = sb("moe_sume", [128, 1], F32)
    NMX = sb("moe_nmx", [128, 1], F32)
    RCP = sb("moe_rcp", [128, 1], F32)
    EQ = sb("moe_eq", [128, 8], F32)
    PU2 = sb("moe_pu2", [128, 8], F32)
    M2 = sb("moe_m2", [128, 1], F32)
    MK2 = sb("moe_mk2", [128, 8], F32)
    SIL = sb("moe_sil", [128, 1024], F32)
    INTER = sb("moe_inter", [128, 4 * 1024], BF16)
    OUT = [sb(f"moe_out{i}", [128, 1024], F32) for i in range(2)]
    PRT = [sb(f"moe_prt{i}", [128, 1024], F32) for i in range(2)]
    WGU4 = sb("moe_wgu4", [128, 4 * 8 * 1024], BF16)
    WD4 = sb("moe_wd4", [128, 4 * 4 * 1024], BF16)
    WSH = WGU4[:, 0:8 * 2048]
    WSHD = WGU4[:, 8 * 2048:8 * 2048 + 8 * 1024]
    wr = WBuf(P, ctx, "moe_wr", 1024, 8)

    part = nc.dram_tensor(P.tag + "moe_part", [NTILE, 128, 1024], F32, kind="Internal").ap()

    wr_tok = wr.load(P, aps["router"])
    xn_war, xt_war = None, None
    for t in range(NTILE):
        xn_tok = emit_ln_tile(P, t, xn_war=xn_war)
        c_m = P.do("vector", lambda v, t=t: v.tensor_copy(
            out=MRS[:, 2 * t:2 * t + 1], in_=P.NEGMEAN[:, 0:1]), deps=(xn_tok,))
        c_r = P.do("vector", lambda v, t=t: v.tensor_copy(
            out=MRS[:, 2 * t + 1:2 * t + 2], in_=P.RSTD[:, 0:1]))
        xt_tok = emit_transpose_1024(P, P.XN, P.XT, (xn_tok,), pt_bank=t % 2,
                                     war_dep=xt_war)
        xn_war = xt_tok
        bank = 4 + (t % 2)
        mm = emit_proj_1024(P, P.XT, wr.sb, wr_tok, out_banks=[bank], fout=8,
                            extra_deps=(xt_tok,))
        xt_war = mm[-1]
        t_mx = P.do("vector", lambda v, bank=bank: v.tensor_reduce(
            out=NMX[:, 0:1], in_=P.PS[bank][:, 0:8], axis=AX.X, op=ALU.max,
            negate=True), deps=(mm[0], P.ps_free[bank]))
        t_pb = P.do("scalar", lambda s, bank=bank: s.activation(
            out=PRB[:, :], in_=P.PS[bank][:, 0:8], func=AF.Exp, bias=NMX[:, 0:1],
            scale=IM(1.0), accum_out=SUME[:, 0:1]), deps=(t_mx,))
        P.ps_free[bank] = t_pb
        t_rc = P.do("vector", lambda v: v.reciprocal(out=RCP[:, 0:1], in_=SUME[:, 0:1]),
                    deps=(t_pb,))
        t_eq = P.do("vector", lambda v: v.tensor_scalar(
            out=EQ[:, :], in0=PRB[:, :], scalar1=1.0, scalar2=None, op0=ALU.is_ge),
            deps=(t_pb,))
        t_p2 = P.do("vector", lambda v: v.tensor_tensor(
            out=PU2[:, :], in0=PRB[:, :], in1=EQ[:, :], op=ALU.subtract), deps=(t_eq,))
        t_m2 = P.do("vector", lambda v: v.tensor_reduce(
            out=M2[:, 0:1], in_=PU2[:, :], axis=AX.X, op=ALU.max), deps=(t_p2,))
        t_mk = P.do("vector", lambda v: v.tensor_scalar(
            out=MK2[:, :], in0=PRB[:, :], scalar1=M2[:, 0:1], scalar2=None,
            op0=ALU.is_ge), deps=(t_m2,))
        t_cu = P.do("vector", lambda v: v.tensor_tensor(
            out=MK2[:, :], in0=PRB[:, :], in1=MK2[:, :], op=ALU.mult), deps=(t_mk,))
        P.do("vector", lambda v, t=t: v.tensor_scalar(
            out=COEF[:, 8 * t:8 * t + 8], in0=MK2[:, :], scalar1=RCP[:, 0:1],
            scalar2=None, op0=ALU.mult), deps=(t_cu, t_rc))
    P.barrier()

    def load_expert_block(e0):
        toks = []
        for el in range(4):
            e = e0 + el
            dstg = WGU4[:, el * 8192:(el + 1) * 8192].rearrange(
                "p (c f) -> p c f", f=1024)[:, :, 0:512]
            dstu = WGU4[:, el * 8192:(el + 1) * 8192].rearrange(
                "p (c f) -> p c f", f=1024)[:, :, 512:1024]
            srcg = aps["experts_gate"][e0 + el].rearrange("(c p) f -> p c f", p=128)
            srcu = aps["experts_up"][e0 + el].rearrange("(c p) f -> p c f", p=128)
            toks.append(P.dma("gpsimd", lambda g, dstg=dstg, srcg=srcg:
                              g.dma_start(out=dstg, in_=srcg), deps=(w_war,)))
            toks.append(P.dma("gpsimd", lambda g, dstu=dstu, srcu=srcu:
                              g.dma_start(out=dstu, in_=srcu)))
            dstd = WD4[:, el * 4096:(el + 1) * 4096].rearrange("p (c f) -> p c f", f=1024)
            srcd = aps["experts_down"][e0 + el].rearrange("(c p) f -> p c f", p=128)
            toks.append(P.dma("gpsimd", lambda g, dstd=dstd, srcd=srcd:
                              g.dma_start(out=dstd, in_=srcd)))
        return toks

    part_tok = [None] * NTILE
    out_war = [None, None]
    prt_rd = [None, None]
    prt_war = [None, None]
    for ipass in range(2):
        e0 = 4 * ipass
        w_war = P.fence("tensor")
        w_tok = load_expert_block(e0)
        acc_free = None

        def mk_moe(t, ap, war):
            return P.do("vector", lambda v, t=t, ap=ap: v.tensor_scalar(
                out=ap, in0=r_tile(P, t), scalar1=MRS[:, 2 * t:2 * t + 1],
                scalar2=MRS[:, 2 * t + 1:2 * t + 2], op0=ALU.add, op1=ALU.mult),
                deps=(war,) + tuple(P.r_toks))

        pipe = XnPipe(P, mk_moe, NTILE)
        for t in range(NTILE):
            rt = r_tile(P, t)
            xt_tok = pipe.xt_tok[t]
            xtv = pipe.xt_ap(t)
            if ipass == 1:
                prt_rd[t % 2] = P.dma("sync", lambda s_, t=t: s_.dma_start(
                    out=PRT[t % 2][:, :], in_=part[t]),
                    deps=(prt_war[t % 2], part_tok[t]))
            gu_banks = [(0, 1), (2, 3)]
            acc_banks = (4, 5)
            int_toks = {}
            tr_toks = {}
            sil_prev = None
            down_tok = None
            last_gu = None

            def emit_trdown(e, first, last):
                nonlocal down_tok
                slot = (e % 2) * 1024
                tr_tok = emit_transpose_1024(P, INTER[:, slot:slot + 512],
                                             INTER[:, slot + 512:slot + 1024],
                                             (int_toks[e],), pt_bank=t % 2,
                                             war_dep=down_tok, nchunk=4)
                tr_toks[e] = tr_tok
                for bi, bank in enumerate(acc_banks):
                    for k in range(4):
                        deps = []
                        if k == 0 and bi == 0 and first:
                            deps = [acc_free] + w_tok
                        if k == 0:
                            deps = deps + [tr_tok]
                        tok = P.do("tensor", lambda t_, k=k, bank=bank, bi=bi, e=e, slot=slot:
                                   t_.matmul(P.PS[bank][:, 0:512],
                                             INTER[:, slot + 512 + k * 128:slot + 512 + (k + 1) * 128],
                                             WD4[:, e * 4096 + k * 1024 + 512 * bi:
                                                  e * 4096 + k * 1024 + 512 * bi + 512],
                                             start=(k == 0 and first), stop=(k == 3 and last)),
                                   deps=deps,
                                   inc="pe" if (k == 3 and bi == 1) else None)
                down_tok = tok

            for e in range(4):
                gb = gu_banks[e % 2]
                mm_gu = []
                for bi, bank in enumerate(gb):
                    for k in range(8):
                        deps = []
                        if k == 0:
                            deps = [P.ps_free[bank], xt_tok] + w_tok
                        tok = P.do("tensor", lambda t_, k=k, bank=bank, bi=bi, e=e,
                                   xtv=xtv:
                                   t_.matmul(P.PS[bank][:, 0:512],
                                             xtv[:, k * 128:(k + 1) * 128],
                                             WGU4[:, e * 8192 + k * 1024 + 512 * bi:
                                                  e * 8192 + k * 1024 + 512 * bi + 512],
                                             start=(k == 0), stop=(k == 7)),
                                   deps=deps, inc="pe" if k == 7 else None)
                    mm_gu.append(tok)
                last_gu = mm_gu[-1]
                t_sil = P.do("scalar", lambda s_, gb=gb: s_.activation(
                    out=SIL[:, 0:512], in_=P.PS[gb[0]][:, 0:512], func=AF.Silu),
                    deps=(mm_gu[0], int_toks.get(e - 1)))
                slot = (e % 2) * 1024
                t_int = P.do("vector", lambda v, gb=gb, e=e, t=t, e0=e0, slot=slot:
                             v.scalar_tensor_tensor(
                                 out=INTER[:, slot:slot + 512], in0=P.PS[gb[1]][:, 0:512],
                                 scalar=COEF[:, 8 * t + e0 + e:8 * t + e0 + e + 1],
                                 in1=SIL[:, 0:512], op0=ALU.mult, op1=ALU.mult),
                             deps=(mm_gu[1], t_sil, tr_toks.get(e - 2)))
                int_toks[e] = t_int
                P.ps_free[gb[0]] = t_int
                P.ps_free[gb[1]] = t_int
                if e > 0:
                    emit_trdown(e - 1, first=(e - 1 == 0), last=False)
            emit_trdown(3, first=False, last=True)
            pipe.done(t, last_gu)
            ob = OUT[t % 2]
            ev_toks = []
            for bi, bank in enumerate(acc_banks):
                lo, hi = bi * 512, (bi + 1) * 512
                if ipass == 0:
                    ev = P.do("vector", lambda v, bank=bank, lo=lo, hi=hi, ob=ob:
                              v.tensor_scalar(out=ob[:, lo:hi], in0=P.PS[bank][:, 0:512],
                                              scalar1=0.5, scalar2=None, op0=ALU.mult),
                              deps=(down_tok, out_war[t % 2]))
                else:
                    ev = P.do("vector", lambda v, bank=bank, lo=lo, hi=hi, ob=ob, t=t:
                              v.scalar_tensor_tensor(out=ob[:, lo:hi],
                                                     in0=P.PS[bank][:, 0:512], scalar=0.5,
                                                     in1=PRT[t % 2][:, lo:hi],
                                                     op0=ALU.mult, op1=ALU.add),
                              deps=(down_tok, out_war[t % 2], prt_rd[t % 2]))
                ev_toks.append(ev)
            acc_free = ev_toks[-1]
            if ipass == 1:
                prt_war[t % 2] = ev_toks[-1]
            out_war[t % 2] = P.dma("sync", lambda s_, t=t, ob=ob: s_.dma_start(
                out=part[t], in_=ob[:, :]), deps=tuple(ev_toks) + (part_tok[t],))
            part_tok[t] = out_war[t % 2]
        P.barrier()

    w_war = P.fence("tensor")
    tg = P.dma("gpsimd", lambda g: g.dma_start(
        out=WSH.rearrange("p (c f) -> p c f", f=2048)[:, :, 0:1024],
        in_=aps["shared_gate"].rearrange("(c p) f -> p c f", p=128)), deps=(w_war,))
    tu = P.dma("gpsimd", lambda g: g.dma_start(
        out=WSH.rearrange("p (c f) -> p c f", f=2048)[:, :, 1024:2048],
        in_=aps["shared_up"].rearrange("(c p) f -> p c f", p=128)))
    td = P.dma("gpsimd", lambda g: g.dma_start(
        out=WSHD.rearrange("p (c f) -> p c f", f=1024),
        in_=aps["shared_down"].rearrange("(c p) f -> p c f", p=128)))
    acc_free = None

    def mk_moe4(t, ap, war):
        return P.do("vector", lambda v, t=t, ap=ap: v.tensor_scalar(
            out=ap, in0=r_tile(P, t), scalar1=MRS[:, 2 * t:2 * t + 1],
            scalar2=MRS[:, 2 * t + 1:2 * t + 2], op0=ALU.add, op1=ALU.mult),
            deps=(war,))

    P._l0f_p4_down = None
    pipe = XnPipe(P, mk_moe4, NTILE)

    def p4_trdown(t, int_t):
        nonlocal acc_free
        rt = r_tile(P, t)
        sl = (t % 2) * 2048
        tr_tok = emit_transpose_1024(P, INTER[:, sl:sl + 1024],
                                     INTER[:, sl + 1024:sl + 2048],
                                     (int_t[0], int_t[1]),
                                     pt_bank=t % 2, war_dep=P._l0f_p4_down)
        acc_banks = (4, 5)
        down_tok = None
        for bi, bank in enumerate(acc_banks):
            for k in range(8):
                deps = []
                if k == 0:
                    deps = [tr_tok, acc_free]
                tok = P.do("tensor", lambda t_, k=k, bank=bank, bi=bi, sl=sl:
                           t_.matmul(P.PS[bank][:, 0:512],
                                     INTER[:, sl + 1024 + k * 128:sl + 1024 + (k + 1) * 128],
                                     WSHD[:, k * 1024 + 512 * bi: k * 1024 + 512 * bi + 512],
                                     start=(k == 0), stop=(k == 7)),
                           deps=deps, inc="pe" if k == 7 else None)
            down_tok = tok
        P._l0f_p4_down = down_tok
        ev_toks = []
        for bi, bank in enumerate(acc_banks):
            lo, hi = bi * 512, (bi + 1) * 512
            e1 = P.do("vector", lambda v, bank=bank, lo=lo, hi=hi, t=t:
                      v.scalar_tensor_tensor(out=EVS[:, lo:hi], in0=P.PS[bank][:, 0:512],
                                             scalar=0.5, in1=PRT[t % 2][:, lo:hi],
                                             op0=ALU.mult, op1=ALU.add),
                      deps=(down_tok, prt_rd[t % 2]))
            e2 = P.do("vector", lambda v, lo=lo, hi=hi, rt=rt: v.tensor_tensor(
                out=rt[:, lo:hi], in0=EVS[:, lo:hi], in1=rt[:, lo:hi], op=ALU.add),
                deps=(e1,))
            ev_toks.append(e2)
        prt_war[t % 2] = ev_toks[-1]
        acc_free = ev_toks[-1]

    pend = None
    for t in range(NTILE):
        xt_tok = pipe.xt_tok[t]
        xtv = pipe.xt_ap(t)
        prt_rd[t % 2] = P.dma("sync", lambda s_, t=t: s_.dma_start(
            out=PRT[t % 2][:, :], in_=part[t]),
            deps=(prt_war[t % 2], part_tok[t]))
        mm_gu = []
        for bi in range(4):
            bank = bi
            for k in range(8):
                deps = []
                if k == 0:
                    deps = [P.ps_free[bank], tg, tu, td, xt_tok]
                tok = P.do("tensor", lambda t_, k=k, bank=bank, bi=bi, xtv=xtv:
                           t_.matmul(P.PS[bank][:, 0:512],
                                     xtv[:, k * 128:(k + 1) * 128],
                                     WSH[:, k * 2048 + 512 * bi: k * 2048 + 512 * bi + 512],
                                     start=(k == 0), stop=(k == 7)),
                           deps=deps, inc="pe" if k == 7 else None)
            mm_gu.append(tok)
        sil_t = []
        for bi in range(2):
            sil_t.append(P.do("scalar", lambda s_, bi=bi: s_.activation(
                out=SIL[:, bi * 512:(bi + 1) * 512], in_=P.PS[bi][:, 0:512],
                func=AF.Silu), deps=(mm_gu[bi], getattr(P, "_l0f_p4_int", None))))
        int_t = []
        for bi in range(2):
            sl = (t % 2) * 2048
            tok = P.do("vector", lambda v, bi=bi, sl=sl: v.scalar_tensor_tensor(
                out=INTER[:, sl + bi * 512:sl + (bi + 1) * 512], in0=P.PS[2 + bi][:, 0:512],
                scalar=1.0, in1=SIL[:, bi * 512:(bi + 1) * 512],
                op0=ALU.mult, op1=ALU.mult),
                deps=(mm_gu[2 + bi], sil_t[bi],
                      P._l0f_p4_down if t >= 2 else None))
            int_t.append(tok)
        P._l0f_p4_int = int_t[-1]
        for bi in range(4):
            P.ps_free[bi] = int_t[-1]
        if pend is not None:
            p4_trdown(*pend)
        pend = (t, int_t)
        pipe.done(t, mm_gu[3])
    p4_trdown(*pend)
    P.barrier()



def emit_L3_attn(P, ctx, aps):
    nc = P.nc
    sb = lambda name, shape, dt: ctx.enter_context(nc.sbuf_tensor(P.tag + name, shape, dt))
    MRS = sb("l3a_mrs", [128, 2 * NTILE], F32)
    WALL = WBuf(P, ctx, "l3a_wall", 1024, 3072)
    BROW = sb("l3a_brow", [1, 3072], BF16)
    ONES1B = sb("l3a_ones1b", [1, 128], BF16)
    o1b_tok = P.do("vector", lambda v: v.memset(ONES1B[0:1, :], 1.0))
    LIN = sb("l3a_lin", [128, 3072], BF16)
    QP = sb("l3a_qp", [128, 512], BF16)
    KP = sb("l3a_kp", [128, 512], BF16)
    QTS = sb("l3a_qts", [128, 4 * 2048], BF16)
    KTS = sb("l3a_kts", [128, 4 * 2048], BF16)
    VSAUG = sb("l3a_vsaug", [128, 16 * 16 * 65], BF16)
    EXPT = sb("l3a_expt", [128, 3 * 512], BF16)
    PRW = sb("l3a_prw", [1, 512], BF16)
    SPW = LIN[0:1, 0:1024]
    VPR = LIN[0:1, 1024:2048]
    KPAD17S = sb("l3a_kpad", [128, 4 * 128], BF16)
    VA17S = sb("l3a_va17", [128, 16 * 65], BF16)
    CTXT = sb("l3a_ctxt", [65, 512], BF16)
    CTX = sb("l3a_ctx", [128, 65], F32)
    REC = sb("l3a_rec", [128, 1], F32)
    STAB = 8.0

    xn_war = None
    for t in range(NTILE):
        xn_tok = emit_ln_tile(P, t, xn_war=xn_war)
        xn_war = xn_tok
        P.do("vector", lambda v, t=t: v.tensor_copy(
            out=MRS[:, 2 * t:2 * t + 1], in_=P.NEGMEAN[:, 0:1]), deps=(xn_tok,))
        P.do("vector", lambda v, t=t: v.tensor_copy(
            out=MRS[:, 2 * t + 1:2 * t + 2], in_=P.RSTD[:, 0:1]))
    P.barrier()

    offs = {"Wq": 0, "Wqc": 256, "Wk": 512, "Wkc": 768, "Wv": 1024, "Wvc": 2048}
    wtoks = []
    for nm, off in offs.items():
        f = 1024 if nm in ("Wv", "Wvc") else 256
        dst = WALL.sb[:, :].rearrange("p (c ff) -> p c ff", ff=3072)[:, :, off:off + f]
        src = aps[nm].rearrange("(c p) f -> p c f", p=128)
        wtoks.append(P.dma("gpsimd", lambda g, dst=dst, src=src: g.dma_start(
            out=dst, in_=src)))
    boffs = {"bq": 0, "bqc": 256, "bk": 512, "bkc": 768, "bv": 1024, "bvc": 2048}
    btoks = []
    for nm, off in boffs.items():
        f = 1024 if nm in ("bv", "bvc") else 256
        btoks.append(P.dma("gpsimd", lambda s_, nm=nm, off=off, f=f: s_.dma_start(
            out=BROW[0:1, off:off + f], in_=aps[nm])))
    t1 = P.do("scalar", lambda s_: s_.activation(out=SPW[0:1, 0:256],
                                                 in_=BROW[0:1, 768:1024],
                                                 func=AF.Sigmoid), deps=tuple(btoks))
    t2 = P.do("vector", lambda v: v.memset(PRW[0:1, :], 0.0))
    t3 = P.do("vector", lambda v: v.tensor_tensor(
        out=PRW[0:1, :].rearrange("o (h f) -> o h f", f=32)[:, :, 0:16],
        in0=BROW[0:1, 512:768].rearrange("o (h f) -> o h f", f=16),
        in1=SPW[0:1, 0:256].rearrange("o (h f) -> o h f", f=16),
        op=ALU.mult), deps=(t1, t2))
    km = P.do("vector", lambda v: v.memset(KPAD17S[:, :], 0.0))
    for c in range(4):
        trk = P.do("tensor", lambda t, c=c: t.transpose(
            P.PT[0][0:128, c * 128:c * 128 + 1],
            PRW[0:1, c * 128:(c + 1) * 128], P.IDN[0:1, 0:1]),
            deps=(t3, P.pt_free[0], P.idn_tok) if c == 0 else (),
            inc="pe" if c == 3 else None)
    kp_tok = None
    for c in range(4):
        kp_tok = P.do("vector", lambda v, c=c: v.tensor_copy(
            out=KPAD17S[:, c * 128:c * 128 + 1], in_=P.PT[0][0:128, c * 128:c * 128 + 1]),
            deps=(trk, km))
    P.pt_free[0] = kp_tok
    t4 = P.do("scalar", lambda s_: s_.activation(out=SPW[0:1, :],
                                                 in_=BROW[0:1, 2048:3072],
                                                 func=AF.Sigmoid), deps=(t1, t3))
    t5 = P.do("vector", lambda v: v.scalar_tensor_tensor(
        out=VPR, in0=BROW[0:1, 1024:2048], scalar=16.0, in1=SPW[0:1, :],
        op0=ALU.mult, op1=ALU.mult), deps=(t4,))
    vm = P.do("vector", lambda v: v.memset(VA17S[:, :], 0.0))
    v1 = P.do("vector", lambda v: v.tensor_copy(
        out=VA17S[0:1, :].rearrange("o (h f) -> o h f", f=65)[:, :, 0:64],
        in_=VPR.rearrange("o (h f) -> o h f", f=64)), deps=(t5, vm))
    v2 = P.do("vector", lambda v: v.memset(
        VA17S[0:1, :].rearrange("o (h f) -> o h f", f=65)[:, :, 64:65], 16.0),
        deps=(vm,))
    va17_tok = (v1, v2)
    qkm = P.do("vector", lambda v: v.memset(QP[:, :], 0.0))
    qkm2 = P.do("vector", lambda v: v.memset(KP[:, :], 0.0))
    vs_ones = P.do("vector", lambda v: v.memset(VSAUG[:, :], 1.0))

    wtok_all = tuple(wtoks)

    for s in range(2):
        qts_war, kts_war, vs_war = None, None, None
        lin_war = vs_ones if s == 0 else None

        def mk_l3a(tl, ap, war, s=s):
            t = s * TPS + tl
            return P.do("vector", lambda v, t=t, ap=ap: v.tensor_scalar(
                out=ap, in0=r_tile(P, t), scalar1=MRS[:, 2 * t:2 * t + 1],
                scalar2=MRS[:, 2 * t + 1:2 * t + 2], op0=ALU.add, op1=ALU.mult),
                deps=(war,) + tuple(P.r_toks))

        pipe = XnPipe(P, mk_l3a, TPS)
        for tl in range(TPS):
            t = s * TPS + tl
            xt_tok = pipe.xt_tok[tl]
            xtv = pipe.xt_ap(tl)
            banks = [0, 1, 2, 3, 4, 5]
            for bi, bank in enumerate(banks):
                lo, hi = bi * 512, (bi + 1) * 512
                P.do("tensor", lambda t_, bank=bank, lo=lo, hi=hi: t_.matmul(
                    P.PS[bank][:, 0:512], ONES1B[0:1, :], BROW[0:1, lo:hi],
                    start=True, stop=False),
                    deps=tuple(btoks) + (o1b_tok, P.ps_free[bank]))
            mm = emit_proj_1024(P, xtv, WALL.sb, wtok_all[-1], out_banks=banks,
                                fout=3072, extra_deps=(xt_tok,) + wtok_all,
                                accumulate=True)
            pipe.done(tl, mm[-1])
            lin_toks = []
            for bi, bank in enumerate(banks):
                lo, hi = bi * 512, (bi + 1) * 512
                tok = P.do("vector", lambda v, bank=bank, lo=lo, hi=hi: v.tensor_copy(
                    out=LIN[:, lo:hi], in_=P.PS[bank][:, 0:512]),
                    deps=(mm[bi], P.ps_free[bank], lin_war))
                lin_toks.append(tok)
                P.ps_free[bank] = tok
            sg1 = P.do("scalar", lambda s_: s_.activation(
                out=LIN[:, 256:512], in_=LIN[:, 256:512], func=AF.Sigmoid),
                deps=(lin_toks[0],))
            sg2 = P.do("scalar", lambda s_: s_.activation(
                out=LIN[:, 768:1024], in_=LIN[:, 768:1024], func=AF.Sigmoid),
                deps=(lin_toks[1],))
            sg3 = P.do("scalar", lambda s_: s_.activation(
                out=LIN[:, 2048:3072], in_=LIN[:, 2048:3072], func=AF.Sigmoid),
                deps=(lin_toks[5], lin_toks[4], lin_toks[3], lin_toks[2]))
            tq = P.do("vector", lambda v: v.tensor_tensor(
                out=QP[:, :].rearrange("p (h f) -> p h f", f=32)[:, :, 0:16],
                in0=LIN[:, 0:256].rearrange("p (h f) -> p h f", f=16),
                in1=LIN[:, 256:512].rearrange("p (h f) -> p h f", f=16),
                op=ALU.mult), deps=(sg1, qkm, qts_war))
            tk = P.do("vector", lambda v: v.tensor_tensor(
                out=KP[:, :].rearrange("p (h f) -> p h f", f=32)[:, :, 0:16],
                in0=LIN[:, 512:768].rearrange("p (h f) -> p h f", f=16),
                in1=LIN[:, 768:1024].rearrange("p (h f) -> p h f", f=16),
                op=ALU.mult), deps=(sg2, qkm2, kts_war))
            tv = P.do("vector", lambda v, tl=tl: v.tensor_tensor(
                out=VSAUG[:, tl * 1040:(tl + 1) * 1040].rearrange(
                    "p (h f) -> p h f", f=65)[:, :, 0:64],
                in0=LIN[:, 1024:2048].rearrange("p (h f) -> p h f", f=64),
                in1=LIN[:, 2048:3072].rearrange("p (h f) -> p h f", f=64),
                op=ALU.mult), deps=(sg3, vs_ones, vs_war))
            lin_war = tv
            for (src, dststore, wtok_dep) in ((QP, QTS, tq), (KP, KTS, tk)):
                pt_b = tl % 2
                for c in range(4):
                    trq = P.do("tensor", lambda t_, c=c, src=src, pt_b=pt_b: t_.transpose(
                        P.PT[pt_b][:, c * 128:(c + 1) * 128],
                        src[:, c * 128:(c + 1) * 128], P.IDN[:]),
                        deps=(wtok_dep, P.pt_free[pt_b]) if c == 0 else (),
                        inc="pe" if c == 3 else None)
                dstv = dststore[:, 0:8192].rearrange(
                    "p (c n) -> p c n", n=2048)[:, :, tl * 128:(tl + 1) * 128]
                ev = P.do("vector", lambda v, dstv=dstv, pt_b=pt_b: v.tensor_copy(
                    out=dstv, in_=P.PT[pt_b][:, 0:512].rearrange(
                        "p (c n) -> p c n", n=128)), deps=(trq,))
                P.pt_free[pt_b] = ev
                if dststore is QTS:
                    qts_done = ev
                else:
                    kts_done = ev
            qts_war = qts_done
            kts_war = kts_done
            vs_war = None
        P.barrier()

        for h in range(16):
            hc, j = h // 4, h % 4
            base = 32 * j
            tp = {} if base < 96 else {"tile_position": (base, 0)}
            slot_free = [None, None, None]
            for qb in range(4):
                av_tok = None
                for c in range(17):
                    bank = [0, 1, 3][c % 3]
                    slot = c % 3
                    if c < 16:
                        lhsT = KTS[base:base + 16,
                                   hc * 2048 + c * 128: hc * 2048 + (c + 1) * 128]
                    else:
                        lhsT = KPAD17S[base:base + 16, hc * 128:(hc + 1) * 128]
                    rhs = QTS[base:base + 16, hc * 2048 + qb * 512: hc * 2048 + (qb + 1) * 512]
                    sc = P.do("tensor", lambda t_, lhsT=lhsT, rhs=rhs, bank=bank, tp=tp:
                              t_.matmul(P.PS[bank][:, 0:512], lhsT, rhs,
                                        start=True, stop=True, **tp),
                              deps=(P.ps_free[bank],), inc="pe")
                    ex = P.do("scalar", lambda s_, slot=slot, bank=bank: s_.activation(
                        out=EXPT[:, slot * 512:(slot + 1) * 512], in_=P.PS[bank][:, 0:512],
                        func=AF.Exp, bias=IM(-STAB), scale=IM(0.125)),
                        deps=(sc, slot_free[slot]))
                    P.ps_free[bank] = ex
                    if c < 16:
                        lhsT_v = VSAUG[:, (c * 16 + h) * 65:(c * 16 + h) * 65 + 65]
                    else:
                        lhsT_v = VA17S[:, h * 65:(h + 1) * 65]
                    deps = [ex]
                    if c == 0:
                        deps += [P.ps_free[2], va17_tok[0], va17_tok[1]]
                    av_tok = P.do("tensor", lambda t_, lhsT_v=lhsT_v, c=c, slot=slot:
                                  t_.matmul(P.PS[2][0:65, 0:512], lhsT_v,
                                            EXPT[:, slot * 512:(slot + 1) * 512],
                                            start=(c == 0), stop=(c == 16)),
                                  deps=deps, inc="pe")
                    slot_free[slot] = av_tok
                ct = P.do("vector", lambda v: v.tensor_copy(
                    out=CTXT[0:65, :], in_=P.PS[2][0:65, 0:512]), deps=(av_tok,))
                P.ps_free[2] = ct
                prev_v = ct
                for i in range(4):
                    t_glob = s * TPS + qb * 4 + i
                    rt = r_tile(P, t_glob)
                    trc = P.do("tensor", lambda t_, i=i: t_.transpose(
                        P.PT[0][0:128, i * 128:i * 128 + 65],
                        CTXT[0:65, i * 128:(i + 1) * 128], P.IDN[0:65, 0:65]),
                        deps=(ct, P.pt_free[0]) if i == 0 else (),
                        inc="pe" if i == 3 else None)
                for i in range(4):
                    t_glob = s * TPS + qb * 4 + i
                    rt = r_tile(P, t_glob)
                    ev = P.do("vector", lambda v, i=i: v.tensor_copy(
                        out=CTX[:, :], in_=P.PT[0][0:128, i * 128:i * 128 + 65]),
                        deps=(trc, prev_v))
                    rc = P.do("vector", lambda v: v.reciprocal(
                        out=REC[:, 0:1], in_=CTX[:, 64:65]), deps=(ev,))
                    prev_v = P.do("vector", lambda v, rt=rt, h=h: v.scalar_tensor_tensor(
                        out=rt[:, 64 * h:64 * h + 64], in0=CTX[:, 0:64],
                        scalar=REC[:, 0:1], in1=rt[:, 64 * h:64 * h + 64],
                        op0=ALU.mult, op1=ALU.add), deps=(rc,))
                P.pt_free[0] = prev_v
        P.barrier()
    P.barrier()



L2_WCHUNKS = {0: (0, 6), 1: (2, 10), 2: (6, 14), 3: (10, 16)}


def emit_L2_attn(P, ctx, aps):
    nc = P.nc
    sb = lambda name, shape, dt: ctx.enter_context(nc.sbuf_tensor(P.tag + name, shape, dt))
    MRS = sb("l2a_mrs", [128, 2 * NTILE], F32)
    WALL = WBuf(P, ctx, "l2a_w", 1024, 1024)
    WG = WBuf(P, ctx, "l2a_wg", 1024, 1)
    QTS = sb("l2a_qts", [128, 8 * 2048], BF16)
    KTS = sb("l2a_kts", [128, 8 * 2048], BF16)
    VSAUG = sb("l2a_vsaug", [128, 16 * 16 * 65], BF16)
    GATES = sb("l2a_gates", [128, 16], F32)
    OMGS = sb("l2a_omgs", [128, 16], F32)
    GSC = sb("l2a_gsc", [128, 1], F32)
    ONESB = sb("l2a_onesb", [1, 128], BF16)
    REC2 = sb("l2a_rec2", [128, 4], F32)
    RECG = sb("l2a_recg", [128, 4], F32)
    GA = sb("l2a_ga", [128, 4], F32)
    GB = sb("l2a_gb", [128, 4], F32)
    T1 = sb("l2a_t1", [128, 260], F32)
    COMBW = sb("l2a_combw", [128, 4096], BF16)
    EXPT = sb("l2a_expt", [128, 1536], BF16)
    EXPTG = sb("l2a_exptg", [64, 512], BF16)
    CTL = sb("l2a_ctl", [65, 512], BF16)
    CTG = sb("l2a_ctg", [65, 512], BF16)
    LOCN = sb("l2a_locn", [128, 260], BF16)
    GLBN = sb("l2a_glbn", [128, 260], BF16)
    VSTRA = sb("l2a_vstra", [64, 16 * 65], BF16)
    SELC = sb("l2a_selc", [128, 1024], BF16)
    BOROW = sb("l2a_borow", [1, 1024], BF16)
    WOUTV = WALL.sb[:, :]

    o1b = P.do("vector", lambda v: v.memset(ONESB[0:1, :], 1.0))
    nbg_row, nbg_t = load_row_f32(P, ctx, "l2a_bg", aps["b_gate"], 1)
    NEGBG, negbg_t = emit_bcast_row(P, ctx, "l2a_negbg", nbg_row, nbg_t, 1, scale=-1.0)
    sel_tok = P.dma("gpsimd", lambda g: g.dma_start(out=SELC[:, :], in_=aps["selc"]))
    bo_tok = P.dma("gpsimd", lambda g: g.dma_start(out=BOROW[0:1, :], in_=aps["b_out"]))

    xn_war = None
    for t in range(NTILE):
        xn_tok = emit_ln_tile(P, t, xn_war=xn_war)
        xn_war = xn_tok
        P.do("vector", lambda v, t=t: v.tensor_copy(
            out=MRS[:, 2 * t:2 * t + 1], in_=P.NEGMEAN[:, 0:1]), deps=(xn_tok,))
        P.do("vector", lambda v, t=t: v.tensor_copy(
            out=MRS[:, 2 * t + 1:2 * t + 2], in_=P.RSTD[:, 0:1]))
    P.barrier()

    WGROW = VSAUG[0:1, 0:1024]
    wgr_tok = P.dma("gpsimd", lambda g: g.dma_start(
        out=WGROW[0:1, :], in_=aps["W_gate"].rearrange("f one -> one f")))
    for c in range(8):
        trw = P.do("tensor", lambda t, c=c: t.transpose(
            P.PT[0][0:128, c * 128:c * 128 + 1],
            WGROW[0:1, c * 128:(c + 1) * 128], P.IDN[0:1, 0:1]),
            deps=(wgr_tok, P.pt_free[0], P.idn_tok) if c == 0 else (),
            inc="pe" if c == 7 else None)
    wg_tok = None
    for c in range(8):
        wg_tok = P.do("vector", lambda v, c=c: v.tensor_copy(
            out=WG.sb[:, c:c + 1], in_=P.PT[0][0:128, c * 128:c * 128 + 1]),
            deps=(trw,))
    P.pt_free[0] = wg_tok

    for s in range(2):
        vs_ones = P.do("vector", lambda v: v.memset(VSAUG[:, :], 1.0),
                       deps=(getattr(P, "_l2_vs_war", None), wg_tok))
        for sweep in range(3):
            wlo = sweep * 1024
            w_war = P.fence("tensor")
            wq_tok = P.dma("gpsimd", lambda g, wlo=wlo: g.dma_start(
                out=WALL.sb[:, :].rearrange("p (c f) -> p c f", f=1024),
                in_=aps["W_qkv"][:, wlo:wlo + 1024].rearrange("(c p) f -> p c f", p=128)),
                deps=(w_war,))
            def mk_l2a(tl, ap, war, s=s):
                t = s * TPS + tl
                return P.do("vector", lambda v, t=t, ap=ap: v.tensor_scalar(
                    out=ap, in0=r_tile(P, t), scalar1=MRS[:, 2 * t:2 * t + 1],
                    scalar2=MRS[:, 2 * t + 1:2 * t + 2], op0=ALU.add, op1=ALU.mult),
                    deps=(war,) + tuple(P.r_toks))

            pipe = XnPipe(P, mk_l2a, TPS)
            store_war = None
            for tl in range(TPS):
                t = s * TPS + tl
                xt_tok = pipe.xt_tok[tl]
                xnv = pipe.xn_ap(tl)
                xtv = pipe.xt_ap(tl)
                if sweep == 0:
                    for k in range(8):
                        deps = [P.ps_free[2], wg_tok, xt_tok] if k == 0 else []
                        gmm = P.do("tensor", lambda t_, k=k, xtv=xtv: t_.matmul(
                            P.PS[2][:, 0:1], xtv[:, k * 128:(k + 1) * 128], WG.chunk(k),
                            start=(k == 0), stop=(k == 7)),
                            deps=deps, inc="pe" if k == 7 else None)
                    ge = P.do("scalar", lambda s_: s_.activation(
                        out=GSC[:, 0:1], in_=P.PS[2][:, 0:1], func=AF.Exp,
                        bias=NEGBG[:, 0:1], scale=IM(-1.0)), deps=(gmm, negbg_t))
                    P.ps_free[2] = ge
                    g1 = P.do("vector", lambda v: v.tensor_scalar(
                        out=GSC[:, 0:1], in0=GSC[:, 0:1], scalar1=1.0, scalar2=None,
                        op0=ALU.add), deps=(ge,))
                    g2 = P.do("vector", lambda v, tl=tl: v.reciprocal(
                        out=GATES[:, tl:tl + 1], in_=GSC[:, 0:1]), deps=(g1,))
                    P.do("vector", lambda v, tl=tl: v.tensor_scalar(
                        out=OMGS[:, tl:tl + 1], in0=GATES[:, tl:tl + 1], scalar1=-1.0,
                        scalar2=1.0, op0=ALU.mult, op1=ALU.add), deps=(g2,))
                banks = [0, 1]
                mm = emit_proj_1024(P, xtv, WALL.sb, wq_tok, out_banks=banks,
                                    fout=1024, extra_deps=(xt_tok,))
                pipe.done(tl, mm[-1])
                if sweep == 2:
                    for bi in range(2):
                        dstv = VSAUG[:, tl * 1040:(tl + 1) * 1040].rearrange(
                            "p (h f) -> p h f", f=65)[:, bi * 8:(bi + 1) * 8, 0:64]
                        e = P.do("vector", lambda v, bi=bi, dstv=dstv: v.tensor_copy(
                            out=dstv, in_=P.PS[bi][:, 0:512].rearrange(
                                "p (h f) -> p h f", f=64)),
                            deps=(mm[bi], P.ps_free[bi], vs_ones))
                        P.ps_free[bi] = e
                else:
                    evs = []
                    for bi in range(2):
                        if sweep == 0:
                            e = P.do("vector", lambda v, bi=bi, xnv=xnv: v.tensor_copy(
                                out=xnv[:, bi * 512:(bi + 1) * 512],
                                in_=P.PS[bi][:, 0:512]),
                                deps=(mm[bi], P.ps_free[bi], xt_tok))
                        else:
                            e = P.do("vector", lambda v, bi=bi, xnv=xnv: v.tensor_scalar(
                                out=xnv[:, bi * 512:(bi + 1) * 512],
                                in0=P.PS[bi][:, 0:512], scalar1=0.125, scalar2=None,
                                op0=ALU.mult),
                                deps=(mm[bi], P.ps_free[bi], xt_tok))
                        evs.append(e)
                        P.ps_free[bi] = e
                    dststore = QTS if sweep == 0 else KTS
                    pt_b = tl % 2
                    for c in range(8):
                        trq = P.do("tensor", lambda t_, c=c, pt_b=pt_b, xnv=xnv: t_.transpose(
                            P.PT[pt_b][:, c * 128:(c + 1) * 128],
                            xnv[:, c * 128:(c + 1) * 128], P.IDN[:]),
                            deps=(evs[-1], P.pt_free[pt_b], store_war) if c == 0 else (),
                            inc="pe" if c == 7 else None)
                    dstv = dststore[:, :].rearrange(
                        "p (c n) -> p c n", n=2048)[:, :, tl * 128:(tl + 1) * 128]
                    evt = P.do("vector", lambda v, dstv=dstv, pt_b=pt_b: v.tensor_copy(
                        out=dstv, in_=P.PT[pt_b][:, :].rearrange(
                            "p (c n) -> p c n", n=128)), deps=(trq,))
                    P.pt_free[pt_b] = evt
                    store_war = evt
                    pipe.set_xn_reader(tl, evt)
            P.barrier()

        w_war = P.fence("tensor")
        wo_tok = P.dma("gpsimd", lambda g: g.dma_start(
            out=WOUTV.rearrange("p (c f) -> p c f", f=1024),
            in_=aps["W_out"].rearrange("(c p) f -> p c f", p=128)), deps=(w_war,))
        vz = P.do("vector", lambda v: v.memset(VSTRA[:, :], 1.0), deps=(w_war,))
        vst_mm = [None, None]
        for tl in range(TPS):
            rhsv = VSAUG[:, tl * 1040:(tl + 1) * 1040].rearrange(
                "p (h f) -> p h f", f=65)[:, :, 0:64]
            for bi in range(2):
                deps = [sel_tok, P.ps_free[bi]] if tl == 0 else []
                vst_mm[bi] = P.do("tensor", lambda t_, tl=tl, bi=bi, rhsv=rhsv: t_.matmul(
                    P.PS[bi][0:64, 0:512], SELC[:, tl * 64:(tl + 1) * 64],
                    rhsv[:, bi * 8:(bi + 1) * 8, :],
                    start=(tl == 0), stop=(tl == TPS - 1)),
                    deps=deps, inc="pe" if tl == TPS - 1 else None)
        vstr_tok = None
        for bi in range(2):
            dstv = VSTRA[:, :].rearrange("p (h f) -> p h f", f=65)[:, bi * 8:(bi + 1) * 8, 0:64]
            vstr_tok = P.do("vector", lambda v, bi=bi, dstv=dstv: v.tensor_copy(
                out=dstv, in_=P.PS[bi][0:64, 0:512].rearrange("p (h f) -> p h f", f=64)),
                deps=(vst_mm[bi], vz))
            P.ps_free[bi] = vstr_tok

        STAB2 = 4.0
        comb_done = None
        for w in range(4):
            c0, c1 = L2_WCHUNKS[w]
            for h in range(16):
                hc, base = h // 2, 64 * (h % 2)
                qrhs = QTS[base:base + 64, hc * 2048 + w * 512: hc * 2048 + (w + 1) * 512]
                slot_free = [None, None, None]
                av_tok = None
                nk = c1 - c0
                for ci, kc in enumerate(range(c0, c1)):
                    bank = [2, 3, 0][ci % 3]
                    slot = ci % 3
                    sc = P.do("tensor", lambda t_, kc=kc, bank=bank, hc=hc, base=base,
                              qrhs=qrhs: t_.matmul(
                                  P.PS[bank][:, 0:512],
                                  KTS[base:base + 64, hc * 2048 + kc * 128: hc * 2048 + (kc + 1) * 128],
                                  qrhs, start=True, stop=True),
                              deps=(P.ps_free[bank],), inc="pe")
                    ex = P.do("scalar", lambda s_, slot=slot, bank=bank: s_.activation(
                        out=EXPT[:, slot * 512:(slot + 1) * 512], in_=P.PS[bank][:, 0:512],
                        func=AF.Exp, bias=IM(-STAB2), scale=IM(1.0)),
                        deps=(sc, slot_free[slot]))
                    P.ps_free[bank] = ex
                    deps = [ex]
                    if ci == 0:
                        deps += [P.ps_free[4], vs_ones]
                    av_tok = P.do("tensor", lambda t_, kc=kc, slot=slot, h=h, ci=ci,
                                  nk=nk: t_.matmul(
                                      P.PS[4][0:65, 0:512],
                                      VSAUG[:, (kc * 16 + h) * 65:(kc * 16 + h) * 65 + 65],
                                      EXPT[:, slot * 512:(slot + 1) * 512],
                                      start=(ci == 0), stop=(ci == nk - 1)),
                                  deps=deps, inc="pe")
                    slot_free[slot] = av_tok
                kstr = KTS[base:base + 64, hc * 2048: (hc + 1) * 2048].rearrange(
                    "p (a b) -> p a b", b=32)[:, :, 0:1].squeeze(2)
                scg = P.do("tensor", lambda t_, kstr=kstr, qrhs=qrhs: t_.matmul(
                    P.PS[2][0:64, 0:512], kstr, qrhs, start=True, stop=True),
                    deps=(P.ps_free[2],), inc="pe")
                exg = P.do("scalar", lambda s_: s_.activation(
                    out=EXPTG[0:64, 0:512], in_=P.PS[2][0:64, 0:512], func=AF.Exp,
                    bias=IM(-STAB2), scale=IM(1.0)),
                    deps=(scg, getattr(P, "_l2_avg", None)))
                P.ps_free[2] = exg
                avg = P.do("tensor", lambda t_, h=h: t_.matmul(
                    P.PS[5][0:65, 0:512], VSTRA[0:64, h * 65:(h + 1) * 65],
                    EXPTG[0:64, 0:512], start=True, stop=True),
                    deps=(exg, P.ps_free[5], vstr_tok), inc="pe")
                P._l2_avg = avg
                ctl = P.do("vector", lambda v: v.tensor_copy(
                    out=CTL[0:65, :], in_=P.PS[4][0:65, 0:512]), deps=(av_tok,))
                P.ps_free[4] = ctl
                ctg = P.do("vector", lambda v: v.tensor_copy(
                    out=CTG[0:65, :], in_=P.PS[5][0:65, 0:512]), deps=(avg,))
                P.ps_free[5] = ctg
                for i in range(4):
                    trl = P.do("tensor", lambda t_, i=i: t_.transpose(
                        P.PT[0][0:128, i * 128:i * 128 + 65],
                        CTL[0:65, i * 128:(i + 1) * 128], P.IDN[0:65, 0:65]),
                        deps=(ctl, P.pt_free[0]) if i == 0 else (),
                        inc="pe" if i == 3 else None)
                evl = P.do("vector", lambda v: v.tensor_copy(
                    out=LOCN[:, :].rearrange("p (i f) -> p i f", f=65),
                    in_=P.PT[0][:, 0:512].rearrange("p (i f) -> p i f", f=128)[:, :, 0:65]),
                    deps=(trl, comb_done))
                P.pt_free[0] = evl
                for i in range(4):
                    trg = P.do("tensor", lambda t_, i=i: t_.transpose(
                        P.PT[1][0:128, i * 128:i * 128 + 65],
                        CTG[0:65, i * 128:(i + 1) * 128], P.IDN[0:65, 0:65]),
                        deps=(ctg, P.pt_free[1]) if i == 0 else (),
                        inc="pe" if i == 3 else None)
                evg = P.do("vector", lambda v: v.tensor_copy(
                    out=GLBN[:, :].rearrange("p (i f) -> p i f", f=65),
                    in_=P.PT[1][:, 0:512].rearrange("p (i f) -> p i f", f=128)[:, :, 0:65]),
                    deps=(trg, comb_done))
                P.pt_free[1] = evg
                r1 = P.do("vector", lambda v: v.reciprocal(
                    out=REC2[:, :],
                    in_=LOCN[:, :].rearrange("p (i f) -> p i f", f=65)[:, :, 64:65].squeeze(2)),
                    deps=(evl,))
                r2 = P.do("vector", lambda v: v.reciprocal(
                    out=RECG[:, :],
                    in_=GLBN[:, :].rearrange("p (i f) -> p i f", f=65)[:, :, 64:65].squeeze(2)),
                    deps=(evg,))
                ga = P.do("vector", lambda v, w=w: v.tensor_tensor(
                    out=GA[:, :], in0=REC2[:, :], in1=GATES[:, 4 * w:4 * w + 4],
                    op=ALU.mult), deps=(r1,))
                gb = P.do("vector", lambda v, w=w: v.tensor_tensor(
                    out=GB[:, :], in0=RECG[:, :], in1=OMGS[:, 4 * w:4 * w + 4],
                    op=ALU.mult), deps=(r2,))
                tt1 = P.do("vector", lambda v: v.tensor_tensor(
                    out=T1[:, :].rearrange("p (i f) -> p i f", f=65)[:, :, 0:64],
                    in0=LOCN[:, :].rearrange("p (i f) -> p i f", f=65)[:, :, 0:64],
                    in1=GA[:, :].unsqueeze(2).broadcast_to([128, 4, 64]),
                    op=ALU.mult), deps=(ga, evl))
                cw = COMBW[:, :].rearrange("p (i d) -> p i d", d=1024)[:, :, 64 * h:64 * h + 64]
                c1_ = P.do("vector", lambda v, cw=cw: v.tensor_tensor(
                    out=cw, in0=GLBN[:, :].rearrange("p (i f) -> p i f", f=65)[:, :, 0:64],
                    in1=GB[:, :].unsqueeze(2).broadcast_to([128, 4, 64]),
                    op=ALU.mult), deps=(gb, evg, comb_done))
                comb_done = P.do("vector", lambda v, cw=cw: v.tensor_tensor(
                    out=cw, in0=cw,
                    in1=T1[:, :].rearrange("p (i f) -> p i f", f=65)[:, :, 0:64],
                    op=ALU.add), deps=(c1_, tt1))
            xt_war2 = None
            for i in range(4):
                tg = s * TPS + w * 4 + i
                rt = r_tile(P, tg)
                xt_tok = emit_transpose_1024(P, COMBW[:, i * 1024:(i + 1) * 1024],
                                             P.XT, (comb_done,), pt_bank=i % 2,
                                             war_dep=xt_war2)
                for bi in range(2):
                    P.do("tensor", lambda t_, bi=bi: t_.matmul(
                        P.PS[bi][:, 0:512], ONESB[0:1, :],
                        BOROW[0:1, bi * 512:(bi + 1) * 512], start=True, stop=False),
                        deps=(P.ps_free[bi], bo_tok, o1b))
                mmo = emit_proj_1024(P, P.XT, WOUTV, wo_tok, out_banks=[0, 1],
                                     fout=1024, extra_deps=(xt_tok,), accumulate=True)
                xt_war2 = mmo[-1]
                for bi in range(2):
                    lo, hi = bi * 512, (bi + 1) * 512
                    e = P.do("vector", lambda v, bi=bi, lo=lo, hi=hi, rt=rt: v.tensor_tensor(
                        out=rt[:, lo:hi], in0=P.PS[bi][:, 0:512], in1=rt[:, lo:hi],
                        op=ALU.add), deps=(mmo[bi],))
                    P.ps_free[bi] = e
        P._l2_vs_war = P.fence("tensor")
        P.barrier()
    P.barrier()



def emit_L0_attn(P, ctx, aps):
    nc = P.nc
    sb = lambda name, shape, dt: ctx.enter_context(nc.sbuf_tensor(P.tag + name, shape, dt))
    MRS = sb("l0a_mrs", [128, 2 * NTILE], F32)
    W1 = WBuf(P, ctx, "l0a_w1", 1024, 1024)
    QTS = sb("l0a_qts", [128, 8 * 2048], BF16)
    KTS = sb("l0a_kts", [128, 8 * 2048], BF16)
    VSAUG = sb("l0a_vsaug", [128, 16 * 16 * 65], BF16)
    BROW = sb("l0a_brow", [1, 3072], BF16)
    BOROW = sb("l0a_borow", [1, 1024], BF16)
    ONESB = sb("l0a_onesb", [1, 128], BF16)
    MTC = sb("l0a_mtc", [128, 128], BF16)
    ES = sb("l0a_es", [128, 512], F32)
    AS = sb("l0a_as", [128, 512], BF16)
    RS = sb("l0a_rs", [128, 512], BF16)
    ATM = sb("l0a_atm", [128, 256], BF16)
    KTOK1 = sb("l0a_ktok1", [128, 1024], BF16)
    NUMS = sb("l0a_nums", [128, 16 * 65], BF16)
    DEN = sb("l0a_den", [128, 16], F32)
    RECS = sb("l0a_recs", [128, 16], F32)
    OUT1 = sb("l0a_out1", [128, 1024], BF16)
    SSB = sb("l0a_ssb", [128, 8 * 65], F32)
    SBF = sb("l0a_sbf", [128, 8 * 65], BF16)

    o1b = P.do("vector", lambda v: v.memset(ONESB[0:1, :], 1.0))
    mt_tok = P.dma("gpsimd", lambda g: g.dma_start(out=MTC[:, :], in_=aps["mtc"]))
    bq_tok = P.dma("gpsimd", lambda g: g.dma_start(out=BROW[0:1, :], in_=aps["b_qkv"]))
    bo_tok = P.dma("gpsimd", lambda g: g.dma_start(out=BOROW[0:1, :], in_=aps["b_out"]))

    xn_war = None
    for t in range(NTILE):
        xn_tok = emit_ln_tile(P, t, xn_war=xn_war)
        xn_war = xn_tok
        P.do("vector", lambda v, t=t: v.tensor_copy(
            out=MRS[:, 2 * t:2 * t + 1], in_=P.NEGMEAN[:, 0:1]), deps=(xn_tok,))
        P.do("vector", lambda v, t=t: v.tensor_copy(
            out=MRS[:, 2 * t + 1:2 * t + 2], in_=P.RSTD[:, 0:1]))
    P.barrier()

    for s in range(2):
        vs_ones = P.do("vector", lambda v: v.memset(VSAUG[:, :], 1.0),
                       deps=(getattr(P, "_l0_vs_war", None),))
        for sweep in range(3):
            wlo = sweep * 1024
            w_war = P.fence("tensor")
            wq_tok = P.dma("gpsimd", lambda g, wlo=wlo: g.dma_start(
                out=W1.sb[:, :].rearrange("p (c f) -> p c f", f=1024),
                in_=aps["W_qkv"][:, wlo:wlo + 1024].rearrange("(c p) f -> p c f", p=128)),
                deps=(w_war,))
            def mk_l0a(t, ap, war):
                return P.do("vector", lambda v, t=t, ap=ap: v.tensor_scalar(
                    out=ap, in0=r_tile(P, t), scalar1=MRS[:, 2 * t:2 * t + 1],
                    scalar2=MRS[:, 2 * t + 1:2 * t + 2], op0=ALU.add, op1=ALU.mult),
                    deps=(war,) + tuple(P.r_toks))

            pipe = XnPipe(P, lambda tl, ap, war: mk_l0a(s * TPS + tl, ap, war), TPS)
            store_war = None
            for tl in range(TPS):
                t = s * TPS + tl
                xt_tok = pipe.xt_tok[tl]
                xnv = pipe.xn_ap(tl)
                for bi in range(2):
                    P.do("tensor", lambda t_, bi=bi, wlo=wlo: t_.matmul(
                        P.PS[bi][:, 0:512], ONESB[0:1, :],
                        BROW[0:1, wlo + bi * 512:wlo + (bi + 1) * 512],
                        start=True, stop=False),
                        deps=(P.ps_free[bi], bq_tok, o1b))
                mm = emit_proj_1024(P, pipe.xt_ap(tl), W1.sb, wq_tok, out_banks=[0, 1],
                                    fout=1024, extra_deps=(xt_tok,), accumulate=True)
                pipe.done(tl, mm[-1])
                if sweep == 2:
                    for bi in range(2):
                        dstv = VSAUG[:, tl * 1040:(tl + 1) * 1040].rearrange(
                            "p (h f) -> p h f", f=65)[:, bi * 8:(bi + 1) * 8, 0:64]
                        e = P.do("vector", lambda v, bi=bi, dstv=dstv: v.tensor_copy(
                            out=dstv, in_=P.PS[bi][:, 0:512].rearrange(
                                "p (h f) -> p h f", f=64)),
                            deps=(mm[bi], P.ps_free[bi], vs_ones))
                        P.ps_free[bi] = e
                else:
                    scale = 0.125 if sweep == 0 else 1.0
                    evs = []
                    for bi in range(2):
                        e1 = P.do("scalar", lambda s_, bi=bi, scale=scale: s_.activation(
                            out=ES[:, :], in_=P.PS[bi][:, 0:512], func=AF.Exp,
                            scale=IM(scale)),
                            deps=(mm[bi], getattr(P, "_l0_es_war", None)))
                        a1 = P.do("vector", lambda v: v.tensor_scalar(
                            out=AS[:, :], in0=ES[:, :], scalar1=1.0, scalar2=1e-4,
                            op0=ALU.min, op1=ALU.add), deps=(e1,))
                        r1 = P.do("vector", lambda v, bi=bi, scale=scale: v.tensor_scalar(
                            out=RS[:, :], in0=P.PS[bi][:, 0:512], scalar1=scale,
                            scalar2=0.0, op0=ALU.mult, op1=ALU.max),
                            deps=(mm[bi], P.ps_free[bi], a1))
                        e = P.do("vector", lambda v, bi=bi, xnv=xnv: v.tensor_tensor(
                            out=xnv[:, bi * 512:(bi + 1) * 512], in0=RS[:, :],
                            in1=AS[:, :], op=ALU.add), deps=(r1, xt_tok))
                        P._l0_es_war = e
                        evs.append(e)
                        P.ps_free[bi] = e
                    dststore = QTS if sweep == 0 else KTS
                    pt_b = tl % 2
                    for c in range(8):
                        trq = P.do("tensor", lambda t_, c=c, pt_b=pt_b, xnv=xnv: t_.transpose(
                            P.PT[pt_b][:, c * 128:(c + 1) * 128],
                            xnv[:, c * 128:(c + 1) * 128], P.IDN[:]),
                            deps=(evs[-1], P.pt_free[pt_b], store_war) if c == 0 else (),
                            inc="pe" if c == 7 else None)
                    dstv = dststore[:, :].rearrange(
                        "p (c n) -> p c n", n=2048)[:, :, tl * 128:(tl + 1) * 128]
                    evt = P.do("vector", lambda v, dstv=dstv, pt_b=pt_b: v.tensor_copy(
                        out=dstv, in_=P.PT[pt_b][:, :].rearrange("p (c n) -> p c n", n=128)),
                        deps=(trq,))
                    P.pt_free[pt_b] = evt
                    store_war = evt
                    pipe.set_xn_reader(tl, evt)
            P.barrier()

        w_war = P.fence("tensor")
        wo_tok = P.dma("gpsimd", lambda g: g.dma_start(
            out=W1.sb[:, :].rearrange("p (c f) -> p c f", f=1024),
            in_=aps["W_out"].rearrange("(c p) f -> p c f", p=128)), deps=(w_war,))
        sz = P.do("vector", lambda v: v.memset(SSB[:, :], 0.0))
        szb = P.do("vector", lambda v: v.memset(SBF[:, :], 0.0))
        num_evac_prev = [None, None]
        sstate_rd = None
        out_done = None
        for u in range(TPS):
            pt_b = u % 2
            for c in range(8):
                trk = P.do("tensor", lambda t_, c=c, u=u, pt_b=pt_b: t_.transpose(
                    P.PT[pt_b][:, c * 128:(c + 1) * 128],
                    KTS[:, c * 2048 + u * 128: c * 2048 + (u + 1) * 128], P.IDN[:]),
                    deps=(P.pt_free[pt_b], getattr(P, "_l0_kt_war", None)) if c == 0 else (),
                    inc="pe" if c == 7 else None)
            ktk = P.do("vector", lambda v, pt_b=pt_b: v.tensor_copy(
                out=KTOK1[:, :], in_=P.PT[pt_b][:, :]),
                deps=(trk, getattr(P, "_l0_ktok_war", None)))
            P.pt_free[pt_b] = ktk
            atm_war = getattr(P, "_l0_atm_war", {})
            state_rd = getattr(P, "_l0_state_rd", {})
            for h in range(16):
                hc, base = h // 2, 64 * (h % 2)
                sl = h % 2
                abank = sl
                nbank = 2 + sl
                sbank = 4 if sl == 0 else 5
                amm = P.do("tensor", lambda t_, hc=hc, base=base, u=u, abank=abank: t_.matmul(
                    P.PS[abank][:, 0:128],
                    KTS[base:base + 64, hc * 2048 + u * 128: hc * 2048 + (u + 1) * 128],
                    QTS[base:base + 64, hc * 2048 + u * 128: hc * 2048 + (u + 1) * 128],
                    start=True, stop=True),
                    deps=(P.ps_free[abank],), inc="pe")
                msk = P.do("vector", lambda v, abank=abank, sl=sl: v.tensor_tensor(
                    out=ATM[:, sl * 128:(sl + 1) * 128], in0=P.PS[abank][:, 0:128],
                    in1=MTC[:, :],
                    op=ALU.mult), deps=(amm, mt_tok, atm_war.get(sl)))
                P.ps_free[abank] = msk
                vsl = VSAUG[:, (u * 16 + h) * 65:(u * 16 + h) * 65 + 65]
                nmm = P.do("tensor", lambda t_, vsl=vsl, nbank=nbank, u=u, sl=sl: t_.matmul(
                    P.PS[nbank][:, 0:65], ATM[:, sl * 128:(sl + 1) * 128], vsl,
                    start=True, stop=(u == 0)),
                    deps=(msk, P.ps_free[nbank], vs_ones), inc="pe" if u == 0 else None)
                if u > 0:
                    nmm = P.do("tensor", lambda t_, hc=hc, base=base, u=u, nbank=nbank,
                               h=h: t_.matmul(
                                   P.PS[nbank][:, 0:65],
                                   QTS[base:base + 64, hc * 2048 + u * 128: hc * 2048 + (u + 1) * 128],
                                   SBF[base:base + 64, (h // 2) * 65:(h // 2 + 1) * 65],
                                   start=False, stop=True),
                               deps=(state_rd.get(h),), inc="pe")
                atm_war[sl] = nmm
                ne = P.do("vector", lambda v, h=h, nbank=nbank: v.tensor_copy(
                    out=NUMS[:, h * 65:(h + 1) * 65], in_=P.PS[nbank][:, 0:65]),
                    deps=(nmm, num_evac_prev[h % 2], out_done))
                num_evac_prev[h % 2] = ne
                P.ps_free[nbank] = ne
                blk = (h // 2) * 65
                smm = P.do("tensor", lambda t_, h=h, vsl=vsl, base=base, sbank=sbank:
                           t_.matmul(
                    P.PS[sbank][base:base + 64, 0:65], KTOK1[:, 64 * h:64 * h + 64], vsl,
                    start=True, stop=True, tile_position=(0, base)),
                    deps=(ktk, P.ps_free[sbank]), inc="pe")
                f1 = P.do("vector", lambda v, base=base, blk=blk, sbank=sbank: v.tensor_tensor(
                    out=SSB[base:base + 64, blk:blk + 65],
                    in0=P.PS[sbank][base:base + 64, 0:65],
                    in1=SSB[base:base + 64, blk:blk + 65], op=ALU.add),
                    deps=(smm, sz))
                P.ps_free[sbank] = f1
                state_rd[h] = P.do("vector", lambda v, base=base, blk=blk: v.tensor_copy(
                    out=SBF[base:base + 64, blk:blk + 65],
                    in_=SSB[base:base + 64, blk:blk + 65]),
                    deps=(f1, szb, nmm))
            P._l0_atm_war = atm_war
            P._l0_state_rd = state_rd
            P._l0_kt_war = amm
            P._l0_ktok_war = smm
            d1 = P.do("vector", lambda v: v.tensor_scalar(
                out=DEN[:, :],
                in0=NUMS[:, :].rearrange("p (h f) -> p h f", f=65)[:, :, 64:65].squeeze(2),
                scalar1=1e-6, scalar2=None, op0=ALU.add),
                deps=(num_evac_prev[0], num_evac_prev[1]))
            d2 = P.do("vector", lambda v: v.reciprocal(out=RECS[:, :], in_=DEN[:, :]),
                      deps=(d1,))
            o1 = P.do("vector", lambda v: v.tensor_tensor(
                out=OUT1[:, :].rearrange("p (h f) -> p h f", f=64),
                in0=NUMS[:, :].rearrange("p (h f) -> p h f", f=65)[:, :, 0:64],
                in1=RECS[:, :].unsqueeze(2).broadcast_to([128, 16, 64]),
                op=ALU.mult), deps=(d2, out_done))
            t_glob = s * TPS + u
            rt = r_tile(P, t_glob)
            xt_tok = emit_transpose_1024(P, OUT1, P.XT, (o1,), pt_bank=u % 2,
                                         war_dep=getattr(P, "_l0_xt_war", None))
            for bi in range(2):
                P.do("tensor", lambda t_, bi=bi: t_.matmul(
                    P.PS[bi][:, 0:512], ONESB[0:1, :],
                    BOROW[0:1, bi * 512:(bi + 1) * 512], start=True, stop=False),
                    deps=(P.ps_free[bi], bo_tok, o1b))
            mmo = emit_proj_1024(P, P.XT, W1.sb, wo_tok, out_banks=[0, 1],
                                 fout=1024, extra_deps=(xt_tok,), accumulate=True)
            P._l0_xt_war = mmo[-1]
            for bi in range(2):
                lo, hi = bi * 512, (bi + 1) * 512
                out_done = P.do("vector", lambda v, bi=bi, lo=lo, hi=hi, rt=rt:
                                v.tensor_tensor(out=rt[:, lo:hi], in0=P.PS[bi][:, 0:512],
                                                in1=rt[:, lo:hi], op=ALU.add),
                                deps=(mmo[bi],))
                P.ps_free[bi] = out_done
        P._l0_vs_war = P.fence("tensor")
        P.barrier()
    P.barrier()



LN16 = 2.772588722239781
L1_LOCAL_STAB = False


def emit_L1_attn(P, ctx, aps):
    nc = P.nc
    sb = lambda name, shape, dt: ctx.enter_context(nc.sbuf_tensor(P.tag + name, shape, dt))
    MRS = sb("l1a_mrs", [128, 2 * NTILE], F32)
    W1 = WBuf(P, ctx, "l1a_w1", 1024, 1024)
    QTS = sb("l1a_qts", [128, 8 * 2048], BF16)
    KTS = sb("l1a_kts", [128, 8 * 2048], BF16)
    VSAUG = sb("l1a_vsaug", [128, 16 * 16 * 65], BF16)
    DIAGQ = sb("l1a_diagq", [128, 16 * 16], F32)
    DIAGK = sb("l1a_diagk", [128, 16 * 16], F32)
    SQS = sb("l1a_sqs", [128, 1024], F32)
    PROJ0 = sb("l1a_proj0", [128, 2 * 64], BF16)
    PROJT = sb("l1a_projt", [128, 256], BF16)
    MAXT = sb("l1a_maxt", [128, 256], F32)
    MAX1 = sb("l1a_max1", [128, 1], F32)
    MROW = sb("l1a_mrow", [1, 128], F32)
    STABR = sb("l1a_stabr", [1, 8], F32)
    NEGST = sb("l1a_negst", [128, 1], F32)
    CTXS = sb("l1a_ctxs", [128, 16 * 130], BF16)
    CTXT = sb("l1a_ctxt", [65, 256], BF16)
    MX = sb("l1a_mx", [128, 4], F32)
    NDS = sb("l1a_nds", [128, 4], F32)
    QPB = sb("l1a_qpb", [128, 1024], BF16)
    QPT = sb("l1a_qpt", [128, 1024], BF16)
    NUMS = sb("l1a_nums", [128, 16 * 65], BF16)
    DEN = sb("l1a_den", [128, 16], F32)
    RECS = sb("l1a_recs", [128, 16], F32)
    OUT1 = sb("l1a_out1", [128, 1024], BF16)
    BOROW = sb("l1a_borow", [1, 1024], BF16)
    ONESB = sb("l1a_onesb", [1, 128], BF16)
    IDNF = sb("l1a_idnf", [128, 128], F32)
    stab_loc = nc.dram_tensor(P.tag + "l1a_stab_loc", [1, 8], F32, kind="Internal").ap()
    stab_out = nc.dram_tensor(P.tag + "l1a_stab_out", [1, 8], F32, kind="Internal").ap()

    o1b = P.do("vector", lambda v: v.memset(ONESB[0:1, :], 1.0))
    idnf_tok = P.dma("sync", lambda s_: s_.dma_start(out=IDNF[:, :], in_=aps["idn"]))
    bo_tok = P.dma("gpsimd", lambda g: g.dma_start(out=BOROW[0:1, :], in_=aps["bo"]))
    pj_tok = P.dma("gpsimd", lambda g: g.dma_start(
        out=PROJ0[:, :].rearrange("p (c f) -> p c f", f=64),
        in_=aps["proj"].rearrange("(c p) f -> p c f", p=128)))
    for c in range(2):
        trp = P.do("tensor", lambda t, c=c: t.transpose(
            P.PT[0][0:64, c * 128:c * 128 + 128], PROJ0[:, c * 64:(c + 1) * 64],
            P.IDN[:]),
            deps=(pj_tok, P.pt_free[0], P.idn_tok) if c == 0 else ())
        trp = P.do("tensor", lambda t, c=c: t.transpose(
            P.PT[0][64:128, c * 128 + 512:c * 128 + 640],
            PROJ0[:, c * 64:(c + 1) * 64], P.IDN[:], tile_position=(0, 64)),
            deps=(), inc="pe" if c == 1 else None)
    pjt_tok0 = P.do("vector", lambda v: v.tensor_scalar(
        out=PROJT[0:64, :].rearrange("p (c n) -> p c n", n=128),
        in0=P.PT[0][0:64, 0:256].rearrange("p (c n) -> p c n", n=128),
        scalar1=0.35355339059327373, scalar2=None, op0=ALU.mult), deps=(trp,))
    pjt_tok = P.do("vector", lambda v: v.tensor_scalar(
        out=PROJT[64:128, :].rearrange("p (c n) -> p c n", n=128),
        in0=P.PT[0][64:128, 512:768].rearrange("p (c n) -> p c n", n=128),
        scalar1=0.35355339059327373, scalar2=None, op0=ALU.mult), deps=(pjt_tok0,))
    P.pt_free[0] = pjt_tok

    xn_war = None
    for t in range(NTILE):
        xn_tok = emit_ln_tile(P, t, xn_war=xn_war)
        xn_war = xn_tok
        P.do("vector", lambda v, t=t: v.tensor_copy(
            out=MRS[:, 2 * t:2 * t + 1], in_=P.NEGMEAN[:, 0:1]), deps=(xn_tok,))
        P.do("vector", lambda v, t=t: v.tensor_copy(
            out=MRS[:, 2 * t + 1:2 * t + 2], in_=P.RSTD[:, 0:1]))
    P.barrier()

    def sweep(s, which, store, diag_store):
        wname = ("wq", "wk", "wv")[which]
        w_war = P.fence("tensor")
        w_tok = P.dma("gpsimd", lambda g, wname=wname: g.dma_start(
            out=W1.sb[:, :].rearrange("p (c f) -> p c f", f=1024),
            in_=aps[wname].rearrange("(c p) f -> p c f", p=128)), deps=(w_war,))
        store_war = None
        dq_war = None

        def mk_l1s(tl, ap, war, s=s):
            t = s * TPS + tl
            return P.do("vector", lambda v, t=t, ap=ap: v.tensor_scalar(
                out=ap, in0=r_tile(P, t), scalar1=MRS[:, 2 * t:2 * t + 1],
                scalar2=MRS[:, 2 * t + 1:2 * t + 2], op0=ALU.add, op1=ALU.mult),
                deps=(war,) + tuple(P.r_toks))

        pipe = XnPipe(P, mk_l1s, TPS)
        for tl in range(TPS):
            t = s * TPS + tl
            xt_tok = pipe.xt_tok[tl]
            xnv = pipe.xn_ap(tl)
            mm = emit_proj_1024(P, pipe.xt_ap(tl), W1.sb, w_tok, out_banks=[0, 1],
                                fout=1024, extra_deps=(xt_tok,))
            pipe.done(tl, mm[-1])
            if store is VSAUG:
                for bi in range(2):
                    dstv = VSAUG[:, tl * 1040:(tl + 1) * 1040].rearrange(
                        "p (h f) -> p h f", f=65)[:, bi * 8:(bi + 1) * 8, 0:64]
                    e = P.do("vector", lambda v, bi=bi, dstv=dstv: v.tensor_copy(
                        out=dstv, in_=P.PS[bi][:, 0:512].rearrange(
                            "p (h f) -> p h f", f=64)),
                        deps=(mm[bi], P.ps_free[bi], P._l1_vs_ones))
                    P.ps_free[bi] = e
            else:
                evs = []
                for bi in range(2):
                    e = P.do("vector", lambda v, bi=bi, xnv=xnv: v.tensor_copy(
                        out=xnv[:, bi * 512:(bi + 1) * 512], in_=P.PS[bi][:, 0:512]),
                        deps=(mm[bi], P.ps_free[bi], xt_tok))
                    evs.append(e)
                    P.ps_free[bi] = e
                if diag_store is not None:
                    sq = P.do("scalar", lambda s_, xnv=xnv: s_.activation(
                        out=SQS[:, :], in_=xnv, func=AF.Square),
                        deps=(evs[-1], dq_war))
                    rd = P.do("vector", lambda v, tl=tl, diag_store=diag_store: v.tensor_reduce(
                        out=diag_store[:, tl * 16:(tl + 1) * 16],
                        in_=SQS[:, :].rearrange("p (h f) -> p h f", f=64),
                        axis=AX.X, op=ALU.add), deps=(sq,))
                    dq = P.do("vector", lambda v, tl=tl, diag_store=diag_store: v.tensor_scalar(
                        out=diag_store[:, tl * 16:(tl + 1) * 16],
                        in0=diag_store[:, tl * 16:(tl + 1) * 16],
                        scalar1=0.0625, scalar2=LN16, op0=ALU.mult, op1=ALU.add),
                        deps=(rd,))
                    dq_war = dq
                pt_b = tl % 2
                for c in range(8):
                    trq = P.do("tensor", lambda t_, c=c, pt_b=pt_b, xnv=xnv: t_.transpose(
                        P.PT[pt_b][:, c * 128:(c + 1) * 128],
                        xnv[:, c * 128:(c + 1) * 128], P.IDN[:]),
                        deps=(evs[-1], P.pt_free[pt_b], store_war) if c == 0 else (),
                        inc="pe" if c == 7 else None)
                dstv = store[:, :].rearrange(
                    "p (c n) -> p c n", n=2048)[:, :, tl * 128:(tl + 1) * 128]
                evt = P.do("vector", lambda v, dstv=dstv, pt_b=pt_b: v.tensor_copy(
                    out=dstv, in_=P.PT[pt_b][:, :].rearrange("p (c n) -> p c n", n=128)),
                    deps=(trq,))
                P.pt_free[pt_b] = evt
                store_war = evt
                pipe.set_xn_reader(tl, evt)
        P.barrier()
        return

    mz = P.do("vector", lambda v: v.memset(MAXT[:, :], -1e30))
    P._l1_vs_ones = None
    for s in range(2):
        sweep(s, 1, KTS, DIAGK)
        mprev = mz
        for h in range(16):
            hc, base = h // 2, 64 * (h % 2)
            for tl in range(TPS):
                bank = 2 + (tl % 2)
                dmm = P.do("tensor", lambda t_, hc=hc, base=base, tl=tl, bank=bank:
                           t_.matmul(P.PS[bank][:, 0:256],
                                     KTS[base:base + 64, hc * 2048 + tl * 128: hc * 2048 + (tl + 1) * 128],
                                     PROJT[base:base + 64, :], start=True, stop=True),
                           deps=(P.ps_free[bank], pjt_tok), inc="pe")
                mt = P.do("vector", lambda v, bank=bank: v.tensor_tensor(
                    out=MAXT[:, :], in0=P.PS[bank][:, 0:256], in1=MAXT[:, :],
                    op=ALU.max), deps=(dmm, mprev))
                mprev = mt
                P.ps_free[bank] = mt
        P._l1_mprev = mprev
        P.barrier()
    m1 = P.do("vector", lambda v: v.tensor_reduce(out=MAX1[:, 0:1], in_=MAXT[:, :],
                                                  axis=AX.X, op=ALU.max),
              deps=(P._l1_mprev,))
    trm = P.do("tensor", lambda t: t.transpose(P.PS[5][0:1, 0:128], MAX1[:, 0:1],
                                               IDNF[:, :]),
               deps=(m1, P.ps_free[5], idnf_tok), inc="pe")
    m2 = P.do("vector", lambda v: v.tensor_copy(out=MROW[0:1, :], in_=P.PS[5][0:1, 0:128]),
              deps=(trm,))
    P.ps_free[5] = m2
    m3 = P.do("vector", lambda v: v.memset(STABR[0:1, :], -1e30), deps=())
    m4 = P.do("vector", lambda v: v.tensor_reduce(out=STABR[0:1, 0:1], in_=MROW[0:1, :],
                                                  axis=AX.X, op=ALU.max), deps=(m2, m3))
    st_tok = P.dma("gpsimd", lambda g: g.dma_start(out=stab_loc, in_=STABR[0:1, :]),
                   deps=(m4,))
    if L1_LOCAL_STAB:
        cc_tok = st_tok
        src_stab = stab_loc
    else:
        cc_tok = P.do("gpsimd", lambda g: g.collective_compute(
            "AllReduce", ALU.max, replica_groups=[list(range(8))],
            ins=[stab_loc], outs=[stab_out]), deps=(st_tok,), inc="pool")
        src_stab = stab_out
    ld_tok = P.dma("gpsimd", lambda g: g.dma_start(out=STABR[0:1, :], in_=src_stab),
                   deps=(cc_tok,))
    bmm = P.do("tensor", lambda t: t.matmul(P.PS[0][:, 0:1], P.ONES1[0:1, :],
                                            STABR[0:1, 0:1], start=True, stop=True),
               deps=(ld_tok, P.ones_tok, P.ps_free[0]), inc="pe")
    nst = P.do("vector", lambda v: v.tensor_scalar(
        out=NEGST[:, 0:1], in0=P.PS[0][:, 0:1], scalar1=-1.0, scalar2=None,
        op0=ALU.mult), deps=(bmm,))
    P.ps_free[0] = nst

    for s in range(2):
        P._l1_vs_ones = P.do("vector", lambda v: v.memset(VSAUG[:, :], 1.0),
                             deps=(getattr(P, "_l1_vs_war", None),))
        sweep(s, 0, QTS, DIAGQ)
        sweep(s, 1, KTS, DIAGK)
        sweep(s, 2, VSAUG, None)
        ITD = [(h, tl) for h in range(16) for tl in range(TPS)]
        NKI = len(ITD)
        ctx_free = getattr(P, "_l1_ctx_free", {})
        nds_rd = getattr(P, "_l1_nds_rd", {})
        sqs_rd = getattr(P, "_l1_sqs_rd", {})
        qpb_rd = getattr(P, "_l1_qpb_rd", {})
        ctxt_war = getattr(P, "_l1_ctxt_war", None)
        dd_t, kp_t = {}, {}
        last_ex = last_cmm = cse = None

        def k_stage1(i):
            h, tl = ITD[i]
            hc, base = h // 2, 64 * (h % 2)
            bank = tl % 4
            dd_t[i] = P.do("tensor", lambda t_, hc=hc, base=base, tl=tl, bank=bank:
                           t_.matmul(P.PS[bank][:, 0:256],
                                     KTS[base:base + 64, hc * 2048 + tl * 128: hc * 2048 + (tl + 1) * 128],
                                     PROJT[base:base + 64, :], start=True, stop=True),
                           deps=(P.ps_free[bank], pjt_tok), inc="pe")

        def k_stage2(i):
            nonlocal last_ex
            h, tl = ITD[i]
            sl = tl % 4
            bank = sl
            nd = P.do("vector", lambda v, tl=tl, h=h, sl=sl: v.scalar_tensor_tensor(
                out=NDS[:, sl:sl + 1], in0=DIAGK[:, tl * 16 + h:tl * 16 + h + 1],
                scalar=-1.0, in1=NEGST[:, 0:1], op0=ALU.mult, op1=ALU.add),
                deps=(nst, nds_rd.get(sl)))
            ex = P.do("scalar", lambda s_, bank=bank, sl=sl: s_.activation(
                out=SQS[:, sl * 256:sl * 256 + 256], in_=P.PS[bank][:, 0:256],
                func=AF.Exp, bias=NDS[:, sl:sl + 1], scale=IM(1.0)),
                deps=(dd_t[i], nd, sqs_rd.get(sl)))
            nds_rd[sl] = ex
            P.ps_free[bank] = ex
            last_ex = ex
            kp_t[i] = P.do("vector", lambda v, sl=sl: v.tensor_scalar(
                out=QPB[:, sl * 256:sl * 256 + 256],
                in0=SQS[:, sl * 256:sl * 256 + 256], scalar1=6.25e-6, scalar2=None,
                op0=ALU.add), deps=(ex, qpb_rd.get(sl)))
            sqs_rd[sl] = kp_t[i]

        def k_stage3(i):
            nonlocal last_cmm, cse, ctxt_war
            h, tl = ITD[i]
            sl = tl % 4
            cb = 4 + (h % 2)
            deps = [kp_t[i]]
            if tl == 0:
                deps += [ctx_free.get(cb), P._l1_vs_ones]
            cmm = P.do("tensor", lambda t_, tl=tl, h=h, sl=sl, cb=cb: t_.matmul(
                P.PS[cb][0:65, 0:256],
                VSAUG[:, (tl * 16 + h) * 65:(tl * 16 + h) * 65 + 65],
                QPB[:, sl * 256:sl * 256 + 256],
                start=(tl == 0), stop=(tl == TPS - 1)),
                deps=deps, inc="pe")
            qpb_rd[sl] = cmm
            last_cmm = cmm
            if tl == TPS - 1:
                ce = P.do("vector", lambda v, cb=cb: v.tensor_copy(
                    out=CTXT[0:65, :], in_=P.PS[cb][0:65, 0:256]),
                    deps=(cmm, ctxt_war))
                P.ps_free[cb] = ce
                for c in range(2):
                    trc = P.do("tensor", lambda t_, c=c: t_.transpose(
                        P.PT[0][0:128, c * 128:c * 128 + 65],
                        CTXT[0:65, c * 128:(c + 1) * 128], P.IDN[0:65, 0:65]),
                        deps=(ce, P.pt_free[0]) if c == 0 else (),
                        inc="pe" if c == 1 else None)
                ctxt_war = trc
                cse = P.do("vector", lambda v, h=h: v.tensor_copy(
                    out=CTXS[:, h * 130:(h + 1) * 130].rearrange("p (c f) -> p c f", f=65),
                    in_=P.PT[0][:, 0:256].rearrange("p (c f) -> p c f", f=128)[:, :, 0:65]),
                    deps=(trc,))
                P.pt_free[0] = cse
                ctx_free[cb] = cse

        for i in range(min(4, NKI)):
            k_stage1(i)
        k_stage2(0)
        k_stage2(1)
        for i in range(NKI):
            k_stage3(i)
            if i + 4 < NKI:
                k_stage1(i + 4)
            if i + 2 < NKI:
                k_stage2(i + 2)
        P._l1_nds_rd = nds_rd
        P._l1_sqs_rd = sqs_rd
        P._l1_qpb_rd = qpb_rd
        P._l1_ctx_free = ctx_free
        P._l1_ctxt_war = ctxt_war
        w_war = P.fence("tensor")
        wo_tok = P.dma("gpsimd", lambda g: g.dma_start(
            out=W1.sb[:, :].rearrange("p (c f) -> p c f", f=1024),
            in_=aps["wo"].rearrange("(c p) f -> p c f", p=128)), deps=(w_war,))
        num_prev = [None, None]
        out_done = None
        mx_rd, nds_rd2, qpb_rd2, qpt_rd = {}, {}, {}, {}
        for _sl in range(3):
            nds_rd2[_sl] = last_ex
            qpb_rd2[_sl] = last_cmm
            mx_rd[_sl] = cse
        dd_toks, qp_toks = {}, {}

        def q_stage1(tl, h):
            hc, base = h // 2, 64 * (h % 2)
            gh = tl * 16 + h
            sl = gh % 3
            bank = 2 + sl
            dd_toks[gh] = P.do("tensor", lambda t_, hc=hc, base=base, tl=tl, bank=bank:
                              t_.matmul(P.PS[bank][:, 0:256],
                                        QTS[base:base + 64, hc * 2048 + tl * 128: hc * 2048 + (tl + 1) * 128],
                                        PROJT[base:base + 64, :], start=True, stop=True),
                              deps=(P.ps_free[bank],), inc="pe")

        def q_stage2(tl, h):
            gh = tl * 16 + h
            sl = gh % 3
            bank = 2 + sl
            dmm = dd_toks[gh]
            mx = P.do("vector", lambda v, bank=bank, sl=sl: v.tensor_reduce(
                out=MX[:, sl:sl + 1], in_=P.PS[bank][:, 0:256], axis=AX.X, op=ALU.max),
                deps=(dmm, mx_rd.get(sl)))
            nd = P.do("vector", lambda v, tl=tl, h=h, sl=sl: v.scalar_tensor_tensor(
                out=NDS[:, sl:sl + 1], in0=MX[:, sl:sl + 1], scalar=-1.0,
                in1=DIAGQ[:, tl * 16 + h:tl * 16 + h + 1],
                op0=ALU.mult, op1=ALU.subtract),
                deps=(mx, nds_rd2.get(sl)))
            ex = P.do("scalar", lambda s_, bank=bank, sl=sl: s_.activation(
                out=SQS[:, sl * 256:sl * 256 + 256], in_=P.PS[bank][:, 0:256],
                func=AF.Exp, bias=NDS[:, sl:sl + 1], scale=IM(1.0)),
                deps=(nd, qpb_rd2.get(sl)))
            nds_rd2[sl] = ex
            mx_rd[sl] = ex
            qp_toks[gh] = P.do("scalar", lambda s_, sl=sl: s_.add(
                out=QPB[:, sl * 256:sl * 256 + 256],
                in_=SQS[:, sl * 256:sl * 256 + 256], add=IM(6.25e-6)), deps=(ex,))
            P.ps_free[bank] = qp_toks[h]

        def q_stage3(tl, h, nxt):
            nonlocal out_done
            gh = tl * 16 + h
            sl = gh % 3
            nbank = gh % 2
            qp = qp_toks[gh]
            pt_b = gh % 2
            for c in range(2):
                trq = P.do("tensor", lambda t_, c=c, pt_b=pt_b, sl=sl: t_.transpose(
                    P.PT[pt_b][:, c * 128:(c + 1) * 128],
                    QPB[:, sl * 256 + c * 128:sl * 256 + (c + 1) * 128], P.IDN[:]),
                    deps=(qp, P.pt_free[pt_b]) if c == 0 else (),
                    inc="pe" if c == 1 else None)
            qpb_rd2[sl] = trq
            qpt = P.do("vector", lambda v, pt_b=pt_b, sl=sl: v.tensor_copy(
                out=QPT[:, sl * 256:sl * 256 + 256], in_=P.PT[pt_b][:, 0:256]),
                deps=(trq, qpt_rd.get(sl)))
            P.pt_free[pt_b] = qpt
            if nxt is not None:
                q_stage1(*nxt)
            nmm = None
            for c in range(2):
                deps = [qpt]
                if c == 0:
                    deps += [num_prev[nbank], cse]
                nmm = P.do("tensor", lambda t_, c=c, h=h, nbank=nbank, sl=sl: t_.matmul(
                    P.PS[nbank][:, 0:65], QPT[:, sl * 256 + c * 128:sl * 256 + (c + 1) * 128],
                    CTXS[:, h * 130 + c * 65: h * 130 + (c + 1) * 65],
                    start=(c == 0), stop=(c == 1)),
                    deps=deps, inc="pe" if c == 1 else None)
            qpt_rd[sl] = nmm
            ne = P.do("vector", lambda v, h=h, nbank=nbank: v.tensor_copy(
                out=NUMS[:, h * 65:(h + 1) * 65], in_=P.PS[nbank][0:128, 0:65]),
                deps=(nmm, out_done))
            num_prev[nbank] = ne
            P.ps_free[nbank] = ne

        DEPTH = 3
        for tl in range(TPS):
            for h in range(DEPTH if tl == 0 else 0):
                q_stage1(tl, h)
                q_stage2(tl, h)
            for h in range(16):
                nh = h + DEPTH
                nxt = None
                if nh < 16:
                    nxt = (tl, nh)
                elif tl + 1 < TPS:
                    nxt = (tl + 1, nh - 16)
                q_stage3(tl, h, nxt)
                if nxt is not None:
                    q_stage2(*nxt)
            d1 = P.do("vector", lambda v: v.tensor_copy(
                out=DEN[:, :],
                in_=NUMS[:, :].rearrange("p (h f) -> p h f", f=65)[:, :, 64:65].squeeze(2)),
                deps=(num_prev[0], num_prev[1]))
            d2 = P.do("vector", lambda v: v.reciprocal(out=RECS[:, :], in_=DEN[:, :]),
                      deps=(d1,))
            o1 = P.do("vector", lambda v: v.tensor_tensor(
                out=OUT1[:, :].rearrange("p (h f) -> p h f", f=64),
                in0=NUMS[:, :].rearrange("p (h f) -> p h f", f=65)[:, :, 0:64],
                in1=RECS[:, :].unsqueeze(2).broadcast_to([128, 16, 64]),
                op=ALU.mult), deps=(d2, out_done))
            t_glob = s * TPS + tl
            rt = r_tile(P, t_glob)
            xt_tok = emit_transpose_1024(P, OUT1, P.XT, (o1,), pt_bank=tl % 2,
                                         war_dep=getattr(P, "_l1_xt_war", None))
            for bi in range(2):
                P.do("tensor", lambda t_, bi=bi: t_.matmul(
                    P.PS[bi][:, 0:512], ONESB[0:1, :],
                    BOROW[0:1, bi * 512:(bi + 1) * 512], start=True, stop=False),
                    deps=(P.ps_free[bi], bo_tok, o1b))
            mmo = emit_proj_1024(P, P.XT, W1.sb, wo_tok, out_banks=[0, 1],
                                 fout=1024, extra_deps=(xt_tok,), accumulate=True)
            P._l1_xt_war = mmo[-1]
            for bi in range(2):
                lo, hi = bi * 512, (bi + 1) * 512
                out_done = P.do("vector", lambda v, bi=bi, lo=lo, hi=hi, rt=rt:
                                v.tensor_tensor(out=rt[:, lo:hi], in0=P.PS[bi][:, 0:512],
                                                in1=rt[:, lo:hi], op=ALU.add),
                                deps=(mmo[bi],))
                P.ps_free[bi] = out_done
            num_prev = [out_done, out_done]
        P._l1_vs_war = P.fence("tensor")
        P.barrier()
    P.barrier()



def _declare_io(nc, names_shapes, out_shape):
    aps = {}
    for name, shape in names_shapes.items():
        aps[name] = nc.dram_tensor(name, list(shape), F32, kind="ExternalInput").ap()
    y = nc.dram_tensor("y", list(out_shape), F32, kind="ExternalOutput").ap()
    return aps, y


def _idn():
    return np.eye(128, dtype=np.float32)


def _mtc():
    return np.triu(np.ones((128, 128), np.float32))


def _selc():
    sel = np.zeros((128, 16 * 64), np.float32)
    for tl in range(16):
        for j in range(4):
            sel[32 * j, tl * 64 + 4 * tl + j] = 1.0
    return sel


SPEC = {
    "L2_ffn": dict(
        shapes={"W": (1024, 1024), "b": (1, 1024), "lora_down": (4, 100),
                "lora_up": (50, 4), "down_aux": (100, 1024), "up_aux": (1024, 50)},
        emit=emit_L2_ffn,
        in_map=lambda inp: dict(
            W=inp["L2_ffn_W"], b=np.asarray(inp["L2_ffn_b"]).reshape(1, 1024),
            lora_down=inp["L2_ffn_lora_down"], lora_up=inp["L2_ffn_lora_up"],
            down_aux=inp["L2_ffn_down_aux"], up_aux=inp["L2_ffn_up_aux"]),
    ),
    "L3_ffn": dict(
        shapes={"weight": (1024, 1024)},
        emit=emit_L3_ffn,
        in_map=lambda inp: dict(weight=inp["L3_ffn_weight"]),
    ),
    "L0_ffn": dict(
        shapes={"router": (1024, 8), "experts_gate": (8, 1024, 512),
                "experts_up": (8, 1024, 512), "experts_down": (8, 512, 1024),
                "shared_gate": (1024, 1024), "shared_up": (1024, 1024),
                "shared_down": (1024, 1024)},
        emit=emit_L0_ffn,
        in_map=lambda inp: dict(
            router=inp["L0_ffn_router"], experts_gate=inp["L0_ffn_experts_gate"],
            experts_up=inp["L0_ffn_experts_up"], experts_down=inp["L0_ffn_experts_down"],
            shared_gate=inp["L0_ffn_shared_gate"], shared_up=inp["L0_ffn_shared_up"],
            shared_down=inp["L0_ffn_shared_down"]),
    ),
    "L3_attn": dict(
        shapes={"Wq": (1024, 256), "bq": (1, 256), "Wqc": (1024, 256), "bqc": (1, 256),
                "Wk": (1024, 256), "bk": (1, 256), "Wkc": (1024, 256), "bkc": (1, 256),
                "Wv": (1024, 1024), "bv": (1, 1024), "Wvc": (1024, 1024), "bvc": (1, 1024)},
        emit=emit_L3_attn,
        in_map=lambda inp: dict(
            Wq=inp["L3_attn_Wq"], bq=np.asarray(inp["L3_attn_bq"]).reshape(1, 256),
            Wqc=inp["L3_attn_Wqc"], bqc=np.asarray(inp["L3_attn_bqc"]).reshape(1, 256),
            Wk=inp["L3_attn_Wk"], bk=np.asarray(inp["L3_attn_bk"]).reshape(1, 256),
            Wkc=inp["L3_attn_Wkc"], bkc=np.asarray(inp["L3_attn_bkc"]).reshape(1, 256),
            Wv=inp["L3_attn_Wv"], bv=np.asarray(inp["L3_attn_bv"]).reshape(1, 1024),
            Wvc=inp["L3_attn_Wvc"], bvc=np.asarray(inp["L3_attn_bvc"]).reshape(1, 1024)),
    ),
    "L2_attn": dict(
        shapes={"W_qkv": (1024, 3072), "W_gate": (1024, 1), "b_gate": (1, 1),
                "W_out": (1024, 1024), "b_out": (1, 1024), "selc": (128, 1024)},
        emit=emit_L2_attn,
        in_map=lambda inp: dict(
            W_qkv=inp["L2_attn_W_qkv"], W_gate=inp["L2_attn_W_gate"],
            b_gate=np.asarray(inp["L2_attn_b_gate"]).reshape(1, 1),
            W_out=inp["L2_attn_W_out"],
            b_out=np.asarray(inp["L2_attn_b_out"]).reshape(1, 1024),
            selc=_selc()),
    ),
    "L0_attn": dict(
        shapes={"W_qkv": (1024, 3072), "b_qkv": (1, 3072), "W_out": (1024, 1024),
                "b_out": (1, 1024), "mtc": (128, 128)},
        emit=emit_L0_attn,
        in_map=lambda inp: dict(
            W_qkv=inp["L0_attn_W_qkv"],
            b_qkv=np.asarray(inp["L0_attn_b_qkv"]).reshape(1, 3072),
            W_out=inp["L0_attn_W_out"],
            b_out=np.asarray(inp["L0_attn_b_out"]).reshape(1, 1024),
            mtc=_mtc()),
    ),
    "L1_attn": dict(
        shapes={"wq": (1024, 1024), "wk": (1024, 1024), "wv": (1024, 1024),
                "wo": (1024, 1024), "bo": (1, 1024), "proj": (256, 64)},
        emit=emit_L1_attn,
        in_map=lambda inp: dict(
            wq=inp["L1_attn_wq"], wk=inp["L1_attn_wk"], wv=inp["L1_attn_wv"],
            wo=inp["L1_attn_wo"], bo=np.asarray(inp["L1_attn_bo"]).reshape(1, 1024),
            proj=inp["L1_attn_proj"]),
    ),
    "L1_ffn": dict(
        shapes={"Win": (1024, 64), "bin_": (1, 64), "W1": (1, 1024), "b1": (1, 1024),
                "W2": (1, 1024), "b2": (1, 64), "Wout": (64, 1024), "bout": (1, 1024)},
        emit=emit_L1_ffn,
        in_map=lambda inp: dict(
            Win=inp["L1_ffn_Win"], bin_=np.asarray(inp["L1_ffn_bin_"]).reshape(1, 64),
            W1=np.asarray(inp["L1_ffn_W1"]).reshape(1, 1024),
            b1=np.asarray(inp["L1_ffn_b1"]).reshape(1, 1024),
            W2=np.asarray(inp["L1_ffn_W2"]).reshape(1, 1024),
            b2=np.asarray(inp["L1_ffn_b2"]).reshape(1, 64),
            Wout=inp["L1_ffn_Wout"],
            bout=np.asarray(inp["L1_ffn_bout"]).reshape(1, 1024)),
    ),
}

CHAIN = ["L0_ffn", "L0_attn", "L0_ffn",
         "L1_ffn", "L1_attn", "L1_ffn",
         "L2_ffn", "L2_attn", "L2_ffn",
         "L3_ffn", "L3_attn", "L3_ffn"]

_launch_cache = {}


def build_launch(sub):
    if sub in _launch_cache:
        return _launch_cache[sub]
    spec = SPEC[sub]
    nc = bass.Bass("TRN2", target_bir_lowering=False)
    shapes = {"x": (2, 2048, 1024), "idn": (128, 128)}
    shapes.update(spec["shapes"])
    aps, y = _declare_io(nc, shapes, (2, 2048, 1024))
    ctx = ExitStack()
    with ctx:
        P = Prog(nc)
        alloc_common(P, ctx, evs=(sub not in ("L3_attn", "L2_attn", "L0_attn")))
        P.idn_tok = P.dma("gpsimd", lambda g: g.dma_start(out=P.IDN[:], in_=aps["idn"]))
        load_residual(P, aps["x"])
        spec["emit"](P, ctx, aps)
        done = P.fence("vector")
        store_residual(P, y, deps=(done,))
        for s_ in ST_POOL:
            if P.cnt[s_] > 0:
                P.wait("gpsimd", (s_, P.cnt[s_]))
        P.do("gpsimd", lambda g: g.nop())
        P.build(ctx)
    _launch_cache[sub] = nc
    return nc


def build_fused():
    nc = bass.Bass("TRN2", target_bir_lowering=False)
    shapes = {"x": (2, 2048, 1024), "idn": (128, 128)}
    for sub, spec in SPEC.items():
        for nm, shp in spec["shapes"].items():
            shapes[f"{sub}_{nm}"] = shp
    aps, y = _declare_io(nc, shapes, (2, 2048, 1024))
    ctx = ExitStack()
    with ctx:
        P = Prog(nc)
        alloc_common(P, ctx, evs=True)
        P.idn_tok = P.dma("gpsimd", lambda g: g.dma_start(out=P.IDN[:], in_=aps["idn"]))
        load_residual(P, aps["x"])
        for i, sub in enumerate(CHAIN):
            P.tag = f"s{i}_"
            sub_aps = {nm: aps[f"{sub}_{nm}"] for nm in SPEC[sub]["shapes"]}
            sub_aps["idn"] = aps["idn"]
            with ExitStack() as subctx:
                SPEC[sub]["emit"](P, subctx, sub_aps)
        P.tag = ""
        done = P.fence("vector")
        store_residual(P, y, deps=(done,))
        for s_ in ST_POOL:
            if P.cnt[s_] > 0:
                P.wait("gpsimd", (s_, P.cnt[s_]))
        P.do("gpsimd", lambda g: g.nop())
        P.build(ctx)
    return nc


def run_fused(inputs):
    x = np.ascontiguousarray(inputs["L0_attn_x"])
    nc = build_fused()
    base = dict(idn=_idn())
    for sub, spec in SPEC.items():
        for nm, v in spec["in_map"](inputs).items():
            base[f"{sub}_{nm}"] = np.ascontiguousarray(v)
    in_maps = [dict(x=np.ascontiguousarray(x[2 * c:2 * c + 2]), **base)
               for c in range(8)]
    res = run_bass_kernel_spmd(nc, in_maps, core_ids=list(range(8)))
    return np.concatenate([r["y"] for r in res.results], axis=0)


def run_chain(inputs, chain):
    x = np.ascontiguousarray(inputs["L0_attn_x"])
    idn = _idn()
    cur = [np.ascontiguousarray(x[2 * c:2 * c + 2]) for c in range(8)]
    for sub in chain:
        spec = SPEC[sub]
        nc = build_launch(sub)
        wmap = spec["in_map"](inputs)
        in_maps = [dict(x=cur[c], idn=idn, **wmap) for c in range(8)]
        res = run_bass_kernel_spmd(nc, in_maps, core_ids=list(range(8)))
        cur = [np.ascontiguousarray(res.results[c]["y"]) for c in range(8)]
    return np.concatenate(cur, axis=0)


FUSED = True


def kernel(**inputs):
    if FUSED:
        return run_fused(inputs)
    chain = [s for s in CHAIN if s in SPEC]
    return run_chain(inputs, chain)


# version 08
# speedup vs baseline: 1.0776x; 1.0014x over previous
import numpy as np
import ml_dtypes
import concourse.bass as bass
import concourse.mybir as mybir
from concourse.bass_utils import run_bass_kernel_spmd
from contextlib import ExitStack

F32 = mybir.dt.float32
BF16 = mybir.dt.bfloat16
AF = mybir.ActivationFunctionType
ALU = mybir.AluOpType
AX = mybir.AxisListType
BF16NP = ml_dtypes.bfloat16
IM = np.float32

ENGINES = ("sync", "scalar", "vector", "tensor", "gpsimd")
ESEM = {"sync": "syn", "scalar": "act", "vector": "dve", "tensor": "pe", "gpsimd": "pool"}
DMA_POOL = {"sync": [f"dmas{i}" for i in range(3)],
            "gpsimd": [f"dmag{i}" for i in range(6)],
            "scalar": [f"dmac{i}" for i in range(3)]}
ST_POOL = [f"st{i}" for i in range(2)]
SEM_NAMES = ("pe", "act", "dve", "pool", "syn") + tuple(
    s for pool in DMA_POOL.values() for s in pool) + tuple(ST_POOL)

LN_EPS = 1e-5
NTILE = 32
TPS = 16
D = 1024


class Prog:

    def __init__(self, nc):
        self.nc = nc
        self.ops = {e: [] for e in ENGINES}
        self.cnt = {s: 0 for s in SEM_NAMES}
        self.handles = {}
        self.waits = []
        self.waited = {e: {s: -1 for s in SEM_NAMES} for e in ENGINES}
        self.last_tok = {e: None for e in ENGINES}
        self.dirty = {e: False for e in ENGINES}
        self.dma_rr = {}
        self.tag = ""
        self._fence_scratch = None

    def wait(self, e, tok):
        if tok is None:
            return
        s, v = tok
        if self.waited[e][s] >= v:
            return
        self.waited[e][s] = v
        self.waits.append((e, s, v))
        self.ops[e].append(lambda eng, s=s, v=v: eng.wait_ge(self.handles[s], v))

    def do(self, e, fn, deps=(), inc="auto", inc_by=1):
        for d in deps:
            self.wait(e, d)
        if inc == "auto":
            inc = ESEM[e] if e in ("vector", "scalar", "gpsimd") else None
        if inc is None:
            self.ops[e].append(fn)
            self.dirty[e] = True
            return None
        self.cnt[inc] += inc_by
        tok = (inc, self.cnt[inc])
        self.ops[e].append(
            lambda eng, fn=fn, s=inc, b=inc_by: fn(eng).then_inc(self.handles[s], b))
        self.last_tok[e] = tok
        self.dirty[e] = False
        return tok

    def dma(self, queue, fn, deps=(), pool=None):
        pool = DMA_POOL[queue] if pool is None else pool
        idx = self.dma_rr.get(queue, 0)
        self.dma_rr[queue] = idx + 1
        sem = pool[idx % len(pool)]
        prev = (sem, self.cnt[sem]) if self.cnt[sem] > 0 else None
        return self.do(queue, fn, deps=tuple(deps) + ((prev,) if prev else ()),
                       inc=sem, inc_by=16)

    def store(self, queue, fn, deps=()):
        idx = self.dma_rr.get("st", 0)
        self.dma_rr["st"] = idx + 1
        sem = ST_POOL[idx % len(ST_POOL)]
        prev = (sem, self.cnt[sem]) if self.cnt[sem] > 0 else None
        return self.do(queue, fn, deps=tuple(deps) + ((prev,) if prev else ()),
                       inc=sem, inc_by=16)

    def fence(self, e):
        if self.last_tok[e] is not None and not self.dirty[e]:
            return self.last_tok[e]
        sem = ESEM[e]
        if e in ("vector", "gpsimd"):
            return self.do(e, lambda eng: eng.engine_nop(), inc=sem)
        if e == "scalar":
            fs = self._fence_scratch
            return self.do(e, lambda eng: eng.copy(out=fs[0:1, 0:1], in_=fs[0:1, 0:1]),
                           inc=sem)
        return self.do(e, lambda eng: eng.nop(), inc=sem)

    def barrier(self, engines=("scalar", "vector", "tensor", "gpsimd", "sync")):
        toks = {e: self.fence(e) for e in engines}
        for e in engines:
            for o, t in toks.items():
                if o != e:
                    self.wait(e, t)

    def finalize_check(self):
        for e, s, v in self.waits:
            assert v <= self.cnt[s], f"wait on {s}>={v} but final count {self.cnt[s]} (engine {e})"

    def build(self, ctx):
        self.finalize_check()
        for s in SEM_NAMES:
            self.handles[s] = ctx.enter_context(self.nc.semaphore(s))
        block = ctx.enter_context(self.nc.Block())
        for e in ENGINES:
            ops = self.ops[e]
            if not ops:
                continue
            def runner(eng, ops=ops):
                for op in ops:
                    op(eng)
            getattr(block, e)(runner)



def alloc_common(P, ctx, evs=True):
    nc = P.nc
    sb = lambda name, shape, dt: ctx.enter_context(nc.sbuf_tensor(name, shape, dt))
    P._fence_scratch = sb("fence_scr", [1, 1], F32)
    P.R = sb("resid", [128, NTILE * D], BF16)
    P.XN = sb("xn", [128, 2 * D], BF16)
    P.XT = sb("xt", [128, D], BF16)
    P.STAT = sb("stat", [128, 12], F32)
    P.MV = sb("mv", [128, 2], F32)
    P.NEGMEAN = sb("negmean", [128, 1], F32)
    P.RSTD = sb("rstd", [128, 1], F32)
    P.IDN = sb("idn_sb", [128, 128], BF16)

    P.ONES1 = sb("ones1", [1, 128], F32)
    P.ONESC = sb("onesc", [128, 1], F32)
    t_o1 = P.do("vector", lambda v: v.memset(P.ONES1[0:1, :], 1.0), inc="dve")
    P.ones_tok = P.do("vector", lambda v: v.memset(P.ONESC[:, 0:1], 1.0), inc="dve")
    P.PS = [ctx.enter_context(nc.psum_tensor(f"ps{i}", [128, 512], F32)) for i in range(6)]
    P.PT = [ctx.enter_context(nc.psum_tensor(f"pt{i}", [128, 1024], BF16)) for i in range(2)]
    P.ps_free = [None] * 6
    P.pt_free = [None] * 2
    P.idn_tok = None


def load_identity(P, idn_dram):
    P.idn_tok = P.dma("sync", lambda s: s.dma_start(out=P.IDN[:], in_=idn_dram))


def load_residual(P, x_dram):
    toks = []
    for s in range(2):
        src = x_dram[s].rearrange("(t p) d -> p t d", p=128)
        dst = P.R[:, s * TPS * D:(s + 1) * TPS * D].rearrange("p (t d) -> p t d", d=D)
        toks.append(P.dma("gpsimd", lambda g, src=src, dst=dst: g.dma_start(out=dst, in_=src)))
    P.r_toks = toks
    return toks


def store_residual(P, y_dram, deps):
    toks = []
    for s in range(2):
        dst = y_dram[s].rearrange("(t p) d -> p t d", p=128)
        src = P.R[:, s * TPS * D:(s + 1) * TPS * D].rearrange("p (t d) -> p t d", d=D)
        toks.append(P.store("gpsimd",
                            lambda g, src=src, dst=dst: g.dma_start(out=dst, in_=src),
                            deps=deps))
    return toks


def r_tile(P, t):
    return P.R[:, t * D:(t + 1) * D]


def emit_ln_tile(P, t, xn_war=None, xn_buf=None, extra_dep=None):
    xn = P.XN[:, 0:D] if xn_buf is None else xn_buf
    rt = r_tile(P, t)
    prev_xn, prev_act = getattr(P, "ln_prev", (None, None))
    t1 = P.do("vector", lambda v: v.bn_stats(out=P.STAT[:, 0:6], in_=rt[:, 0:512]),
              deps=tuple(P.r_toks) + (prev_xn, extra_dep))
    t2 = P.do("vector", lambda v: v.bn_stats(out=P.STAT[:, 6:12], in_=rt[:, 512:1024]))
    t3 = P.do("vector", lambda v: v.bn_aggr(out=P.MV[:], in_=P.STAT[:]),
              deps=(t2, prev_act))
    t4 = P.do("vector", lambda v: v.tensor_scalar(out=P.NEGMEAN[:], in0=P.MV[:, 0:1],
                                                  scalar1=-1.0, scalar2=None, op0=ALU.mult),
              deps=(t3,))
    t5 = P.do("scalar", lambda s: s.activation(out=P.RSTD[:], in_=P.MV[:, 1:2],
                                               func=AF.Ln, bias=IM(LN_EPS), scale=IM(1.0)),
              deps=(t3,))
    t6 = P.do("scalar", lambda s: s.activation(out=P.RSTD[:], in_=P.RSTD[:],
                                               func=AF.Exp, bias=IM(0.0), scale=IM(-0.5)),
              deps=(t5,))
    tok = P.do("vector", lambda v: v.tensor_scalar(out=xn[:, :], in0=rt,
                                                   scalar1=P.NEGMEAN[:], scalar2=P.RSTD[:],
                                                   op0=ALU.add, op1=ALU.mult),
               deps=(t4, t6, xn_war))
    P.ln_prev = (tok, t6)
    return tok


def emit_transpose_1024(P, src, dst, deps, pt_bank=0, war_dep=None, nchunk=8):
    pt = P.PT[pt_bank]
    mm_toks = []
    for c in range(nchunk):
        dd = (list(deps) + [P.pt_free[pt_bank], P.idn_tok]) if c == 0 else ()
        mm_toks.append(P.do("tensor",
                            lambda t, c=c: t.transpose(pt[:, c * 128:(c + 1) * 128],
                                                       src[:, c * 128:(c + 1) * 128],
                                                       P.IDN[:]),
                            deps=dd, inc="pe" if c == nchunk - 1 else None))
    w = 128 * nchunk
    ev = P.do("vector", lambda v: v.tensor_copy(out=dst[:, 0:w], in_=pt[:, 0:w]),
              deps=(mm_toks[-1], war_dep), inc="dve")
    P.pt_free[pt_bank] = ev
    return ev


class XnPipe:

    def __init__(self, P, make_xn, T):
        self.P, self.make_xn, self.T = P, make_xn, T
        self.xn_tok, self.xt_tok = {}, {}
        self.xn_reader, self.xt_reader = {}, {}
        self._emit(0)

    def xn_ap(self, t):
        sl = t % 2
        return self.P.XN[:, sl * D:(sl + 1) * D]

    def xt_ap(self, t):
        return self.P.XT[:, 0:D]

    def _emit(self, t):
        tok = self.make_xn(t, self.xn_ap(t), self.xn_reader.get(t - 2))
        tr = emit_transpose_1024(self.P, self.xn_ap(t), self.xt_ap(t), (tok,),
                                 pt_bank=t % 2, war_dep=self.xt_reader.get(t - 1))
        self.xn_tok[t], self.xt_tok[t] = tok, tr
        self.xn_reader[t] = tr

    def done(self, t, reader_tok):
        self.xt_reader[t] = reader_tok
        if t + 1 < self.T:
            self._emit(t + 1)

    def set_xn_reader(self, t, tok):
        self.xn_reader[t] = tok

    def set_xt_reader(self, t, tok):
        self.xt_reader[t] = tok


def emit_proj_1024(P, xT, w_sbuf, w_tok, out_banks, fout, extra_deps=(), nk=8,
                   accumulate=False):
    toks = []
    for bi, bank in enumerate(out_banks):
        ncols = min(512, fout - 512 * bi)
        ps = P.PS[bank]
        for k in range(nk):
            deps = []
            if k == 0:
                deps = ([] if accumulate else [P.ps_free[bank]]) + [w_tok, *extra_deps]
            tok = P.do("tensor",
                       lambda t, k=k, bi=bi, ncols=ncols, ps=ps:
                       t.matmul(ps[:, 0:ncols],
                                xT[:, k * 128:(k + 1) * 128],
                                w_sbuf[:, k * fout + 512 * bi: k * fout + 512 * bi + ncols],
                                start=(k == 0 and not accumulate), stop=(k == nk - 1)),
                       deps=deps, inc="pe" if k == nk - 1 else None)
        toks.append(tok)
    return toks


class WBuf:

    def __init__(self, P, ctx, name, k, f):
        assert k % 128 == 0
        self.k, self.f = k, f
        self.nchunk = k // 128
        self.sb = ctx.enter_context(
            P.nc.sbuf_tensor(P.tag + name, [128, self.nchunk * f], BF16))
        self.tok = None

    def load(self, P, dram_ap):
        src = dram_ap.rearrange("(c p) f -> p c f", p=128)
        dst = self.sb[:, :].rearrange("p (c f) -> p c f", f=self.f)
        self.tok = P.dma("gpsimd", lambda g: g.dma_start(out=dst, in_=src))
        return self.tok

    def chunk(self, c, lo=0, hi=None):
        hi = self.f if hi is None else hi
        return self.sb[:, c * self.f + lo: c * self.f + hi]


def load_row_f32(P, ctx, name, dram_ap, n):
    sb = ctx.enter_context(P.nc.sbuf_tensor(P.tag + name, [1, n], F32))
    tok = P.dma("sync", lambda s: s.dma_start(out=sb[0:1, :], in_=dram_ap))
    return sb, tok


def emit_bcast_row(P, ctx, name, row_sb, row_tok, n, scale=None, banks=(0, 1)):
    out = ctx.enter_context(P.nc.sbuf_tensor(P.tag + name, [128, n], F32))
    toks = []
    for ci in range((n + 511) // 512):
        lo, hi = ci * 512, min(n, (ci + 1) * 512)
        bank = banks[ci % len(banks)]
        mm = P.do("tensor", lambda t, lo=lo, hi=hi, bank=bank: t.matmul(
            P.PS[bank][:, 0:hi - lo], P.ONES1[0:1, :], row_sb[0:1, lo:hi],
            start=True, stop=True),
            deps=(row_tok, P.ones_tok, P.ps_free[bank]), inc="pe")
        if scale is None:
            ev = P.do("vector", lambda v, lo=lo, hi=hi, bank=bank: v.tensor_copy(
                out=out[:, lo:hi], in_=P.PS[bank][:, 0:hi - lo]), deps=(mm,), inc="dve")
        else:
            ev = P.do("vector", lambda v, lo=lo, hi=hi, bank=bank: v.tensor_scalar(
                out=out[:, lo:hi], in0=P.PS[bank][:, 0:hi - lo],
                scalar1=float(scale), scalar2=None, op0=ALU.mult), deps=(mm,), inc="dve")
        P.ps_free[bank] = ev
        toks.append(ev)
    return out, toks[-1]



def emit_l2ffn_prep(P, ctx, aps):
    nc = P.nc
    sb = lambda name, shape, dt: ctx.enter_context(nc.sbuf_tensor(P.tag + name, shape, dt))
    wb = WBuf(P, ctx, "l2f_w", 1024, 1024)
    wtok = wb.load(P, aps["W"])
    ld_t = sb("l2f_ldT", [128, 4], BF16)
    lu_t = sb("l2f_luT", [128, 50], BF16)
    da = sb("l2f_da", [128, 1024], BF16)
    ua = WBuf(P, ctx, "l2f_ua", 1024, 50)
    t_ld = P.dma("gpsimd", lambda g: g.dma_start(
        out=ld_t[0:100, :], in_=aps["lora_down"].rearrange("r d -> d r"),
        allow_slow_non_contiguous=True))
    t_lu = P.dma("gpsimd", lambda g: g.dma_start(
        out=lu_t[0:4, :], in_=aps["lora_up"].rearrange("u r -> r u"),
        allow_slow_non_contiguous=True))
    t_da = P.dma("gpsimd", lambda g: g.dma_start(out=da[0:100, :], in_=aps["down_aux"]))
    t_ua = ua.load(P, aps["up_aux"])

    t1 = sb("l2f_t1", [128, 1024], BF16)
    t2 = sb("l2f_t2", [128, 1024], BF16)
    uaT = sb("l2f_uaT", [128, 8 * 128], BF16)
    wsum = sb("l2f_wsum", [128, 8 * 1024], BF16)
    wsumT = sb("l2f_wsumT", [128, 8 * 1024], BF16)

    ps, pt = P.PS, P.PT
    mm1 = []
    for half in range(2):
        tok = P.do("tensor", lambda t, h=half: t.matmul(
            ps[h][0:4, 0:512], ld_t[0:100, :], da[0:100, h * 512:(h + 1) * 512],
            start=True, stop=True),
            deps=(t_ld, t_da, P.ps_free[half]), inc="pe")
        mm1.append(tok)
    ev1 = P.do("vector", lambda v: v.tensor_copy(out=t1[0:4, 0:512], in_=ps[0][0:4, 0:512]),
               deps=(mm1[0],), inc="dve")
    ev1b = P.do("vector", lambda v: v.tensor_copy(out=t1[0:4, 512:1024], in_=ps[1][0:4, 0:512]),
                deps=(mm1[1],), inc="dve")
    mm2 = []
    for half in range(2):
        tok = P.do("tensor", lambda t, h=half: t.matmul(
            ps[2 + h][0:50, 0:512], lu_t[0:4, :], t1[0:4, h * 512:(h + 1) * 512],
            start=True, stop=True),
            deps=(t_lu, ev1, ev1b, P.ps_free[2 + half]), inc="pe")
        mm2.append(tok)
    ev2 = P.do("vector", lambda v: v.tensor_copy(out=t2[0:50, 0:512], in_=ps[2][0:50, 0:512]),
               deps=(mm2[0],), inc="dve")
    ev2b = P.do("vector", lambda v: v.tensor_copy(out=t2[0:50, 512:1024], in_=ps[3][0:50, 0:512]),
                deps=(mm2[1],), inc="dve")
    tr_toks = []
    for c in range(8):
        tok = P.do("tensor", lambda t, c=c: t.transpose(
            pt[0][0:50, c * 128:(c + 1) * 128], ua.chunk(c), P.IDN[:]),
            deps=(t_ua, P.idn_tok, P.pt_free[0]) if c == 0 else (), inc="pe" if c == 7 else None)
        tr_toks.append(tok)
    evT = P.do("vector", lambda v: v.tensor_copy(out=uaT[0:50, :], in_=pt[0][0:50, :]),
               deps=(tr_toks[-1],), inc="dve")
    P.pt_free[0] = evT
    for c in range(8):
        for half in range(2):
            bank = 4 + half
            mm = P.do("tensor", lambda t, c=c, h=half: t.matmul(
                P.PS[4 + h][:, 0:512], uaT[0:50, c * 128:(c + 1) * 128],
                t2[0:50, h * 512:(h + 1) * 512], start=True, stop=True),
                deps=(evT, ev2, ev2b, P.ps_free[bank]), inc="pe")
            ev = P.do("vector", lambda v, c=c, h=half: v.scalar_tensor_tensor(
                out=wsum[:, c * 1024 + h * 512: c * 1024 + (h + 1) * 512],
                in0=P.PS[4 + h][:, 0:512], scalar=1.0,
                in1=wb.chunk(c, h * 512, (h + 1) * 512),
                op0=ALU.mult, op1=ALU.add),
                deps=(mm, wtok), inc="dve")
            P.ps_free[bank] = ev
    wsum_tok = P.last_tok["vector"]
    for ci in range(8):
        pt_b = ci % 2
        for co in range(8):
            tok = P.do("tensor", lambda t, ci=ci, co=co, pt_b=pt_b: t.transpose(
                P.PT[pt_b][:, co * 128:(co + 1) * 128],
                wsum[:, co * 1024 + ci * 128: co * 1024 + (ci + 1) * 128],
                P.IDN[:]),
                deps=(wsum_tok, P.pt_free[pt_b]) if co == 0 else (),
                inc="pe" if co == 7 else None)
        ev = P.do("vector", lambda v, ci=ci, pt_b=pt_b: v.tensor_copy(
            out=wsumT[:, ci * 1024:(ci + 1) * 1024], in_=P.PT[pt_b][:, :]),
            deps=(tok,), inc="dve")
        P.pt_free[pt_b] = ev
    wsumT_tok = P.last_tok["vector"]
    return wsum, wsumT, wsumT_tok


def emit_l2ffn(P, ctx, aps, prep):
    _, wsumT, wtok = prep
    EVS = ctx.enter_context(P.nc.sbuf_tensor(P.tag + "l2f_evs", [128, 1024], F32))
    brow, btok = load_row_f32(P, ctx, "l2f_b", aps["b"], 1024)
    bb, bb_tok = emit_bcast_row(P, ctx, "l2f_bb", brow, btok, 1024)
    pipe = XnPipe(P, lambda t, ap, war: emit_ln_tile(P, t, xn_war=war, xn_buf=ap),
                  NTILE)
    for t in range(NTILE):
        banks = [2 * (t % 2), 1 + 2 * (t % 2)]
        mm = emit_proj_1024(P, pipe.xt_ap(t), wsumT, wtok, out_banks=banks, fout=1024,
                            extra_deps=(pipe.xt_tok[t],))
        pipe.done(t, mm[-1])
        rt = r_tile(P, t)
        for bi, bank in enumerate(banks):
            lo, hi = bi * 512, (bi + 1) * 512
            e1 = P.do("vector", lambda v, bank=bank, lo=lo, hi=hi: v.tensor_tensor(
                out=EVS[:, lo:hi], in0=P.PS[bank][:, 0:512], in1=bb[:, lo:hi], op=ALU.add),
                deps=(mm[bi], bb_tok, P.ps_free[bank]), inc="dve")
            e2 = P.do("vector", lambda v, bank=bank, lo=lo, hi=hi, rt=rt: v.scalar_tensor_tensor(
                out=rt[:, lo:hi], in0=EVS[:, lo:hi], scalar=0.5, in1=rt[:, lo:hi],
                op0=ALU.mult, op1=ALU.add),
                deps=(e1,), inc="dve")
            P.ps_free[bank] = e2
    P.barrier()


def emit_L2_ffn(P, ctx, aps):
    prep = emit_l2ffn_prep(P, ctx, aps)
    emit_l2ffn(P, ctx, aps, prep)


def emit_grid_transpose(P, ctx, src, dst, src_tok, name=None):
    ev = None
    for ci in range(8):
        pt_b = ci % 2
        for co in range(8):
            tok = P.do("tensor", lambda t, ci=ci, co=co, pt_b=pt_b: t.transpose(
                P.PT[pt_b][:, co * 128:(co + 1) * 128],
                src[:, co * 1024 + ci * 128: co * 1024 + (ci + 1) * 128],
                P.IDN[:]),
                deps=(src_tok, P.pt_free[pt_b], P.idn_tok) if co == 0 else (),
                inc="pe" if co == 7 else None)
        ev = P.do("vector", lambda v, ci=ci, pt_b=pt_b: v.tensor_copy(
            out=dst[:, ci * 1024:(ci + 1) * 1024], in_=P.PT[pt_b][:, :]),
            deps=(tok,), inc="dve")
        P.pt_free[pt_b] = ev
    return ev



def emit_L3_ffn(P, ctx, aps):
    nc = P.nc
    sb = lambda name, shape, dt: ctx.enter_context(nc.sbuf_tensor(P.tag + name, shape, dt))
    wf = sb("l3f_wf", [128, 8 * 1024], F32)
    WQ = sb("l3f_wq", [128, 8 * 1024], BF16)
    WQT = sb("l3f_wqT", [128, 8 * 1024], BF16)
    U = sb("l3f_u", [128, 1024], F32)
    U2 = sb("l3f_u2", [128, 1024], F32)
    SG = sb("l3f_sg", [128, 1024], BF16)
    MSK = sb("l3f_msk", [128, 1024], BF16)
    ABS1 = sb("l3f_abs1", [128, 9], F32)
    WS = sb("l3f_ws", [1, 1], F32)
    INVB = sb("l3f_invb", [128, 1], F32)
    WSB = sb("l3f_wsb", [128, 1], F32)
    ASC = sb("l3f_asc", [128, 1], F32)
    RECA = sb("l3f_reca", [128, 1], F32)
    SV = sb("l3f_sv", [128, 2], F32)
    Z = sb("l3f_z", [128, 1024], F32)
    AQ = sb("l3f_aq", [128, 1024], BF16)

    src = aps["weight"].rearrange("(c p) f -> p c f", p=128)
    wtok = P.dma("sync", lambda s: s.dma_start(
        out=wf[:, :].rearrange("p (c f) -> p c f", f=1024), in_=src))

    tprev = None
    for c in range(8):
        tprev = P.do("vector", lambda v, c=c: v.reduce_sum(
            out=ABS1[:, c:c + 1], in_=wf[:, c * 1024:(c + 1) * 1024], axis=AX.X,
            apply_absolute_value=True), deps=(wtok,))
    t_rs = P.do("vector", lambda v: v.reduce_sum(out=ABS1[:, 8:9], in_=ABS1[:, 0:8],
                                                 axis=AX.X), deps=(tprev,))
    mm_ws = P.do("tensor", lambda t: t.matmul(P.PS[0][0:1, 0:1], ABS1[:, 8:9],
                                              P.ONESC[:, 0:1], start=True, stop=True),
                 deps=(t_rs, P.ones_tok, P.ps_free[0]), inc="pe")
    t_ws = P.do("vector", lambda v: v.tensor_scalar(
        out=WS[0:1, 0:1], in0=P.PS[0][0:1, 0:1], scalar1=1.0 / (1024.0 * 1024.0),
        scalar2=1e-8, op0=ALU.mult, op1=ALU.add), deps=(mm_ws,))
    P.ps_free[0] = t_ws
    t_inv = P.do("vector", lambda v: v.reciprocal(out=WS[0:1, 0:1], in_=WS[0:1, 0:1]),
                 deps=(t_ws,))
    mm_b1 = P.do("tensor", lambda t: t.matmul(P.PS[0][:, 0:1], P.ONES1[0:1, :],
                                              WS[0:1, 0:1], start=True, stop=True),
                 deps=(t_inv, P.ps_free[0]), inc="pe")
    t_invb = P.do("vector", lambda v: v.tensor_copy(out=INVB[:, 0:1], in_=P.PS[0][:, 0:1]),
                  deps=(mm_b1,))
    P.ps_free[0] = t_invb
    t_wsb0 = P.do("vector", lambda v: v.reciprocal(out=WSB[:, 0:1], in_=INVB[:, 0:1]),
                  deps=(t_invb,))
    t_wsb = P.do("vector", lambda v: v.tensor_scalar(out=WSB[:, 0:1], in0=WSB[:, 0:1],
                                                     scalar1=0.5 / 127.0, scalar2=None,
                                                     op0=ALU.mult), deps=(t_wsb0,))
    wq_tok = None
    prev_sg = None
    prev_mk = None
    for c in range(8):
        t_u = P.do("vector", lambda v, c=c: v.tensor_scalar(
            out=U[:, :], in0=wf[:, c * 1024:(c + 1) * 1024], scalar1=INVB[:, 0:1],
            scalar2=None, op0=ALU.mult), deps=(t_invb, wq_tok, prev_sg))
        t_sg = P.do("scalar", lambda s: s.sign(out=SG[:, :], in_=U[:, :]), deps=(t_u,))
        t_sq = P.do("scalar", lambda s: s.square(out=U2[:, :], in_=U[:, :]),
                    deps=(t_u, prev_mk))
        prev_sg = t_sq
        t_mk = P.do("vector", lambda v: v.tensor_scalar(
            out=MSK[:, :], in0=U2[:, :], scalar1=0.25, scalar2=None,
            op0=ALU.is_ge), deps=(t_sq,))
        wq_tok = P.do("vector", lambda v, c=c: v.tensor_tensor(
            out=WQ[:, c * 1024:(c + 1) * 1024], in0=SG[:, :], in1=MSK[:, :],
            op=ALU.mult), deps=(t_sg, t_mk))
        prev_mk = wq_tok
    wqT_tok = emit_grid_transpose(P, ctx, WQ, WQT, wq_tok)

    sv_toks = {}

    def mk_l3f(t, ap, war):
        sl = t % 2
        xn_tok = emit_ln_tile(P, t, xn_war=war)
        t_am = P.do("vector", lambda v: v.tensor_reduce(
            out=ASC[:, 0:1], in_=P.XN[:, 0:1024], axis=AX.X, op=ALU.max,
            apply_absolute_value=True), deps=(xn_tok,))
        t_am2 = P.do("vector", lambda v: v.tensor_scalar(
            out=ASC[:, 0:1], in0=ASC[:, 0:1], scalar1=1e-8, scalar2=None,
            op0=ALU.add), deps=(t_am,))
        t_rec = P.do("vector", lambda v: v.reciprocal(out=RECA[:, 0:1], in_=ASC[:, 0:1]),
                     deps=(t_am2,))
        t_z = P.do("vector", lambda v: v.tensor_scalar(
            out=Z[:, :], in0=P.XN[:, 0:1024], scalar1=RECA[:, 0:1], scalar2=127.0,
            op0=ALU.mult, op1=ALU.mult), deps=(t_rec,))
        t_aq = P.do("vector", lambda v, ap=ap: v.tensor_scalar(
            out=ap, in0=Z[:, :], scalar1=8388608.0, scalar2=-8388608.0,
            op0=ALU.add, op1=ALU.add), deps=(t_z, war))
        sv_toks[t] = P.do("vector", lambda v, sl=sl: v.tensor_tensor(
            out=SV[:, sl:sl + 1], in0=ASC[:, 0:1], in1=WSB[:, 0:1], op=ALU.mult),
            deps=(t_wsb, t_am2, sv_toks.get((t - 2, "rd"))))
        return t_aq

    pipe = XnPipe(P, mk_l3f, NTILE)
    for t in range(NTILE):
        sl = t % 2
        banks = [2 * sl, 1 + 2 * sl]
        mm = emit_proj_1024(P, pipe.xt_ap(t), WQT, wqT_tok, out_banks=banks, fout=1024,
                            extra_deps=(pipe.xt_tok[t],))
        pipe.done(t, mm[-1])
        rt = r_tile(P, t)
        ev = None
        for bi, bank in enumerate(banks):
            lo, hi = bi * 512, (bi + 1) * 512
            ev = P.do("vector", lambda v, bank=bank, lo=lo, hi=hi, rt=rt, sl=sl:
                      v.scalar_tensor_tensor(
                          out=rt[:, lo:hi], in0=P.PS[bank][:, 0:512],
                          scalar=SV[:, sl:sl + 1],
                          in1=rt[:, lo:hi], op0=ALU.mult, op1=ALU.add),
                      deps=(mm[bi], sv_toks[t], P.ps_free[bank]))
            P.ps_free[bank] = ev
        sv_toks[(t, "rd")] = ev
    P.barrier()



def emit_L1_ffn(P, ctx, aps):
    nc = P.nc
    sb = lambda name, shape, dt: ctx.enter_context(nc.sbuf_tensor(P.tag + name, shape, dt))
    EVS = sb("evs", [128, 1024], F32)
    win = WBuf(P, ctx, "l1f_win", 1024, 64)
    wtok = win.load(P, aps["Win"])
    wout = sb("l1f_wout", [64, 1024], BF16)
    wout_tok = P.dma("gpsimd", lambda g: g.dma_start(out=wout[0:64, :], in_=aps["Wout"]))

    rows = {}
    for nm, n in [("bin_", 64), ("W1", 1024), ("b1", 1024), ("W2", 1024),
                  ("b2", 64), ("bout", 1024)]:
        rows[nm] = load_row_f32(P, ctx, f"l1f_{nm}r", aps[nm], n)
    BINB, t_binb = emit_bcast_row(P, ctx, "l1f_binb", *rows["bin_"], 64)
    W1B, t_w1b = emit_bcast_row(P, ctx, "l1f_w1b", *rows["W1"], 1024)
    B1B, t_b1b = emit_bcast_row(P, ctx, "l1f_b1b", *rows["b1"], 1024)
    W2B, t_w2b = emit_bcast_row(P, ctx, "l1f_w2b", *rows["W2"], 1024)
    B2B, t_b2b = emit_bcast_row(P, ctx, "l1f_b2b", *rows["b2"], 64)
    BOUTB, t_boutb = emit_bcast_row(P, ctx, "l1f_boutb", *rows["bout"], 1024)

    S = sb("l1f_s", [128, 64], F32)
    H = sb("l1f_h", [128, 1024], F32)
    H3 = sb("l1f_h3", [128, 1024], F32)
    G1 = sb("l1f_g1", [128, 1024], F32)
    G = sb("l1f_g", [128, 64], BF16)
    GT = sb("l1f_gt", [64, 128], BF16)

    xn_war = None
    xt_war = None
    gt_war = None
    for t in range(NTILE):
        xn_tok = emit_ln_tile(P, t, xn_war=xn_war)
        xt_tok = emit_transpose_1024(P, P.XN, P.XT, (xn_tok,), pt_bank=t % 2,
                                     war_dep=xt_war)
        xn_war = xt_tok
        bank_s = 4
        mm_s = emit_proj_1024(P, P.XT, win.sb, wtok, out_banks=[bank_s], fout=64,
                              extra_deps=(xt_tok,))
        xt_war = mm_s[-1]
        t_s = P.do("vector", lambda v: v.tensor_tensor(
            out=S[:, :], in0=P.PS[bank_s][:, 0:64], in1=BINB[:, :], op=ALU.add),
            deps=(mm_s[0], t_binb, P.ps_free[bank_s]))
        P.ps_free[bank_s] = t_s
        s_view = S[:, :].unsqueeze(2).broadcast_to([128, 64, 16])
        w1_view = W1B[:, :].rearrange("p (m j) -> p m j", j=16)
        h_view = H[:, :].rearrange("p (m j) -> p m j", j=16)
        t_h = P.do("vector", lambda v: v.tensor_tensor(
            out=h_view, in0=s_view, in1=w1_view, op=ALU.mult), deps=(t_s, t_w1b))
        t_h2 = P.do("vector", lambda v: v.tensor_tensor(
            out=H[:, :], in0=H[:, :], in1=B1B[:, :], op=ALU.add), deps=(t_h, t_b1b))
        t_h3 = P.do("scalar", lambda s: s.activation(out=H3[:, :], in_=H[:, :],
                                                     func=AF.Relu), deps=(t_h2,))
        t_g1 = P.do("vector", lambda v: v.tensor_tensor(
            out=G1[:, :], in0=H3[:, :], in1=W2B[:, :], op=ALU.mult), deps=(t_h3, t_w2b))
        g1_view = G1[:, :].rearrange("p (m j) -> p m j", j=16)
        t_gr = P.do("vector", lambda v: v.tensor_reduce(
            out=G1[:, 0:64], in_=g1_view, axis=AX.X, op=ALU.add), deps=(t_g1,))
        t_g = P.do("vector", lambda v: v.tensor_tensor(
            out=G[:, :], in0=G1[:, 0:64], in1=B2B[:, :], op=ALU.add),
            deps=(t_gr, t_b2b, gt_war))
        pt_b = t % 2
        tr = P.do("tensor", lambda t_, pt_b=pt_b: t_.transpose(
            P.PT[pt_b][0:64, 0:128], G[:, :], P.IDN[:]),
            deps=(t_g, P.pt_free[pt_b], P.idn_tok), inc="pe")
        t_gt = P.do("vector", lambda v, pt_b=pt_b: v.tensor_copy(
            out=GT[0:64, :], in_=P.PT[pt_b][0:64, 0:128]), deps=(tr, gt_war))
        P.pt_free[pt_b] = t_gt
        rt = r_tile(P, t)
        for bi in range(2):
            bank = bi
            lo, hi = bi * 512, (bi + 1) * 512
            mm_o = P.do("tensor", lambda t_, bank=bank, lo=lo, hi=hi: t_.matmul(
                P.PS[bank][:, 0:512], GT[0:64, :], wout[0:64, lo:hi],
                start=True, stop=True),
                deps=(t_gt, wout_tok, P.ps_free[bank]), inc="pe")
            gt_war = mm_o
            e1 = P.do("vector", lambda v, bank=bank, lo=lo, hi=hi: v.tensor_tensor(
                out=EVS[:, lo:hi], in0=P.PS[bank][:, 0:512], in1=BOUTB[:, lo:hi],
                op=ALU.add), deps=(mm_o, t_boutb, P.ps_free[bank]))
            e2 = P.do("vector", lambda v, lo=lo, hi=hi, rt=rt: v.scalar_tensor_tensor(
                out=rt[:, lo:hi], in0=EVS[:, lo:hi], scalar=0.5, in1=rt[:, lo:hi],
                op0=ALU.mult, op1=ALU.add), deps=(e1,))
            P.ps_free[bank] = e2
    P.barrier()



def emit_L0_ffn(P, ctx, aps):
    nc = P.nc
    sb = lambda name, shape, dt: ctx.enter_context(nc.sbuf_tensor(P.tag + name, shape, dt))
    EVS = sb("evs", [128, 1024], F32)
    COEF = sb("moe_coef", [128, 8 * NTILE], F32)
    MRS = sb("moe_mrs", [128, 2 * NTILE], F32)
    PRB = sb("moe_prb", [128, 8], F32)
    SUME = sb("moe_sume", [128, 1], F32)
    NMX = sb("moe_nmx", [128, 1], F32)
    RCP = sb("moe_rcp", [128, 1], F32)
    EQ = sb("moe_eq", [128, 8], F32)
    PU2 = sb("moe_pu2", [128, 8], F32)
    M2 = sb("moe_m2", [128, 1], F32)
    MK2 = sb("moe_mk2", [128, 8], F32)
    SIL = sb("moe_sil", [128, 1024], F32)
    INTER = sb("moe_inter", [128, 4 * 1024], BF16)
    OUT = [sb(f"moe_out{i}", [128, 1024], F32) for i in range(2)]
    PRT = [sb(f"moe_prt{i}", [128, 1024], F32) for i in range(2)]
    WGU4 = sb("moe_wgu4", [128, 4 * 8 * 1024], BF16)
    WD4 = sb("moe_wd4", [128, 4 * 4 * 1024], BF16)
    WSH = WGU4[:, 0:8 * 2048]
    WSHD = WGU4[:, 8 * 2048:8 * 2048 + 8 * 1024]
    wr = WBuf(P, ctx, "moe_wr", 1024, 8)

    part = nc.dram_tensor(P.tag + "moe_part", [NTILE, 128, 1024], F32, kind="Internal").ap()

    wr_tok = wr.load(P, aps["router"])
    xn_war, xt_war = None, None
    for t in range(NTILE):
        xn_tok = emit_ln_tile(P, t, xn_war=xn_war)
        c_m = P.do("vector", lambda v, t=t: v.tensor_copy(
            out=MRS[:, 2 * t:2 * t + 1], in_=P.NEGMEAN[:, 0:1]), deps=(xn_tok,))
        c_r = P.do("vector", lambda v, t=t: v.tensor_copy(
            out=MRS[:, 2 * t + 1:2 * t + 2], in_=P.RSTD[:, 0:1]))
        xt_tok = emit_transpose_1024(P, P.XN, P.XT, (xn_tok,), pt_bank=t % 2,
                                     war_dep=xt_war)
        xn_war = xt_tok
        bank = 4 + (t % 2)
        mm = emit_proj_1024(P, P.XT, wr.sb, wr_tok, out_banks=[bank], fout=8,
                            extra_deps=(xt_tok,))
        xt_war = mm[-1]
        t_mx = P.do("vector", lambda v, bank=bank: v.tensor_reduce(
            out=NMX[:, 0:1], in_=P.PS[bank][:, 0:8], axis=AX.X, op=ALU.max,
            negate=True), deps=(mm[0], P.ps_free[bank]))
        t_pb = P.do("scalar", lambda s, bank=bank: s.activation(
            out=PRB[:, :], in_=P.PS[bank][:, 0:8], func=AF.Exp, bias=NMX[:, 0:1],
            scale=IM(1.0), accum_out=SUME[:, 0:1]), deps=(t_mx,))
        P.ps_free[bank] = t_pb
        t_rc = P.do("vector", lambda v: v.reciprocal(out=RCP[:, 0:1], in_=SUME[:, 0:1]),
                    deps=(t_pb,))
        t_eq = P.do("vector", lambda v: v.tensor_scalar(
            out=EQ[:, :], in0=PRB[:, :], scalar1=1.0, scalar2=None, op0=ALU.is_ge),
            deps=(t_pb,))
        t_p2 = P.do("vector", lambda v: v.tensor_tensor(
            out=PU2[:, :], in0=PRB[:, :], in1=EQ[:, :], op=ALU.subtract), deps=(t_eq,))
        t_m2 = P.do("vector", lambda v: v.tensor_reduce(
            out=M2[:, 0:1], in_=PU2[:, :], axis=AX.X, op=ALU.max), deps=(t_p2,))
        t_mk = P.do("vector", lambda v: v.tensor_scalar(
            out=MK2[:, :], in0=PRB[:, :], scalar1=M2[:, 0:1], scalar2=None,
            op0=ALU.is_ge), deps=(t_m2,))
        t_cu = P.do("vector", lambda v: v.tensor_tensor(
            out=MK2[:, :], in0=PRB[:, :], in1=MK2[:, :], op=ALU.mult), deps=(t_mk,))
        P.do("vector", lambda v, t=t: v.tensor_scalar(
            out=COEF[:, 8 * t:8 * t + 8], in0=MK2[:, :], scalar1=RCP[:, 0:1],
            scalar2=None, op0=ALU.mult), deps=(t_cu, t_rc))
    P.barrier()

    def load_expert_block(e0):
        toks = []
        for el in range(4):
            e = e0 + el
            dstg = WGU4[:, el * 8192:(el + 1) * 8192].rearrange(
                "p (c f) -> p c f", f=1024)[:, :, 0:512]
            dstu = WGU4[:, el * 8192:(el + 1) * 8192].rearrange(
                "p (c f) -> p c f", f=1024)[:, :, 512:1024]
            srcg = aps["experts_gate"][e0 + el].rearrange("(c p) f -> p c f", p=128)
            srcu = aps["experts_up"][e0 + el].rearrange("(c p) f -> p c f", p=128)
            toks.append(P.dma("gpsimd", lambda g, dstg=dstg, srcg=srcg:
                              g.dma_start(out=dstg, in_=srcg), deps=(w_war,)))
            toks.append(P.dma("gpsimd", lambda g, dstu=dstu, srcu=srcu:
                              g.dma_start(out=dstu, in_=srcu)))
            dstd = WD4[:, el * 4096:(el + 1) * 4096].rearrange("p (c f) -> p c f", f=1024)
            srcd = aps["experts_down"][e0 + el].rearrange("(c p) f -> p c f", p=128)
            toks.append(P.dma("gpsimd", lambda g, dstd=dstd, srcd=srcd:
                              g.dma_start(out=dstd, in_=srcd)))
        return toks

    part_tok = [None] * NTILE
    out_war = [None, None]
    prt_rd = [None, None]
    prt_war = [None, None]
    for ipass in range(2):
        e0 = 4 * ipass
        w_war = P.fence("tensor")
        w_tok = load_expert_block(e0)
        acc_free = None

        def mk_moe(t, ap, war):
            return P.do("vector", lambda v, t=t, ap=ap: v.tensor_scalar(
                out=ap, in0=r_tile(P, t), scalar1=MRS[:, 2 * t:2 * t + 1],
                scalar2=MRS[:, 2 * t + 1:2 * t + 2], op0=ALU.add, op1=ALU.mult),
                deps=(war,) + tuple(P.r_toks))

        pipe = XnPipe(P, mk_moe, NTILE)
        for t in range(NTILE):
            rt = r_tile(P, t)
            xt_tok = pipe.xt_tok[t]
            xtv = pipe.xt_ap(t)
            if ipass == 1:
                prt_rd[t % 2] = P.dma("sync", lambda s_, t=t: s_.dma_start(
                    out=PRT[t % 2][:, :], in_=part[t]),
                    deps=(prt_war[t % 2], part_tok[t]))
            gu_banks = [(0, 1), (2, 3)]
            acc_banks = (4, 5)
            int_toks = {}
            tr_toks = {}
            sil_prev = None
            down_tok = None
            last_gu = None

            def emit_trdown(e, first, last):
                nonlocal down_tok
                slot = (e % 2) * 1024
                tr_tok = emit_transpose_1024(P, INTER[:, slot:slot + 512],
                                             INTER[:, slot + 512:slot + 1024],
                                             (int_toks[e],), pt_bank=t % 2,
                                             war_dep=down_tok, nchunk=4)
                tr_toks[e] = tr_tok
                for bi, bank in enumerate(acc_banks):
                    for k in range(4):
                        deps = []
                        if k == 0 and bi == 0 and first:
                            deps = [acc_free] + w_tok
                        if k == 0:
                            deps = deps + [tr_tok]
                        tok = P.do("tensor", lambda t_, k=k, bank=bank, bi=bi, e=e, slot=slot:
                                   t_.matmul(P.PS[bank][:, 0:512],
                                             INTER[:, slot + 512 + k * 128:slot + 512 + (k + 1) * 128],
                                             WD4[:, e * 4096 + k * 1024 + 512 * bi:
                                                  e * 4096 + k * 1024 + 512 * bi + 512],
                                             start=(k == 0 and first), stop=(k == 3 and last)),
                                   deps=deps,
                                   inc="pe" if (k == 3 and bi == 1) else None)
                down_tok = tok

            for e in range(4):
                gb = gu_banks[e % 2]
                mm_gu = []
                for bi, bank in enumerate(gb):
                    for k in range(8):
                        deps = []
                        if k == 0:
                            deps = [P.ps_free[bank], xt_tok] + w_tok
                        tok = P.do("tensor", lambda t_, k=k, bank=bank, bi=bi, e=e,
                                   xtv=xtv:
                                   t_.matmul(P.PS[bank][:, 0:512],
                                             xtv[:, k * 128:(k + 1) * 128],
                                             WGU4[:, e * 8192 + k * 1024 + 512 * bi:
                                                  e * 8192 + k * 1024 + 512 * bi + 512],
                                             start=(k == 0), stop=(k == 7)),
                                   deps=deps, inc="pe" if k == 7 else None)
                    mm_gu.append(tok)
                last_gu = mm_gu[-1]
                t_sil = P.do("scalar", lambda s_, gb=gb: s_.activation(
                    out=SIL[:, 0:512], in_=P.PS[gb[0]][:, 0:512], func=AF.Silu),
                    deps=(mm_gu[0], int_toks.get(e - 1)))
                slot = (e % 2) * 1024
                t_int = P.do("vector", lambda v, gb=gb, e=e, t=t, e0=e0, slot=slot:
                             v.scalar_tensor_tensor(
                                 out=INTER[:, slot:slot + 512], in0=P.PS[gb[1]][:, 0:512],
                                 scalar=COEF[:, 8 * t + e0 + e:8 * t + e0 + e + 1],
                                 in1=SIL[:, 0:512], op0=ALU.mult, op1=ALU.mult),
                             deps=(mm_gu[1], t_sil, tr_toks.get(e - 2)))
                int_toks[e] = t_int
                P.ps_free[gb[0]] = t_int
                P.ps_free[gb[1]] = t_int
                if e > 0:
                    emit_trdown(e - 1, first=(e - 1 == 0), last=False)
            emit_trdown(3, first=False, last=True)
            pipe.done(t, last_gu)
            ob = OUT[t % 2]
            ev_toks = []
            for bi, bank in enumerate(acc_banks):
                lo, hi = bi * 512, (bi + 1) * 512
                if ipass == 0:
                    ev = P.do("vector", lambda v, bank=bank, lo=lo, hi=hi, ob=ob:
                              v.tensor_scalar(out=ob[:, lo:hi], in0=P.PS[bank][:, 0:512],
                                              scalar1=0.5, scalar2=None, op0=ALU.mult),
                              deps=(down_tok, out_war[t % 2]))
                else:
                    ev = P.do("vector", lambda v, bank=bank, lo=lo, hi=hi, ob=ob, t=t:
                              v.scalar_tensor_tensor(out=ob[:, lo:hi],
                                                     in0=P.PS[bank][:, 0:512], scalar=0.5,
                                                     in1=PRT[t % 2][:, lo:hi],
                                                     op0=ALU.mult, op1=ALU.add),
                              deps=(down_tok, out_war[t % 2], prt_rd[t % 2]))
                ev_toks.append(ev)
            acc_free = ev_toks[-1]
            if ipass == 1:
                prt_war[t % 2] = ev_toks[-1]
            out_war[t % 2] = P.dma("sync", lambda s_, t=t, ob=ob: s_.dma_start(
                out=part[t], in_=ob[:, :]), deps=tuple(ev_toks) + (part_tok[t],))
            part_tok[t] = out_war[t % 2]
        P.barrier()

    w_war = P.fence("tensor")
    tg = P.dma("gpsimd", lambda g: g.dma_start(
        out=WSH.rearrange("p (c f) -> p c f", f=2048)[:, :, 0:1024],
        in_=aps["shared_gate"].rearrange("(c p) f -> p c f", p=128)), deps=(w_war,))
    tu = P.dma("gpsimd", lambda g: g.dma_start(
        out=WSH.rearrange("p (c f) -> p c f", f=2048)[:, :, 1024:2048],
        in_=aps["shared_up"].rearrange("(c p) f -> p c f", p=128)))
    td = P.dma("gpsimd", lambda g: g.dma_start(
        out=WSHD.rearrange("p (c f) -> p c f", f=1024),
        in_=aps["shared_down"].rearrange("(c p) f -> p c f", p=128)))
    acc_free = None

    def mk_moe4(t, ap, war):
        return P.do("vector", lambda v, t=t, ap=ap: v.tensor_scalar(
            out=ap, in0=r_tile(P, t), scalar1=MRS[:, 2 * t:2 * t + 1],
            scalar2=MRS[:, 2 * t + 1:2 * t + 2], op0=ALU.add, op1=ALU.mult),
            deps=(war,))

    P._l0f_p4_down = None
    pipe = XnPipe(P, mk_moe4, NTILE)

    def p4_trdown(t, int_t):
        nonlocal acc_free
        rt = r_tile(P, t)
        sl = (t % 2) * 2048
        tr_tok = emit_transpose_1024(P, INTER[:, sl:sl + 1024],
                                     INTER[:, sl + 1024:sl + 2048],
                                     (int_t[0], int_t[1]),
                                     pt_bank=t % 2, war_dep=P._l0f_p4_down)
        acc_banks = (4, 5)
        down_tok = None
        for bi, bank in enumerate(acc_banks):
            for k in range(8):
                deps = []
                if k == 0:
                    deps = [tr_tok, acc_free]
                tok = P.do("tensor", lambda t_, k=k, bank=bank, bi=bi, sl=sl:
                           t_.matmul(P.PS[bank][:, 0:512],
                                     INTER[:, sl + 1024 + k * 128:sl + 1024 + (k + 1) * 128],
                                     WSHD[:, k * 1024 + 512 * bi: k * 1024 + 512 * bi + 512],
                                     start=(k == 0), stop=(k == 7)),
                           deps=deps, inc="pe" if k == 7 else None)
            down_tok = tok
        P._l0f_p4_down = down_tok
        ev_toks = []
        for bi, bank in enumerate(acc_banks):
            lo, hi = bi * 512, (bi + 1) * 512
            e1 = P.do("vector", lambda v, bank=bank, lo=lo, hi=hi, t=t:
                      v.scalar_tensor_tensor(out=EVS[:, lo:hi], in0=P.PS[bank][:, 0:512],
                                             scalar=0.5, in1=PRT[t % 2][:, lo:hi],
                                             op0=ALU.mult, op1=ALU.add),
                      deps=(down_tok, prt_rd[t % 2]))
            e2 = P.do("vector", lambda v, lo=lo, hi=hi, rt=rt: v.tensor_tensor(
                out=rt[:, lo:hi], in0=EVS[:, lo:hi], in1=rt[:, lo:hi], op=ALU.add),
                deps=(e1,))
            ev_toks.append(e2)
        prt_war[t % 2] = ev_toks[-1]
        acc_free = ev_toks[-1]

    pend = None
    for t in range(NTILE):
        xt_tok = pipe.xt_tok[t]
        xtv = pipe.xt_ap(t)
        prt_rd[t % 2] = P.dma("sync", lambda s_, t=t: s_.dma_start(
            out=PRT[t % 2][:, :], in_=part[t]),
            deps=(prt_war[t % 2], part_tok[t]))
        mm_gu = []
        for bi in range(4):
            bank = bi
            for k in range(8):
                deps = []
                if k == 0:
                    deps = [P.ps_free[bank], tg, tu, td, xt_tok]
                tok = P.do("tensor", lambda t_, k=k, bank=bank, bi=bi, xtv=xtv:
                           t_.matmul(P.PS[bank][:, 0:512],
                                     xtv[:, k * 128:(k + 1) * 128],
                                     WSH[:, k * 2048 + 512 * bi: k * 2048 + 512 * bi + 512],
                                     start=(k == 0), stop=(k == 7)),
                           deps=deps, inc="pe" if k == 7 else None)
            mm_gu.append(tok)
        sil_t = []
        for bi in range(2):
            sil_t.append(P.do("scalar", lambda s_, bi=bi: s_.activation(
                out=SIL[:, bi * 512:(bi + 1) * 512], in_=P.PS[bi][:, 0:512],
                func=AF.Silu), deps=(mm_gu[bi], getattr(P, "_l0f_p4_int", None))))
        int_t = []
        for bi in range(2):
            sl = (t % 2) * 2048
            tok = P.do("vector", lambda v, bi=bi, sl=sl: v.scalar_tensor_tensor(
                out=INTER[:, sl + bi * 512:sl + (bi + 1) * 512], in0=P.PS[2 + bi][:, 0:512],
                scalar=1.0, in1=SIL[:, bi * 512:(bi + 1) * 512],
                op0=ALU.mult, op1=ALU.mult),
                deps=(mm_gu[2 + bi], sil_t[bi],
                      P._l0f_p4_down if t >= 2 else None))
            int_t.append(tok)
        P._l0f_p4_int = int_t[-1]
        for bi in range(4):
            P.ps_free[bi] = int_t[-1]
        if pend is not None:
            p4_trdown(*pend)
        pend = (t, int_t)
        pipe.done(t, mm_gu[3])
    p4_trdown(*pend)
    P.barrier()



def emit_L3_attn(P, ctx, aps):
    nc = P.nc
    sb = lambda name, shape, dt: ctx.enter_context(nc.sbuf_tensor(P.tag + name, shape, dt))
    MRS = sb("l3a_mrs", [128, 2 * NTILE], F32)
    WALL = WBuf(P, ctx, "l3a_wall", 1024, 3072)
    BROW = sb("l3a_brow", [1, 3072], BF16)
    ONES1B = sb("l3a_ones1b", [1, 128], BF16)
    o1b_tok = P.do("vector", lambda v: v.memset(ONES1B[0:1, :], 1.0))
    LIN = sb("l3a_lin", [128, 3072], BF16)
    QP = sb("l3a_qp", [128, 512], BF16)
    KP = sb("l3a_kp", [128, 512], BF16)
    QTS = sb("l3a_qts", [128, 4 * 2048], BF16)
    KTS = sb("l3a_kts", [128, 4 * 2048], BF16)
    VSAUG = sb("l3a_vsaug", [128, 16 * 16 * 65], BF16)
    EXPT = sb("l3a_expt", [128, 3 * 512], BF16)
    PRW = sb("l3a_prw", [1, 512], BF16)
    SPW = LIN[0:1, 0:1024]
    VPR = LIN[0:1, 1024:2048]
    KPAD17S = sb("l3a_kpad", [128, 4 * 128], BF16)
    VA17S = sb("l3a_va17", [128, 16 * 65], BF16)
    CTXT = sb("l3a_ctxt", [65, 512], BF16)
    CTX = sb("l3a_ctx", [128, 65], F32)
    REC = sb("l3a_rec", [128, 1], F32)
    STAB = 8.0

    xn_war = None
    for t in range(NTILE):
        xn_tok = emit_ln_tile(P, t, xn_war=xn_war)
        xn_war = xn_tok
        P.do("vector", lambda v, t=t: v.tensor_copy(
            out=MRS[:, 2 * t:2 * t + 1], in_=P.NEGMEAN[:, 0:1]), deps=(xn_tok,))
        P.do("vector", lambda v, t=t: v.tensor_copy(
            out=MRS[:, 2 * t + 1:2 * t + 2], in_=P.RSTD[:, 0:1]))
    P.barrier()

    offs = {"Wq": 0, "Wqc": 256, "Wk": 512, "Wkc": 768, "Wv": 1024, "Wvc": 2048}
    wtoks = []
    for nm, off in offs.items():
        f = 1024 if nm in ("Wv", "Wvc") else 256
        dst = WALL.sb[:, :].rearrange("p (c ff) -> p c ff", ff=3072)[:, :, off:off + f]
        src = aps[nm].rearrange("(c p) f -> p c f", p=128)
        wtoks.append(P.dma("gpsimd", lambda g, dst=dst, src=src: g.dma_start(
            out=dst, in_=src)))
    boffs = {"bq": 0, "bqc": 256, "bk": 512, "bkc": 768, "bv": 1024, "bvc": 2048}
    btoks = []
    for nm, off in boffs.items():
        f = 1024 if nm in ("bv", "bvc") else 256
        btoks.append(P.dma("gpsimd", lambda s_, nm=nm, off=off, f=f: s_.dma_start(
            out=BROW[0:1, off:off + f], in_=aps[nm])))
    t1 = P.do("scalar", lambda s_: s_.activation(out=SPW[0:1, 0:256],
                                                 in_=BROW[0:1, 768:1024],
                                                 func=AF.Sigmoid), deps=tuple(btoks))
    t2 = P.do("vector", lambda v: v.memset(PRW[0:1, :], 0.0))
    t3 = P.do("vector", lambda v: v.tensor_tensor(
        out=PRW[0:1, :].rearrange("o (h f) -> o h f", f=32)[:, :, 0:16],
        in0=BROW[0:1, 512:768].rearrange("o (h f) -> o h f", f=16),
        in1=SPW[0:1, 0:256].rearrange("o (h f) -> o h f", f=16),
        op=ALU.mult), deps=(t1, t2))
    km = P.do("vector", lambda v: v.memset(KPAD17S[:, :], 0.0))
    for c in range(4):
        trk = P.do("tensor", lambda t, c=c: t.transpose(
            P.PT[0][0:128, c * 128:c * 128 + 1],
            PRW[0:1, c * 128:(c + 1) * 128], P.IDN[0:1, 0:1]),
            deps=(t3, P.pt_free[0], P.idn_tok) if c == 0 else (),
            inc="pe" if c == 3 else None)
    kp_tok = None
    for c in range(4):
        kp_tok = P.do("vector", lambda v, c=c: v.tensor_copy(
            out=KPAD17S[:, c * 128:c * 128 + 1], in_=P.PT[0][0:128, c * 128:c * 128 + 1]),
            deps=(trk, km))
    P.pt_free[0] = kp_tok
    t4 = P.do("scalar", lambda s_: s_.activation(out=SPW[0:1, :],
                                                 in_=BROW[0:1, 2048:3072],
                                                 func=AF.Sigmoid), deps=(t1, t3))
    t5 = P.do("vector", lambda v: v.scalar_tensor_tensor(
        out=VPR, in0=BROW[0:1, 1024:2048], scalar=16.0, in1=SPW[0:1, :],
        op0=ALU.mult, op1=ALU.mult), deps=(t4,))
    vm = P.do("vector", lambda v: v.memset(VA17S[:, :], 0.0))
    v1 = P.do("vector", lambda v: v.tensor_copy(
        out=VA17S[0:1, :].rearrange("o (h f) -> o h f", f=65)[:, :, 0:64],
        in_=VPR.rearrange("o (h f) -> o h f", f=64)), deps=(t5, vm))
    v2 = P.do("vector", lambda v: v.memset(
        VA17S[0:1, :].rearrange("o (h f) -> o h f", f=65)[:, :, 64:65], 16.0),
        deps=(vm,))
    va17_tok = (v1, v2)
    qkm = P.do("vector", lambda v: v.memset(QP[:, :], 0.0))
    qkm2 = P.do("vector", lambda v: v.memset(KP[:, :], 0.0))
    vs_ones = P.do("vector", lambda v: v.memset(VSAUG[:, :], 1.0))

    wtok_all = tuple(wtoks)

    for s in range(2):
        qts_war, kts_war, vs_war = None, None, None
        lin_war = vs_ones if s == 0 else None

        def mk_l3a(tl, ap, war, s=s):
            t = s * TPS + tl
            return P.do("vector", lambda v, t=t, ap=ap: v.tensor_scalar(
                out=ap, in0=r_tile(P, t), scalar1=MRS[:, 2 * t:2 * t + 1],
                scalar2=MRS[:, 2 * t + 1:2 * t + 2], op0=ALU.add, op1=ALU.mult),
                deps=(war,) + tuple(P.r_toks))

        pipe = XnPipe(P, mk_l3a, TPS)
        for tl in range(TPS):
            t = s * TPS + tl
            xt_tok = pipe.xt_tok[tl]
            xtv = pipe.xt_ap(tl)
            banks = [0, 1, 2, 3, 4, 5]
            for bi, bank in enumerate(banks):
                lo, hi = bi * 512, (bi + 1) * 512
                P.do("tensor", lambda t_, bank=bank, lo=lo, hi=hi: t_.matmul(
                    P.PS[bank][:, 0:512], ONES1B[0:1, :], BROW[0:1, lo:hi],
                    start=True, stop=False),
                    deps=tuple(btoks) + (o1b_tok, P.ps_free[bank]))
            mm = emit_proj_1024(P, xtv, WALL.sb, wtok_all[-1], out_banks=banks,
                                fout=3072, extra_deps=(xt_tok,) + wtok_all,
                                accumulate=True)
            pipe.done(tl, mm[-1])
            lin_toks = []
            for bi, bank in enumerate(banks):
                lo, hi = bi * 512, (bi + 1) * 512
                tok = P.do("vector", lambda v, bank=bank, lo=lo, hi=hi: v.tensor_copy(
                    out=LIN[:, lo:hi], in_=P.PS[bank][:, 0:512]),
                    deps=(mm[bi], P.ps_free[bank], lin_war))
                lin_toks.append(tok)
                P.ps_free[bank] = tok
            sg1 = P.do("scalar", lambda s_: s_.activation(
                out=LIN[:, 256:512], in_=LIN[:, 256:512], func=AF.Sigmoid),
                deps=(lin_toks[0],))
            sg2 = P.do("scalar", lambda s_: s_.activation(
                out=LIN[:, 768:1024], in_=LIN[:, 768:1024], func=AF.Sigmoid),
                deps=(lin_toks[1],))
            sg3 = P.do("scalar", lambda s_: s_.activation(
                out=LIN[:, 2048:3072], in_=LIN[:, 2048:3072], func=AF.Sigmoid),
                deps=(lin_toks[5], lin_toks[4], lin_toks[3], lin_toks[2]))
            tq = P.do("vector", lambda v: v.tensor_tensor(
                out=QP[:, :].rearrange("p (h f) -> p h f", f=32)[:, :, 0:16],
                in0=LIN[:, 0:256].rearrange("p (h f) -> p h f", f=16),
                in1=LIN[:, 256:512].rearrange("p (h f) -> p h f", f=16),
                op=ALU.mult), deps=(sg1, qkm, qts_war))
            tk = P.do("vector", lambda v: v.tensor_tensor(
                out=KP[:, :].rearrange("p (h f) -> p h f", f=32)[:, :, 0:16],
                in0=LIN[:, 512:768].rearrange("p (h f) -> p h f", f=16),
                in1=LIN[:, 768:1024].rearrange("p (h f) -> p h f", f=16),
                op=ALU.mult), deps=(sg2, qkm2, kts_war))
            tv = P.do("vector", lambda v, tl=tl: v.tensor_tensor(
                out=VSAUG[:, tl * 1040:(tl + 1) * 1040].rearrange(
                    "p (h f) -> p h f", f=65)[:, :, 0:64],
                in0=LIN[:, 1024:2048].rearrange("p (h f) -> p h f", f=64),
                in1=LIN[:, 2048:3072].rearrange("p (h f) -> p h f", f=64),
                op=ALU.mult), deps=(sg3, vs_ones, vs_war))
            lin_war = tv
            for (src, dststore, wtok_dep) in ((QP, QTS, tq), (KP, KTS, tk)):
                pt_b = tl % 2
                for c in range(4):
                    trq = P.do("tensor", lambda t_, c=c, src=src, pt_b=pt_b: t_.transpose(
                        P.PT[pt_b][:, c * 128:(c + 1) * 128],
                        src[:, c * 128:(c + 1) * 128], P.IDN[:]),
                        deps=(wtok_dep, P.pt_free[pt_b]) if c == 0 else (),
                        inc="pe" if c == 3 else None)
                dstv = dststore[:, 0:8192].rearrange(
                    "p (c n) -> p c n", n=2048)[:, :, tl * 128:(tl + 1) * 128]
                ev = P.do("vector", lambda v, dstv=dstv, pt_b=pt_b: v.tensor_copy(
                    out=dstv, in_=P.PT[pt_b][:, 0:512].rearrange(
                        "p (c n) -> p c n", n=128)), deps=(trq,))
                P.pt_free[pt_b] = ev
                if dststore is QTS:
                    qts_done = ev
                else:
                    kts_done = ev
            qts_war = qts_done
            kts_war = kts_done
            vs_war = None
        P.barrier()

        for h in range(16):
            hc, j = h // 4, h % 4
            base = 32 * j
            tp = {} if base < 96 else {"tile_position": (base, 0)}
            slot_free = [None, None, None]
            for qb in range(4):
                av_tok = None
                for c in range(17):
                    bank = [0, 1, 3][c % 3]
                    slot = c % 3
                    if c < 16:
                        lhsT = KTS[base:base + 16,
                                   hc * 2048 + c * 128: hc * 2048 + (c + 1) * 128]
                    else:
                        lhsT = KPAD17S[base:base + 16, hc * 128:(hc + 1) * 128]
                    rhs = QTS[base:base + 16, hc * 2048 + qb * 512: hc * 2048 + (qb + 1) * 512]
                    sc = P.do("tensor", lambda t_, lhsT=lhsT, rhs=rhs, bank=bank, tp=tp:
                              t_.matmul(P.PS[bank][:, 0:512], lhsT, rhs,
                                        start=True, stop=True, **tp),
                              deps=(P.ps_free[bank],), inc="pe")
                    ex = P.do("scalar", lambda s_, slot=slot, bank=bank: s_.activation(
                        out=EXPT[:, slot * 512:(slot + 1) * 512], in_=P.PS[bank][:, 0:512],
                        func=AF.Exp, bias=IM(-STAB), scale=IM(0.125)),
                        deps=(sc, slot_free[slot]))
                    P.ps_free[bank] = ex
                    if c < 16:
                        lhsT_v = VSAUG[:, (c * 16 + h) * 65:(c * 16 + h) * 65 + 65]
                    else:
                        lhsT_v = VA17S[:, h * 65:(h + 1) * 65]
                    deps = [ex]
                    if c == 0:
                        deps += [P.ps_free[2], va17_tok[0], va17_tok[1]]
                    av_tok = P.do("tensor", lambda t_, lhsT_v=lhsT_v, c=c, slot=slot:
                                  t_.matmul(P.PS[2][0:65, 0:512], lhsT_v,
                                            EXPT[:, slot * 512:(slot + 1) * 512],
                                            start=(c == 0), stop=(c == 16)),
                                  deps=deps, inc="pe")
                    slot_free[slot] = av_tok
                ct = P.do("vector", lambda v: v.tensor_copy(
                    out=CTXT[0:65, :], in_=P.PS[2][0:65, 0:512]), deps=(av_tok,))
                P.ps_free[2] = ct
                prev_v = ct
                for i in range(4):
                    t_glob = s * TPS + qb * 4 + i
                    rt = r_tile(P, t_glob)
                    trc = P.do("tensor", lambda t_, i=i: t_.transpose(
                        P.PT[0][0:128, i * 128:i * 128 + 65],
                        CTXT[0:65, i * 128:(i + 1) * 128], P.IDN[0:65, 0:65]),
                        deps=(ct, P.pt_free[0]) if i == 0 else (),
                        inc="pe" if i == 3 else None)
                for i in range(4):
                    t_glob = s * TPS + qb * 4 + i
                    rt = r_tile(P, t_glob)
                    ev = P.do("vector", lambda v, i=i: v.tensor_copy(
                        out=CTX[:, :], in_=P.PT[0][0:128, i * 128:i * 128 + 65]),
                        deps=(trc, prev_v))
                    rc = P.do("vector", lambda v: v.reciprocal(
                        out=REC[:, 0:1], in_=CTX[:, 64:65]), deps=(ev,))
                    prev_v = P.do("vector", lambda v, rt=rt, h=h: v.scalar_tensor_tensor(
                        out=rt[:, 64 * h:64 * h + 64], in0=CTX[:, 0:64],
                        scalar=REC[:, 0:1], in1=rt[:, 64 * h:64 * h + 64],
                        op0=ALU.mult, op1=ALU.add), deps=(rc,))
                P.pt_free[0] = prev_v
        P.barrier()
    P.barrier()



L2_WCHUNKS = {0: (0, 6), 1: (2, 10), 2: (6, 14), 3: (10, 16)}


def emit_L2_attn(P, ctx, aps):
    nc = P.nc
    sb = lambda name, shape, dt: ctx.enter_context(nc.sbuf_tensor(P.tag + name, shape, dt))
    MRS = sb("l2a_mrs", [128, 2 * NTILE], F32)
    WALL = WBuf(P, ctx, "l2a_w", 1024, 1024)
    WG = WBuf(P, ctx, "l2a_wg", 1024, 1)
    QTS = sb("l2a_qts", [128, 8 * 2048], BF16)
    KTS = sb("l2a_kts", [128, 8 * 2048], BF16)
    VSAUG = sb("l2a_vsaug", [128, 16 * 16 * 65], BF16)
    GATES = sb("l2a_gates", [128, 16], F32)
    OMGS = sb("l2a_omgs", [128, 16], F32)
    GSC = sb("l2a_gsc", [128, 1], F32)
    ONESB = sb("l2a_onesb", [1, 128], BF16)
    REC2 = sb("l2a_rec2", [128, 4], F32)
    RECG = sb("l2a_recg", [128, 4], F32)
    GA = sb("l2a_ga", [128, 4], F32)
    GB = sb("l2a_gb", [128, 4], F32)
    T1 = sb("l2a_t1", [128, 260], F32)
    COMBW = sb("l2a_combw", [128, 4096], BF16)
    EXPT = sb("l2a_expt", [128, 1536], BF16)
    EXPTG = sb("l2a_exptg", [64, 512], BF16)
    CTL = sb("l2a_ctl", [65, 512], BF16)
    CTG = sb("l2a_ctg", [65, 512], BF16)
    LOCN = sb("l2a_locn", [128, 260], BF16)
    GLBN = sb("l2a_glbn", [128, 260], BF16)
    VSTRA = sb("l2a_vstra", [64, 16 * 65], BF16)
    SELC = sb("l2a_selc", [128, 1024], BF16)
    BOROW = sb("l2a_borow", [1, 1024], BF16)
    WOUTV = WALL.sb[:, :]

    o1b = P.do("vector", lambda v: v.memset(ONESB[0:1, :], 1.0))
    nbg_row, nbg_t = load_row_f32(P, ctx, "l2a_bg", aps["b_gate"], 1)
    NEGBG, negbg_t = emit_bcast_row(P, ctx, "l2a_negbg", nbg_row, nbg_t, 1, scale=-1.0)
    sel_tok = P.dma("gpsimd", lambda g: g.dma_start(out=SELC[:, :], in_=aps["selc"]))
    bo_tok = P.dma("gpsimd", lambda g: g.dma_start(out=BOROW[0:1, :], in_=aps["b_out"]))

    xn_war = None
    for t in range(NTILE):
        xn_tok = emit_ln_tile(P, t, xn_war=xn_war)
        xn_war = xn_tok
        P.do("vector", lambda v, t=t: v.tensor_copy(
            out=MRS[:, 2 * t:2 * t + 1], in_=P.NEGMEAN[:, 0:1]), deps=(xn_tok,))
        P.do("vector", lambda v, t=t: v.tensor_copy(
            out=MRS[:, 2 * t + 1:2 * t + 2], in_=P.RSTD[:, 0:1]))
    P.barrier()

    WGROW = VSAUG[0:1, 0:1024]
    wgr_tok = P.dma("gpsimd", lambda g: g.dma_start(
        out=WGROW[0:1, :], in_=aps["W_gate"].rearrange("f one -> one f")))
    for c in range(8):
        trw = P.do("tensor", lambda t, c=c: t.transpose(
            P.PT[0][0:128, c * 128:c * 128 + 1],
            WGROW[0:1, c * 128:(c + 1) * 128], P.IDN[0:1, 0:1]),
            deps=(wgr_tok, P.pt_free[0], P.idn_tok) if c == 0 else (),
            inc="pe" if c == 7 else None)
    wg_tok = None
    for c in range(8):
        wg_tok = P.do("vector", lambda v, c=c: v.tensor_copy(
            out=WG.sb[:, c:c + 1], in_=P.PT[0][0:128, c * 128:c * 128 + 1]),
            deps=(trw,))
    P.pt_free[0] = wg_tok

    for s in range(2):
        vs_ones = P.do("vector", lambda v: v.memset(VSAUG[:, :], 1.0),
                       deps=(getattr(P, "_l2_vs_war", None), wg_tok))
        for sweep in range(3):
            wlo = sweep * 1024
            w_war = P.fence("tensor")
            wq_tok = P.dma("gpsimd", lambda g, wlo=wlo: g.dma_start(
                out=WALL.sb[:, :].rearrange("p (c f) -> p c f", f=1024),
                in_=aps["W_qkv"][:, wlo:wlo + 1024].rearrange("(c p) f -> p c f", p=128)),
                deps=(w_war,))
            def mk_l2a(tl, ap, war, s=s):
                t = s * TPS + tl
                return P.do("vector", lambda v, t=t, ap=ap: v.tensor_scalar(
                    out=ap, in0=r_tile(P, t), scalar1=MRS[:, 2 * t:2 * t + 1],
                    scalar2=MRS[:, 2 * t + 1:2 * t + 2], op0=ALU.add, op1=ALU.mult),
                    deps=(war,) + tuple(P.r_toks))

            pipe = XnPipe(P, mk_l2a, TPS)
            store_war = None
            for tl in range(TPS):
                t = s * TPS + tl
                xt_tok = pipe.xt_tok[tl]
                xnv = pipe.xn_ap(tl)
                xtv = pipe.xt_ap(tl)
                if sweep == 0:
                    for k in range(8):
                        deps = [P.ps_free[2], wg_tok, xt_tok] if k == 0 else []
                        gmm = P.do("tensor", lambda t_, k=k, xtv=xtv: t_.matmul(
                            P.PS[2][:, 0:1], xtv[:, k * 128:(k + 1) * 128], WG.chunk(k),
                            start=(k == 0), stop=(k == 7)),
                            deps=deps, inc="pe" if k == 7 else None)
                    ge = P.do("scalar", lambda s_: s_.activation(
                        out=GSC[:, 0:1], in_=P.PS[2][:, 0:1], func=AF.Exp,
                        bias=NEGBG[:, 0:1], scale=IM(-1.0)), deps=(gmm, negbg_t))
                    P.ps_free[2] = ge
                    g1 = P.do("vector", lambda v: v.tensor_scalar(
                        out=GSC[:, 0:1], in0=GSC[:, 0:1], scalar1=1.0, scalar2=None,
                        op0=ALU.add), deps=(ge,))
                    g2 = P.do("vector", lambda v, tl=tl: v.reciprocal(
                        out=GATES[:, tl:tl + 1], in_=GSC[:, 0:1]), deps=(g1,))
                    P.do("vector", lambda v, tl=tl: v.tensor_scalar(
                        out=OMGS[:, tl:tl + 1], in0=GATES[:, tl:tl + 1], scalar1=-1.0,
                        scalar2=1.0, op0=ALU.mult, op1=ALU.add), deps=(g2,))
                banks = [0, 1]
                mm = emit_proj_1024(P, xtv, WALL.sb, wq_tok, out_banks=banks,
                                    fout=1024, extra_deps=(xt_tok,))
                pipe.done(tl, mm[-1])
                if sweep == 2:
                    for bi in range(2):
                        dstv = VSAUG[:, tl * 1040:(tl + 1) * 1040].rearrange(
                            "p (h f) -> p h f", f=65)[:, bi * 8:(bi + 1) * 8, 0:64]
                        e = P.do("vector", lambda v, bi=bi, dstv=dstv: v.tensor_copy(
                            out=dstv, in_=P.PS[bi][:, 0:512].rearrange(
                                "p (h f) -> p h f", f=64)),
                            deps=(mm[bi], P.ps_free[bi], vs_ones))
                        P.ps_free[bi] = e
                else:
                    evs = []
                    for bi in range(2):
                        if sweep == 0:
                            e = P.do("vector", lambda v, bi=bi, xnv=xnv: v.tensor_copy(
                                out=xnv[:, bi * 512:(bi + 1) * 512],
                                in_=P.PS[bi][:, 0:512]),
                                deps=(mm[bi], P.ps_free[bi], xt_tok))
                        else:
                            e = P.do("vector", lambda v, bi=bi, xnv=xnv: v.tensor_scalar(
                                out=xnv[:, bi * 512:(bi + 1) * 512],
                                in0=P.PS[bi][:, 0:512], scalar1=0.125, scalar2=None,
                                op0=ALU.mult),
                                deps=(mm[bi], P.ps_free[bi], xt_tok))
                        evs.append(e)
                        P.ps_free[bi] = e
                    dststore = QTS if sweep == 0 else KTS
                    pt_b = tl % 2
                    for c in range(8):
                        trq = P.do("tensor", lambda t_, c=c, pt_b=pt_b, xnv=xnv: t_.transpose(
                            P.PT[pt_b][:, c * 128:(c + 1) * 128],
                            xnv[:, c * 128:(c + 1) * 128], P.IDN[:]),
                            deps=(evs[-1], P.pt_free[pt_b], store_war) if c == 0 else (),
                            inc="pe" if c == 7 else None)
                    dstv = dststore[:, :].rearrange(
                        "p (c n) -> p c n", n=2048)[:, :, tl * 128:(tl + 1) * 128]
                    evt = P.do("vector", lambda v, dstv=dstv, pt_b=pt_b: v.tensor_copy(
                        out=dstv, in_=P.PT[pt_b][:, :].rearrange(
                            "p (c n) -> p c n", n=128)), deps=(trq,))
                    P.pt_free[pt_b] = evt
                    store_war = evt
                    pipe.set_xn_reader(tl, evt)
            P.barrier()

        w_war = P.fence("tensor")
        wo_tok = P.dma("gpsimd", lambda g: g.dma_start(
            out=WOUTV.rearrange("p (c f) -> p c f", f=1024),
            in_=aps["W_out"].rearrange("(c p) f -> p c f", p=128)), deps=(w_war,))
        vz = P.do("vector", lambda v: v.memset(VSTRA[:, :], 1.0), deps=(w_war,))
        vst_mm = [None, None]
        for tl in range(TPS):
            rhsv = VSAUG[:, tl * 1040:(tl + 1) * 1040].rearrange(
                "p (h f) -> p h f", f=65)[:, :, 0:64]
            for bi in range(2):
                deps = [sel_tok, P.ps_free[bi]] if tl == 0 else []
                vst_mm[bi] = P.do("tensor", lambda t_, tl=tl, bi=bi, rhsv=rhsv: t_.matmul(
                    P.PS[bi][0:64, 0:512], SELC[:, tl * 64:(tl + 1) * 64],
                    rhsv[:, bi * 8:(bi + 1) * 8, :],
                    start=(tl == 0), stop=(tl == TPS - 1)),
                    deps=deps, inc="pe" if tl == TPS - 1 else None)
        vstr_tok = None
        for bi in range(2):
            dstv = VSTRA[:, :].rearrange("p (h f) -> p h f", f=65)[:, bi * 8:(bi + 1) * 8, 0:64]
            vstr_tok = P.do("vector", lambda v, bi=bi, dstv=dstv: v.tensor_copy(
                out=dstv, in_=P.PS[bi][0:64, 0:512].rearrange("p (h f) -> p h f", f=64)),
                deps=(vst_mm[bi], vz))
            P.ps_free[bi] = vstr_tok

        STAB2 = 4.0
        comb_done = None
        for w in range(4):
            c0, c1 = L2_WCHUNKS[w]
            for h in range(16):
                hc, base = h // 2, 64 * (h % 2)
                qrhs = QTS[base:base + 64, hc * 2048 + w * 512: hc * 2048 + (w + 1) * 512]
                slot_free = [None, None, None]
                av_tok = None
                nk = c1 - c0
                for ci, kc in enumerate(range(c0, c1)):
                    bank = [2, 3, 0][ci % 3]
                    slot = ci % 3
                    sc = P.do("tensor", lambda t_, kc=kc, bank=bank, hc=hc, base=base,
                              qrhs=qrhs: t_.matmul(
                                  P.PS[bank][:, 0:512],
                                  KTS[base:base + 64, hc * 2048 + kc * 128: hc * 2048 + (kc + 1) * 128],
                                  qrhs, start=True, stop=True),
                              deps=(P.ps_free[bank],), inc="pe")
                    ex = P.do("scalar", lambda s_, slot=slot, bank=bank: s_.activation(
                        out=EXPT[:, slot * 512:(slot + 1) * 512], in_=P.PS[bank][:, 0:512],
                        func=AF.Exp, bias=IM(-STAB2), scale=IM(1.0)),
                        deps=(sc, slot_free[slot]))
                    P.ps_free[bank] = ex
                    deps = [ex]
                    if ci == 0:
                        deps += [P.ps_free[4], vs_ones]
                    av_tok = P.do("tensor", lambda t_, kc=kc, slot=slot, h=h, ci=ci,
                                  nk=nk: t_.matmul(
                                      P.PS[4][0:65, 0:512],
                                      VSAUG[:, (kc * 16 + h) * 65:(kc * 16 + h) * 65 + 65],
                                      EXPT[:, slot * 512:(slot + 1) * 512],
                                      start=(ci == 0), stop=(ci == nk - 1)),
                                  deps=deps, inc="pe")
                    slot_free[slot] = av_tok
                kstr = KTS[base:base + 64, hc * 2048: (hc + 1) * 2048].rearrange(
                    "p (a b) -> p a b", b=32)[:, :, 0:1].squeeze(2)
                scg = P.do("tensor", lambda t_, kstr=kstr, qrhs=qrhs: t_.matmul(
                    P.PS[2][0:64, 0:512], kstr, qrhs, start=True, stop=True),
                    deps=(P.ps_free[2],), inc="pe")
                exg = P.do("scalar", lambda s_: s_.activation(
                    out=EXPTG[0:64, 0:512], in_=P.PS[2][0:64, 0:512], func=AF.Exp,
                    bias=IM(-STAB2), scale=IM(1.0)),
                    deps=(scg, getattr(P, "_l2_avg", None)))
                P.ps_free[2] = exg
                avg = P.do("tensor", lambda t_, h=h: t_.matmul(
                    P.PS[5][0:65, 0:512], VSTRA[0:64, h * 65:(h + 1) * 65],
                    EXPTG[0:64, 0:512], start=True, stop=True),
                    deps=(exg, P.ps_free[5], vstr_tok), inc="pe")
                P._l2_avg = avg
                ctl = P.do("vector", lambda v: v.tensor_copy(
                    out=CTL[0:65, :], in_=P.PS[4][0:65, 0:512]), deps=(av_tok,))
                P.ps_free[4] = ctl
                ctg = P.do("vector", lambda v: v.tensor_copy(
                    out=CTG[0:65, :], in_=P.PS[5][0:65, 0:512]), deps=(avg,))
                P.ps_free[5] = ctg
                for i in range(4):
                    trl = P.do("tensor", lambda t_, i=i: t_.transpose(
                        P.PT[0][0:128, i * 128:i * 128 + 65],
                        CTL[0:65, i * 128:(i + 1) * 128], P.IDN[0:65, 0:65]),
                        deps=(ctl, P.pt_free[0]) if i == 0 else (),
                        inc="pe" if i == 3 else None)
                evl = P.do("vector", lambda v: v.tensor_copy(
                    out=LOCN[:, :].rearrange("p (i f) -> p i f", f=65),
                    in_=P.PT[0][:, 0:512].rearrange("p (i f) -> p i f", f=128)[:, :, 0:65]),
                    deps=(trl, comb_done))
                P.pt_free[0] = evl
                for i in range(4):
                    trg = P.do("tensor", lambda t_, i=i: t_.transpose(
                        P.PT[1][0:128, i * 128:i * 128 + 65],
                        CTG[0:65, i * 128:(i + 1) * 128], P.IDN[0:65, 0:65]),
                        deps=(ctg, P.pt_free[1]) if i == 0 else (),
                        inc="pe" if i == 3 else None)
                evg = P.do("vector", lambda v: v.tensor_copy(
                    out=GLBN[:, :].rearrange("p (i f) -> p i f", f=65),
                    in_=P.PT[1][:, 0:512].rearrange("p (i f) -> p i f", f=128)[:, :, 0:65]),
                    deps=(trg, comb_done))
                P.pt_free[1] = evg
                r1 = P.do("vector", lambda v: v.reciprocal(
                    out=REC2[:, :],
                    in_=LOCN[:, :].rearrange("p (i f) -> p i f", f=65)[:, :, 64:65].squeeze(2)),
                    deps=(evl,))
                r2 = P.do("vector", lambda v: v.reciprocal(
                    out=RECG[:, :],
                    in_=GLBN[:, :].rearrange("p (i f) -> p i f", f=65)[:, :, 64:65].squeeze(2)),
                    deps=(evg,))
                ga = P.do("vector", lambda v, w=w: v.tensor_tensor(
                    out=GA[:, :], in0=REC2[:, :], in1=GATES[:, 4 * w:4 * w + 4],
                    op=ALU.mult), deps=(r1,))
                gb = P.do("vector", lambda v, w=w: v.tensor_tensor(
                    out=GB[:, :], in0=RECG[:, :], in1=OMGS[:, 4 * w:4 * w + 4],
                    op=ALU.mult), deps=(r2,))
                tt1 = P.do("vector", lambda v: v.tensor_tensor(
                    out=T1[:, :].rearrange("p (i f) -> p i f", f=65)[:, :, 0:64],
                    in0=LOCN[:, :].rearrange("p (i f) -> p i f", f=65)[:, :, 0:64],
                    in1=GA[:, :].unsqueeze(2).broadcast_to([128, 4, 64]),
                    op=ALU.mult), deps=(ga, evl))
                cw = COMBW[:, :].rearrange("p (i d) -> p i d", d=1024)[:, :, 64 * h:64 * h + 64]
                c1_ = P.do("vector", lambda v, cw=cw: v.tensor_tensor(
                    out=cw, in0=GLBN[:, :].rearrange("p (i f) -> p i f", f=65)[:, :, 0:64],
                    in1=GB[:, :].unsqueeze(2).broadcast_to([128, 4, 64]),
                    op=ALU.mult), deps=(gb, evg, comb_done))
                comb_done = P.do("vector", lambda v, cw=cw: v.tensor_tensor(
                    out=cw, in0=cw,
                    in1=T1[:, :].rearrange("p (i f) -> p i f", f=65)[:, :, 0:64],
                    op=ALU.add), deps=(c1_, tt1))
            xt_war2 = None
            for i in range(4):
                tg = s * TPS + w * 4 + i
                rt = r_tile(P, tg)
                xt_tok = emit_transpose_1024(P, COMBW[:, i * 1024:(i + 1) * 1024],
                                             P.XT, (comb_done,), pt_bank=i % 2,
                                             war_dep=xt_war2)
                for bi in range(2):
                    P.do("tensor", lambda t_, bi=bi: t_.matmul(
                        P.PS[bi][:, 0:512], ONESB[0:1, :],
                        BOROW[0:1, bi * 512:(bi + 1) * 512], start=True, stop=False),
                        deps=(P.ps_free[bi], bo_tok, o1b))
                mmo = emit_proj_1024(P, P.XT, WOUTV, wo_tok, out_banks=[0, 1],
                                     fout=1024, extra_deps=(xt_tok,), accumulate=True)
                xt_war2 = mmo[-1]
                for bi in range(2):
                    lo, hi = bi * 512, (bi + 1) * 512
                    e = P.do("vector", lambda v, bi=bi, lo=lo, hi=hi, rt=rt: v.tensor_tensor(
                        out=rt[:, lo:hi], in0=P.PS[bi][:, 0:512], in1=rt[:, lo:hi],
                        op=ALU.add), deps=(mmo[bi],))
                    P.ps_free[bi] = e
        P._l2_vs_war = P.fence("tensor")
        P.barrier()
    P.barrier()



def emit_L0_attn(P, ctx, aps):
    nc = P.nc
    sb = lambda name, shape, dt: ctx.enter_context(nc.sbuf_tensor(P.tag + name, shape, dt))
    MRS = sb("l0a_mrs", [128, 2 * NTILE], F32)
    W1 = WBuf(P, ctx, "l0a_w1", 1024, 1024)
    QTS = sb("l0a_qts", [128, 8 * 2048], BF16)
    KTS = sb("l0a_kts", [128, 8 * 2048], BF16)
    VSAUG = sb("l0a_vsaug", [128, 16 * 16 * 65], BF16)
    BROW = sb("l0a_brow", [1, 3072], BF16)
    BOROW = sb("l0a_borow", [1, 1024], BF16)
    ONESB = sb("l0a_onesb", [1, 128], BF16)
    MTC = sb("l0a_mtc", [128, 128], BF16)
    ES = sb("l0a_es", [128, 512], F32)
    AS = sb("l0a_as", [128, 512], BF16)
    RS = sb("l0a_rs", [128, 512], BF16)
    ATM = sb("l0a_atm", [128, 256], BF16)
    KTOK1 = sb("l0a_ktok1", [128, 1024], BF16)
    NUMS = sb("l0a_nums", [128, 16 * 65], BF16)
    DEN = sb("l0a_den", [128, 16], F32)
    RECS = sb("l0a_recs", [128, 16], F32)
    OUT1 = sb("l0a_out1", [128, 1024], BF16)
    SSB = sb("l0a_ssb", [128, 8 * 65], F32)
    SBF = sb("l0a_sbf", [128, 8 * 65], BF16)

    o1b = P.do("vector", lambda v: v.memset(ONESB[0:1, :], 1.0))
    mt_tok = P.dma("gpsimd", lambda g: g.dma_start(out=MTC[:, :], in_=aps["mtc"]))
    bq_tok = P.dma("gpsimd", lambda g: g.dma_start(out=BROW[0:1, :], in_=aps["b_qkv"]))
    bo_tok = P.dma("gpsimd", lambda g: g.dma_start(out=BOROW[0:1, :], in_=aps["b_out"]))

    xn_war = None
    for t in range(NTILE):
        xn_tok = emit_ln_tile(P, t, xn_war=xn_war)
        xn_war = xn_tok
        P.do("vector", lambda v, t=t: v.tensor_copy(
            out=MRS[:, 2 * t:2 * t + 1], in_=P.NEGMEAN[:, 0:1]), deps=(xn_tok,))
        P.do("vector", lambda v, t=t: v.tensor_copy(
            out=MRS[:, 2 * t + 1:2 * t + 2], in_=P.RSTD[:, 0:1]))
    P.barrier()

    for s in range(2):
        vs_ones = P.do("vector", lambda v: v.memset(VSAUG[:, :], 1.0),
                       deps=(getattr(P, "_l0_vs_war", None),))
        for sweep in range(3):
            wlo = sweep * 1024
            w_war = P.fence("tensor")
            wq_tok = P.dma("gpsimd", lambda g, wlo=wlo: g.dma_start(
                out=W1.sb[:, :].rearrange("p (c f) -> p c f", f=1024),
                in_=aps["W_qkv"][:, wlo:wlo + 1024].rearrange("(c p) f -> p c f", p=128)),
                deps=(w_war,))
            def mk_l0a(t, ap, war):
                return P.do("vector", lambda v, t=t, ap=ap: v.tensor_scalar(
                    out=ap, in0=r_tile(P, t), scalar1=MRS[:, 2 * t:2 * t + 1],
                    scalar2=MRS[:, 2 * t + 1:2 * t + 2], op0=ALU.add, op1=ALU.mult),
                    deps=(war,) + tuple(P.r_toks))

            pipe = XnPipe(P, lambda tl, ap, war: mk_l0a(s * TPS + tl, ap, war), TPS)
            store_war = None
            for tl in range(TPS):
                t = s * TPS + tl
                xt_tok = pipe.xt_tok[tl]
                xnv = pipe.xn_ap(tl)
                for bi in range(2):
                    P.do("tensor", lambda t_, bi=bi, wlo=wlo: t_.matmul(
                        P.PS[bi][:, 0:512], ONESB[0:1, :],
                        BROW[0:1, wlo + bi * 512:wlo + (bi + 1) * 512],
                        start=True, stop=False),
                        deps=(P.ps_free[bi], bq_tok, o1b))
                mm = emit_proj_1024(P, pipe.xt_ap(tl), W1.sb, wq_tok, out_banks=[0, 1],
                                    fout=1024, extra_deps=(xt_tok,), accumulate=True)
                pipe.done(tl, mm[-1])
                if sweep == 2:
                    for bi in range(2):
                        dstv = VSAUG[:, tl * 1040:(tl + 1) * 1040].rearrange(
                            "p (h f) -> p h f", f=65)[:, bi * 8:(bi + 1) * 8, 0:64]
                        e = P.do("vector", lambda v, bi=bi, dstv=dstv: v.tensor_copy(
                            out=dstv, in_=P.PS[bi][:, 0:512].rearrange(
                                "p (h f) -> p h f", f=64)),
                            deps=(mm[bi], P.ps_free[bi], vs_ones))
                        P.ps_free[bi] = e
                else:
                    scale = 0.125 if sweep == 0 else 1.0
                    evs = []
                    for bi in range(2):
                        e1 = P.do("scalar", lambda s_, bi=bi, scale=scale: s_.activation(
                            out=ES[:, :], in_=P.PS[bi][:, 0:512], func=AF.Exp,
                            scale=IM(scale)),
                            deps=(mm[bi], getattr(P, "_l0_es_war", None)))
                        a1 = P.do("vector", lambda v: v.tensor_scalar(
                            out=AS[:, :], in0=ES[:, :], scalar1=1.0, scalar2=1e-4,
                            op0=ALU.min, op1=ALU.add), deps=(e1,))
                        r1 = P.do("vector", lambda v, bi=bi, scale=scale: v.tensor_scalar(
                            out=RS[:, :], in0=P.PS[bi][:, 0:512], scalar1=scale,
                            scalar2=0.0, op0=ALU.mult, op1=ALU.max),
                            deps=(mm[bi], P.ps_free[bi], a1))
                        e = P.do("vector", lambda v, bi=bi, xnv=xnv: v.tensor_tensor(
                            out=xnv[:, bi * 512:(bi + 1) * 512], in0=RS[:, :],
                            in1=AS[:, :], op=ALU.add), deps=(r1, xt_tok))
                        P._l0_es_war = e
                        evs.append(e)
                        P.ps_free[bi] = e
                    dststore = QTS if sweep == 0 else KTS
                    pt_b = tl % 2
                    for c in range(8):
                        trq = P.do("tensor", lambda t_, c=c, pt_b=pt_b, xnv=xnv: t_.transpose(
                            P.PT[pt_b][:, c * 128:(c + 1) * 128],
                            xnv[:, c * 128:(c + 1) * 128], P.IDN[:]),
                            deps=(evs[-1], P.pt_free[pt_b], store_war) if c == 0 else (),
                            inc="pe" if c == 7 else None)
                    dstv = dststore[:, :].rearrange(
                        "p (c n) -> p c n", n=2048)[:, :, tl * 128:(tl + 1) * 128]
                    evt = P.do("vector", lambda v, dstv=dstv, pt_b=pt_b: v.tensor_copy(
                        out=dstv, in_=P.PT[pt_b][:, :].rearrange("p (c n) -> p c n", n=128)),
                        deps=(trq,))
                    P.pt_free[pt_b] = evt
                    store_war = evt
                    pipe.set_xn_reader(tl, evt)
            P.barrier()

        w_war = P.fence("tensor")
        wo_tok = P.dma("gpsimd", lambda g: g.dma_start(
            out=W1.sb[:, :].rearrange("p (c f) -> p c f", f=1024),
            in_=aps["W_out"].rearrange("(c p) f -> p c f", p=128)), deps=(w_war,))
        sz = P.do("vector", lambda v: v.memset(SSB[:, :], 0.0))
        szb = P.do("vector", lambda v: v.memset(SBF[:, :], 0.0))
        num_evac_prev = [None, None]
        out_done = None
        atm_war = getattr(P, "_l0_atm_war", {})
        state_rd = getattr(P, "_l0_state_rd", {})
        amm_t, msk_t, ktk_t = {}, {}, {}

        def prelude(u):
            pt_b = u % 2
            for c in range(8):
                trk = P.do("tensor", lambda t_, c=c, u=u, pt_b=pt_b: t_.transpose(
                    P.PT[pt_b][:, c * 128:(c + 1) * 128],
                    KTS[:, c * 2048 + u * 128: c * 2048 + (u + 1) * 128], P.IDN[:]),
                    deps=(P.pt_free[pt_b], getattr(P, "_l0_kt_war", None)) if c == 0 else (),
                    inc="pe" if c == 7 else None)
            ktk = P.do("vector", lambda v, pt_b=pt_b: v.tensor_copy(
                out=KTOK1[:, :], in_=P.PT[pt_b][:, :]),
                deps=(trk, getattr(P, "_l0_ktok_war", None)))
            P.pt_free[pt_b] = ktk
            ktk_t[u] = ktk

        def sc_s1(u, h):
            hc, base = h // 2, 64 * (h % 2)
            abank = h % 2
            amm_t[(u, h)] = P.do("tensor", lambda t_, hc=hc, base=base, u=u, abank=abank:
                                 t_.matmul(
                P.PS[abank][:, 0:128],
                KTS[base:base + 64, hc * 2048 + u * 128: hc * 2048 + (u + 1) * 128],
                QTS[base:base + 64, hc * 2048 + u * 128: hc * 2048 + (u + 1) * 128],
                start=True, stop=True),
                deps=(P.ps_free[abank],), inc="pe")

        def sc_s2(u, h):
            sl = h % 2
            abank = sl
            msk_t[(u, h)] = P.do("vector", lambda v, abank=abank, sl=sl: v.tensor_tensor(
                out=ATM[:, sl * 128:(sl + 1) * 128], in0=P.PS[abank][:, 0:128],
                in1=MTC[:, :],
                op=ALU.mult), deps=(amm_t[(u, h)], mt_tok, atm_war.get(sl)))
            P.ps_free[abank] = msk_t[(u, h)]

        def sc_s3(u, h):
            nonlocal out_done
            hc, base = h // 2, 64 * (h % 2)
            sl = h % 2
            nbank = 2 + sl
            sbank = 4 if sl == 0 else 5
            msk = msk_t[(u, h)]
            vsl = VSAUG[:, (u * 16 + h) * 65:(u * 16 + h) * 65 + 65]
            nmm = P.do("tensor", lambda t_, vsl=vsl, nbank=nbank, u=u, sl=sl: t_.matmul(
                P.PS[nbank][:, 0:65], ATM[:, sl * 128:(sl + 1) * 128], vsl,
                start=True, stop=(u == 0)),
                deps=(msk, P.ps_free[nbank], vs_ones), inc="pe" if u == 0 else None)
            if u > 0:
                nmm = P.do("tensor", lambda t_, hc=hc, base=base, u=u, nbank=nbank,
                           h=h: t_.matmul(
                               P.PS[nbank][:, 0:65],
                               QTS[base:base + 64, hc * 2048 + u * 128: hc * 2048 + (u + 1) * 128],
                               SBF[base:base + 64, (h // 2) * 65:(h // 2 + 1) * 65],
                               start=False, stop=True),
                           deps=(state_rd.get(h),), inc="pe")
            atm_war[sl] = nmm
            ne = P.do("vector", lambda v, h=h, nbank=nbank: v.tensor_copy(
                out=NUMS[:, h * 65:(h + 1) * 65], in_=P.PS[nbank][:, 0:65]),
                deps=(nmm, num_evac_prev[h % 2], out_done))
            num_evac_prev[h % 2] = ne
            P.ps_free[nbank] = ne
            blk = (h // 2) * 65
            smm = P.do("tensor", lambda t_, h=h, vsl=vsl, base=base, sbank=sbank:
                       t_.matmul(
                P.PS[sbank][base:base + 64, 0:65],
                KTOK1[:, 64 * h:64 * h + 64], vsl,
                start=True, stop=True, tile_position=(0, base)),
                deps=(ktk_t[u], P.ps_free[sbank]), inc="pe")
            f1 = P.do("vector", lambda v, base=base, blk=blk, sbank=sbank: v.tensor_tensor(
                out=SSB[base:base + 64, blk:blk + 65],
                in0=P.PS[sbank][base:base + 64, 0:65],
                in1=SSB[base:base + 64, blk:blk + 65], op=ALU.add),
                deps=(smm, sz))
            P.ps_free[sbank] = f1
            state_rd[h] = P.do("vector", lambda v, base=base, blk=blk: v.tensor_copy(
                out=SBF[base:base + 64, blk:blk + 65],
                in_=SSB[base:base + 64, blk:blk + 65]),
                deps=(f1, szb, nmm))
            P._l0_kt_war = amm_t[(u, h)]
            P._l0_ktok_war = smm

        def epilogue(u):
            nonlocal out_done
            d1 = P.do("vector", lambda v: v.tensor_scalar(
                out=DEN[:, :],
                in0=NUMS[:, :].rearrange("p (h f) -> p h f", f=65)[:, :, 64:65].squeeze(2),
                scalar1=1e-6, scalar2=None, op0=ALU.add),
                deps=(num_evac_prev[0], num_evac_prev[1]))
            d2 = P.do("vector", lambda v: v.reciprocal(out=RECS[:, :], in_=DEN[:, :]),
                      deps=(d1,))
            o1 = P.do("vector", lambda v: v.tensor_tensor(
                out=OUT1[:, :].rearrange("p (h f) -> p h f", f=64),
                in0=NUMS[:, :].rearrange("p (h f) -> p h f", f=65)[:, :, 0:64],
                in1=RECS[:, :].unsqueeze(2).broadcast_to([128, 16, 64]),
                op=ALU.mult), deps=(d2, out_done))
            t_glob = s * TPS + u
            rt = r_tile(P, t_glob)
            xt_tok = emit_transpose_1024(P, OUT1, P.XT, (o1,), pt_bank=u % 2,
                                         war_dep=getattr(P, "_l0_xt_war", None))
            for bi in range(2):
                P.do("tensor", lambda t_, bi=bi: t_.matmul(
                    P.PS[bi][:, 0:512], ONESB[0:1, :],
                    BOROW[0:1, bi * 512:(bi + 1) * 512], start=True, stop=False),
                    deps=(P.ps_free[bi], bo_tok, o1b))
            mmo = emit_proj_1024(P, P.XT, W1.sb, wo_tok, out_banks=[0, 1],
                                 fout=1024, extra_deps=(xt_tok,), accumulate=True)
            P._l0_xt_war = mmo[-1]
            for bi in range(2):
                lo, hi = bi * 512, (bi + 1) * 512
                out_done = P.do("vector", lambda v, bi=bi, lo=lo, hi=hi, rt=rt:
                                v.tensor_tensor(out=rt[:, lo:hi], in0=P.PS[bi][:, 0:512],
                                                in1=rt[:, lo:hi], op=ALU.add),
                                deps=(mmo[bi],))
                P.ps_free[bi] = out_done

        IT0 = [(u, h) for u in range(TPS) for h in range(16)]
        N0 = len(IT0)
        prelude(0)
        for j in range(2):
            sc_s1(*IT0[j])
            sc_s2(*IT0[j])
        for i in range(N0):
            u, h = IT0[i]
            sc_s3(u, h)
            if h == 15 and u + 1 < TPS:
                prelude(u + 1)
            if i + 2 < N0:
                un, hn = IT0[i + 2]
                sc_s1(un, hn)
                sc_s2(un, hn)
            if h == 15:
                epilogue(u)
        P._l0_atm_war = atm_war
        P._l0_state_rd = state_rd
        P._l0_vs_war = P.fence("tensor")
        P.barrier()
    P.barrier()



LN16 = 2.772588722239781
L1_LOCAL_STAB = False


def emit_L1_attn(P, ctx, aps):
    nc = P.nc
    sb = lambda name, shape, dt: ctx.enter_context(nc.sbuf_tensor(P.tag + name, shape, dt))
    MRS = sb("l1a_mrs", [128, 2 * NTILE], F32)
    W1 = WBuf(P, ctx, "l1a_w1", 1024, 1024)
    QTS = sb("l1a_qts", [128, 8 * 2048], BF16)
    KTS = sb("l1a_kts", [128, 8 * 2048], BF16)
    VSAUG = sb("l1a_vsaug", [128, 16 * 16 * 65], BF16)
    DIAGQ = sb("l1a_diagq", [128, 16 * 16], F32)
    DIAGK = sb("l1a_diagk", [128, 16 * 16], F32)
    SQS = sb("l1a_sqs", [128, 1024], F32)
    PROJ0 = sb("l1a_proj0", [128, 2 * 64], BF16)
    PROJT = sb("l1a_projt", [128, 256], BF16)
    MAXT = sb("l1a_maxt", [128, 256], F32)
    MAX1 = sb("l1a_max1", [128, 1], F32)
    MROW = sb("l1a_mrow", [1, 128], F32)
    STABR = sb("l1a_stabr", [1, 8], F32)
    NEGST = sb("l1a_negst", [128, 1], F32)
    CTXS = sb("l1a_ctxs", [128, 16 * 130], BF16)
    CTXT = sb("l1a_ctxt", [65, 256], BF16)
    MX = sb("l1a_mx", [128, 4], F32)
    NDS = sb("l1a_nds", [128, 4], F32)
    QPB = sb("l1a_qpb", [128, 1024], BF16)
    QPT = sb("l1a_qpt", [128, 1024], BF16)
    NUMS = sb("l1a_nums", [128, 16 * 65], BF16)
    DEN = sb("l1a_den", [128, 16], F32)
    RECS = sb("l1a_recs", [128, 16], F32)
    OUT1 = sb("l1a_out1", [128, 1024], BF16)
    BOROW = sb("l1a_borow", [1, 1024], BF16)
    ONESB = sb("l1a_onesb", [1, 128], BF16)
    IDNF = sb("l1a_idnf", [128, 128], F32)
    stab_loc = nc.dram_tensor(P.tag + "l1a_stab_loc", [1, 8], F32, kind="Internal").ap()
    stab_out = nc.dram_tensor(P.tag + "l1a_stab_out", [1, 8], F32, kind="Internal").ap()

    o1b = P.do("vector", lambda v: v.memset(ONESB[0:1, :], 1.0))
    idnf_tok = P.dma("sync", lambda s_: s_.dma_start(out=IDNF[:, :], in_=aps["idn"]))
    bo_tok = P.dma("gpsimd", lambda g: g.dma_start(out=BOROW[0:1, :], in_=aps["bo"]))
    pj_tok = P.dma("gpsimd", lambda g: g.dma_start(
        out=PROJ0[:, :].rearrange("p (c f) -> p c f", f=64),
        in_=aps["proj"].rearrange("(c p) f -> p c f", p=128)))
    for c in range(2):
        trp = P.do("tensor", lambda t, c=c: t.transpose(
            P.PT[0][0:64, c * 128:c * 128 + 128], PROJ0[:, c * 64:(c + 1) * 64],
            P.IDN[:]),
            deps=(pj_tok, P.pt_free[0], P.idn_tok) if c == 0 else ())
        trp = P.do("tensor", lambda t, c=c: t.transpose(
            P.PT[0][64:128, c * 128 + 512:c * 128 + 640],
            PROJ0[:, c * 64:(c + 1) * 64], P.IDN[:], tile_position=(0, 64)),
            deps=(), inc="pe" if c == 1 else None)
    pjt_tok0 = P.do("vector", lambda v: v.tensor_scalar(
        out=PROJT[0:64, :].rearrange("p (c n) -> p c n", n=128),
        in0=P.PT[0][0:64, 0:256].rearrange("p (c n) -> p c n", n=128),
        scalar1=0.35355339059327373, scalar2=None, op0=ALU.mult), deps=(trp,))
    pjt_tok = P.do("vector", lambda v: v.tensor_scalar(
        out=PROJT[64:128, :].rearrange("p (c n) -> p c n", n=128),
        in0=P.PT[0][64:128, 512:768].rearrange("p (c n) -> p c n", n=128),
        scalar1=0.35355339059327373, scalar2=None, op0=ALU.mult), deps=(pjt_tok0,))
    P.pt_free[0] = pjt_tok

    xn_war = None
    for t in range(NTILE):
        xn_tok = emit_ln_tile(P, t, xn_war=xn_war)
        xn_war = xn_tok
        P.do("vector", lambda v, t=t: v.tensor_copy(
            out=MRS[:, 2 * t:2 * t + 1], in_=P.NEGMEAN[:, 0:1]), deps=(xn_tok,))
        P.do("vector", lambda v, t=t: v.tensor_copy(
            out=MRS[:, 2 * t + 1:2 * t + 2], in_=P.RSTD[:, 0:1]))
    P.barrier()

    def sweep(s, which, store, diag_store):
        wname = ("wq", "wk", "wv")[which]
        w_war = P.fence("tensor")
        w_tok = P.dma("gpsimd", lambda g, wname=wname: g.dma_start(
            out=W1.sb[:, :].rearrange("p (c f) -> p c f", f=1024),
            in_=aps[wname].rearrange("(c p) f -> p c f", p=128)), deps=(w_war,))
        store_war = None
        dq_war = None

        def mk_l1s(tl, ap, war, s=s):
            t = s * TPS + tl
            return P.do("vector", lambda v, t=t, ap=ap: v.tensor_scalar(
                out=ap, in0=r_tile(P, t), scalar1=MRS[:, 2 * t:2 * t + 1],
                scalar2=MRS[:, 2 * t + 1:2 * t + 2], op0=ALU.add, op1=ALU.mult),
                deps=(war,) + tuple(P.r_toks))

        pipe = XnPipe(P, mk_l1s, TPS)
        for tl in range(TPS):
            t = s * TPS + tl
            xt_tok = pipe.xt_tok[tl]
            xnv = pipe.xn_ap(tl)
            mm = emit_proj_1024(P, pipe.xt_ap(tl), W1.sb, w_tok, out_banks=[0, 1],
                                fout=1024, extra_deps=(xt_tok,))
            pipe.done(tl, mm[-1])
            if store is VSAUG:
                for bi in range(2):
                    dstv = VSAUG[:, tl * 1040:(tl + 1) * 1040].rearrange(
                        "p (h f) -> p h f", f=65)[:, bi * 8:(bi + 1) * 8, 0:64]
                    e = P.do("vector", lambda v, bi=bi, dstv=dstv: v.tensor_copy(
                        out=dstv, in_=P.PS[bi][:, 0:512].rearrange(
                            "p (h f) -> p h f", f=64)),
                        deps=(mm[bi], P.ps_free[bi], P._l1_vs_ones))
                    P.ps_free[bi] = e
            else:
                evs = []
                for bi in range(2):
                    e = P.do("vector", lambda v, bi=bi, xnv=xnv: v.tensor_copy(
                        out=xnv[:, bi * 512:(bi + 1) * 512], in_=P.PS[bi][:, 0:512]),
                        deps=(mm[bi], P.ps_free[bi], xt_tok))
                    evs.append(e)
                    P.ps_free[bi] = e
                if diag_store is not None:
                    sq = P.do("scalar", lambda s_, xnv=xnv: s_.activation(
                        out=SQS[:, :], in_=xnv, func=AF.Square),
                        deps=(evs[-1], dq_war))
                    rd = P.do("vector", lambda v, tl=tl, diag_store=diag_store: v.tensor_reduce(
                        out=diag_store[:, tl * 16:(tl + 1) * 16],
                        in_=SQS[:, :].rearrange("p (h f) -> p h f", f=64),
                        axis=AX.X, op=ALU.add), deps=(sq,))
                    dq = P.do("vector", lambda v, tl=tl, diag_store=diag_store: v.tensor_scalar(
                        out=diag_store[:, tl * 16:(tl + 1) * 16],
                        in0=diag_store[:, tl * 16:(tl + 1) * 16],
                        scalar1=0.0625, scalar2=LN16, op0=ALU.mult, op1=ALU.add),
                        deps=(rd,))
                    dq_war = dq
                pt_b = tl % 2
                for c in range(8):
                    trq = P.do("tensor", lambda t_, c=c, pt_b=pt_b, xnv=xnv: t_.transpose(
                        P.PT[pt_b][:, c * 128:(c + 1) * 128],
                        xnv[:, c * 128:(c + 1) * 128], P.IDN[:]),
                        deps=(evs[-1], P.pt_free[pt_b], store_war) if c == 0 else (),
                        inc="pe" if c == 7 else None)
                dstv = store[:, :].rearrange(
                    "p (c n) -> p c n", n=2048)[:, :, tl * 128:(tl + 1) * 128]
                evt = P.do("vector", lambda v, dstv=dstv, pt_b=pt_b: v.tensor_copy(
                    out=dstv, in_=P.PT[pt_b][:, :].rearrange("p (c n) -> p c n", n=128)),
                    deps=(trq,))
                P.pt_free[pt_b] = evt
                store_war = evt
                pipe.set_xn_reader(tl, evt)
        P.barrier()
        return

    mz = P.do("vector", lambda v: v.memset(MAXT[:, :], -1e30))
    P._l1_vs_ones = None
    for s in range(2):
        sweep(s, 1, KTS, DIAGK)
        mprev = mz
        for h in range(16):
            hc, base = h // 2, 64 * (h % 2)
            for tl in range(TPS):
                bank = 2 + (tl % 2)
                dmm = P.do("tensor", lambda t_, hc=hc, base=base, tl=tl, bank=bank:
                           t_.matmul(P.PS[bank][:, 0:256],
                                     KTS[base:base + 64, hc * 2048 + tl * 128: hc * 2048 + (tl + 1) * 128],
                                     PROJT[base:base + 64, :], start=True, stop=True),
                           deps=(P.ps_free[bank], pjt_tok), inc="pe")
                mt = P.do("vector", lambda v, bank=bank: v.tensor_tensor(
                    out=MAXT[:, :], in0=P.PS[bank][:, 0:256], in1=MAXT[:, :],
                    op=ALU.max), deps=(dmm, mprev))
                mprev = mt
                P.ps_free[bank] = mt
        P._l1_mprev = mprev
        P.barrier()
    m1 = P.do("vector", lambda v: v.tensor_reduce(out=MAX1[:, 0:1], in_=MAXT[:, :],
                                                  axis=AX.X, op=ALU.max),
              deps=(P._l1_mprev,))
    trm = P.do("tensor", lambda t: t.transpose(P.PS[5][0:1, 0:128], MAX1[:, 0:1],
                                               IDNF[:, :]),
               deps=(m1, P.ps_free[5], idnf_tok), inc="pe")
    m2 = P.do("vector", lambda v: v.tensor_copy(out=MROW[0:1, :], in_=P.PS[5][0:1, 0:128]),
              deps=(trm,))
    P.ps_free[5] = m2
    m3 = P.do("vector", lambda v: v.memset(STABR[0:1, :], -1e30), deps=())
    m4 = P.do("vector", lambda v: v.tensor_reduce(out=STABR[0:1, 0:1], in_=MROW[0:1, :],
                                                  axis=AX.X, op=ALU.max), deps=(m2, m3))
    st_tok = P.dma("gpsimd", lambda g: g.dma_start(out=stab_loc, in_=STABR[0:1, :]),
                   deps=(m4,))
    if L1_LOCAL_STAB:
        cc_tok = st_tok
        src_stab = stab_loc
    else:
        cc_tok = P.do("gpsimd", lambda g: g.collective_compute(
            "AllReduce", ALU.max, replica_groups=[list(range(8))],
            ins=[stab_loc], outs=[stab_out]), deps=(st_tok,), inc="pool")
        src_stab = stab_out
    ld_tok = P.dma("gpsimd", lambda g: g.dma_start(out=STABR[0:1, :], in_=src_stab),
                   deps=(cc_tok,))
    bmm = P.do("tensor", lambda t: t.matmul(P.PS[0][:, 0:1], P.ONES1[0:1, :],
                                            STABR[0:1, 0:1], start=True, stop=True),
               deps=(ld_tok, P.ones_tok, P.ps_free[0]), inc="pe")
    nst = P.do("vector", lambda v: v.tensor_scalar(
        out=NEGST[:, 0:1], in0=P.PS[0][:, 0:1], scalar1=-1.0, scalar2=None,
        op0=ALU.mult), deps=(bmm,))
    P.ps_free[0] = nst

    for s in range(2):
        P._l1_vs_ones = P.do("vector", lambda v: v.memset(VSAUG[:, :], 1.0),
                             deps=(getattr(P, "_l1_vs_war", None),))
        sweep(s, 0, QTS, DIAGQ)
        sweep(s, 1, KTS, DIAGK)
        sweep(s, 2, VSAUG, None)
        ITD = [(h, tl) for h in range(16) for tl in range(TPS)]
        NKI = len(ITD)
        ctx_free = getattr(P, "_l1_ctx_free", {})
        nds_rd = getattr(P, "_l1_nds_rd", {})
        sqs_rd = getattr(P, "_l1_sqs_rd", {})
        qpb_rd = getattr(P, "_l1_qpb_rd", {})
        ctxt_war = getattr(P, "_l1_ctxt_war", None)
        dd_t, kp_t = {}, {}
        last_ex = last_cmm = cse = None

        def k_stage1(i):
            h, tl = ITD[i]
            hc, base = h // 2, 64 * (h % 2)
            bank = tl % 4
            dd_t[i] = P.do("tensor", lambda t_, hc=hc, base=base, tl=tl, bank=bank:
                           t_.matmul(P.PS[bank][:, 0:256],
                                     KTS[base:base + 64, hc * 2048 + tl * 128: hc * 2048 + (tl + 1) * 128],
                                     PROJT[base:base + 64, :], start=True, stop=True),
                           deps=(P.ps_free[bank], pjt_tok), inc="pe")

        def k_stage2(i):
            nonlocal last_ex
            h, tl = ITD[i]
            sl = tl % 4
            bank = sl
            nd = P.do("vector", lambda v, tl=tl, h=h, sl=sl: v.scalar_tensor_tensor(
                out=NDS[:, sl:sl + 1], in0=DIAGK[:, tl * 16 + h:tl * 16 + h + 1],
                scalar=-1.0, in1=NEGST[:, 0:1], op0=ALU.mult, op1=ALU.add),
                deps=(nst, nds_rd.get(sl)))
            ex = P.do("scalar", lambda s_, bank=bank, sl=sl: s_.activation(
                out=SQS[:, sl * 256:sl * 256 + 256], in_=P.PS[bank][:, 0:256],
                func=AF.Exp, bias=NDS[:, sl:sl + 1], scale=IM(1.0)),
                deps=(dd_t[i], nd, sqs_rd.get(sl)))
            nds_rd[sl] = ex
            P.ps_free[bank] = ex
            last_ex = ex
            kp_t[i] = P.do("vector", lambda v, sl=sl: v.tensor_scalar(
                out=QPB[:, sl * 256:sl * 256 + 256],
                in0=SQS[:, sl * 256:sl * 256 + 256], scalar1=6.25e-6, scalar2=None,
                op0=ALU.add), deps=(ex, qpb_rd.get(sl)))
            sqs_rd[sl] = kp_t[i]

        def k_stage3(i):
            nonlocal last_cmm, cse, ctxt_war
            h, tl = ITD[i]
            sl = tl % 4
            cb = 4 + (h % 2)
            deps = [kp_t[i]]
            if tl == 0:
                deps += [ctx_free.get(cb), P._l1_vs_ones]
            cmm = P.do("tensor", lambda t_, tl=tl, h=h, sl=sl, cb=cb: t_.matmul(
                P.PS[cb][0:65, 0:256],
                VSAUG[:, (tl * 16 + h) * 65:(tl * 16 + h) * 65 + 65],
                QPB[:, sl * 256:sl * 256 + 256],
                start=(tl == 0), stop=(tl == TPS - 1)),
                deps=deps, inc="pe")
            qpb_rd[sl] = cmm
            last_cmm = cmm
            if tl == TPS - 1:
                ce = P.do("vector", lambda v, cb=cb: v.tensor_copy(
                    out=CTXT[0:65, :], in_=P.PS[cb][0:65, 0:256]),
                    deps=(cmm, ctxt_war))
                P.ps_free[cb] = ce
                for c in range(2):
                    trc = P.do("tensor", lambda t_, c=c: t_.transpose(
                        P.PT[0][0:128, c * 128:c * 128 + 65],
                        CTXT[0:65, c * 128:(c + 1) * 128], P.IDN[0:65, 0:65]),
                        deps=(ce, P.pt_free[0]) if c == 0 else (),
                        inc="pe" if c == 1 else None)
                ctxt_war = trc
                cse = P.do("vector", lambda v, h=h: v.tensor_copy(
                    out=CTXS[:, h * 130:(h + 1) * 130].rearrange("p (c f) -> p c f", f=65),
                    in_=P.PT[0][:, 0:256].rearrange("p (c f) -> p c f", f=128)[:, :, 0:65]),
                    deps=(trc,))
                P.pt_free[0] = cse
                ctx_free[cb] = cse

        for i in range(min(4, NKI)):
            k_stage1(i)
        k_stage2(0)
        k_stage2(1)
        for i in range(NKI):
            k_stage3(i)
            if i + 4 < NKI:
                k_stage1(i + 4)
            if i + 2 < NKI:
                k_stage2(i + 2)
        P._l1_nds_rd = nds_rd
        P._l1_sqs_rd = sqs_rd
        P._l1_qpb_rd = qpb_rd
        P._l1_ctx_free = ctx_free
        P._l1_ctxt_war = ctxt_war
        w_war = P.fence("tensor")
        wo_tok = P.dma("gpsimd", lambda g: g.dma_start(
            out=W1.sb[:, :].rearrange("p (c f) -> p c f", f=1024),
            in_=aps["wo"].rearrange("(c p) f -> p c f", p=128)), deps=(w_war,))
        num_prev = [None, None]
        out_done = None
        mx_rd, nds_rd2, qpb_rd2, qpt_rd = {}, {}, {}, {}
        for _sl in range(3):
            nds_rd2[_sl] = last_ex
            qpb_rd2[_sl] = last_cmm
            mx_rd[_sl] = cse
        dd_toks, qp_toks = {}, {}

        def q_stage1(tl, h):
            hc, base = h // 2, 64 * (h % 2)
            gh = tl * 16 + h
            sl = gh % 3
            bank = 2 + sl
            dd_toks[gh] = P.do("tensor", lambda t_, hc=hc, base=base, tl=tl, bank=bank:
                              t_.matmul(P.PS[bank][:, 0:256],
                                        QTS[base:base + 64, hc * 2048 + tl * 128: hc * 2048 + (tl + 1) * 128],
                                        PROJT[base:base + 64, :], start=True, stop=True),
                              deps=(P.ps_free[bank],), inc="pe")

        def q_stage2(tl, h):
            gh = tl * 16 + h
            sl = gh % 3
            bank = 2 + sl
            dmm = dd_toks[gh]
            mx = P.do("vector", lambda v, bank=bank, sl=sl: v.tensor_reduce(
                out=MX[:, sl:sl + 1], in_=P.PS[bank][:, 0:256], axis=AX.X, op=ALU.max),
                deps=(dmm, mx_rd.get(sl)))
            nd = P.do("vector", lambda v, tl=tl, h=h, sl=sl: v.scalar_tensor_tensor(
                out=NDS[:, sl:sl + 1], in0=MX[:, sl:sl + 1], scalar=-1.0,
                in1=DIAGQ[:, tl * 16 + h:tl * 16 + h + 1],
                op0=ALU.mult, op1=ALU.subtract),
                deps=(mx, nds_rd2.get(sl)))
            ex = P.do("scalar", lambda s_, bank=bank, sl=sl: s_.activation(
                out=SQS[:, sl * 256:sl * 256 + 256], in_=P.PS[bank][:, 0:256],
                func=AF.Exp, bias=NDS[:, sl:sl + 1], scale=IM(1.0)),
                deps=(nd, qpb_rd2.get(sl)))
            nds_rd2[sl] = ex
            mx_rd[sl] = ex
            qp_toks[gh] = P.do("scalar", lambda s_, sl=sl: s_.add(
                out=QPB[:, sl * 256:sl * 256 + 256],
                in_=SQS[:, sl * 256:sl * 256 + 256], add=IM(6.25e-6)), deps=(ex,))
            P.ps_free[bank] = qp_toks[h]

        def q_stage3(tl, h, nxt):
            nonlocal out_done
            gh = tl * 16 + h
            sl = gh % 3
            nbank = gh % 2
            qp = qp_toks[gh]
            pt_b = gh % 2
            for c in range(2):
                trq = P.do("tensor", lambda t_, c=c, pt_b=pt_b, sl=sl: t_.transpose(
                    P.PT[pt_b][:, c * 128:(c + 1) * 128],
                    QPB[:, sl * 256 + c * 128:sl * 256 + (c + 1) * 128], P.IDN[:]),
                    deps=(qp, P.pt_free[pt_b]) if c == 0 else (),
                    inc="pe" if c == 1 else None)
            qpb_rd2[sl] = trq
            qpt = P.do("vector", lambda v, pt_b=pt_b, sl=sl: v.tensor_copy(
                out=QPT[:, sl * 256:sl * 256 + 256], in_=P.PT[pt_b][:, 0:256]),
                deps=(trq, qpt_rd.get(sl)))
            P.pt_free[pt_b] = qpt
            if nxt is not None:
                q_stage1(*nxt)
            nmm = None
            for c in range(2):
                deps = [qpt]
                if c == 0:
                    deps += [num_prev[nbank], cse]
                nmm = P.do("tensor", lambda t_, c=c, h=h, nbank=nbank, sl=sl: t_.matmul(
                    P.PS[nbank][:, 0:65], QPT[:, sl * 256 + c * 128:sl * 256 + (c + 1) * 128],
                    CTXS[:, h * 130 + c * 65: h * 130 + (c + 1) * 65],
                    start=(c == 0), stop=(c == 1)),
                    deps=deps, inc="pe" if c == 1 else None)
            qpt_rd[sl] = nmm
            ne = P.do("vector", lambda v, h=h, nbank=nbank: v.tensor_copy(
                out=NUMS[:, h * 65:(h + 1) * 65], in_=P.PS[nbank][0:128, 0:65]),
                deps=(nmm, out_done))
            num_prev[nbank] = ne
            P.ps_free[nbank] = ne

        DEPTH = 3
        for tl in range(TPS):
            for h in range(DEPTH if tl == 0 else 0):
                q_stage1(tl, h)
                q_stage2(tl, h)
            for h in range(16):
                nh = h + DEPTH
                nxt = None
                if nh < 16:
                    nxt = (tl, nh)
                elif tl + 1 < TPS:
                    nxt = (tl + 1, nh - 16)
                q_stage3(tl, h, nxt)
                if nxt is not None:
                    q_stage2(*nxt)
            d1 = P.do("vector", lambda v: v.tensor_copy(
                out=DEN[:, :],
                in_=NUMS[:, :].rearrange("p (h f) -> p h f", f=65)[:, :, 64:65].squeeze(2)),
                deps=(num_prev[0], num_prev[1]))
            d2 = P.do("vector", lambda v: v.reciprocal(out=RECS[:, :], in_=DEN[:, :]),
                      deps=(d1,))
            o1 = P.do("vector", lambda v: v.tensor_tensor(
                out=OUT1[:, :].rearrange("p (h f) -> p h f", f=64),
                in0=NUMS[:, :].rearrange("p (h f) -> p h f", f=65)[:, :, 0:64],
                in1=RECS[:, :].unsqueeze(2).broadcast_to([128, 16, 64]),
                op=ALU.mult), deps=(d2, out_done))
            t_glob = s * TPS + tl
            rt = r_tile(P, t_glob)
            xt_tok = emit_transpose_1024(P, OUT1, P.XT, (o1,), pt_bank=tl % 2,
                                         war_dep=getattr(P, "_l1_xt_war", None))
            for bi in range(2):
                P.do("tensor", lambda t_, bi=bi: t_.matmul(
                    P.PS[bi][:, 0:512], ONESB[0:1, :],
                    BOROW[0:1, bi * 512:(bi + 1) * 512], start=True, stop=False),
                    deps=(P.ps_free[bi], bo_tok, o1b))
            mmo = emit_proj_1024(P, P.XT, W1.sb, wo_tok, out_banks=[0, 1],
                                 fout=1024, extra_deps=(xt_tok,), accumulate=True)
            P._l1_xt_war = mmo[-1]
            for bi in range(2):
                lo, hi = bi * 512, (bi + 1) * 512
                out_done = P.do("vector", lambda v, bi=bi, lo=lo, hi=hi, rt=rt:
                                v.tensor_tensor(out=rt[:, lo:hi], in0=P.PS[bi][:, 0:512],
                                                in1=rt[:, lo:hi], op=ALU.add),
                                deps=(mmo[bi],))
                P.ps_free[bi] = out_done
            num_prev = [out_done, out_done]
        P._l1_vs_war = P.fence("tensor")
        P.barrier()
    P.barrier()



def _declare_io(nc, names_shapes, out_shape):
    aps = {}
    for name, shape in names_shapes.items():
        aps[name] = nc.dram_tensor(name, list(shape), F32, kind="ExternalInput").ap()
    y = nc.dram_tensor("y", list(out_shape), F32, kind="ExternalOutput").ap()
    return aps, y


def _idn():
    return np.eye(128, dtype=np.float32)


def _mtc():
    return np.triu(np.ones((128, 128), np.float32))


def _selc():
    sel = np.zeros((128, 16 * 64), np.float32)
    for tl in range(16):
        for j in range(4):
            sel[32 * j, tl * 64 + 4 * tl + j] = 1.0
    return sel


SPEC = {
    "L2_ffn": dict(
        shapes={"W": (1024, 1024), "b": (1, 1024), "lora_down": (4, 100),
                "lora_up": (50, 4), "down_aux": (100, 1024), "up_aux": (1024, 50)},
        emit=emit_L2_ffn,
        in_map=lambda inp: dict(
            W=inp["L2_ffn_W"], b=np.asarray(inp["L2_ffn_b"]).reshape(1, 1024),
            lora_down=inp["L2_ffn_lora_down"], lora_up=inp["L2_ffn_lora_up"],
            down_aux=inp["L2_ffn_down_aux"], up_aux=inp["L2_ffn_up_aux"]),
    ),
    "L3_ffn": dict(
        shapes={"weight": (1024, 1024)},
        emit=emit_L3_ffn,
        in_map=lambda inp: dict(weight=inp["L3_ffn_weight"]),
    ),
    "L0_ffn": dict(
        shapes={"router": (1024, 8), "experts_gate": (8, 1024, 512),
                "experts_up": (8, 1024, 512), "experts_down": (8, 512, 1024),
                "shared_gate": (1024, 1024), "shared_up": (1024, 1024),
                "shared_down": (1024, 1024)},
        emit=emit_L0_ffn,
        in_map=lambda inp: dict(
            router=inp["L0_ffn_router"], experts_gate=inp["L0_ffn_experts_gate"],
            experts_up=inp["L0_ffn_experts_up"], experts_down=inp["L0_ffn_experts_down"],
            shared_gate=inp["L0_ffn_shared_gate"], shared_up=inp["L0_ffn_shared_up"],
            shared_down=inp["L0_ffn_shared_down"]),
    ),
    "L3_attn": dict(
        shapes={"Wq": (1024, 256), "bq": (1, 256), "Wqc": (1024, 256), "bqc": (1, 256),
                "Wk": (1024, 256), "bk": (1, 256), "Wkc": (1024, 256), "bkc": (1, 256),
                "Wv": (1024, 1024), "bv": (1, 1024), "Wvc": (1024, 1024), "bvc": (1, 1024)},
        emit=emit_L3_attn,
        in_map=lambda inp: dict(
            Wq=inp["L3_attn_Wq"], bq=np.asarray(inp["L3_attn_bq"]).reshape(1, 256),
            Wqc=inp["L3_attn_Wqc"], bqc=np.asarray(inp["L3_attn_bqc"]).reshape(1, 256),
            Wk=inp["L3_attn_Wk"], bk=np.asarray(inp["L3_attn_bk"]).reshape(1, 256),
            Wkc=inp["L3_attn_Wkc"], bkc=np.asarray(inp["L3_attn_bkc"]).reshape(1, 256),
            Wv=inp["L3_attn_Wv"], bv=np.asarray(inp["L3_attn_bv"]).reshape(1, 1024),
            Wvc=inp["L3_attn_Wvc"], bvc=np.asarray(inp["L3_attn_bvc"]).reshape(1, 1024)),
    ),
    "L2_attn": dict(
        shapes={"W_qkv": (1024, 3072), "W_gate": (1024, 1), "b_gate": (1, 1),
                "W_out": (1024, 1024), "b_out": (1, 1024), "selc": (128, 1024)},
        emit=emit_L2_attn,
        in_map=lambda inp: dict(
            W_qkv=inp["L2_attn_W_qkv"], W_gate=inp["L2_attn_W_gate"],
            b_gate=np.asarray(inp["L2_attn_b_gate"]).reshape(1, 1),
            W_out=inp["L2_attn_W_out"],
            b_out=np.asarray(inp["L2_attn_b_out"]).reshape(1, 1024),
            selc=_selc()),
    ),
    "L0_attn": dict(
        shapes={"W_qkv": (1024, 3072), "b_qkv": (1, 3072), "W_out": (1024, 1024),
                "b_out": (1, 1024), "mtc": (128, 128)},
        emit=emit_L0_attn,
        in_map=lambda inp: dict(
            W_qkv=inp["L0_attn_W_qkv"],
            b_qkv=np.asarray(inp["L0_attn_b_qkv"]).reshape(1, 3072),
            W_out=inp["L0_attn_W_out"],
            b_out=np.asarray(inp["L0_attn_b_out"]).reshape(1, 1024),
            mtc=_mtc()),
    ),
    "L1_attn": dict(
        shapes={"wq": (1024, 1024), "wk": (1024, 1024), "wv": (1024, 1024),
                "wo": (1024, 1024), "bo": (1, 1024), "proj": (256, 64)},
        emit=emit_L1_attn,
        in_map=lambda inp: dict(
            wq=inp["L1_attn_wq"], wk=inp["L1_attn_wk"], wv=inp["L1_attn_wv"],
            wo=inp["L1_attn_wo"], bo=np.asarray(inp["L1_attn_bo"]).reshape(1, 1024),
            proj=inp["L1_attn_proj"]),
    ),
    "L1_ffn": dict(
        shapes={"Win": (1024, 64), "bin_": (1, 64), "W1": (1, 1024), "b1": (1, 1024),
                "W2": (1, 1024), "b2": (1, 64), "Wout": (64, 1024), "bout": (1, 1024)},
        emit=emit_L1_ffn,
        in_map=lambda inp: dict(
            Win=inp["L1_ffn_Win"], bin_=np.asarray(inp["L1_ffn_bin_"]).reshape(1, 64),
            W1=np.asarray(inp["L1_ffn_W1"]).reshape(1, 1024),
            b1=np.asarray(inp["L1_ffn_b1"]).reshape(1, 1024),
            W2=np.asarray(inp["L1_ffn_W2"]).reshape(1, 1024),
            b2=np.asarray(inp["L1_ffn_b2"]).reshape(1, 64),
            Wout=inp["L1_ffn_Wout"],
            bout=np.asarray(inp["L1_ffn_bout"]).reshape(1, 1024)),
    ),
}

CHAIN = ["L0_ffn", "L0_attn", "L0_ffn",
         "L1_ffn", "L1_attn", "L1_ffn",
         "L2_ffn", "L2_attn", "L2_ffn",
         "L3_ffn", "L3_attn", "L3_ffn"]

_launch_cache = {}


def build_launch(sub):
    if sub in _launch_cache:
        return _launch_cache[sub]
    spec = SPEC[sub]
    nc = bass.Bass("TRN2", target_bir_lowering=False)
    shapes = {"x": (2, 2048, 1024), "idn": (128, 128)}
    shapes.update(spec["shapes"])
    aps, y = _declare_io(nc, shapes, (2, 2048, 1024))
    ctx = ExitStack()
    with ctx:
        P = Prog(nc)
        alloc_common(P, ctx, evs=(sub not in ("L3_attn", "L2_attn", "L0_attn")))
        P.idn_tok = P.dma("gpsimd", lambda g: g.dma_start(out=P.IDN[:], in_=aps["idn"]))
        load_residual(P, aps["x"])
        spec["emit"](P, ctx, aps)
        done = P.fence("vector")
        store_residual(P, y, deps=(done,))
        for s_ in ST_POOL:
            if P.cnt[s_] > 0:
                P.wait("gpsimd", (s_, P.cnt[s_]))
        P.do("gpsimd", lambda g: g.nop())
        P.build(ctx)
    _launch_cache[sub] = nc
    return nc


def build_fused():
    nc = bass.Bass("TRN2", target_bir_lowering=False)
    shapes = {"x": (2, 2048, 1024), "idn": (128, 128)}
    for sub, spec in SPEC.items():
        for nm, shp in spec["shapes"].items():
            shapes[f"{sub}_{nm}"] = shp
    aps, y = _declare_io(nc, shapes, (2, 2048, 1024))
    ctx = ExitStack()
    with ctx:
        P = Prog(nc)
        alloc_common(P, ctx, evs=True)
        P.idn_tok = P.dma("gpsimd", lambda g: g.dma_start(out=P.IDN[:], in_=aps["idn"]))
        load_residual(P, aps["x"])
        for i, sub in enumerate(CHAIN):
            P.tag = f"s{i}_"
            sub_aps = {nm: aps[f"{sub}_{nm}"] for nm in SPEC[sub]["shapes"]}
            sub_aps["idn"] = aps["idn"]
            with ExitStack() as subctx:
                SPEC[sub]["emit"](P, subctx, sub_aps)
        P.tag = ""
        done = P.fence("vector")
        store_residual(P, y, deps=(done,))
        for s_ in ST_POOL:
            if P.cnt[s_] > 0:
                P.wait("gpsimd", (s_, P.cnt[s_]))
        P.do("gpsimd", lambda g: g.nop())
        P.build(ctx)
    return nc


def run_fused(inputs):
    x = np.ascontiguousarray(inputs["L0_attn_x"])
    nc = build_fused()
    base = dict(idn=_idn())
    for sub, spec in SPEC.items():
        for nm, v in spec["in_map"](inputs).items():
            base[f"{sub}_{nm}"] = np.ascontiguousarray(v)
    in_maps = [dict(x=np.ascontiguousarray(x[2 * c:2 * c + 2]), **base)
               for c in range(8)]
    res = run_bass_kernel_spmd(nc, in_maps, core_ids=list(range(8)))
    return np.concatenate([r["y"] for r in res.results], axis=0)


def run_chain(inputs, chain):
    x = np.ascontiguousarray(inputs["L0_attn_x"])
    idn = _idn()
    cur = [np.ascontiguousarray(x[2 * c:2 * c + 2]) for c in range(8)]
    for sub in chain:
        spec = SPEC[sub]
        nc = build_launch(sub)
        wmap = spec["in_map"](inputs)
        in_maps = [dict(x=cur[c], idn=idn, **wmap) for c in range(8)]
        res = run_bass_kernel_spmd(nc, in_maps, core_ids=list(range(8)))
        cur = [np.ascontiguousarray(res.results[c]["y"]) for c in range(8)]
    return np.concatenate(cur, axis=0)


FUSED = True


def kernel(**inputs):
    if FUSED:
        return run_fused(inputs)
    chain = [s for s in CHAIN if s in SPEC]
    return run_chain(inputs, chain)
